# version 03
# baseline (speedup 1.0000x reference)
import numpy as np
import ml_dtypes
import concourse.bass as bass
import concourse.mybir as mybir
from concourse.bass_utils import run_bass_kernel_spmd

F32 = mybir.dt.float32
BF16 = mybir.dt.bfloat16
AF = mybir.ActivationFunctionType
ALU = mybir.AluOpType
AX = mybir.AxisListType

S = 4096
D = 1024
NCH = 8
TT = 512
NTT = S // TT
BF = np.dtype(ml_dtypes.bfloat16)

COMPUTE_ENGINES = ("t", "a", "v", "g")
DMA_QUEUES = ("ds", "dg", "da", "dv")
ISSUER = {"ds": "s", "dg": "g", "da": "a", "dv": "v"}


class Seq:

    STRICT = False
    STRICT_DS = False
    STRICT_KEYS = ()

    def __init__(self, nc):
        self.nc = nc
        self.ops = []
        self.local_idx = []
        self.hazards = []
        self.counts = {k: 0 for k in COMPUTE_ENGINES + DMA_QUEUES}
        self.last = {}
        self.floor = []

    def emit(self, key, fn, after=(), hazard=False):
        deps = list(after) + self.floor
        if Seq.STRICT and self.ops:
            deps = deps + [len(self.ops) - 1]
            hazard = True
        if Seq.STRICT_DS and key in DMA_QUEUES and self.ops:
            deps = deps + [len(self.ops) - 1]
        if key in Seq.STRICT_KEYS and self.ops:
            deps = deps + [len(self.ops) - 1]
        for d in deps:
            assert 0 <= d < len(self.ops), f"dep {d} not yet emitted"
        self.ops.append((key, fn, tuple(deps)))
        self.local_idx.append(self.counts[key])
        self.hazards.append(bool(hazard))
        self.counts[key] += 1
        gid = len(self.ops) - 1
        self.last[key] = gid
        return gid

    def dma(self, queue, out, in_, after=()):
        assert queue in DMA_QUEUES

        def fn(eng, out=out, in_=in_):
            return eng.dma_start(out=out, in_=in_)
        return self.emit(queue, fn, after)

    def set_floor(self):
        self.floor = list(self.last.values())

    def finalize(self, block, sems, outputs_done=()):
        phys = {"t": [], "a": [], "v": [], "g": [], "s": []}
        for gid, (key, fn, deps) in enumerate(self.ops):
            eng = key if key in COMPUTE_ENGINES else ISSUER[key]
            phys[eng].append(gid)

        def target(gid):
            key = self.ops[gid][0]
            li = self.local_idx[gid]
            return (key, (li + 1) * (16 if key in DMA_QUEUES else 1))

        def mkbody(eng_name, gids):
            def body(eng):
                hw = {}
                for gid in gids:
                    key, fn, deps = self.ops[gid]
                    need = {}
                    for d in deps:
                        dkey = self.ops[d][0]
                        deng = dkey if dkey in COMPUTE_ENGINES else ISSUER[dkey]
                        if deng == eng_name and dkey not in DMA_QUEUES and not self.hazards[d]:
                            continue
                        k2, val = target(d)
                        if need.get(k2, 0) < val:
                            need[k2] = val
                    for k2, val in sorted(need.items()):
                        if hw.get(k2, 0) >= val:
                            continue
                        eng.wait_ge(sems[k2], val)
                        hw[k2] = val
                    ins = fn(eng)
                    assert ins is not None, f"op {gid} ({key}) returned no instruction"
                    ins.then_inc(sems[key], 16 if key in DMA_QUEUES else 1)
                if eng_name == "s":
                    need = {}
                    for d in outputs_done:
                        k2, val = target(d)
                        if need.get(k2, 0) < val:
                            need[k2] = val
                    for k2, val in sorted(need.items()):
                        eng.wait_ge(sems[k2], val)
            return body

        block.tensor(mkbody("t", phys["t"]))
        block.scalar(mkbody("a", phys["a"]))
        block.vector(mkbody("v", phys["v"]))
        block.gpsimd(mkbody("g", phys["g"]))
        block.sync(mkbody("s", phys["s"]))


def alloc_sems(nc, stack):
    return {k: stack.enter_context(nc.semaphore("sem_" + k))
            for k in COMPUTE_ENGINES + DMA_QUEUES}


class NCProxy:

    def __init__(self, nc, prefix, x_in_handle, x_out_handle):
        self._nc = nc
        self._pf = prefix
        self._xin = x_in_handle
        self._xout = x_out_handle

    def dram_tensor(self, name, *a, **k):
        if name == "x_in":
            return self._xin
        if name == "x_out":
            return self._xout
        return self._nc.dram_tensor(self._pf + name, *a, **k)

    def sbuf_tensor(self, name, *a, **k):
        return self._nc.sbuf_tensor(self._pf + name, *a, **k)

    def psum_tensor(self, name, *a, **k):
        return self._nc.psum_tensor(self._pf + name, *a, **k)

    def __getattr__(self, n):
        return getattr(self._nc, n)



def emit_ln(sq, nc, stack, x_hbm, xn_sb, ones128, ones1, cst, col_off=1):
    from contextlib import ExitStack
    xr = x_hbm.rearrange("(c p) t -> p c t", p=128)
    xt = stack.enter_context(nc.sbuf_tensor("ln_x", [128, 2, NCH, TT], F32))
    sqt = stack.enter_context(nc.sbuf_tensor("ln_q", [128, NCH, TT], F32))
    t1 = stack.enter_context(nc.sbuf_tensor("ln_1", [128, TT], F32))
    t2 = stack.enter_context(nc.sbuf_tensor("ln_2", [128, TT], F32))
    ab = stack.enter_context(nc.sbuf_tensor("ln_ab", [1, 2 * TT], F32))
    ps = stack.enter_context(nc.psum_tensor("ln_ps", [1, 2 * TT], F32))
    bc = stack.enter_context(nc.psum_tensor("ln_bc", [128, 2 * TT], F32))

    done = []
    tile_user = [None, None]
    for i in range(NTT):
        dep = [] if tile_user[i % 2] is None else [tile_user[i % 2]]
        ld = sq.dma("ds", xt[:, i % 2], xr[:, :, i * TT:(i + 1) * TT], after=dep)

        def tree(v, i=i):
            for cc in range(NCH):
                v.tensor_tensor(out=sqt[:, cc], in0=xt[:, i % 2, cc], in1=xt[:, i % 2, cc], op=ALU.mult)
            v.tensor_tensor(out=t1[:], in0=xt[:, i % 2, 0], in1=xt[:, i % 2, 1], op=ALU.add)
            for cc in range(2, NCH):
                v.tensor_tensor(out=t1[:], in0=t1[:], in1=xt[:, i % 2, cc], op=ALU.add)
            v.tensor_tensor(out=t2[:], in0=sqt[:, 0], in1=sqt[:, 1], op=ALU.add)
            ins = None
            for cc in range(2, NCH):
                ins = v.tensor_tensor(out=t2[:], in0=t2[:], in1=sqt[:, cc], op=ALU.add)
            return ins
        otree = sq.emit("v", tree, after=[ld])

        def psums(t):
            t.matmul(ps[0:1, 0:TT], ones128[:], t1[:], start=True, stop=True)
            return t.matmul(ps[0:1, TT:2 * TT], ones128[:], t2[:], start=True, stop=True)
        opsum = sq.emit("t", psums, after=[otree])

        def stats(a):
            a.activation(out=ab[0:1, 0:TT], in_=ps[0:1, 0:TT], func=AF.Copy, scale=1.0 / D)
            return a.activation(out=ab[0:1, TT:2 * TT], in_=ps[0:1, TT:2 * TT], func=AF.Copy, scale=1.0 / D)
        omean = sq.emit("a", stats, after=[opsum])

        def var(v):
            v.tensor_tensor(out=t2[0:1, 0:TT], in0=ab[0:1, 0:TT], in1=ab[0:1, 0:TT], op=ALU.mult)
            return v.tensor_tensor(out=t2[0:1, 0:TT], in0=ab[0:1, TT:2 * TT], in1=t2[0:1, 0:TT], op=ALU.subtract)
        ovar = sq.emit("v", var, after=[omean])

        def rstd(a):
            a.activation(out=sqt[0:1, 0, 0:TT], in_=t2[0:1, 0:TT], func=AF.Ln, bias=cst[0:1, 0:1])
            return a.activation(out=ab[0:1, TT:2 * TT], in_=sqt[0:1, 0, 0:TT], func=AF.Exp, scale=-0.5)
        orstd = sq.emit("a", rstd, after=[ovar])

        def nab(v):
            v.tensor_tensor(out=ab[0:1, 0:TT], in0=ab[0:1, 0:TT], in1=ab[0:1, TT:2 * TT], op=ALU.mult)
            return v.tensor_scalar(out=ab[0:1, 0:TT], in0=ab[0:1, 0:TT], scalar1=-1.0, scalar2=None, op0=ALU.mult)
        onab = sq.emit("v", nab, after=[orstd])

        def bcast(t):
            t.matmul(bc[:, 0:TT], ones1[:], ab[0:1, 0:TT], start=True, stop=True)
            return t.matmul(bc[:, TT:2 * TT], ones1[:], ab[0:1, TT:2 * TT], start=True, stop=True)
        obc = sq.emit("t", bcast, after=[onab])

        def apply(v, i=i):
            ins = None
            for cc in range(NCH):
                v.tensor_tensor(out=sqt[:, cc], in0=xt[:, i % 2, cc], in1=bc[:, TT:2 * TT], op=ALU.mult)
                ins = v.tensor_tensor(out=xn_sb[:, cc, col_off + i * TT:col_off + (i + 1) * TT],
                                      in0=sqt[:, cc], in1=bc[:, 0:TT], op=ALU.add)
            return ins
        oap = sq.emit("v", apply, after=[obc])
        tile_user[i % 2] = oap
        done.append(oap)
    return done



WSTRICT = True
L2SUB = 9
L0SUB = 9


def build_l3(debug=False, ctx=None):
    from contextlib import ExitStack
    nc = bass.Bass("TRN2", target_bir_lowering=False, debug=False) if ctx is None else ctx["nc"]
    x_in = nc.dram_tensor("x_in", [D, S], F32, kind="ExternalInput").ap()
    wkT = nc.dram_tensor("wkT", [D, D], F32, kind="ExternalInput").ap()
    wvT = nc.dram_tensor("wvT", [D, D], F32, kind="ExternalInput").ap()
    wrT = nc.dram_tensor("wrT", [D, D], F32, kind="ExternalInput").ap()
    woT = nc.dram_tensor("woT", [D, D], F32, kind="ExternalInput").ap()
    w1T = nc.dram_tensor("w1T", [D, 4 * D], F32, kind="ExternalInput").ap()
    w2T = nc.dram_tensor("w2T", [4 * D, D], F32, kind="ExternalInput").ap()
    b1f = nc.dram_tensor("b1f", [128, 32], F32, kind="ExternalInput").ap()
    b2f = nc.dram_tensor("b2f", [128, 8], F32, kind="ExternalInput").ap()
    tdf = nc.dram_tensor("tdf", [128, 8], F32, kind="ExternalInput").ap()
    tff = nc.dram_tensor("tff", [128, 8], F32, kind="ExternalInput").ap()
    tmkf = nc.dram_tensor("tmkf", [128, 8], F32, kind="ExternalInput").ap()
    tmvf = nc.dram_tensor("tmvf", [128, 8], F32, kind="ExternalInput").ap()
    tmrf = nc.dram_tensor("tmrf", [128, 8], F32, kind="ExternalInput").ap()
    ones128_h = nc.dram_tensor("ones128_h", [128, 1], F32, kind="ExternalInput").ap()
    ones1_h = nc.dram_tensor("ones1_h", [1, 128], F32, kind="ExternalInput").ap()
    x_out = nc.dram_tensor("x_out", [D, S], F32, kind="ExternalOutput").ap()

    kind = "ExternalOutput" if debug else "Internal"
    w1t_h = nc.dram_tensor("w1t_h", [D, 4 * D], BF16, kind=kind).ap()
    w2t_h = nc.dram_tensor("w2t_h", [4 * D, D], BF16, kind=kind).ap()
    sry_h = nc.dram_tensor("sry_h", [D, S], BF16, kind=kind).ap()
    s_hbm = nc.dram_tensor("s_hbm", [D, S], BF16, kind=kind).ap()
    xn_h = nc.dram_tensor("xn_h", [D, S], BF16, kind=kind).ap() if debug else None
    thr_h = nc.dram_tensor("thr_h", [2, 128, 1], F32, kind=kind).ap() if debug else None
    g_h = nc.dram_tensor("g_h", [D, S], BF16, kind=kind).ap() if debug else None
    h_h = nc.dram_tensor("h_h", [4 * D, S], BF16, kind=kind).ap() if debug else None
    tred_h = nc.dram_tensor("tred_h", [2, 128, 1], F32, kind=kind).ap() if debug else None

    from contextlib import ExitStack
    stack = ExitStack()
    with stack:
        if ctx is None:
            sems = alloc_sems(nc, stack)
            block = stack.enter_context(nc.Block())
            sq = Seq(nc)
        else:
            sems, block, sq = ctx["sems"], ctx["block"], ctx["sq"]
        sb = lambda *a: stack.enter_context(nc.sbuf_tensor(*a))

        ones128 = sb("c_ones128", [128, 1], F32)
        ones1 = sb("c_ones1", [1, 128], F32)
        mixp = sb("mixp", [128, 12, 8], F32)
        cst = sb("cst", [128, 2], F32)
        xn = sb("xn", [128, NCH, S + 1], BF16)

        sq.dma("ds", ones128[:], ones128_h[:, :])
        sq.dma("ds", ones1[:], ones1_h[:, :])
        o_mx = []
        for col, src in [(0, tmkf), (2, tmvf), (4, tmrf), (6, tdf), (7, tff)]:
            o_mx.append(sq.dma("ds", mixp[:, col, :], src[:, :]))

        def premix(v):
            v.memset(cst[:, 0:1], 1e-5)
            for cc in range(NCH):
                v.memset(xn[:, cc, 0:1], 0.0)
            ins = None
            for src, dst in [(0, 1), (2, 3), (4, 5)]:
                ins = v.tensor_scalar(out=mixp[:, dst, :], in0=mixp[:, src, :], scalar1=-1.0, scalar2=1.0,
                                      op0=ALU.mult, op1=ALU.add)
            return ins
        o_premix = sq.emit("v", premix, after=o_mx)

        o_pp0 = sq.emit("a", lambda a: a.activation(out=mixp[:, 6, :], in_=mixp[:, 6, :], func=AF.Exp),
                        after=o_mx, hazard=True)
        o_pp1 = sq.emit("a", lambda a: a.activation(out=mixp[:, 6, :], in_=mixp[:, 6, :], func=AF.Exp, scale=-1.0),
                        after=[o_pp0], hazard=True)
        o_preparam = sq.emit("a", lambda a: a.activation(out=mixp[:, 7, :], in_=mixp[:, 7, :], func=AF.Exp),
                             after=[o_pp1], hazard=True)
        sq.set_floor()

        with ExitStack() as ph:
            ln_done = emit_ln(sq, nc, ph, x_in, xn, ones128, ones1, cst, col_off=1)
            sq.set_floor()

        if debug:
            xn_r = xn_h.rearrange("(c p) t -> p c t", p=128)
            for i in range(NTT):
                sq.dma("dg", xn_r[:, :, i * TT:(i + 1) * TT], xn[:, :, 1 + i * TT:1 + (i + 1) * TT],
                       after=[ln_done[i]])
            sq.set_floor()

        with ExitStack() as ph:
            psb = lambda *a: ph.enter_context(nc.sbuf_tensor(*a))
            ppt = lambda *a: ph.enter_context(nc.psum_tensor(*a))
            tw = psb("tw", [128, 2, 4096], F32)
            twq = psb("twq", [128, 4096], BF16)
            tred = psb("tred", [128, 33], F32)
            tth = psb("tth", [128, 1], F32)
            tsc = psb("tsc", [1, 1], F32)
            tps = ppt("tps", [128, 2], F32)

            for wi, (wh, wth, nrow, ncol) in enumerate([(w1T, w1t_h, D, 4 * D), (w2T, w2t_h, 4 * D, D)]):
                nrt = nrow // 128
                reds = []
                for r in range(nrt):
                    dep = [reds[r - 2]] if r >= 2 else []
                    ld = sq.dma("ds", tw[:, r % 2, 0:ncol], wh[r * 128:(r + 1) * 128, :], after=dep)

                    def red(v, r=r, ncol=ncol):
                        return v.tensor_reduce(out=tred[:, 1 + r:2 + r], in_=tw[:, r % 2, 0:ncol], axis=AX.X,
                                               op=ALU.add, apply_absolute_value=True)
                    ored = sq.emit("v", red, after=[ld], hazard=True)
                    reds.append(ored)

                def combine(v, nrt=nrt):
                    return v.tensor_reduce(out=tred[:, 0:1], in_=tred[:, 1:1 + nrt], axis=AX.X, op=ALU.add)
                reds.append(sq.emit("v", combine, after=reds[-3:], hazard=True))

                if debug:
                    sq.dma("dg", tred_h[wi], tred[:, 0:1], after=[reds[-1]])

                def colsum(t, wi=wi):
                    return t.matmul(tps[0:1, wi:wi + 1], ones128[:], tred[:, 0:1], start=True, stop=True)
                o_cs = sq.emit("t", colsum, after=[reds[-1]])

                def mkthr(a, wi=wi, nrow=nrow, ncol=ncol):
                    return a.activation(out=tsc[0:1, 0:1], in_=tps[0:1, wi:wi + 1], func=AF.Copy,
                                        scale=0.5 / (nrow * ncol))
                o_thr = sq.emit("a", mkthr, after=[o_cs])

                def bthr(t):
                    return t.matmul(tps[:, 0:1], ones1[:], tsc[0:1, 0:1], start=True, stop=True)
                o_bthr = sq.emit("t", bthr, after=[o_thr])

                def cpthr(a):
                    return a.copy(out=tth[:, 0:1], in_=tps[:, 0:1])
                o_cpt = sq.emit("a", cpthr, after=[o_bthr])
                if debug:
                    sq.dma("dg", thr_h[wi], tth[:, 0:1], after=[o_cpt])

                prev_q = None
                for r in range(nrt):
                    dep = [o_cpt] + ([prev_q] if prev_q is not None else [])
                    ld = sq.dma("ds", tw[:, r % 2, 0:ncol], wh[r * 128:(r + 1) * 128, :], after=dep)

                    def absop(a, r=r, ncol=ncol):
                        return a.activation(out=tw[:, (r + 1) % 2, 0:ncol], in_=tw[:, r % 2, 0:ncol], func=AF.Abs)
                    o_abs = sq.emit("a", absop, after=[ld])

                    def cmp(v, r=r, ncol=ncol):
                        return v.tensor_scalar(out=tw[:, (r + 1) % 2, 0:ncol], in0=tw[:, (r + 1) % 2, 0:ncol],
                                               scalar1=tth[:, 0:1], scalar2=None, op0=ALU.is_gt)
                    o_cmp = sq.emit("v", cmp, after=[o_abs])

                    def sgn(a, r=r, ncol=ncol):
                        return a.activation(out=twq[:, 0:ncol], in_=tw[:, r % 2, 0:ncol], func=AF.Sign)
                    o_sgn = sq.emit("a", sgn, after=[ld])

                    def mul(v, r=r, ncol=ncol):
                        return v.tensor_tensor(out=twq[:, 0:ncol], in0=twq[:, 0:ncol],
                                               in1=tw[:, (r + 1) % 2, 0:ncol], op=ALU.mult)
                    o_mul = sq.emit("v", mul, after=[o_cmp, o_sgn])
                    prev_q = sq.dma("dg", wth[r * 128:(r + 1) * 128, :], twq[:, 0:ncol], after=[o_mul])
                sq.set_floor()
            sq.set_floor()

        with ExitStack() as ph:
            psb = lambda *a: ph.enter_context(nc.sbuf_tensor(*a))
            ppt = lambda *a: ph.enter_context(nc.psum_tensor(*a))
            wk_sb = psb("wk_sb", [128, NCH, D], BF16)
            wv_sb = psb("wv_sb", [128, NCH, D], BF16)
            wr_sb = psb("wr_sb", [128, NCH, D], BF16)
            o_wk = sq.dma("dg", wk_sb[:], wkT.rearrange("(c p) f -> p c f", p=128))
            o_wv = sq.dma("dg", wv_sb[:], wvT.rearrange("(c p) f -> p c f", p=128))
            o_wr = sq.dma("dg", wr_sb[:], wrT.rearrange("(c p) f -> p c f", p=128))
            xk_sb = psb("xk_sb", [128, NCH, TT], BF16)
            xv_sb = psb("xv_sb", [128, NCH, TT], BF16)
            xr_sb = psb("xr_sb", [128, NCH, TT], BF16)
            mixtmp = psb("mixtmp", [128, TT], F32)
            kb = psb("kb", [128, TT], F32)
            vb = psb("vb", [128, TT], F32)
            rb = psb("rb", [128, TT], F32)
            e1 = psb("e1", [128, TT], F32)
            e2 = psb("e2", [128, TT], F32)
            e3 = psb("e3", [128, TT], F32)
            e4 = psb("e4", [128, TT], F32)
            Pt = psb("Pt", [128, NCH, TT + 1], F32)
            Qt = psb("Qt", [128, NCH, TT + 1], F32)
            sry_t = psb("sry_t", [128, NCH, TT], BF16)
            kps = ppt("kps", [128, 2, TT], F32)
            vps = ppt("vps", [128, 2, TT], F32)
            rps = ppt("rps", [128, 2, TT], F32)

            def zpq(v):
                ins = None
                for cc in range(NCH):
                    v.memset(Pt[:, cc, 0:1], 0.0)
                    ins = v.memset(Qt[:, cc, 0:1], 0.0)
                return ins
            sq.emit("v", zpq)

            prev_scan = [None] * NCH
            prev_evac = [None, None]
            prev_proj_last = None
            prev_store = None
            sry_r = sry_h.rearrange("(c p) t -> p c t", p=128)
            for i in range(NTT):
                def mix(v, i=i):
                    ins = None
                    for dc in range(NCH):
                        cur = xn[:, dc, 1 + i * TT:1 + (i + 1) * TT]
                        prv = xn[:, dc, i * TT:i * TT + TT]
                        for (pcol, dst) in [(0, xk_sb), (2, xv_sb), (4, xr_sb)]:
                            v.tensor_scalar(out=mixtmp[:], in0=prv, scalar1=mixp[:, pcol + 1, dc:dc + 1],
                                            scalar2=None, op0=ALU.mult)
                            ins = v.scalar_tensor_tensor(out=dst[:, dc], in0=cur, scalar=mixp[:, pcol, dc:dc + 1],
                                                         in1=mixtmp[:], op0=ALU.mult, op1=ALU.add)
                    return ins
                mdep = [ln_done[i]]
                if prev_proj_last is not None:
                    mdep.append(prev_proj_last)
                o_mix = sq.emit("v", mix, after=mdep)

                for cc in range(NCH):
                    pb = (i * NCH + cc) % 2

                    def proj(t, cc=cc, pb=pb):
                        for dc in range(NCH):
                            t.matmul(kps[:, pb], wk_sb[:, dc, cc * 128:(cc + 1) * 128], xk_sb[:, dc],
                                     start=(dc == 0), stop=(dc == NCH - 1))
                        for dc in range(NCH):
                            t.matmul(vps[:, pb], wv_sb[:, dc, cc * 128:(cc + 1) * 128], xv_sb[:, dc],
                                     start=(dc == 0), stop=(dc == NCH - 1))
                        ins = None
                        for dc in range(NCH):
                            ins = t.matmul(rps[:, pb], wr_sb[:, dc, cc * 128:(cc + 1) * 128], xr_sb[:, dc],
                                           start=(dc == 0), stop=(dc == NCH - 1))
                        return ins
                    pdep = [o_mix, o_wk, o_wv, o_wr]
                    if prev_evac[pb] is not None:
                        pdep.append(prev_evac[pb])
                    o_proj = sq.emit("t", proj, after=pdep)
                    prev_proj_last = o_proj

                    def evac(a, pb=pb):
                        a.copy(out=kb[:], in_=kps[:, pb])
                        a.copy(out=vb[:], in_=vps[:, pb])
                        return a.copy(out=rb[:], in_=rps[:, pb])
                    edep = [o_proj]
                    if prev_scan[(cc + NCH - 1) % NCH] is not None:
                        edep.append(prev_scan[(cc + NCH - 1) % NCH])
                    o_evac = sq.emit("a", evac, after=edep)
                    prev_evac[pb] = o_evac

                    def exps(a):
                        a.activation(out=e1[:], in_=kb[:], func=AF.Exp)
                        return a.activation(out=rb[:], in_=rb[:], func=AF.Sigmoid)
                    o_exps = sq.emit("a", exps, after=[o_evac])

                    def scan_a(v, cc=cc, i=i):
                        v.tensor_tensor(out=e2[:], in0=e1[:], in1=vb[:], op=ALU.mult)
                        if i > 0:
                            v.tensor_copy(out=Pt[:, cc, 0:1], in_=Pt[:, cc, TT:TT + 1])
                            v.tensor_copy(out=Qt[:, cc, 0:1], in_=Qt[:, cc, TT:TT + 1])
                        dbc = mixp[:, 6, cc:cc + 1].broadcast_to((128, TT))
                        v.tensor_tensor_scan(Pt[:, cc, 1:TT + 1], dbc, e2[:], Pt[:, cc, 0:1], ALU.mult, ALU.add)
                        return v.tensor_tensor_scan(Qt[:, cc, 1:TT + 1], dbc, e1[:], Qt[:, cc, 0:1], ALU.mult, ALU.add)
                    sdep = [o_exps]
                    if prev_store is not None:
                        sdep.append(prev_store)
                    o_scana = sq.emit("v", scan_a, after=sdep, hazard=True)

                    def scan_b(v, cc=cc):
                        v.tensor_scalar(out=e3[:], in0=e1[:], scalar1=mixp[:, 7, cc:cc + 1], scalar2=None, op0=ALU.mult)
                        v.tensor_tensor(out=e4[:], in0=e3[:], in1=vb[:], op=ALU.mult)
                        v.tensor_tensor(out=e2[:], in0=Pt[:, cc, 0:TT], in1=e4[:], op=ALU.add)
                        v.tensor_tensor(out=e3[:], in0=Qt[:, cc, 0:TT], in1=e3[:], op=ALU.add)
                        v.reciprocal(out=e3[:], in_=e3[:])
                        v.tensor_tensor(out=e2[:], in0=e2[:], in1=e3[:], op=ALU.mult)
                        return v.tensor_tensor(out=sry_t[:, cc], in0=e2[:], in1=rb[:], op=ALU.mult)
                    o_scan = sq.emit("v", scan_b, after=[o_scana])
                    prev_scan[cc] = o_scan

                prev_store = sq.dma("dg", sry_r[:, :, i * TT:(i + 1) * TT], sry_t[:],
                                    after=[prev_scan[NCH - 1]])
            sq.set_floor()

        with ExitStack() as ph:
            psb = lambda *a: ph.enter_context(nc.sbuf_tensor(*a))
            ppt = lambda *a: ph.enter_context(nc.psum_tensor(*a))
            wo_sb = psb("wo_sb", [128, NCH, D], BF16)
            o_wo = sq.dma("dg", wo_sb[:], woT.rearrange("(c p) f -> p c f", p=128))
            sy_in = psb("sy_in", [128, 2, NCH, TT], BF16)
            so_t = psb("so_t", [128, NCH, TT], BF16)
            sps = ppt("sps", [128, 2, TT], F32)
            sry_r = sry_h.rearrange("(c p) t -> p c t", p=128)
            s_r = s_hbm.rearrange("(c p) t -> p c t", p=128)
            prev_ev = [None, None]
            prev_st = None
            ld_user = [None, None]
            for i in range(NTT):
                dep = [] if ld_user[i % 2] is None else [ld_user[i % 2]]
                ld = sq.dma("ds", sy_in[:, i % 2], sry_r[:, :, i * TT:(i + 1) * TT], after=dep)
                last_ev = None
                for cc in range(NCH):
                    pb = (i * NCH + cc) % 2

                    def proj(t, cc=cc, i=i, pb=pb):
                        ins = None
                        for dc in range(NCH):
                            ins = t.matmul(sps[:, pb], wo_sb[:, dc, cc * 128:(cc + 1) * 128],
                                           sy_in[:, i % 2, dc], start=(dc == 0), stop=(dc == NCH - 1))
                        return ins
                    pdep = [ld, o_wo]
                    if prev_ev[pb] is not None:
                        pdep.append(prev_ev[pb])
                    o_p = sq.emit("t", proj, after=pdep)

                    def evac(a, cc=cc, pb=pb):
                        return a.copy(out=so_t[:, cc], in_=sps[:, pb])
                    edep = [o_p] + ([prev_st] if prev_st is not None else [])
                    o_e = sq.emit("a", evac, after=edep)
                    prev_ev[pb] = o_e
                    last_ev = o_e
                ld_user[i % 2] = last_ev
                prev_st = sq.dma("dg", s_r[:, :, i * TT:(i + 1) * TT], so_t[:], after=[last_ev])
            sq.set_floor()

        with ExitStack() as ph:
            psb = lambda *a: ph.enter_context(nc.sbuf_tensor(*a))
            ppt = lambda *a: ph.enter_context(nc.psum_tensor(*a))
            NFH = 4 * D // 128
            b1_sb = psb("b1_sb", [128, 32], F32)
            b2_sb = psb("b2_sb", [128, 8], F32)
            o_b1 = sq.dma("ds", b1_sb[:], b1f[:, :])
            o_b2 = sq.dma("ds", b2_sb[:], b2f[:, :])
            w1_t = psb("w1_t", [128, 2, NCH, 128], BF16)
            w2_t = psb("w2_t", [128, 2, NFH, 128], BF16)
            hbuf = psb("hbuf", [128, NFH, TT], BF16)
            gbuf = psb("gbuf", [128, TT], BF16)
            xres = psb("xres", [128, 2, NCH, TT], F32)
            sres = psb("sres", [128, 2, NCH, TT], BF16)
            onew = psb("onew", [128, NCH, TT], F32)
            fps = ppt("fps", [128, 2, TT], F32)
            gps = ppt("gps", [128, 2, TT], F32)
            w1r = w1t_h.rearrange("(c p) f -> p c f", p=128)
            w2r = w2t_h.rearrange("(c p) f -> p c f", p=128)
            xi_r = x_in.rearrange("(c p) t -> p c t", p=128)
            xo_r = x_out.rearrange("(c p) t -> p c t", p=128)
            s_r = s_hbm.rearrange("(c p) t -> p c t", p=128)

            prev_hev = [None, None]
            prev_gev = [None, None]
            prev_resid = None
            prev_xst = None
            lastw1 = [None, None]
            lastw2 = [None, None]
            xld_user = [None, None]
            out_stores = []
            for i in range(NTT):
                dep = [] if xld_user[i % 2] is None else [xld_user[i % 2]]
                o_lx = sq.dma("ds", xres[:, i % 2], xi_r[:, :, i * TT:(i + 1) * TT], after=dep)
                o_ls = sq.dma("ds", sres[:, i % 2], s_r[:, :, i * TT:(i + 1) * TT], after=dep)
                h_evs = []
                for fc in range(NFH):
                    wb = fc % 2
                    wdep = [lastw1[wb]] if lastw1[wb] is not None else []
                    if WSTRICT and sq.last.get("t") is not None:
                        wdep.append(sq.last["t"])
                    o_lw = sq.dma("ds", w1_t[:, wb], w1r[:, :, fc * 128:(fc + 1) * 128], after=wdep)

                    def hmm(t, fc=fc, wb=wb, i=i):
                        ins = None
                        for dc in range(NCH):
                            ins = t.matmul(fps[:, fc % 2], w1_t[:, wb, dc],
                                           xn[:, dc, 1 + i * TT:1 + (i + 1) * TT],
                                           start=(dc == 0), stop=(dc == NCH - 1))
                        return ins
                    hdep = [o_lw, ln_done[i]]
                    if prev_hev[fc % 2] is not None:
                        hdep.append(prev_hev[fc % 2])
                    o_hm = sq.emit("t", hmm, after=hdep)
                    lastw1[wb] = o_hm

                    def hev(a, fc=fc):
                        return a.activation(out=hbuf[:, fc], in_=fps[:, fc % 2], func=AF.Gelu,
                                            bias=b1_sb[:, fc:fc + 1])
                    o_he = sq.emit("a", hev, after=[o_hm, o_b1])
                    prev_hev[fc % 2] = o_he
                    h_evs.append(o_he)
                    if debug:
                        hr = h_h.rearrange("(c p) t -> p c t", p=128)
                        o_hd = sq.dma("dg", hr[:, fc, i * TT:(i + 1) * TT], hbuf[:, fc], after=[o_he])
                        prev_hev[fc % 2] = o_hd
                        h_evs[-1] = o_hd

                res_ops = []
                for dc in range(NCH):
                    wb = dc % 2
                    wdep = [lastw2[wb]] if lastw2[wb] is not None else []
                    if WSTRICT and sq.last.get("t") is not None:
                        wdep.append(sq.last["t"])
                    o_lw2 = sq.dma("ds", w2_t[:, wb], w2r[:, :, dc * 128:(dc + 1) * 128], after=wdep)

                    def gmm(t, dc=dc, wb=wb):
                        ins = None
                        for hc in range(NFH):
                            ins = t.matmul(gps[:, dc % 2], w2_t[:, wb, hc], hbuf[:, hc],
                                           start=(hc == 0), stop=(hc == NFH - 1))
                        return ins
                    gdep = [o_lw2] + h_evs
                    if prev_gev[dc % 2] is not None:
                        gdep.append(prev_gev[dc % 2])
                    o_gm = sq.emit("t", gmm, after=gdep)
                    lastw2[wb] = o_gm

                    def gev(a, dc=dc):
                        return a.activation(out=gbuf[:], in_=gps[:, dc % 2], func=AF.Silu,
                                            bias=b2_sb[:, dc:dc + 1])
                    gevdep = [o_gm, o_b2]
                    if prev_resid is not None:
                        gevdep.append(prev_resid)
                    o_ge = sq.emit("a", gev, after=gevdep)
                    prev_gev[dc % 2] = o_ge
                    o_gd = None
                    if debug:
                        gr = g_h.rearrange("(c p) t -> p c t", p=128)
                        o_gd = sq.dma("dg", gr[:, dc, i * TT:(i + 1) * TT], gbuf[:], after=[o_ge])

                    def resid(v, dc=dc, i=i):
                        v.tensor_tensor(out=onew[:, dc], in0=sres[:, i % 2, dc], in1=gbuf[:], op=ALU.mult)
                        return v.tensor_tensor(out=onew[:, dc], in0=onew[:, dc], in1=xres[:, i % 2, dc], op=ALU.add)
                    rdep = [o_ge, o_lx, o_ls]
                    if prev_xst is not None:
                        rdep.append(prev_xst)
                    o_r = sq.emit("v", resid, after=rdep)
                    prev_resid = o_r if not debug else sq.emit("v", lambda v: v.nop(), after=[o_r, o_gd])
                    res_ops.append(o_r)
                xld_user[i % 2] = res_ops[-1]
                prev_xst = sq.dma("dg", xo_r[:, :, i * TT:(i + 1) * TT], onew[:], after=res_ops)
                out_stores.append(prev_xst)

            if ctx is None:
                sq.finalize(block, sems, outputs_done=out_stores)
            else:
                ctx["out_stores"] = out_stores
                sq.set_floor()
    return nc



def fm(a):
    a = np.asarray(a).reshape(-1)
    return np.ascontiguousarray(a.reshape(-1, 128).T.astype(np.float32))


def l3_inmap(xcore_fm, I, pre=""):
    g = lambda n: I[pre + n] if pre else I[n]
    return {
        "x_in": xcore_fm,
        "wkT": np.ascontiguousarray(g("L3_attn_W_key").T),
        "wvT": np.ascontiguousarray(g("L3_attn_W_value").T),
        "wrT": np.ascontiguousarray(g("L3_attn_W_receptance").T),
        "woT": np.ascontiguousarray(g("L3_attn_W_output").T),
        "w1T": np.ascontiguousarray(g("L3_ffn_W1").T),
        "w2T": np.ascontiguousarray(g("L3_ffn_W2").T),
        "b1f": fm(g("L3_ffn_b1")),
        "b2f": fm(g("L3_ffn_b2")),
        "tdf": fm(g("L3_attn_time_decay")),
        "tff": fm(g("L3_attn_time_first")),
        "tmkf": fm(g("L3_attn_time_mix_k")),
        "tmvf": fm(g("L3_attn_time_mix_v")),
        "tmrf": fm(g("L3_attn_time_mix_r")),
        "ones128_h": np.ones((128, 1), np.float32),
        "ones1_h": np.ones((1, 128), np.float32),
    }





def build_l2(debug=False, upto=99, ctx=None):
    from contextlib import ExitStack
    nc = bass.Bass("TRN2", target_bir_lowering=False, debug=False) if ctx is None else ctx["nc"]
    x_in = nc.dram_tensor("x_in", [D, S], F32, kind="ExternalInput").ap()
    wqkvT = nc.dram_tensor("wqkvT", [D, 3 * D], F32, kind="ExternalInput").ap()
    woutT = nc.dram_tensor("woutT", [D, D], F32, kind="ExternalInput").ap()
    boutf = nc.dram_tensor("boutf", [128, 8], F32, kind="ExternalInput").ap()
    projh = nc.dram_tensor("projh", [8, 128, 128], F32, kind="ExternalInput").ap()
    w1T = nc.dram_tensor("w1T", [D, 4 * D], F32, kind="ExternalInput").ap()
    w2T = nc.dram_tensor("w2T", [4 * D, D], F32, kind="ExternalInput").ap()
    w1N = nc.dram_tensor("w1N", [4 * D, D], F32, kind="ExternalInput").ap()
    w2N = nc.dram_tensor("w2N", [D, 4 * D], F32, kind="ExternalInput").ap()
    b1f = nc.dram_tensor("b1f", [128, 32], F32, kind="ExternalInput").ap()
    b2f = nc.dram_tensor("b2f", [128, 8], F32, kind="ExternalInput").ap()
    iden_h = nc.dram_tensor("iden_h", [128, 128], BF16, kind="ExternalInput").ap()
    ones128_h = nc.dram_tensor("ones128_h", [128, 1], F32, kind="ExternalInput").ap()
    ones1_h = nc.dram_tensor("ones1_h", [1, 128], F32, kind="ExternalInput").ap()
    x_out = nc.dram_tensor("x_out", [D, S], F32, kind="ExternalOutput").ap()

    kind = "ExternalOutput" if debug else "Internal"
    qf_h = nc.dram_tensor("qf_h", [D, S], BF16, kind=kind).ap()
    kf_h = nc.dram_tensor("kf_h", [D, S], BF16, kind=kind).ap()
    vt_h = nc.dram_tensor("vt_h", [S, 8 * 129], BF16, kind=kind).ap()
    of_h = nc.dram_tensor("of_h", [D, S], BF16, kind=kind).ap()
    s_hbm = nc.dram_tensor("s_hbm", [D, S], BF16, kind=kind).ap()
    w1s_h = nc.dram_tensor("w1s_h", [D, 4 * D], BF16, kind="Internal").ap()
    w2s_h = nc.dram_tensor("w2s_h", [4 * D, D], BF16, kind="Internal").ap()
    xn_h = nc.dram_tensor("xn_h", [D, S], BF16, kind=kind).ap() if debug else None
    dbg_h = nc.dram_tensor("dbg_h", [4, 128, 520], F32, kind=kind).ap() if debug else None

    stack = ExitStack()
    with stack:
        if ctx is None:
            sems = alloc_sems(nc, stack)
            block = stack.enter_context(nc.Block())
            sq = Seq(nc)
        else:
            sems, block, sq = ctx["sems"], ctx["block"], ctx["sq"]
        sb = lambda *a: stack.enter_context(nc.sbuf_tensor(*a))

        ones128 = sb("c_ones128", [128, 1], F32)
        ones1 = sb("c_ones1", [1, 128], F32)
        iden = sb("c_iden", [128, 128], BF16)
        cst = sb("cst", [128, 2], F32)
        xn = sb("xn", [128, NCH, S], BF16)

        sq.dma("ds", ones128[:], ones128_h[:, :])
        sq.dma("ds", ones1[:], ones1_h[:, :])
        sq.dma("ds", iden[:], iden_h[:, :])

        def pre(v):
            return v.memset(cst[:, 0:1], 1e-5)
        sq.emit("v", pre)
        sq.set_floor()

        with ExitStack() as ph:
            ln_done = emit_ln(sq, nc, ph, x_in, xn, ones128, ones1, cst, col_off=0)
            sq.set_floor()
            if debug:
                xn_r = xn_h.rearrange("(c p) t -> p c t", p=128)
                for i in range(NTT):
                    sq.dma("dg", xn_r[:, :, i * TT:(i + 1) * TT], xn[:, :, i * TT:(i + 1) * TT])
                sq.set_floor()

        qf_r = qf_h.rearrange("(c p) t -> p c t", p=128)
        kf_r = kf_h.rearrange("(c p) t -> p c t", p=128)
        wq_r = wqkvT.rearrange("(c p) f -> p c f", p=128)

        with ExitStack() as ph:
            psb = lambda *a: ph.enter_context(nc.sbuf_tensor(*a))
            ppt = lambda *a: ph.enter_context(nc.psum_tensor(*a))
            wt = psb("wt", [128, 2, NCH, 128], BF16)
            ob = psb("ob", [128, 2, TT], BF16)
            vtm = psb("vtm", [128, 8, 129], BF16)
            qps = ppt("qps", [128, 2, TT], F32)
            vps = ppt("vps", [128, 2, 512], F32)
            lastw = [None, None]
            prev_ev = [None, None]
            prev_st = [None, None]
            for fc in range(16):
                wb = fc % 2
                wdep = [sq.last["t"]] if sq.last.get("t") is not None else []
                o_lw = sq.dma("dg", wt[:, wb], wq_r[:, :, fc * 128:(fc + 1) * 128], after=wdep)
                for i in range(NTT):
                    pb = (fc * NTT + i) % 2

                    def mm(t, fc=fc, wb=wb, i=i, pb=pb):
                        ins = None
                        for dc in range(NCH):
                            ins = t.matmul(qps[:, pb], wt[:, wb, dc], xn[:, dc, i * TT:(i + 1) * TT],
                                           start=(dc == 0), stop=(dc == NCH - 1))
                        return ins
                    mdep = [o_lw, ln_done[i]]
                    if prev_ev[pb] is not None:
                        mdep.append(prev_ev[pb])
                    o_mm = sq.emit("t", mm, after=mdep)
                    lastw[wb] = o_mm

                    def ev(a, pb=pb):
                        return a.copy(out=ob[:, pb], in_=qps[:, pb])
                    edep = [o_mm]
                    if prev_st[pb] is not None:
                        edep.append(prev_st[pb])
                    o_ev = sq.emit("a", ev, after=edep)
                    prev_ev[pb] = o_ev
                    dst = qf_r[:, fc, i * TT:(i + 1) * TT] if fc < 8 else kf_r[:, fc - 8, i * TT:(i + 1) * TT]
                    prev_st[pb] = sq.dma("dg", dst, ob[:, pb], after=[o_ev])
            def vones(v):
                ins = None
                for h in range(8):
                    ins = v.memset(vtm[:, h, 128:129], 1.0)
                return ins
            o_vone = sq.emit("v", vones)
            vwdep = [sq.last["t"]] if sq.last.get("t") is not None else []
            vw = psb("vw", [128, NCH, 1024], BF16)
            o_vw = sq.dma("dg", vw[:], wq_r[:, :, 2048:3072], after=vwdep)
            prev_vst = None
            prev_vev = None
            for i2 in range(S // 128):

                def vmm(t, i2=i2):
                    ins = None
                    for half in range(2):
                        for dc in range(NCH):
                            ins = t.matmul(vps[:, half], xn[:, dc, i2 * 128:(i2 + 1) * 128],
                                           vw[:, dc, half * 512:(half + 1) * 512],
                                           start=(dc == 0), stop=(dc == NCH - 1))
                    return ins
                vdep = [o_vw, ln_done[min(i2 // 4, NTT - 1)]]
                if prev_vev is not None:
                    vdep.append(prev_vev)
                o_vm = sq.emit("t", vmm, after=vdep)

                def vev(a, i2=i2):
                    ins = None
                    for h in range(8):
                        half, off = divmod(h * 128, 512)
                        ins = a.copy(out=vtm[:, h, 0:128], in_=vps[:, half, off:off + 128]) if False else                             a.copy(out=vtm[:, h, 0:128], in_=vps[:, half, off:off + 128])
                    return ins
                vedep = [o_vm, o_vone]
                if prev_vst is not None:
                    vedep.append(prev_vst)
                o_ve = sq.emit("a", vev, after=vedep)
                prev_vev = o_ve
                prev_vst = sq.dma("dg", vt_h[i2 * 128:(i2 + 1) * 128, :], vtm[:], after=[o_ve])
            sq.set_floor()

        with ExitStack() as ph:
            psb = lambda *a: ph.enter_context(nc.sbuf_tensor(*a))
            ppt = lambda *a: ph.enter_context(nc.psum_tensor(*a))
            prj = psb("prj", [128, 8, 128], BF16)
            o_prj = sq.dma("dg", prj[:], projh.transpose((1, 0, 2)))
            qs = psb("qs", [128, 2, 128], BF16)
            ks = psb("ks", [128, 2, 128], BF16)
            vts = psb("vts", [128, 2, 8, 129], BF16)
            kp_tm = psb("kp_tm", [128, 128], BF16)
            qp_fm = psb("qp_fm", [128, 128], BF16)
            kv_sb = psb("kv_sb", [128, 4, 129], BF16)
            et = psb("et", [128, 130], F32)
            bcs = psb("bcs", [128, 128], F32)
            obuf = psb("obuf", [128, NCH, 128], BF16)
            pps = ppt("pps", [128, 2, 128], F32)
            kvps = ppt("kvps", [128, 4, 512], F32)
            nups = ppt("nups", [128, 4, 32], F32)
            zps = ppt("zps", [1, 4, 32], F32)
            bcps = ppt("bcps", [128, 128], F32)
            of_r = of_h.rearrange("(c p) t -> p c t", p=128)

            prev = {}
            prev_ost = None
            for i2 in range(S // 128):
                for h in range(8):
                    hb = i2 % 2
                    dep_h = [sq.last["t"]] if sq.last.get("t") is not None else []
                    o_lq = sq.dma("ds", qs[:, hb], qf_r[:, h, i2 * 128:(i2 + 1) * 128], after=dep_h)
                    o_lk = sq.dma("ds", ks[:, hb], kf_r[:, h, i2 * 128:(i2 + 1) * 128], after=dep_h)
                    if h == 0:
                        o_lv = sq.dma("ds", vts[:, hb], vt_h[i2 * 128:(i2 + 1) * 128, :], after=dep_h)
                        prev[("lv", hb)] = o_lv
                    o_lv = prev[("lv", hb)]

                    def pmm(t, h=h, hb=hb):
                        t.matmul(pps[:, 0], ks[:, hb], prj[:, h], start=True, stop=True)
                        return t.matmul(pps[:, 1], prj[:, h], qs[:, hb], start=True, stop=True)
                    pdep = [o_lq, o_lk, o_prj]
                    if ("pev",) in prev:
                        pdep.append(prev[("pev",)])
                    o_pm = sq.emit("t", pmm, after=pdep)

                    o_e1 = sq.emit("v", lambda v: v.tensor_scalar(out=et[:, 0:128], in0=pps[:, 0], scalar1=0.0, scalar2=None, op0=ALU.min), after=[o_pm])
                    o_e2 = sq.emit("a", lambda a: a.activation(out=kp_tm[:], in_=et[:, 0:128], func=AF.Exp), after=[o_e1])
                    o_e3 = sq.emit("v", lambda v: v.scalar_tensor_tensor(out=kp_tm[:], in0=pps[:, 0], scalar=1.0, in1=kp_tm[:], op0=ALU.add, op1=ALU.max), after=[o_e2])
                    o_f1 = sq.emit("v", lambda v: v.tensor_scalar(out=et[:, 0:128], in0=pps[:, 1], scalar1=0.0, scalar2=None, op0=ALU.min), after=[o_pm, o_e3])
                    o_f2 = sq.emit("a", lambda a: a.activation(out=qp_fm[:], in_=et[:, 0:128], func=AF.Exp), after=[o_f1])
                    o_f3 = sq.emit("v", lambda v: v.scalar_tensor_tensor(out=qp_fm[:], in0=pps[:, 1], scalar=1.0, in1=qp_fm[:], op0=ALU.add, op1=ALU.max), after=[o_f2])
                    prev[("pev",)] = o_f3

                    def kvmm(t, hb=hb, h=h):
                        ins = None
                        for j in range(4):
                            ins = t.matmul(kvps[:, j, 0:129], kp_tm[32 * j:32 * (j + 1), :],
                                           vts[32 * j:32 * (j + 1), hb, h], start=True, stop=True,
                                           tile_position=(32 * j, 0))
                        return ins
                    kvdep = [o_e3, o_lv]
                    if ("kvev",) in prev:
                        kvdep.append(prev[("kvev",)])
                    o_kv = sq.emit("t", kvmm, after=kvdep)

                    def kvev(a):
                        ins = None
                        for j in range(4):
                            ins = a.copy(out=kv_sb[:, j], in_=kvps[:, j, 0:129])
                        return ins
                    kvedep = [o_kv]
                    if ("num",) in prev:
                        kvedep.append(prev[("num",)])
                    o_kve = sq.emit("a", kvev, after=kvedep)
                    prev[("kvev",)] = o_kve

                    def numm(t, h=h):
                        ins = None
                        for j in range(4):
                            t.matmul(nups[:, j], kv_sb[:, j, 0:128], qp_fm[:, 32 * j:32 * (j + 1)],
                                     start=True, stop=True)
                            ins = t.matmul(zps[0:1, j], kv_sb[:, j, 128:129], qp_fm[:, 32 * j:32 * (j + 1)],
                                           start=True, stop=True)
                        return ins
                    ndep = [o_f3, o_kve]
                    if ("nev",) in prev:
                        ndep.append(prev[("nev",)])
                    o_nu = sq.emit("t", numm, after=ndep)
                    prev[("num",)] = o_nu

                    def nrm_a(v):
                        v.tensor_scalar(out=et[0:1, 0:128], in0=zps[0:1, :].rearrange("p a b -> p (a b)"),
                                        scalar1=1e-8, scalar2=None, op0=ALU.add)
                        return v.reciprocal(out=et[0:1, 128:129] if False else et[0:1, 0:128], in_=et[0:1, 0:128])
                    o_na = sq.emit("v", nrm_a, after=[o_nu], hazard=True)

                    def bcmm(t):
                        return t.matmul(bcps[:], ones1[:], et[0:1, 0:128], start=True, stop=True)
                    bcdep = [o_na]
                    if ("omul",) in prev:
                        bcdep.append(prev[("omul",)])
                    o_bc = sq.emit("t", bcmm, after=bcdep)

                    def bcev(a):
                        return a.copy(out=bcs[:], in_=bcps[:])
                    o_bce = sq.emit("a", bcev, after=[o_bc])

                    def omul(v, h=h):
                        return v.tensor_tensor(out=obuf[:, h], in0=nups[:, :].rearrange("p a b -> p (a b)"),
                                               in1=bcs[:], op=ALU.mult)
                    odep = [o_bce, o_nu]
                    if prev_ost is not None:
                        odep.append(prev_ost)
                    o_om = sq.emit("v", omul, after=odep)
                    prev[("omul",)] = o_om
                    prev[("nev",)] = o_om
                    prev[("onorm", h)] = o_om

                prev_ost = sq.dma("dg", of_r[:, :, i2 * 128:(i2 + 1) * 128], obuf[:],
                                  after=[prev[("onorm", h2)] for h2 in range(8)])
            sq.set_floor()

        if upto <= 2:
            sq.finalize(block, sems)
            return nc

        with ExitStack() as ph:
            psb = lambda *a: ph.enter_context(nc.sbuf_tensor(*a))
            ppt = lambda *a: ph.enter_context(nc.psum_tensor(*a))
            wo_sb = psb("wo_sb", [128, NCH, D], BF16)
            bo_sb = psb("bo_sb", [128, 8], F32)
            sq.dma("dg", wo_sb[:], woutT.rearrange("(c p) f -> p c f", p=128))
            sq.dma("ds", bo_sb[:], boutf[:, :])
            sq.set_floor()
            oin = psb("oin", [128, 2, NCH, TT], BF16)
            so_t = psb("so_t", [128, NCH, TT], BF16)
            sps = ppt("sps", [128, 2, TT], F32)
            s_r = s_hbm.rearrange("(c p) t -> p c t", p=128)
            prev_ev = [None, None]
            prev_st = None
            ld_user = [None, None]
            for i in range(NTT):
                dep = [] if ld_user[i % 2] is None else [ld_user[i % 2]]
                ld = sq.dma("ds", oin[:, i % 2], of_r[:, :, i * TT:(i + 1) * TT], after=dep)
                last_ev = None
                for cc in range(NCH):
                    pb = (i * NCH + cc) % 2

                    def proj(t, cc=cc, i=i, pb=pb):
                        ins = None
                        for dc in range(NCH):
                            ins = t.matmul(sps[:, pb], wo_sb[:, dc, cc * 128:(cc + 1) * 128],
                                           oin[:, i % 2, dc], start=(dc == 0), stop=(dc == NCH - 1))
                        return ins
                    pdep = [ld]
                    if prev_ev[pb] is not None:
                        pdep.append(prev_ev[pb])
                    o_p = sq.emit("t", proj, after=pdep)

                    def evac(a, cc=cc, pb=pb):
                        return a.activation(out=so_t[:, cc], in_=sps[:, pb], func=AF.Identity,
                                            bias=bo_sb[:, cc:cc + 1])
                    edep = [o_p] + ([prev_st] if prev_st is not None else [])
                    o_e = sq.emit("a", evac, after=edep)
                    prev_ev[pb] = o_e
                    last_ev = o_e
                ld_user[i % 2] = last_ev
                prev_st = sq.dma("dg", s_r[:, :, i * TT:(i + 1) * TT], so_t[:], after=[last_ev])
            sq.set_floor()

        if upto <= 3:
            sq.finalize(block, sems)
            return nc

        a1_sb = sb("a1_sb", [128, 32], F32)
        a2_sb = sb("a2_sb", [128, 8], F32)
        with ExitStack() as ph:
            psb = lambda *a: ph.enter_context(nc.sbuf_tensor(*a))
            wn = psb("wn", [128, 2, 4096], F32)
            wq = psb("wq", [128, 4096], BF16)
            for wi, (whn, whT, wst, asb, nrt, ncolN, ncolT) in enumerate([
                    (w1N, w1T, w1s_h, a1_sb, 32, 1024, 4096), (w2N, w2T, w2s_h, a2_sb, 8, 4096, 1024)]):
                reds = []
                for r in range(nrt):
                    dep = [reds[r - 2]] if r >= 2 else []
                    ld = sq.dma("ds", wn[:, r % 2, 0:ncolN], whn[r * 128:(r + 1) * 128, :], after=dep)

                    def red(v, r=r, wi=wi, ncolN=ncolN, asb=asb):
                        return v.tensor_reduce(out=asb[:, r:r + 1], in_=wn[:, r % 2, 0:ncolN], axis=AX.X,
                                               op=ALU.add, apply_absolute_value=True)
                    reds.append(sq.emit("v", red, after=[ld], hazard=True))

                def ascale(v, asb=asb, nrt=nrt, ncolN=ncolN):
                    return v.tensor_scalar(out=asb[:, 0:nrt], in0=asb[:, 0:nrt], scalar1=1.0 / ncolN,
                                           scalar2=None, op0=ALU.mult)
                o_as = sq.emit("v", ascale, after=reds[-2:], hazard=True)
                sq.set_floor()
                nrtT = (ncolN * nrt * 128 // ncolT) // 128
                prev_q = None
                for r in range(nrtT):
                    dep = [prev_q] if prev_q is not None else []
                    ld = sq.dma("ds", wn[:, r % 2, 0:ncolT], whT[r * 128:(r + 1) * 128, :], after=dep)

                    def sgn(a, r=r, ncolT=ncolT):
                        return a.activation(out=wq[:, 0:ncolT], in_=wn[:, r % 2, 0:ncolT], func=AF.Sign)
                    o_s = sq.emit("a", sgn, after=[ld])
                    prev_q = sq.dma("dg", wst[r * 128:(r + 1) * 128, :], wq[:, 0:ncolT], after=[o_s])
                sq.set_floor()

        with ExitStack() as ph:
            psb = lambda *a: ph.enter_context(nc.sbuf_tensor(*a))
            ppt = lambda *a: ph.enter_context(nc.psum_tensor(*a))
            NFH = 4 * D // 128
            b1_sb = psb("b1_sb", [128, 32], F32)
            b2_sb = psb("b2_sb", [128, 8], F32)
            sq.dma("ds", b1_sb[:], b1f[:, :])
            sq.dma("ds", b2_sb[:], b2f[:, :])
            sq.set_floor()
            w1_t = psb("w1_t", [128, 2, NCH, 128], BF16)
            w2_t = psb("w2_t", [128, 2, NFH, 128], BF16)
            hbuf = psb("hbuf", [128, NFH, TT], BF16)
            gbuf = psb("gbuf", [128, TT], BF16)
            xres = psb("xres", [128, 2, NCH, TT], F32)
            sres = psb("sres", [128, 2, NCH, TT], BF16)
            onew = psb("onew", [128, NCH, TT], F32)
            fps = ppt("fps", [128, 2, TT], F32)
            gps = ppt("gps", [128, 2, TT], F32)
            w1r = w1s_h.rearrange("(c p) f -> p c f", p=128)
            w2r = w2s_h.rearrange("(c p) f -> p c f", p=128)
            xi_r = x_in.rearrange("(c p) t -> p c t", p=128)
            xo_r = x_out.rearrange("(c p) t -> p c t", p=128)
            s_r = s_hbm.rearrange("(c p) t -> p c t", p=128)

            prev_hev = [None, None]
            prev_gev = [None, None]
            prev_resid = None
            prev_xst = None
            xld_user = [None, None]
            out_stores = []
            for i in range(NTT):
                dep = [] if xld_user[i % 2] is None else [xld_user[i % 2]]
                o_lx = sq.dma("ds", xres[:, i % 2], xi_r[:, :, i * TT:(i + 1) * TT], after=dep)
                o_ls = sq.dma("ds", sres[:, i % 2], s_r[:, :, i * TT:(i + 1) * TT], after=dep)
                h_evs = []
                for fc in range(NFH):
                    wb = fc % 2
                    wdep = [sq.last["t"]] if sq.last.get("t") is not None else []
                    o_lw = sq.dma("ds", w1_t[:, wb], w1r[:, :, fc * 128:(fc + 1) * 128], after=wdep)

                    def hmm(t, fc=fc, wb=wb, i=i):
                        ins = None
                        for dc in range(NCH):
                            ins = t.matmul(fps[:, fc % 2], w1_t[:, wb, dc], xn[:, dc, i * TT:(i + 1) * TT],
                                           start=(dc == 0), stop=(dc == NCH - 1))
                        return ins
                    hdep = [o_lw, ln_done[i]]
                    if prev_hev[fc % 2] is not None:
                        hdep.append(prev_hev[fc % 2])
                    o_hm = sq.emit("t", hmm, after=hdep)

                    def hev(a, fc=fc):
                        return a.activation(out=hbuf[:, fc], in_=fps[:, fc % 2], func=AF.Gelu,
                                            scale=a1_sb[:, fc:fc + 1], bias=b1_sb[:, fc:fc + 1])
                    o_he = sq.emit("a", hev, after=[o_hm])
                    prev_hev[fc % 2] = o_he
                    h_evs.append(o_he)

                res_ops = []
                for dc in range(NCH):
                    wb = dc % 2
                    wdep = [sq.last["t"]] if sq.last.get("t") is not None else []
                    o_lw2 = sq.dma("ds", w2_t[:, wb], w2r[:, :, dc * 128:(dc + 1) * 128], after=wdep)

                    def gmm(t, dc=dc, wb=wb):
                        ins = None
                        for hc in range(NFH):
                            ins = t.matmul(gps[:, dc % 2], w2_t[:, wb, hc], hbuf[:, hc],
                                           start=(hc == 0), stop=(hc == NFH - 1))
                        return ins
                    gdep = [o_lw2] + h_evs
                    if prev_gev[dc % 2] is not None:
                        gdep.append(prev_gev[dc % 2])
                    o_gm = sq.emit("t", gmm, after=gdep)

                    def gev(a, dc=dc):
                        return a.activation(out=gbuf[:], in_=gps[:, dc % 2], func=AF.Silu,
                                            scale=a2_sb[:, dc:dc + 1], bias=b2_sb[:, dc:dc + 1])
                    gevdep = [o_gm]
                    if prev_resid is not None:
                        gevdep.append(prev_resid)
                    o_ge = sq.emit("a", gev, after=gevdep)
                    prev_gev[dc % 2] = o_ge

                    def resid(v, dc=dc, i=i):
                        v.tensor_tensor(out=onew[:, dc], in0=sres[:, i % 2, dc], in1=gbuf[:], op=ALU.mult)
                        return v.tensor_tensor(out=onew[:, dc], in0=onew[:, dc], in1=xres[:, i % 2, dc], op=ALU.add)
                    rdep = [o_ge, o_lx, o_ls]
                    if prev_xst is not None:
                        rdep.append(prev_xst)
                    o_r = sq.emit("v", resid, after=rdep)
                    prev_resid = o_r
                    res_ops.append(o_r)
                xld_user[i % 2] = res_ops[-1]
                prev_xst = sq.dma("dg", xo_r[:, :, i * TT:(i + 1) * TT], onew[:], after=res_ops)
                out_stores.append(prev_xst)

            if ctx is None:
                sq.finalize(block, sems, outputs_done=out_stores)
            else:
                ctx["out_stores"] = out_stores
                sq.set_floor()
    return nc


def l2_inmap(xcore_fm, I):
    return {
        "x_in": xcore_fm,
        "wqkvT": np.ascontiguousarray(I["L2_attn_w_qkv"].T),
        "woutT": np.ascontiguousarray(I["L2_attn_w_out"].T),
        "boutf": fm(I["L2_attn_b_out"]),
        "projh": np.ascontiguousarray(I["L2_attn_proj"]),
        "w1T": np.ascontiguousarray(I["L2_ffn_w1"].T),
        "w2T": np.ascontiguousarray(I["L2_ffn_w2"].T),
        "w1N": np.ascontiguousarray(I["L2_ffn_w1"]),
        "w2N": np.ascontiguousarray(I["L2_ffn_w2"]),
        "b1f": fm(I["L2_ffn_b1"]),
        "b2f": fm(I["L2_ffn_b2"]),
        "iden_h": np.eye(128, dtype=np.float32).astype(BF),
        "ones128_h": np.ones((128, 1), np.float32),
        "ones1_h": np.ones((1, 128), np.float32),
    }



def build_l1(debug=False, upto=99, ctx=None):
    from contextlib import ExitStack
    nc = bass.Bass("TRN2", target_bir_lowering=False, debug=False) if ctx is None else ctx["nc"]
    x_in = nc.dram_tensor("x_in", [D, S], F32, kind="ExternalInput").ap()
    wq = nc.dram_tensor("wq", [D, 256], F32, kind="ExternalInput").ap()
    wk = nc.dram_tensor("wk", [D, 256], F32, kind="ExternalInput").ap()
    wv = nc.dram_tensor("wv", [D, D], F32, kind="ExternalInput").ap()
    wo = nc.dram_tensor("wo", [D, D], F32, kind="ExternalInput").ap()
    wrT = nc.dram_tensor("wrT", [D, D], F32, kind="ExternalInput").ap()
    wkT2 = nc.dram_tensor("wkT2", [D, D], F32, kind="ExternalInput").ap()
    wvT2 = nc.dram_tensor("wvT2", [D, D], F32, kind="ExternalInput").ap()
    murf = nc.dram_tensor("murf", [128, 8], F32, kind="ExternalInput").ap()
    mukf = nc.dram_tensor("mukf", [128, 8], F32, kind="ExternalInput").ap()
    e1m = nc.dram_tensor("e1m", [16, 128, 128], BF16, kind="ExternalInput").ap()
    e2m = nc.dram_tensor("e2m", [16, 128, 128], BF16, kind="ExternalInput").ap()
    trimask = nc.dram_tensor("trimask", [128, 128], BF16, kind="ExternalInput").ap()
    iden_h = nc.dram_tensor("iden_h", [128, 128], BF16, kind="ExternalInput").ap()
    ones128_h = nc.dram_tensor("ones128_h", [128, 1], F32, kind="ExternalInput").ap()
    ones1_h = nc.dram_tensor("ones1_h", [1, 128], F32, kind="ExternalInput").ap()
    x_out = nc.dram_tensor("x_out", [D, S], F32, kind="ExternalOutput").ap()

    kind = "ExternalOutput" if debug else "Internal"
    xn_h = nc.dram_tensor("xn_h", [D, S], BF16, kind=kind).ap()
    qf_h = nc.dram_tensor("qf_h", [256, S], BF16, kind=kind).ap()
    kf_h = nc.dram_tensor("kf_h", [256, S], BF16, kind=kind).ap()
    kt_h = nc.dram_tensor("kt_h", [S, 256], BF16, kind=kind).ap()
    vt_h = nc.dram_tensor("vt_h", [S, 16 * 65], BF16, kind=kind).ap()
    of_h = nc.dram_tensor("of_h", [D, S], BF16, kind=kind).ap()
    s_hbm = nc.dram_tensor("s_hbm", [D, S], BF16, kind=kind).ap()
    dbg_h = nc.dram_tensor("dbg_h", [8, 128, 256], F32, kind=kind).ap() if debug else None

    stack = ExitStack()
    with stack:
        if ctx is None:
            sems = alloc_sems(nc, stack)
            block = stack.enter_context(nc.Block())
            sq = Seq(nc)
        else:
            sems, block, sq = ctx["sems"], ctx["block"], ctx["sq"]
        sb = lambda *a: stack.enter_context(nc.sbuf_tensor(*a))

        ones128 = sb("c_ones128", [128, 1], F32)
        ones1 = sb("c_ones1", [1, 128], F32)
        iden = sb("c_iden", [128, 128], BF16)
        trim = sb("c_trim", [128, 128], BF16)
        cst = sb("cst", [128, 2], F32)

        sq.dma("ds", ones128[:], ones128_h[:, :])
        sq.dma("ds", ones1[:], ones1_h[:, :])
        sq.dma("ds", iden[:], iden_h[:, :])
        sq.dma("ds", trim[:], trimask[:, :])
        sq.emit("v", lambda v: v.memset(cst[:, 0:1], 1e-5))
        sq.set_floor()

        phAB = ExitStack()
        xn = phAB.enter_context(nc.sbuf_tensor("xn", [128, NCH, S], BF16))
        with ExitStack() as ph:
            ln_done = emit_ln(sq, nc, ph, x_in, xn, ones128, ones1, cst, col_off=0)
            xn_r = xn_h.rearrange("(c p) t -> p c t", p=128)
            xn_st = []
            for i in range(NTT):
                xn_st.append(sq.dma("dg", xn_r[:, :, i * TT:(i + 1) * TT],
                                    xn[:, :, i * TT:(i + 1) * TT], after=[ln_done[i]]))
            sq.set_floor()

        if upto <= 0:
            phAB.close()
            sq.finalize(block, sems)
            return nc

        qf_r = qf_h.rearrange("(c p) t -> p c t", p=128)
        kf_r = kf_h.rearrange("(c p) t -> p c t", p=128)
        with ExitStack() as ph:
            psb = lambda *a: ph.enter_context(nc.sbuf_tensor(*a))
            ppt = lambda *a: ph.enter_context(nc.psum_tensor(*a))
            wqs = psb("wqs", [128, NCH, 256], BF16)
            wks = psb("wks", [128, NCH, 256], BF16)
            wvs = psb("wvs", [128, NCH, D], BF16)
            sq.dma("dg", wqs[:], wq.rearrange("(c p) f -> p c f", p=128))
            sq.dma("dg", wks[:], wk.rearrange("(c p) f -> p c f", p=128))
            sq.dma("dg", wvs[:], wv.rearrange("(c p) f -> p c f", p=128))
            sq.set_floor()
            ob = psb("ob", [128, 2, TT], BF16)
            ktt = psb("ktt", [128, 2, 256], BF16)
            vtm = psb("vtm", [128, 2, 16, 65], BF16)
            qps = ppt("qps", [128, 2, TT], F32)
            vps = ppt("vps", [128, 2, 512], F32)
            kps2 = ppt("kps2", [128, 256], F32)
            prev_ev = [None, None]
            prev_st = [None, None]
            for fc in range(4):
                for i in range(NTT):
                    pb = (fc * NTT + i) % 2
                    wsel = wqs if fc < 2 else wks
                    cchunk = fc % 2

                    def mm(t, wsel=wsel, cchunk=cchunk, i=i, pb=pb):
                        ins = None
                        for dc in range(NCH):
                            ins = t.matmul(qps[:, pb], wsel[:, dc, cchunk * 128:(cchunk + 1) * 128],
                                           xn[:, dc, i * TT:(i + 1) * TT],
                                           start=(dc == 0), stop=(dc == NCH - 1))
                        return ins
                    mdep = [ln_done[i]]
                    if prev_ev[pb] is not None:
                        mdep.append(prev_ev[pb])
                    o_mm = sq.emit("t", mm, after=mdep)

                    def ev(a, pb=pb, fc=fc):
                        return a.activation(out=ob[:, pb], in_=qps[:, pb], func=AF.Copy,
                                            scale=(0.25 if fc < 2 else 1.0))
                    edep = [o_mm]
                    if prev_st[pb] is not None:
                        edep.append(prev_st[pb])
                    o_ev = sq.emit("a", ev, after=edep)
                    prev_ev[pb] = o_ev
                    dst = qf_r[:, fc, i * TT:(i + 1) * TT] if fc < 2 else kf_r[:, fc - 2, i * TT:(i + 1) * TT]
                    prev_st[pb] = sq.dma("dg", dst, ob[:, pb], after=[o_ev])
            def vones(v):
                ins = None
                for hb in range(2):
                    for h in range(16):
                        ins = v.memset(vtm[:, hb, h, 64:65], 1.0)
                return ins
            o_vone = sq.emit("v", vones)
            prevk = [None, None]
            prevv = [None, None]
            prev_kev1 = None
            prev_vev1 = None
            for i2 in range(S // 128):
                hb = i2 % 2

                def ktm_mm(t, i2=i2):
                    ins = None
                    for dc in range(NCH):
                        ins = t.matmul(kps2[:, 0:256], xn[:, dc, i2 * 128:(i2 + 1) * 128],
                                       wks[:, dc, :], start=(dc == 0), stop=(dc == NCH - 1))
                    return ins
                kdep = [ln_done[min(i2 // 4, NTT - 1)]]
                if prevk[hb] is not None:
                    kdep.append(prevk[hb])
                if prev_kev1 is not None:
                    kdep.append(prev_kev1)
                o_km = sq.emit("t", ktm_mm, after=kdep)

                def ktm_ev(a, hb=hb):
                    return a.copy(out=ktt[:, hb], in_=kps2[:, 0:256])
                o_ke = sq.emit("a", ktm_ev, after=[o_km] + ([prevk[hb]] if prevk[hb] else []))
                prev_kev1 = o_ke
                o_ks = sq.dma("dg", kt_h[i2 * 128:(i2 + 1) * 128, :], ktt[:, hb], after=[o_ke])
                prevk[hb] = o_ks

                def vtm_mm(t, i2=i2):
                    ins = None
                    for half in range(2):
                        for dc in range(NCH):
                            ins = t.matmul(vps[:, half], xn[:, dc, i2 * 128:(i2 + 1) * 128],
                                           wvs[:, dc, half * 512:(half + 1) * 512],
                                           start=(dc == 0), stop=(dc == NCH - 1))
                    return ins
                vdep = [ln_done[min(i2 // 4, NTT - 1)]]
                if prevv[hb] is not None:
                    vdep.append(prevv[hb])
                if prev_vev1 is not None:
                    vdep.append(prev_vev1)
                o_vm = sq.emit("t", vtm_mm, after=vdep)

                def vtm_ev(a, hb=hb):
                    a.copy(out=vtm[:, hb, 0:8, 0:64], in_=vps[:, 0])
                    return a.copy(out=vtm[:, hb, 8:16, 0:64], in_=vps[:, 1])
                o_ve = sq.emit("a", vtm_ev, after=[o_vm, o_vone] + ([prevv[hb]] if prevv[hb] else []))
                prev_vev1 = o_ve
                o_vs = sq.dma("dg", vt_h[i2 * 128:(i2 + 1) * 128, :], vtm[:, hb], after=[o_ve])
                prevv[hb] = o_vs
            sq.set_floor()
        phAB.close()

        if upto <= 1:
            sq.finalize(block, sems)
            return nc

        with ExitStack() as ph:
            psb = lambda *a: ph.enter_context(nc.sbuf_tensor(*a))
            ppt = lambda *a: ph.enter_context(nc.psum_tensor(*a))
            qslot = psb("qslot", [128, 4, S], BF16)
            kslot = psb("kslot", [128, 4, S], BF16)
            qfm = psb("qfm", [128, 2, S], BF16)
            ktm = psb("ktm", [128, 32, 256], BF16)
            vtm2 = psb("vtm2", [128, 32, 16, 65], BF16)
            esel = psb("esel", [128, 4, 8, 128], BF16)
            Sst = psb("Sst", [128, 16, 2, 65], F32)
            Sbf = psb("Sbf", [128, 16, 2, 65], BF16)
            A2t = psb("A2t", [128, 128], BF16)
            phiq = psb("phiq", [128, 2, 128], BF16)
            phik = psb("phik", [128, 256], BF16)
            a1s = psb("a1s", [128, 128], F32)
            ebuf = psb("ebuf", [128, 4], F32)
            obuf = psb("obuf", [128, 16, 64], BF16)
            otr = psb("otr", [128, NCH, 128], BF16)
            dbgb = psb("dbgb", [128, 256], F32) if debug else None
            dbgb0 = psb("dbgb0", [128, 130], F32) if debug else None
            Aps = ppt("Aps", [128, 4, 512], F32)
            ops_ = ppt("ops_", [128, 65], F32)
            ups = ppt("ups", [128, 2, 256], F32)
            selps = ppt("selps", [128, 2, 128], F32)
            trps = ppt("trps", [128, 128], BF16)

            def zero_slots(v):
                v.memset(qslot[:], 0.0)
                v.memset(kslot[:], 0.0)
                v.memset(Sst[:], 0.0)
                return v.memset(Sbf[:], 0.0)
            o_zs = sq.emit("v", zero_slots)
            for sc in range(4):
                for b2 in range(4):
                    sq.dma("ds", qslot[32 * b2:32 * b2 + 16, sc, :],
                           qf_h[sc * 64 + 16 * b2:sc * 64 + 16 * b2 + 16, :], after=[o_zs])
                    sq.dma("ds", kslot[32 * b2:32 * b2 + 16, sc, :],
                           kf_h[sc * 64 + 16 * b2:sc * 64 + 16 * b2 + 16, :], after=[o_zs])
            sq.dma("ds", qfm[:], qf_h.rearrange("(c p) t -> p c t", p=128))
            sq.dma("ds", ktm[:], kt_h.rearrange("(r p) f -> p r f", p=128))
            sq.dma("ds", vtm2[:], vt_h.rearrange("(r p) (h f) -> p r h f", p=128, h=16))
            sq.dma("dg", esel[:, 0:2], e1m.transpose((1, 0, 2)).rearrange("p (x hh) f -> p x hh f", x=2))
            sq.dma("dg", esel[:, 2:4], e2m.transpose((1, 0, 2)).rearrange("p (x hh) f -> p x hh f", x=2))
            sq.set_floor()

            of_r = of_h.rearrange("(c p) t -> p c t", p=128)
            prev = {}
            prev_ost = None
            for i2 in range(S // 128):
                tsl = slice(i2 * 128, (i2 + 1) * 128)
                for h in range(16):
                    scn, j = divmod(h, 4)
                    if j == 0:
                        def amm(t, scn=scn, tsl=tsl):
                            ins = None
                            for jj in range(4):
                                ins = t.matmul(Aps[:, jj, 0:128], kslot[32 * jj:32 * (jj + 1), scn, tsl],
                                               qslot[32 * jj:32 * (jj + 1), scn, tsl], start=True, stop=True,
                                               tile_position=(32 * jj, 0))
                            return ins
                        adep = []
                        if ("sqdone",) in prev:
                            adep.append(prev[("sqdone",)])
                        prev[("amm", scn)] = sq.emit("t", amm, after=adep)
                    o_amm = prev[("amm", scn)]

                    o_sq = sq.emit("a", lambda a, j=j: a.activation(out=A2t[:], in_=Aps[:, j, 0:128], func=AF.Square),
                                   after=[o_amm] + ([prev[("a2used",)]] if ("a2used",) in prev else []))
                    o_msk = sq.emit("v", lambda v: v.tensor_tensor(out=A2t[:], in0=A2t[:], in1=trim[:], op=ALU.mult),
                                    after=[o_sq])
                    prev[("sqdone",)] = o_sq

                    hh = h % 8
                    xsel = 0 if h < 8 else 1
                    phis = []
                    for half in range(2):
                        def smm(t, hh=hh, half=half, xsel=xsel, tsl=tsl):
                            t.matmul(selps[:, 0], esel[:, half, hh, :], qfm[:, xsel, tsl], start=True, stop=True)
                            return t.matmul(selps[:, 1], esel[:, 2 + half, hh, :], qfm[:, xsel, tsl], start=True, stop=True)
                        sdep = []
                        if ("phimul",) in prev:
                            sdep.append(prev[("phimul",)])
                        if ("oused",) in prev:
                            sdep.append(prev[("oused",)])
                        o_sm = sq.emit("t", smm, after=sdep)
                        o_a1 = sq.emit("a", lambda a: a.copy(out=a1s[:], in_=selps[:, 0]),
                                       after=[o_sm] + ([prev[("phimul",)]] if ("phimul",) in prev else []))
                        phdep = [o_a1]
                        if ("oused",) in prev:
                            phdep.append(prev[("oused",)])
                        o_ph = sq.emit("v", lambda v, half=half: v.tensor_tensor(out=phiq[:, half], in0=selps[:, 1], in1=a1s[:], op=ALU.mult),
                                       after=phdep)
                        prev[("phimul",)] = o_ph
                        phis.append(o_ph)

                    def phik_op(v, h=h, i2=i2):
                        b1 = ktm[:, i2, 16 * h:16 * h + 16].unsqueeze(1).broadcast_to((128, 16, 16))
                        b2 = ktm[:, i2, 16 * h:16 * h + 16].unsqueeze(2).broadcast_to((128, 16, 16))
                        return v.tensor_tensor(out=phik[:].rearrange("p (a b) -> p a b", a=16), in0=b2, in1=b1, op=ALU.mult)
                    o_pk = sq.emit("v", phik_op, after=[prev[("ups",)]] if ("ups",) in prev else [])

                    def omm(t, h=h, i2=i2):
                        t.matmul(ops_[:, 0:65], phiq[:, 0], Sbf[:, h, 0, :], start=True, stop=False)
                        t.matmul(ops_[:, 0:65], phiq[:, 1], Sbf[:, h, 1, :], start=False, stop=False)
                        return t.matmul(ops_[:, 0:65], A2t[:], vtm2[:, i2, h, :], start=False, stop=True)
                    if debug and i2 == 0 and h == 0:
                        d_p = sq.emit("a", lambda a: a.copy(out=dbgb0[:, 0:130], in_=Sbf[:, 0].rearrange("p a b -> p (a b)")), after=phis)
                        sq.dma("dg", dbg_h[7, :, 0:130], dbgb0[:, 0:130], after=[d_p])
                    odep = phis + [o_msk]
                    if ("onrm",) in prev:
                        odep.append(prev[("onrm",)])
                    if ("supd_h", h) in prev:
                        odep.append(prev[("supd_h", h)])
                    if debug and i2 == 1 and h == 0:
                        d_p2 = sq.emit("a", lambda a: a.copy(out=dbgb0[:, 0:130], in_=Sbf[:, 0].rearrange("p a b -> p (a b)")), after=odep)
                        d_p2d = sq.dma("dg", dbg_h[4, :, 0:130], dbgb0[:, 0:130], after=[d_p2])
                        odep = odep + [d_p2]
                    o_om = sq.emit("t", omm, after=odep)

                    def umm(t, h=h, i2=i2):
                        t.matmul(ups[:, 0, 0:65], phik[:, 0:128], vtm2[:, i2, h, :], start=True, stop=True)
                        return t.matmul(ups[:, 1, 0:65], phik[:, 128:256], vtm2[:, i2, h, :], start=True, stop=True)
                    udep = [o_pk, o_om]
                    if ("supd",) in prev:
                        udep.append(prev[("supd",)])
                    o_um = sq.emit("t", umm, after=udep)
                    prev[("ups",)] = o_um

                    def supd(v, h=h):
                        v.tensor_tensor(out=Sst[:, h, 0, :], in0=Sst[:, h, 0, :], in1=ups[:, 0, 0:65], op=ALU.add)
                        v.tensor_tensor(out=Sst[:, h, 1, :], in0=Sst[:, h, 1, :], in1=ups[:, 1, 0:65], op=ALU.add)
                        v.tensor_copy(out=Sbf[:, h, 0, :], in_=Sst[:, h, 0, :])
                        return v.tensor_copy(out=Sbf[:, h, 1, :], in_=Sst[:, h, 1, :])
                    o_su = sq.emit("v", supd, after=[o_um])
                    prev[("supd",)] = o_su
                    prev[("supd_h", h)] = o_su

                    o_na0 = sq.emit("v", lambda v: v.tensor_scalar(out=ebuf[:, 0:1], in0=ops_[:, 64:65], scalar1=1e-5,
                                                                   scalar2=None, op0=ALU.add), after=[o_om], hazard=True)
                    o_na = sq.emit("v", lambda v: v.reciprocal(out=ebuf[:, 1:2], in_=ebuf[:, 0:1]),
                                   after=[o_na0], hazard=True)

                    def nrm_b(v, h=h):
                        return v.tensor_scalar(out=obuf[:, h], in0=ops_[:, 0:64], scalar1=ebuf[:, 1:2],
                                               scalar2=None, op0=ALU.mult)
                    ndep = [o_na]
                    if ("trlast",) in prev:
                        ndep.append(prev[("trlast",)])
                    o_nb = sq.emit("v", nrm_b, after=ndep, hazard=True)
                    prev[("onrm",)] = o_nb
                    prev[("a2used",)] = o_om
                    prev[("oused",)] = o_om
                    prev[("obh", h)] = o_nb
                    if debug and i2 == 1 and h == 0:
                        d_e2 = sq.emit("a", lambda a: a.copy(out=dbgb[:, 65:130], in_=ops_[:, 0:65]), after=[o_om, o_nb])
                        d_e2d = sq.dma("dg", dbg_h[5, :, 0:65], dbgb[:, 65:130], after=[d_e2])
                        prev[("onrm",)] = sq.emit("v", lambda v: v.nop(), after=[d_e2d, o_nb])
                        prev[("a2used",)] = prev[("onrm",)]
                        prev[("oused",)] = prev[("onrm",)]
                    if debug and i2 == 0 and h == 0:
                        sq.dma("dg", dbg_h[0, :, 0:128], A2t[:], after=[o_om])
                        sq.dma("dg", dbg_h[1, :, 0:256], phiq[:].rearrange("p a b -> p (a b)"), after=[o_om])
                        sq.dma("dg", dbg_h[2, :, 0:256], phik[:], after=[o_um])
                        d_e = sq.emit("a", lambda a: a.copy(out=dbgb[:, 0:65], in_=ops_[:, 0:65]), after=[o_om, o_nb])
                        d_ed = sq.dma("dg", dbg_h[3, :, 0:65], dbgb[:, 0:65], after=[d_e])
                        d_q = sq.emit("a", lambda a: a.copy(out=dbgb[:, 65:130], in_=vtm2[:, 0, 0, :]), after=[o_om])
                        d_qd = sq.dma("dg", dbg_h[6, :, 0:65], dbgb[:, 65:130], after=[d_q])
                        d_q2 = sq.emit("a", lambda a: a.copy(out=dbgb[:, 130:256], in_=vtm2[:].rearrange("p a b c -> p (a b c)")[:, 0:126]), after=[o_om])
                        sq.dma("dg", dbg_h[5, :, 0:126], dbgb[:, 130:256], after=[d_q2])
                        prev[("onrm",)] = sq.emit("v", lambda v: v.nop(), after=[d_ed, d_qd, o_nb])
                        prev[("a2used",)] = prev[("onrm",)]
                        prev[("oused",)] = prev[("onrm",)]

                tr_evs = []
                for dc in range(NCH):
                    def trmm(t, dc=dc):
                        return t.transpose(trps[:], obuf[:].rearrange("p a b -> p (a b)")[:, dc * 128:(dc + 1) * 128], iden[:])
                    tdep = [prev[("obh", h2)] for h2 in (2 * dc, 2 * dc + 1)]
                    if tr_evs:
                        tdep.append(tr_evs[-1])
                    o_tr = sq.emit("t", trmm, after=tdep)

                    def trev(a, dc=dc):
                        return a.copy(out=otr[:, dc], in_=trps[:])
                    edep = [o_tr]
                    if prev_ost is not None:
                        edep.append(prev_ost)
                    o_te = sq.emit("a", trev, after=edep)
                    tr_evs.append(o_te)
                prev[("trlast",)] = tr_evs[-1]
                prev_ost = sq.dma("dg", of_r[:, :, tsl], otr[:], after=[tr_evs[-1]])
            sq.set_floor()

        if upto <= 2:
            sq.finalize(block, sems)
            return nc

        with ExitStack() as ph:
            psb = lambda *a: ph.enter_context(nc.sbuf_tensor(*a))
            ppt = lambda *a: ph.enter_context(nc.psum_tensor(*a))
            wo_sb = psb("wo_sb", [128, NCH, D], BF16)
            sq.dma("dg", wo_sb[:], wo.rearrange("(c p) f -> p c f", p=128))
            sq.set_floor()
            oin = psb("oin", [128, 2, NCH, TT], BF16)
            so_t = psb("so_t", [128, NCH, TT], BF16)
            sps = ppt("sps", [128, 2, TT], F32)
            s_r = s_hbm.rearrange("(c p) t -> p c t", p=128)
            of_r = of_h.rearrange("(c p) t -> p c t", p=128)
            prev_ev = [None, None]
            prev_st = None
            ld_user = [None, None]
            for i in range(NTT):
                dep = [] if ld_user[i % 2] is None else [ld_user[i % 2]]
                ld = sq.dma("ds", oin[:, i % 2], of_r[:, :, i * TT:(i + 1) * TT], after=dep)
                last_ev = None
                for cc in range(NCH):
                    pb = (i * NCH + cc) % 2

                    def proj(t, cc=cc, i=i, pb=pb):
                        ins = None
                        for dc in range(NCH):
                            ins = t.matmul(sps[:, pb], wo_sb[:, dc, cc * 128:(cc + 1) * 128],
                                           oin[:, i % 2, dc], start=(dc == 0), stop=(dc == NCH - 1))
                        return ins
                    pdep = [ld]
                    if prev_ev[pb] is not None:
                        pdep.append(prev_ev[pb])
                    o_p = sq.emit("t", proj, after=pdep)

                    def evac(a, cc=cc, pb=pb):
                        return a.copy(out=so_t[:, cc], in_=sps[:, pb])
                    edep = [o_p] + ([prev_st] if prev_st is not None else [])
                    o_e = sq.emit("a", evac, after=edep)
                    prev_ev[pb] = o_e
                    last_ev = o_e
                ld_user[i % 2] = last_ev
                prev_st = sq.dma("dg", s_r[:, :, i * TT:(i + 1) * TT], so_t[:], after=[last_ev])
            sq.set_floor()

        if upto <= 3:
            sq.finalize(block, sems)
            return nc

        with ExitStack() as ph:
            psb = lambda *a: ph.enter_context(nc.sbuf_tensor(*a))
            ppt = lambda *a: ph.enter_context(nc.psum_tensor(*a))
            murs = psb("murs", [128, 4], F32)
            sq.dma("ds", murs[:, 0:1], murf[:, 0:1]) if False else None
            mur = psb("mur", [128, 2, 8], F32)
            mus = psb("mus", [128, 4, 8], F32)
            sq.dma("ds", mus[:, 0, :], murf[:, :])
            sq.dma("ds", mus[:, 1, :], mukf[:, :])

            def muprep(v):
                v.tensor_scalar(out=mus[:, 2, :], in0=mus[:, 0, :], scalar1=-1.0, scalar2=1.0, op0=ALU.mult, op1=ALU.add)
                return v.tensor_scalar(out=mus[:, 3, :], in0=mus[:, 1, :], scalar1=-1.0, scalar2=1.0, op0=ALU.mult, op1=ALU.add)
            sq.emit("v", muprep, after=[])
            wr_sb = psb("wr_sb", [128, NCH, D], BF16)
            wk_sb = psb("wk_sb", [128, NCH, D], BF16)
            wv_sb = psb("wv_sb", [128, NCH, D], BF16)
            sq.dma("dg", wr_sb[:], wrT.rearrange("(c p) f -> p c f", p=128))
            sq.dma("dg", wk_sb[:], wkT2.rearrange("(c p) f -> p c f", p=128))
            sq.dma("dg", wv_sb[:], wvT2.rearrange("(c p) f -> p c f", p=128))
            sq.set_floor()

            xng = psb("xng", [128, 2, NCH, TT + 1], BF16)
            xrb = psb("xrb", [128, NCH, TT], BF16)
            xkb = psb("xkb", [128, NCH, TT], BF16)
            mixt = psb("mixt", [128, TT], F32)
            rbuf = psb("rbuf", [128, NCH, TT], BF16)
            kab = psb("kab", [128, NCH, TT], BF16)
            gb = psb("gb", [128, TT], F32)
            xres = psb("xres", [128, 2, NCH, TT], F32)
            sres = psb("sres", [128, 2, NCH, TT], BF16)
            onew = psb("onew", [128, NCH, TT], F32)
            rps = ppt("rps", [128, 2, TT], F32)
            kps = ppt("kps", [128, 2, TT], F32)
            vps2 = ppt("vps2", [128, 2, TT], F32)
            xn_r = xn_h.rearrange("(c p) t -> p c t", p=128)
            xi_r = x_in.rearrange("(c p) t -> p c t", p=128)
            xo_r = x_out.rearrange("(c p) t -> p c t", p=128)
            s_r = s_hbm.rearrange("(c p) t -> p c t", p=128)

            prev_mix = None
            prev_ke2 = None
            prev_rev = [None, None]
            prev_kev = [None, None]
            prev_vev = [None, None]
            prev_gmul = None
            prev_xst = None
            xld_user = [None, None]
            out_stores = []
            for i in range(NTT):
                dep = [] if xld_user[i % 2] is None else [xld_user[i % 2]]
                if i == 0:
                    o_lxn = sq.dma("ds", xng[:, 0, :, 1:TT + 1], xn_r[:, :, 0:TT], after=dep)
                    o_g0 = sq.emit("v", lambda v: v.memset(xng[:, 0, :, 0:1], 0.0), after=dep)
                    o_lxn_all = [o_lxn, o_g0]
                else:
                    o_lxn = sq.dma("ds", xng[:, i % 2, :, 0:TT + 1], xn_r[:, :, i * TT - 1:(i + 1) * TT], after=dep)
                    o_lxn_all = [o_lxn]
                o_lx = sq.dma("ds", xres[:, i % 2], xi_r[:, :, i * TT:(i + 1) * TT], after=dep)
                o_ls = sq.dma("ds", sres[:, i % 2], s_r[:, :, i * TT:(i + 1) * TT], after=dep)

                def mix(v, i=i):
                    ins = None
                    for dc in range(NCH):
                        cur = xng[:, i % 2, dc, 1:TT + 1]
                        prv = xng[:, i % 2, dc, 0:TT]
                        v.tensor_scalar(out=mixt[:], in0=prv, scalar1=mus[:, 2, dc:dc + 1], scalar2=None, op0=ALU.mult)
                        v.scalar_tensor_tensor(out=xrb[:, dc], in0=cur, scalar=mus[:, 0, dc:dc + 1], in1=mixt[:], op0=ALU.mult, op1=ALU.add)
                        v.tensor_scalar(out=mixt[:], in0=prv, scalar1=mus[:, 3, dc:dc + 1], scalar2=None, op0=ALU.mult)
                        ins = v.scalar_tensor_tensor(out=xkb[:, dc], in0=cur, scalar=mus[:, 1, dc:dc + 1], in1=mixt[:], op0=ALU.mult, op1=ALU.add)
                    return ins
                mdep = o_lxn_all + ([prev_mix] if prev_mix else [])
                o_mix = sq.emit("v", mix, after=mdep)

                res_ops = []
                for cc in range(NCH):
                    pb = (i * NCH + cc) % 2

                    def rmm(t, cc=cc, pb=pb):
                        ins = None
                        for dc in range(NCH):
                            ins = t.matmul(rps[:, pb], wr_sb[:, dc, cc * 128:(cc + 1) * 128], xrb[:, dc],
                                           start=(dc == 0), stop=(dc == NCH - 1))
                        return ins
                    rdep = [o_mix]
                    if prev_rev[pb] is not None:
                        rdep.append(prev_rev[pb])
                    o_rm = sq.emit("t", rmm, after=rdep)

                    def rev(a, cc=cc, pb=pb):
                        return a.activation(out=rbuf[:, cc], in_=rps[:, pb], func=AF.Sigmoid)
                    o_re = sq.emit("a", rev, after=[o_rm])
                    prev_rev[pb] = o_re

                    def kmm(t, cc=cc, pb=pb):
                        ins = None
                        for dc in range(NCH):
                            ins = t.matmul(kps[:, pb], wk_sb[:, dc, cc * 128:(cc + 1) * 128], xkb[:, dc],
                                           start=(dc == 0), stop=(dc == NCH - 1))
                        return ins
                    kdep = [o_mix]
                    if prev_kev[pb] is not None:
                        kdep.append(prev_kev[pb])
                    o_km = sq.emit("t", kmm, after=kdep)

                    def kev(v, cc=cc, pb=pb):
                        return v.tensor_scalar(out=mixt[:], in0=kps[:, pb], scalar1=0.0, scalar2=None, op0=ALU.max)
                    kedep = [o_km, o_mix]
                    if prev_ke2 is not None:
                        kedep.append(prev_ke2)
                    o_ke = sq.emit("v", kev, after=kedep)

                    def kev2(a, cc=cc):
                        return a.activation(out=kab[:, cc], in_=mixt[:], func=AF.Square)
                    o_ke2 = sq.emit("a", kev2, after=[o_ke])
                    prev_kev[pb] = o_ke2
                    prev_ke2 = o_ke2

                kv_evs = []
                for cc in range(NCH):
                    pb = (i * NCH + cc) % 2

                    def vmm(t, cc=cc, pb=pb):
                        ins = None
                        for dc in range(NCH):
                            ins = t.matmul(vps2[:, pb], wv_sb[:, dc, cc * 128:(cc + 1) * 128], kab[:, dc],
                                           start=(dc == 0), stop=(dc == NCH - 1))
                        return ins
                    vdep = [d for d in (prev_kev[0], prev_kev[1]) if d is not None]
                    if prev_vev[pb] is not None:
                        vdep.append(prev_vev[pb])
                    o_vm = sq.emit("t", vmm, after=vdep)

                    def gmul(v, cc=cc, pb=pb):
                        return v.tensor_tensor(out=gb[:], in0=rbuf[:, cc], in1=vps2[:, pb], op=ALU.mult)
                    gdep = [o_vm, prev_rev[0], prev_rev[1]]
                    if prev_gmul is not None:
                        gdep.append(prev_gmul)
                    o_gm = sq.emit("v", gmul, after=gdep)
                    prev_vev[pb] = o_gm

                    def gsil(a, cc=cc):
                        return a.activation(out=gb[:], in_=gb[:], func=AF.Silu)
                    o_gs = sq.emit("a", gsil, after=[o_gm])

                    def resid(v, cc=cc, i=i):
                        v.tensor_tensor(out=onew[:, cc], in0=sres[:, i % 2, cc], in1=gb[:], op=ALU.mult)
                        return v.tensor_tensor(out=onew[:, cc], in0=onew[:, cc], in1=xres[:, i % 2, cc], op=ALU.add)
                    rdep2 = [o_gs, o_lx, o_ls]
                    if prev_xst is not None:
                        rdep2.append(prev_xst)
                    o_r = sq.emit("v", resid, after=rdep2)
                    prev_gmul = o_r
                    res_ops.append(o_r)
                prev_mix = res_ops[-1]
                xld_user[i % 2] = res_ops[-1]
                prev_xst = sq.dma("dg", xo_r[:, :, i * TT:(i + 1) * TT], onew[:], after=res_ops)
                out_stores.append(prev_xst)

            if ctx is None:
                sq.finalize(block, sems, outputs_done=out_stores)
            else:
                ctx["out_stores"] = out_stores
                sq.set_floor()
    return nc


def l1_inmap(xcore_fm, I):
    f = 16
    E1 = np.zeros((16, 128, 128), np.float32)
    E2 = np.zeros((16, 128, 128), np.float32)
    for hh in range(8):
        for half in range(2):
            for jj in range(8):
                for i in range(16):
                    col = jj * 16 + i
                    E1[half * 8 + hh, 16 * hh + i, col] = 1.0
                    E2[half * 8 + hh, 16 * hh + half * 8 + jj, col] = 1.0
    trimask = np.triu(np.ones((128, 128), np.float32))
    return {
        "x_in": xcore_fm,
        "wq": np.ascontiguousarray(I["L1_attn_Wq"]),
        "wk": np.ascontiguousarray(I["L1_attn_Wk"]),
        "wv": np.ascontiguousarray(I["L1_attn_Wv"]),
        "wo": np.ascontiguousarray(I["L1_attn_Wo"]),
        "wrT": np.ascontiguousarray(I["L1_ffn_W_r"].T),
        "wkT2": np.ascontiguousarray(I["L1_ffn_W_k"].T),
        "wvT2": np.ascontiguousarray(I["L1_ffn_W_v"].T),
        "murf": fm(I["L1_ffn_mu_r"]),
        "mukf": fm(I["L1_ffn_mu_k"]),
        "e1m": E1.astype(BF),
        "e2m": E2.astype(BF),
        "trimask": trimask.astype(BF),
        "iden_h": np.eye(128, dtype=np.float32).astype(BF),
        "ones128_h": np.ones((128, 1), np.float32),
        "ones1_h": np.ones((1, 128), np.float32),
    }




def build_l0(debug=False, upto=99, ctx=None):
    from contextlib import ExitStack
    nc = bass.Bass("TRN2", target_bir_lowering=False, debug=False) if ctx is None else ctx["nc"]
    x_in = nc.dram_tensor("x_in", [D, S], F32, kind="ExternalInput").ap()
    wqkvT = nc.dram_tensor("wqkvT", [D, 3 * D], F32, kind="ExternalInput").ap()
    bqkvf = nc.dram_tensor("bqkvf", [128, 16], F32, kind="ExternalInput").ap()
    bvrow = nc.dram_tensor("bvrow", [1, D], F32, kind="ExternalInput").ap()
    wqaT = nc.dram_tensor("wqaT", [D, D], F32, kind="ExternalInput").ap()
    bqaf = nc.dram_tensor("bqaf", [128, 8], F32, kind="ExternalInput").ap()
    wkaT = nc.dram_tensor("wkaT", [D, D], F32, kind="ExternalInput").ap()
    bkaf = nc.dram_tensor("bkaf", [128, 8], F32, kind="ExternalInput").ap()
    wdT = nc.dram_tensor("wdT", [D, D], F32, kind="ExternalInput").ap()
    bdf = nc.dram_tensor("bdf", [128, 8], F32, kind="ExternalInput").ap()
    g1h = nc.dram_tensor("g1h", [2, 4, 32, 32], F32, kind="ExternalInput").ap()
    g2h = nc.dram_tensor("g2h", [2, 4, 16, 16], F32, kind="ExternalInput").ap()
    brf = nc.dram_tensor("brf", [128, 8], F32, kind="ExternalInput").ap()
    seld_h = nc.dram_tensor("seld_h", [32, 512], BF16, kind="ExternalInput").ap()
    selg_h = nc.dram_tensor("selg_h", [16, 128], BF16, kind="ExternalInput").ap()
    cosT_h = nc.dram_tensor("cosT_h", [128, S], BF16, kind="ExternalInput").ap()
    sinT_h = nc.dram_tensor("sinT_h", [128, S], BF16, kind="ExternalInput").ap()
    wmaskd_h = nc.dram_tensor("wmaskd_h", [128, 128], BF16, kind="ExternalInput").ap()
    wmasks_h = nc.dram_tensor("wmasks_h", [128, 128], BF16, kind="ExternalInput").ap()
    cmask_h = nc.dram_tensor("cmask_h", [128, 4, 512], BF16, kind="ExternalInput").ap()
    iden_h = nc.dram_tensor("iden_h", [128, 128], BF16, kind="ExternalInput").ap()
    onescol_h = nc.dram_tensor("onescol_h", [1, 128], BF16, kind="ExternalInput").ap()
    ones128_h = nc.dram_tensor("ones128_h", [128, 1], F32, kind="ExternalInput").ap()
    ones1_h = nc.dram_tensor("ones1_h", [1, 128], F32, kind="ExternalInput").ap()
    onesf_h = nc.dram_tensor("onesf_h", [128, 128], F32, kind="ExternalInput").ap()
    x_out = nc.dram_tensor("x_out", [D, S], F32, kind="ExternalOutput").ap()

    kind = "ExternalOutput" if debug else "Internal"
    xn_h = nc.dram_tensor("xn_h", [D, S], BF16, kind=kind).ap()
    qf_h = nc.dram_tensor("qf_h", [D, S], BF16, kind=kind).ap()
    kf_h = nc.dram_tensor("kf_h", [D, S], BF16, kind=kind).ap()
    vt_h = nc.dram_tensor("vt_h", [S, 16 * 65], BF16, kind=kind).ap()
    at_h = nc.dram_tensor("at_h", [S, 16 * 65], BF16, kind=kind).ap()
    af_h = nc.dram_tensor("af_h", [D, S], BF16, kind=kind).ap()
    qaf_h = nc.dram_tensor("qaf_h", [D, S], BF16, kind=kind).ap()
    kaf_h = nc.dram_tensor("kaf_h", [D, S], BF16, kind=kind).ap()
    s_hbm = nc.dram_tensor("s_hbm", [D, S], BF16, kind=kind).ap()

    stack = ExitStack()
    with stack:
        if ctx is None:
            sems = alloc_sems(nc, stack)
            block = stack.enter_context(nc.Block())
            sq = Seq(nc)
        else:
            sems, block, sq = ctx["sems"], ctx["block"], ctx["sq"]
        sb = lambda *a: stack.enter_context(nc.sbuf_tensor(*a))

        ones128 = sb("c_ones128", [128, 1], F32)
        ones1 = sb("c_ones1", [1, 128], F32)
        onescol = sb("c_onescol", [1, 128], BF16)
        iden = sb("c_iden", [128, 128], BF16)
        cst = sb("cst", [128, 2], F32)

        sq.dma("ds", ones128[:], ones128_h[:, :])
        sq.dma("ds", ones1[:], ones1_h[:, :])
        sq.dma("ds", onescol[:], onescol_h[:, :])
        sq.dma("ds", iden[:], iden_h[:, :])
        sq.emit("v", lambda v: v.memset(cst[:, 0:1], 1e-5))
        sq.set_floor()

        phAB = ExitStack()
        xn = phAB.enter_context(nc.sbuf_tensor("xn", [128, NCH, S], BF16))
        with ExitStack() as ph:
            ln_done = emit_ln(sq, nc, ph, x_in, xn, ones128, ones1, cst, col_off=0)
            xn_r = xn_h.rearrange("(c p) t -> p c t", p=128)
            for i in range(NTT):
                sq.dma("dg", xn_r[:, :, i * TT:(i + 1) * TT], xn[:, :, i * TT:(i + 1) * TT], after=[ln_done[i]])
            sq.set_floor()

        if upto <= 0:
            phAB.close()
            sq.finalize(block, sems)
            return nc

        qf_r = qf_h.rearrange("(c p) t -> p c t", p=128)
        kf_r = kf_h.rearrange("(c p) t -> p c t", p=128)
        wq_r = wqkvT.rearrange("(c p) f -> p c f", p=128)
        with ExitStack() as ph:
            psb = lambda *a: ph.enter_context(nc.sbuf_tensor(*a))
            ppt = lambda *a: ph.enter_context(nc.psum_tensor(*a))
            cosT = psb("cosT", [128, S], BF16)
            sinT = psb("sinT", [128, S], BF16)
            sq.dma("ds", cosT[:], cosT_h[:, :])
            sq.dma("ds", sinT[:], sinT_h[:, :])
            wt = psb("wt", [128, 2, NCH, 128], BF16)
            fullch = psb("fullch", [128, 2, S], BF16)
            ropesc = psb("ropesc", [128, 2, S], BF16)
            bvs = psb("bvs", [1, D], BF16)
            vtm = psb("vtm", [128, 2, 16, 65], BF16)
            qps = ppt("qps", [128, 2, TT], F32)
            vps = ppt("vps", [128, 2, 512], F32)
            sq.dma("dg", bvs[:], bvrow[:, :])
            bqs = psb("bqs", [128, 16], F32)
            sq.dma("ds", bqs[:], bqkvf[:, :])
            sq.set_floor()

            prev_ev = [None, None]
            prev_chst = [None, None]
            for fc in ([] if L0SUB in (-1, -3) else range(16)):
                wb = fc % 2
                cb = fc % 2
                wdep = [sq.last["t"]] if sq.last.get("t") is not None else []
                o_lw = sq.dma("dg", wt[:, wb], wq_r[:, :, fc * 128:(fc + 1) * 128], after=wdep)
                evs = []
                for i in range(NTT):
                    pb = (fc * NTT + i) % 2

                    def mm(t, wb=wb, i=i, pb=pb):
                        ins = None
                        for dc in range(NCH):
                            ins = t.matmul(qps[:, pb], wt[:, wb, dc], xn[:, dc, i * TT:(i + 1) * TT],
                                           start=(dc == 0), stop=(dc == NCH - 1))
                        return ins
                    mdep = [o_lw, ln_done[i]]
                    if prev_ev[pb] is not None:
                        mdep.append(prev_ev[pb])
                    o_mm = sq.emit("t", mm, after=mdep)

                    def ev(a, fc=fc, i=i, pb=pb, cb=cb):
                        return a.activation(out=fullch[:, cb, i * TT:(i + 1) * TT], in_=qps[:, pb],
                                            func=AF.Identity, bias=bqs[:, fc:fc + 1])
                    edep = [o_mm]
                    if prev_chst[cb] is not None:
                        edep.append(prev_chst[cb])
                    o_ev = sq.emit("a", ev, after=edep)
                    prev_ev[pb] = o_ev
                    evs.append(o_ev)
                if L0SUB < 1:
                    dst = qf_r[:, fc, :] if fc < 8 else kf_r[:, fc - 8, :]
                    prev_chst[cb] = sq.dma("dg", dst, fullch[:, cb, :], after=evs)
                    continue
                o_g1 = sq.dma("ds", ropesc[0:8, cb, :], fullch[8:16, cb, :], after=evs)
                o_g2 = sq.dma("ds", ropesc[8:16, cb, :], fullch[0:8, cb, :], after=evs)
                o_g3 = sq.dma("ds", ropesc[64:72, cb, :], fullch[72:80, cb, :], after=evs)
                o_g4 = sq.dma("ds", ropesc[72:80, cb, :], fullch[64:72, cb, :], after=evs)

                def rope(v, cb=cb):
                    for b0 in (0, 64):
                        v.tensor_tensor(out=ropesc[b0:b0 + 16, cb, :], in0=ropesc[b0:b0 + 16, cb, :],
                                        in1=sinT[b0:b0 + 16, :], op=ALU.mult)
                        v.tensor_tensor(out=fullch[b0:b0 + 16, cb, :], in0=fullch[b0:b0 + 16, cb, :],
                                        in1=cosT[b0:b0 + 16, :], op=ALU.mult)
                    ins = None
                    for b0 in (0, 64):
                        ins = v.tensor_tensor(out=fullch[b0:b0 + 16, cb, :], in0=fullch[b0:b0 + 16, cb, :],
                                              in1=ropesc[b0:b0 + 16, cb, :], op=ALU.add)
                    return ins
                o_rope = sq.emit("v", rope, after=[o_g1, o_g2, o_g3, o_g4])
                dst = qf_r[:, fc, :] if fc < 8 else kf_r[:, fc - 8, :]
                prev_chst[cb] = sq.dma("dg", dst, fullch[:, cb, :], after=[o_rope])

            def vones(v):
                ins = None
                for hb in range(2):
                    for h in range(16):
                        ins = v.memset(vtm[:, hb, h, 64:65], 1.0)
                return ins
            o_vone = sq.emit("v", vones)
            vw = psb("vw", [128, NCH, D], BF16)
            o_vws = []
            for dc in range(NCH):
                o_vws.append(sq.dma("dg", vw[:, dc, :], wqkvT[dc * 128:(dc + 1) * 128, 2048:3072],
                                    after=[sq.last["t"]] if sq.last.get("t") is not None else []))
            o_vw = o_vws[-1]
            prevv = [None, None]
            prev_vev = None
            for i2 in ([] if L0SUB == -2 else range(1) if L0SUB == -3 else range(S // 128)):
                hb = i2 % 2

                def vmm(t, i2=i2):
                    ins = None
                    for half in range(2):
                        if L0SUB >= 2:
                            t.matmul(vps[:, half], onescol[:, 0:128], bvs[0:1, half * 512:(half + 1) * 512],
                                     start=True, stop=False)
                        for dc in range(NCH):
                            ins = t.matmul(vps[:, half], xn[:, dc, i2 * 128:(i2 + 1) * 128],
                                           vw[:, dc, half * 512:(half + 1) * 512],
                                           start=(dc == 0 and L0SUB < 2), stop=(dc == NCH - 1))
                    return ins
                vdep = o_vws + [ln_done[min(i2 // 4, NTT - 1)]]
                if prevv[hb] is not None:
                    vdep.append(prevv[hb])
                if prev_vev is not None:
                    vdep.append(prev_vev)
                o_vm = sq.emit("t", vmm, after=vdep)

                def vev(a, hb=hb):
                    a.copy(out=vtm[:, hb, 0:8, 0:64], in_=vps[:, 0])
                    return a.copy(out=vtm[:, hb, 8:16, 0:64], in_=vps[:, 1])
                o_ve = sq.emit("a", vev, after=[o_vm, o_vone] + ([prevv[hb]] if prevv[hb] else []))
                prev_vev = o_ve
                prevv[hb] = sq.dma("dg", vt_h[i2 * 128:(i2 + 1) * 128, :], vtm[:, hb], after=[o_ve])
            sq.set_floor()
        phAB.close()

        if upto <= 1:
            sq.finalize(block, sems)
            return nc

        with ExitStack() as ph:
            psb = lambda *a: ph.enter_context(nc.sbuf_tensor(*a))
            ppt = lambda *a: ph.enter_context(nc.psum_tensor(*a))
            wmd = psb("wmd", [128, 128], BF16)
            wms = psb("wms", [128, 128], BF16)
            sq.dma("ds", wmd[:], wmaskd_h[:, :])
            sq.dma("ds", wms[:], wmasks_h[:, :])
            qsl = psb("qsl", [128, NCH, 128], BF16)
            ksl = psb("ksl", [128, 2, NCH, 128], BF16)
            vsl = psb("vsl", [128, 2, 16, 65], BF16)
            pt_d = psb("pt_d", [128, 2, 128], BF16)
            pt_s = psb("pt_s", [128, 2, 128], BF16)
            ebuf = psb("ebuf", [128, 4], F32)
            att = psb("att", [128, 16, 65], BF16)
            aft = psb("aft", [128, NCH, 128], BF16)
            sps_ = ppt("sps_", [128, 4, 512], F32)
            aps_ = ppt("aps_", [128, 2, 512], F32)
            tps_ = ppt("tps_", [128, 128], BF16)
            sq.set_floor()

            prev = {}
            prev_atst = None
            prev_afst = None
            for i2 in range(S // 128):
                tsl = slice(i2 * 128, (i2 + 1) * 128)
                hb = i2 % 2
                qdep = [sq.last["t"]] if sq.last.get("t") is not None else []
                o_lq = sq.dma("ds", qsl[:], qf_r[:, :, tsl], after=qdep)
                o_lk = sq.dma("ds", ksl[:, hb], kf_r[:, :, tsl], after=qdep)
                o_lv = sq.dma("ds", vsl[:, hb], vt_h[tsl, :], after=qdep)
                for hp in range(8):
                    def smm(t, hp=hp, hb=hb, i2=i2):
                        ins = None
                        for hh in range(2):
                            b0 = 64 * hh
                            ins = t.matmul(sps_[:, hh, 0:128], ksl[b0:b0 + 64, hb, hp, :],
                                           qsl[b0:b0 + 64, hp, :], start=True, stop=True,
                                           tile_position=(b0, 0))
                        if i2 > 0:
                            for hh in range(2):
                                b0 = 64 * hh
                                ins = t.matmul(sps_[:, 2 + hh, 0:128], ksl[b0:b0 + 64, 1 - hb, hp, :],
                                               qsl[b0:b0 + 64, hp, :], start=True, stop=True,
                                               tile_position=(b0, 0))
                        return ins
                    sdep = [o_lq, o_lk]
                    if ("expdone",) in prev:
                        sdep.append(prev[("expdone",)])
                    o_sm = sq.emit("t", smm, after=sdep)

                    def exps(a, i2=i2):
                        a.activation(out=pt_d[:, 0], in_=sps_[:, 0, 0:128], func=AF.Exp, scale=0.125)
                        ins = a.activation(out=pt_d[:, 1], in_=sps_[:, 1, 0:128], func=AF.Exp, scale=0.125)
                        if i2 > 0:
                            a.activation(out=pt_s[:, 0], in_=sps_[:, 2, 0:128], func=AF.Exp, scale=0.125)
                            ins = a.activation(out=pt_s[:, 1], in_=sps_[:, 3, 0:128], func=AF.Exp, scale=0.125)
                        return ins
                    edep = [o_sm]
                    if ("mskdone",) in prev:
                        edep.append(prev[("mskdone",)])
                    o_ex = sq.emit("a", exps, after=edep)
                    prev[("expdone",)] = o_ex

                    def msk(v, i2=i2):
                        v.tensor_tensor(out=pt_d[:, 0], in0=pt_d[:, 0], in1=wmd[:], op=ALU.mult)
                        ins = v.tensor_tensor(out=pt_d[:, 1], in0=pt_d[:, 1], in1=wmd[:], op=ALU.mult)
                        if i2 > 0:
                            v.tensor_tensor(out=pt_s[:, 0], in0=pt_s[:, 0], in1=wms[:], op=ALU.mult)
                            ins = v.tensor_tensor(out=pt_s[:, 1], in0=pt_s[:, 1], in1=wms[:], op=ALU.mult)
                        return ins
                    o_mk = sq.emit("v", msk, after=[o_ex])
                    prev[("mskdone",)] = o_mk

                    for hh in range(2):
                        h = 2 * hp + hh
                        ab_ = h % 2

                        def acc(t, hh=hh, h=h, hb=hb, i2=i2, ab_=ab_):
                            if i2 > 0:
                                t.matmul(aps_[:, ab_, 0:65], pt_s[:, hh], vsl[:, 1 - hb, h, :],
                                         start=True, stop=False)
                                return t.matmul(aps_[:, ab_, 0:65], pt_d[:, hh], vsl[:, hb, h, :],
                                                start=False, stop=True)
                            return t.matmul(aps_[:, ab_, 0:65], pt_d[:, hh], vsl[:, hb, h, :],
                                            start=True, stop=True)
                        adep = [o_mk, o_lv]
                        if ("accev", ab_) in prev:
                            adep.append(prev[("accev", ab_)])
                        o_ac = sq.emit("t", acc, after=adep)

                        o_n0 = sq.emit("v", lambda v, ab_=ab_: v.tensor_copy(out=ebuf[:, 0:1], in_=aps_[:, ab_, 64:65]),
                                       after=[o_ac], hazard=True)
                        o_n1 = sq.emit("v", lambda v: v.reciprocal(out=ebuf[:, 1:2], in_=ebuf[:, 0:1]),
                                       after=[o_n0], hazard=True)
                        ndep = [o_n1]
                        if prev_atst is not None:
                            ndep.append(prev_atst)
                        o_n2 = sq.emit("v", lambda v, h=h, ab_=ab_: v.tensor_scalar(
                            out=att[:, h, 0:64], in0=aps_[:, ab_, 0:64], scalar1=ebuf[:, 1:2],
                            scalar2=None, op0=ALU.mult), after=ndep, hazard=True)
                        prev[("accev", ab_)] = o_n2
                        prev[("ath", h)] = o_n2

                def aones(v):
                    ins = None
                    for h in range(16):
                        ins = v.memset(att[:, h, 64:65], 1.0)
                    return ins
                if i2 == 0:
                    o_ao = sq.emit("v", aones)
                else:
                    o_ao = prev[("aones",)]
                prev[("aones",)] = o_ao

                o_atst = sq.dma("dg", at_h[tsl, :], att[:],
                                after=[prev[("ath", h2)] for h2 in range(16)] + [o_ao])
                prev_atst = o_atst

                tr_evs = []
                for dc in range(NCH):
                    def trmm(t, dc=dc):
                        t.transpose(tps_[0:64, :], att[:, 2 * dc, 0:64], iden[:])
                        return t.transpose(tps_[64:128, :], att[:, 2 * dc + 1, 0:64], iden[:])
                    tdep = [prev[("ath", 2 * dc)], prev[("ath", 2 * dc + 1)]]
                    if tr_evs:
                        tdep.append(tr_evs[-1])
                    o_tr = sq.emit("t", trmm, after=tdep)

                    def trev(a, dc=dc):
                        return a.copy(out=aft[:, dc], in_=tps_[:])
                    edep2 = [o_tr]
                    if prev_afst is not None:
                        edep2.append(prev_afst)
                    o_te = sq.emit("a", trev, after=edep2)
                    tr_evs.append(o_te)
                af_r = af_h.rearrange("(c p) t -> p c t", p=128)
                prev_afst = sq.dma("dg", af_r[:, :, tsl], aft[:], after=[tr_evs[-1]])
            sq.set_floor()

        if upto <= 2:
            sq.finalize(block, sems)
            return nc

        qaf_r = qaf_h.rearrange("(c p) t -> p c t", p=128)
        kaf_r = kaf_h.rearrange("(c p) t -> p c t", p=128)
        af_r = af_h.rearrange("(c p) t -> p c t", p=128)
        with ExitStack() as ph:
            psb = lambda *a: ph.enter_context(nc.sbuf_tensor(*a))
            ppt = lambda *a: ph.enter_context(nc.psum_tensor(*a))
            wqa_sb = psb("wqa_sb", [128, NCH, D], BF16)
            wka_sb = psb("wka_sb", [128, NCH, D], BF16)
            bqa_sb = psb("bqa_sb", [128, 16], F32)
            sq.dma("dg", wqa_sb[:], wqaT.rearrange("(c p) f -> p c f", p=128))
            sq.dma("dg", wka_sb[:], wkaT.rearrange("(c p) f -> p c f", p=128))
            sq.dma("ds", bqa_sb[:, 0:8], bqaf[:, :])
            sq.dma("ds", bqa_sb[:, 8:16], bkaf[:, :])
            sq.set_floor()
            ain = psb("ain", [128, 2, NCH, TT], BF16)
            ob2 = psb("ob2", [128, 2, TT], BF16)
            pps2 = ppt("pps2", [128, 2, TT], F32)
            prev_ev = [None, None]
            prev_st = [None, None]
            ld_user = [None, None]
            for i in range(NTT):
                dep = [] if ld_user[i % 2] is None else [ld_user[i % 2]]
                ld = sq.dma("ds", ain[:, i % 2], af_r[:, :, i * TT:(i + 1) * TT], after=dep)
                last = None
                for fc in range(16):
                    pb = (i * 16 + fc) % 2
                    wsel = wqa_sb if fc < 8 else wka_sb
                    cc = fc % 8

                    def proj(t, wsel=wsel, cc=cc, i=i, pb=pb):
                        ins = None
                        for dc in range(NCH):
                            ins = t.matmul(pps2[:, pb], wsel[:, dc, cc * 128:(cc + 1) * 128],
                                           ain[:, i % 2, dc], start=(dc == 0), stop=(dc == NCH - 1))
                        return ins
                    pdep = [ld]
                    if prev_ev[pb] is not None:
                        pdep.append(prev_ev[pb])
                    o_p = sq.emit("t", proj, after=pdep)

                    def evac(a, fc=fc, pb=pb):
                        return a.activation(out=ob2[:, pb], in_=pps2[:, pb], func=AF.Identity,
                                            bias=bqa_sb[:, fc:fc + 1])
                    edep = [o_p]
                    if prev_st[pb] is not None:
                        edep.append(prev_st[pb])
                    o_e = sq.emit("a", evac, after=edep)
                    prev_ev[pb] = o_e
                    dst = qaf_r[:, cc, i * TT:(i + 1) * TT] if fc < 8 else kaf_r[:, cc, i * TT:(i + 1) * TT]
                    prev_st[pb] = sq.dma("dg", dst, ob2[:, pb], after=[o_e])
                    last = o_e
                ld_user[i % 2] = last
            sq.set_floor()

        if upto <= 3:
            sq.finalize(block, sems)
            return nc

        with ExitStack() as ph:
            psb = lambda *a: ph.enter_context(nc.sbuf_tensor(*a))
            ppt = lambda *a: ph.enter_context(nc.psum_tensor(*a))
            cm = psb("cm", [128, 4, 512], BF16)
            sq.dma("ds", cm[:], cmask_h[:, :, :])
            atr = psb("atr", [128, 32, 16, 65], BF16)
            sq.dma("ds", atr[:], at_h.rearrange("(r p) (h f) -> p r h f", p=128, h=16))
            wd_sb = psb("wd_sb", [128, NCH, D], BF16)
            bd_sb = psb("bd_sb", [128, 8], F32)
            onesT = psb("onesT", [128, 128], F32)
            sq.dma("dg", wd_sb[:], wdT.rearrange("(c p) f -> p c f", p=128))
            sq.dma("ds", bd_sb[:], bdf[:, :])
            sq.dma("ds", onesT[:], onesf_h[:, :])
            sq.set_floor()
            qa_s = psb("qa_s", [128, 512], BF16)
            ka_s = psb("ka_s", [128, S], BF16)
            p2t = psb("p2t", [128, 2, 512], BF16)
            zbuf = psb("zbuf", [128, 1536], F32)
            bcs2 = psb("bcs2", [128, 512], F32)
            o2t = psb("o2t", [128, NCH, 512], BF16)
            so_t = psb("so_t", [128, NCH, TT], BF16)
            s2ps = ppt("s2ps", [128, 2, 512], F32)
            a2ps = ppt("a2ps", [128, 2, 512], F32)
            bc2ps = ppt("bc2ps", [128, 512], F32)
            wdps = ppt("wdps", [128, 2, TT], F32)
            s_r = s_hbm.rearrange("(c p) t -> p c t", p=128)

            prev = {}
            prev_sost = None
            for qt in range(NTT):
                qsl2 = slice(qt * TT, (qt + 1) * TT)
                nkt = (qt + 1) * 4
                for hp in range(8):
                    dep = [sq.last["t"]] if sq.last.get("t") is not None else []
                    o_lqa = sq.dma("ds", qa_s[:], qaf_r[:, hp, qsl2], after=dep)
                    o_lka = sq.dma("ds", ka_s[:, 0:nkt * 128], kaf_r[:, hp, 0:nkt * 128], after=dep)
                    for kt in range(nkt):
                        diag = (kt >= qt * 4)

                        def s2mm(t, hp=hp, kt=kt):
                            ins = None
                            for hh in range(2):
                                b0 = 64 * hh
                                ins = t.matmul(s2ps[:, hh, :], ka_s[b0:b0 + 64, kt * 128:(kt + 1) * 128],
                                               qa_s[b0:b0 + 64, :], start=True, stop=True,
                                               tile_position=(b0, 0))
                            return ins
                        sdep = [o_lqa, o_lka]
                        if ("p2used",) in prev:
                            sdep.append(prev[("p2used",)])
                        o_s2 = sq.emit("t", s2mm, after=sdep)

                        def e2(a):
                            a.activation(out=p2t[:, 0], in_=s2ps[:, 0, :], func=AF.Exp, scale=0.125)
                            return a.activation(out=p2t[:, 1], in_=s2ps[:, 1, :], func=AF.Exp, scale=0.125)
                        edep = [o_s2]
                        if ("m2done",) in prev:
                            edep.append(prev[("m2done",)])
                        o_e2 = sq.emit("a", e2, after=edep)
                        prev[("m2done",)] = o_e2

                        o_mk2 = None
                        if diag:
                            def mk2(v, kt=kt, qt=qt):
                                mi = kt - qt * 4
                                v.tensor_tensor(out=p2t[:, 0], in0=p2t[:, 0], in1=cm[:, mi], op=ALU.mult)
                                return v.tensor_tensor(out=p2t[:, 1], in0=p2t[:, 1], in1=cm[:, mi], op=ALU.mult)
                            o_mk2 = sq.emit("v", mk2, after=[o_e2])
                            prev[("m2done",)] = o_mk2

                        def a2mm(t, hp=hp, kt=kt, first=(kt == 0), last=(kt == nkt - 1)):
                            t.matmul(a2ps[0:65, 0, :], atr[:, kt, 2 * hp, :], p2t[:, 0],
                                     start=first, stop=last)
                            return t.matmul(a2ps[0:65, 1, :], atr[:, kt, 2 * hp + 1, :], p2t[:, 1],
                                            start=first, stop=last)
                        a2dep = [o_e2] + ([o_mk2] if o_mk2 else [])
                        if ("a2ev",) in prev and kt == 0:
                            a2dep.append(prev[("a2ev",)])
                        o_a2 = sq.emit("t", a2mm, after=a2dep)
                        prev[("p2used",)] = o_a2

                    o_z0 = sq.emit("a", lambda a: a.copy(out=zbuf[64:65, 0:512], in_=a2ps[64:65, 0, :]),
                                   after=[prev[("p2used",)]], hazard=True)
                    o_z1 = sq.emit("a", lambda a: a.copy(out=zbuf[64:65, 512:1024], in_=a2ps[64:65, 1, :]),
                                   after=[o_z0], hazard=True)
                    o_z2 = sq.emit("v", lambda v: v.reciprocal(out=zbuf[64:65, 1024:1536], in_=zbuf[64:65, 0:512]),
                                   after=[o_z1], hazard=True)
                    o_z3 = sq.emit("v", lambda v: v.reciprocal(out=zbuf[64:65, 0:512], in_=zbuf[64:65, 512:1024]),
                                   after=[o_z2], hazard=True)

                    def bc1(t):
                        t.matmul(bc2ps[0:64, :], onesT[64:65, 0:64], zbuf[64:65, 1024:1536],
                                 start=True, stop=True, tile_position=(64, 0))
                        return t.matmul(bc2ps[64:128, :], onesT[64:65, 0:64], zbuf[64:65, 0:512],
                                        start=True, stop=True, tile_position=(64, 64))
                    bcdep = [o_z2, o_z3]
                    if ("bcev",) in prev:
                        bcdep.append(prev[("bcev",)])
                    o_bc = sq.emit("t", bc1, after=bcdep)

                    def bcev(a):
                        return a.copy(out=bcs2[:], in_=bc2ps[:])
                    o_bce = sq.emit("a", bcev, after=[o_bc])
                    prev[("bcev",)] = o_bce

                    def oev(a, hp=hp):
                        a.copy(out=o2t[0:64, hp, :], in_=a2ps[0:64, 0, :])
                        return a.copy(out=o2t[64:128, hp, :], in_=a2ps[0:64, 1, :])
                    odep = [prev[("p2used",)]]
                    if ("wdused", hp) in prev:
                        odep.append(prev[("wdused", hp)])
                    o_oe = sq.emit("a", oev, after=odep)
                    prev[("a2ev",)] = o_oe

                    def onrm(v, hp=hp):
                        return v.tensor_tensor(out=o2t[:, hp, :], in0=o2t[:, hp, :], in1=bcs2[:], op=ALU.mult)
                    o_on = sq.emit("v", onrm, after=[o_oe, o_bce])
                    prev[("onrm", hp)] = o_on

                for cc in range(NCH):
                    pb = (qt * NCH + cc) % 2

                    def wdm(t, cc=cc, pb=pb):
                        ins = None
                        for dc in range(NCH):
                            ins = t.matmul(wdps[:, pb], wd_sb[:, dc, cc * 128:(cc + 1) * 128],
                                           o2t[:, dc, :], start=(dc == 0), stop=(dc == NCH - 1))
                        return ins
                    wdep2 = [prev[("onrm", hp2)] for hp2 in range(8)]
                    if ("wdev", pb) in prev:
                        wdep2.append(prev[("wdev", pb)])
                    o_wd = sq.emit("t", wdm, after=wdep2)
                    for hp2 in range(8):
                        prev[("wdused", hp2)] = o_wd

                    def wde(a, cc=cc, pb=pb):
                        return a.activation(out=so_t[:, cc], in_=wdps[:, pb], func=AF.Identity,
                                            bias=bd_sb[:, cc:cc + 1])
                    wedep = [o_wd]
                    if prev_sost is not None:
                        wedep.append(prev_sost)
                    o_we = sq.emit("a", wde, after=wedep)
                    prev[("wdev", pb)] = o_we
                prev_sost = sq.dma("dg", s_r[:, :, qsl2], so_t[:],
                                   after=[prev[("wdev", 0)], prev[("wdev", 1)]])
            sq.set_floor()

        if upto <= 4:
            sq.finalize(block, sems)
            return nc

        with ExitStack() as ph:
            psb = lambda *a: ph.enter_context(nc.sbuf_tensor(*a))
            ppt = lambda *a: ph.enter_context(nc.psum_tensor(*a))
            g1s = psb("g1s", [32, 2, 4, 32], F32)
            g2s = psb("g2s", [16, 2, 4, 16], F32)
            seld = psb("seld", [32, 512], BF16)
            selg = psb("selg", [16, 128], BF16)
            br_sb = psb("br_sb", [128, 8], F32)
            sq.dma("ds", g1s[:], g1h.transpose((2, 0, 1, 3)))
            sq.dma("ds", g2s[:], g2h.transpose((2, 0, 1, 3)))
            sq.dma("ds", seld[:], seld_h[:, :])
            sq.dma("ds", selg[:], selg_h[:, :])
            sq.dma("ds", br_sb[:], brf[:, :])
            gexp = psb("gexp", [32, 512], BF16)
            gexp2 = psb("gexp2", [16, 512], BF16)
            apart = psb("apart", [128, 512], F32)
            wacc = psb("wacc", [128, 3, 4, 512], F32)
            wq_sb = psb("wq_sb", [128, 3, 4, 512], BF16)
            selps2 = ppt("selps2", [128, 2, 512], F32)
            sq.set_floor()

            prev_sel = None
            for ri in range(2):
                for ch in range(4):
                    accs = []
                    for c in range(4):
                        def gex(v, ri=ri, c=c):
                            src = g1s[:, ri, c, :].unsqueeze(2).broadcast_to((32, 32, 16))
                            v.tensor_copy(out=gexp[:].rearrange("p (b e) -> p b e", b=32), in_=src)
                            src2 = g2s[:, ri, c, :].unsqueeze(1).broadcast_to((16, 32, 16))
                            return v.tensor_copy(out=gexp2[:].rearrange("p (b e) -> p b e", b=32), in_=src2)
                        gdep = [prev_sel] if prev_sel is not None else []
                        o_ge = sq.emit("v", gex, after=gdep)

                        def selmm(t, ch=ch):
                            t.matmul(selps2[:, 0, :], seld[:, ch * 128:(ch + 1) * 128], gexp[:], start=True, stop=True)
                            return t.matmul(selps2[:, 1, :], selg[:], gexp2[:], start=True, stop=True)
                        o_sm2 = sq.emit("t", selmm, after=[o_ge] + ([prev_sel] if prev_sel is not None else []))

                        def apev(a):
                            return a.copy(out=apart[:], in_=selps2[:, 0, :])
                        o_ap = sq.emit("a", apev, after=[o_sm2])

                        def wmul(v, ri=ri, ch=ch, c=c):
                            if c == 0:
                                return v.tensor_tensor(out=wacc[:, ri, ch, :], in0=apart[:], in1=selps2[:, 1, :], op=ALU.mult)
                            v.tensor_tensor(out=apart[:], in0=apart[:], in1=selps2[:, 1, :], op=ALU.mult)
                            return v.tensor_tensor(out=wacc[:, ri, ch, :], in0=wacc[:, ri, ch, :], in1=apart[:], op=ALU.add)
                        o_wm = sq.emit("v", wmul, after=[o_ap])
                        prev_sel = o_wm

                    def wcast(a, ri=ri, ch=ch):
                        return a.copy(out=wq_sb[:, ri, ch, :], in_=wacc[:, ri, ch, :])
                    prev_sel = sq.emit("a", wcast, after=[prev_sel])
            def wneg(a):
                return a.activation(out=wq_sb[:, 2, :, :].rearrange("p a b -> p (a b)"),
                                    in_=wq_sb[:, 1, :, :].rearrange("p a b -> p (a b)"), func=AF.Copy, scale=-1.0)
            o_wn = sq.emit("a", wneg, after=[prev_sel])
            sq.set_floor()

            xng2 = psb("xng2", [128, 2, NCH, TT], BF16)
            gb2 = psb("gb2", [128, TT], BF16)
            xres = psb("xres", [128, 2, NCH, TT], F32)
            sres = psb("sres", [128, 2, NCH, TT], BF16)
            onew = psb("onew", [128, NCH, TT], F32)
            fps2 = ppt("fps2", [128, 2, TT], F32)
            xn_r2 = xn_h.rearrange("(c p) t -> p c t", p=128)
            xi_r = x_in.rearrange("(c p) t -> p c t", p=128)
            xo_r = x_out.rearrange("(c p) t -> p c t", p=128)

            prev_gev = [None, None]
            prev_resid = None
            prev_xst = None
            xld_user = [None, None]
            out_stores = []
            for i in range(NTT):
                dep = [] if xld_user[i % 2] is None else [xld_user[i % 2]]
                o_lxn = sq.dma("ds", xng2[:, i % 2], xn_r2[:, :, i * TT:(i + 1) * TT], after=dep)
                o_lx = sq.dma("ds", xres[:, i % 2], xi_r[:, :, i * TT:(i + 1) * TT], after=dep)
                o_ls = sq.dma("ds", sres[:, i % 2], s_r[:, :, i * TT:(i + 1) * TT], after=dep)
                res_ops = []
                for oc in range(NCH):
                    pb = (i * NCH + oc) % 2

                    def fmm(t, oc=oc, pb=pb, i=i):
                        real = oc < 4
                        ch = oc % 4
                        ins = None
                        for dc in range(4):
                            t.matmul(fps2[:, pb], wq_sb[:, 0 if real else 1, dc, ch * 128:(ch + 1) * 128],
                                     xng2[:, i % 2, dc, :], start=(dc == 0), stop=False)
                        for dc in range(4):
                            ins = t.matmul(fps2[:, pb], wq_sb[:, 2 if real else 0, dc, ch * 128:(ch + 1) * 128],
                                           xng2[:, i % 2, 4 + dc, :], start=False, stop=(dc == 3))
                        return ins
                    fdep = [o_lxn, o_wn]
                    if prev_gev[pb] is not None:
                        fdep.append(prev_gev[pb])
                    o_fm = sq.emit("t", fmm, after=fdep)

                    def gev(a, oc=oc, pb=pb):
                        return a.activation(out=gb2[:], in_=fps2[:, pb], func=AF.Silu,
                                            bias=br_sb[:, oc:oc + 1])
                    gdep2 = [o_fm]
                    if prev_resid is not None:
                        gdep2.append(prev_resid)
                    o_gv = sq.emit("a", gev, after=gdep2)
                    prev_gev[pb] = o_gv

                    def resid(v, oc=oc, i=i):
                        v.tensor_tensor(out=onew[:, oc], in0=sres[:, i % 2, oc], in1=gb2[:], op=ALU.mult)
                        return v.tensor_tensor(out=onew[:, oc], in0=onew[:, oc], in1=xres[:, i % 2, oc], op=ALU.add)
                    rdep = [o_gv, o_lx, o_ls]
                    if prev_xst is not None:
                        rdep.append(prev_xst)
                    o_r = sq.emit("v", resid, after=rdep)
                    prev_resid = o_r
                    res_ops.append(o_r)
                xld_user[i % 2] = res_ops[-1]
                prev_xst = sq.dma("dg", xo_r[:, :, i * TT:(i + 1) * TT], onew[:], after=res_ops)
                out_stores.append(prev_xst)

            if ctx is None:
                sq.finalize(block, sems, outputs_done=out_stores)
            else:
                ctx["out_stores"] = out_stores
                sq.set_floor()
    return nc


def l0_inmap(xcore_fm, I):
    RD = 16
    inv = (1.0 / 10000.0 ** (np.arange(0, RD, 2, dtype=np.float64) / RD)).astype(np.float64)
    t = np.arange(S, dtype=np.float64)
    fr = np.outer(inv, t)
    cos8 = np.cos(fr); sin8 = np.sin(fr)
    cosT = np.zeros((128, S), np.float32)
    sinT = np.zeros((128, S), np.float32)
    for b0 in (0, 64):
        cosT[b0 + 0:b0 + 8] = cos8
        cosT[b0 + 8:b0 + 16] = cos8
        sinT[b0 + 0:b0 + 8] = -sin8
        sinT[b0 + 8:b0 + 16] = sin8
    kk, qq = np.meshgrid(np.arange(128), np.arange(128), indexing="ij")
    wmd = ((qq - kk >= 0) & (qq - kk <= 15)).astype(np.float32)
    wms = ((qq + 128 - kk >= 0) & (qq + 128 - kk <= 15)).astype(np.float32)
    cmask = np.zeros((128, 4, 512), np.float32)
    for mi in range(4):
        kx, qx = np.meshgrid(np.arange(128) + mi * 128, np.arange(512), indexing="ij")
        cmask[:, mi, :] = (kx <= qx).astype(np.float32)
    seld = np.zeros((32, 512), np.float32)
    for dd in range(32):
        for g in range(16):
            seld[dd, (dd * 16 + g) % 512] = 0.0
    seld = np.zeros((32, 512), np.float32)
    for r in range(512):
        dd = r // 16
        seld[dd, r] = 1.0
    selg = np.zeros((16, 128), np.float32)
    for r in range(128):
        selg[r % 16, r] = 1.0
    G1R = I["L0_ffn_G1_R"][0]
    G2R = I["L0_ffn_G2_R"]
    G1I = I["L0_ffn_G1_I"][0]
    G2I = I["L0_ffn_G2_I"]
    g1 = np.stack([np.ascontiguousarray(G1R.transpose(1, 2, 0)),
                   np.ascontiguousarray(G1I.transpose(1, 2, 0))])
    g2 = np.stack([np.ascontiguousarray(G2R[:, :, 0, :].transpose(0, 2, 1)),
                   np.ascontiguousarray(G2I[:, :, 0, :].transpose(0, 2, 1))])
    bri = np.concatenate([I["L0_ffn_bias_real"], I["L0_ffn_bias_imag"]])
    return {
        "x_in": xcore_fm,
        "wqkvT": np.ascontiguousarray(I["L0_attn_Wqkv"].T),
        "bqkvf": fm(I["L0_attn_bqkv"][0:2048]),
        "bvrow": np.ascontiguousarray(I["L0_attn_bqkv"][2048:3072].reshape(1, D)),
        "wqaT": np.ascontiguousarray(I["L0_attn_Wqa"].T),
        "bqaf": fm(I["L0_attn_bqa"]),
        "wkaT": np.ascontiguousarray(I["L0_attn_Wka"].T),
        "bkaf": fm(I["L0_attn_bka"]),
        "wdT": np.ascontiguousarray(I["L0_attn_Wd"].T),
        "bdf": fm(I["L0_attn_bd"]),
        "g1h": g1.astype(np.float32),
        "g2h": g2.astype(np.float32),
        "brf": fm(bri),
        "seld_h": seld.astype(BF),
        "selg_h": selg.astype(BF),
        "cosT_h": cosT.astype(BF),
        "sinT_h": sinT.astype(BF),
        "wmaskd_h": wmd.astype(BF),
        "wmasks_h": wms.astype(BF),
        "cmask_h": cmask.astype(BF),
        "iden_h": np.eye(128, dtype=np.float32).astype(BF),
        "onescol_h": np.ones((1, 128), np.float32).astype(BF),
        "ones128_h": np.ones((128, 1), np.float32),
        "ones1_h": np.ones((1, 128), np.float32),
        "onesf_h": np.ones((128, 128), np.float32),
    }



def build_all():
    from contextlib import ExitStack
    nc0 = bass.Bass("TRN2", target_bir_lowering=False, debug=False)
    x0h = nc0.dram_tensor("x_in", [D, S], F32, kind="ExternalInput")
    x1h = nc0.dram_tensor("ix1", [D, S], F32, kind="Internal")
    x2h = nc0.dram_tensor("ix2", [D, S], F32, kind="Internal")
    x3h = nc0.dram_tensor("ix3", [D, S], F32, kind="Internal")
    xoh = nc0.dram_tensor("x_out", [D, S], F32, kind="ExternalOutput")
    stack = ExitStack()
    with stack:
        sems = alloc_sems(nc0, stack)
        block = stack.enter_context(nc0.Block())
        sq = Seq(nc0)
        ctx = {"sems": sems, "block": block, "sq": sq}
        ios = [(x0h, x1h, "A_", build_l0), (x1h, x2h, "B_", build_l1),
               (x2h, x3h, "C_", build_l2), (x3h, xoh, "D_", build_l3)]
        for xin, xout, pf, bfn in ios:
            ctx["nc"] = NCProxy(nc0, pf, xin, xout)
            bfn(ctx=ctx)
            sq.set_floor()
        sq.finalize(block, sems, outputs_done=ctx["out_stores"])
    return nc0


def all_inmap(xcore_fm, I):
    m = {"x_in": xcore_fm}
    for pf, f in [("A_", l0_inmap), ("B_", l1_inmap), ("C_", l2_inmap), ("D_", l3_inmap)]:
        sub = f(xcore_fm, I)
        for k, v in sub.items():
            if k != "x_in":
                m[pf + k] = v
    return m



_BUILDS = {}


def _get_build(name, fn):
    if name not in _BUILDS:
        _BUILDS[name] = fn()
    return _BUILDS[name]


FUSED = True


def kernel(**inputs):
    I = inputs
    x = np.asarray(I["L0_attn_hidden_states"], dtype=np.float32)
    ncores = 8
    x_fm = [np.ascontiguousarray(x[c].T) for c in range(ncores)]

    if FUSED:
        nc = _get_build("all", build_all)
        in_maps = [all_inmap(x_fm[c], I) for c in range(ncores)]
        res = run_bass_kernel_spmd(nc, in_maps, core_ids=list(range(ncores)))
        x_fm = [np.asarray(res.results[c]["x_out"]) for c in range(ncores)]
    else:
        for name, build, inmap in [
            ("l0", build_l0, l0_inmap),
            ("l1", build_l1, l1_inmap),
            ("l2", build_l2, l2_inmap),
            ("l3", build_l3, l3_inmap),
        ]:
            nc = _get_build(name, build)
            in_maps = [inmap(x_fm[c], I) for c in range(ncores)]
            res = run_bass_kernel_spmd(nc, in_maps, core_ids=list(range(ncores)))
            x_fm = [np.asarray(res.results[c]["x_out"]) for c in range(ncores)]

    out = np.stack([xc.T for xc in x_fm]).astype(np.float32)
    return out


# version 05
# speedup vs baseline: 1.1404x; 1.1404x over previous
import numpy as np
import ml_dtypes
import concourse.bass as bass
import concourse.mybir as mybir
from concourse.bass_utils import run_bass_kernel_spmd

F32 = mybir.dt.float32
BF16 = mybir.dt.bfloat16
AF = mybir.ActivationFunctionType
ALU = mybir.AluOpType
AX = mybir.AxisListType

S = 4096
D = 1024
NCH = 8
TT = 512
NTT = S // TT
BF = np.dtype(ml_dtypes.bfloat16)

COMPUTE_ENGINES = ("t", "a", "v", "g")
DMA_QUEUES = ("ds", "dg", "da", "dv")
ISSUER = {"ds": "s", "dg": "g", "da": "a", "dv": "v"}


class Seq:

    STRICT = False
    STRICT_DS = False
    STRICT_KEYS = ()

    def __init__(self, nc):
        self.nc = nc
        self.ops = []
        self.local_idx = []
        self.hazards = []
        self.counts = {k: 0 for k in COMPUTE_ENGINES + DMA_QUEUES}
        self.last = {}
        self.floor = []

    def emit(self, key, fn, after=(), hazard=False):
        deps = list(after) + self.floor
        if Seq.STRICT and self.ops:
            deps = deps + [len(self.ops) - 1]
            hazard = True
        if Seq.STRICT_DS and key in DMA_QUEUES and self.ops:
            deps = deps + [len(self.ops) - 1]
        if key in Seq.STRICT_KEYS and self.ops:
            deps = deps + [len(self.ops) - 1]
        for d in deps:
            assert 0 <= d < len(self.ops), f"dep {d} not yet emitted"
        self.ops.append((key, fn, tuple(deps)))
        self.local_idx.append(self.counts[key])
        self.hazards.append(bool(hazard))
        self.counts[key] += 1
        gid = len(self.ops) - 1
        self.last[key] = gid
        return gid

    def dma(self, queue, out, in_, after=()):
        assert queue in DMA_QUEUES

        def fn(eng, out=out, in_=in_):
            return eng.dma_start(out=out, in_=in_)
        return self.emit(queue, fn, after)

    def set_floor(self):
        self.floor = list(self.last.values())

    def finalize(self, block, sems, outputs_done=()):
        phys = {"t": [], "a": [], "v": [], "g": [], "s": []}
        for gid, (key, fn, deps) in enumerate(self.ops):
            eng = key if key in COMPUTE_ENGINES else ISSUER[key]
            phys[eng].append(gid)

        def target(gid):
            key = self.ops[gid][0]
            li = self.local_idx[gid]
            return (key, (li + 1) * (16 if key in DMA_QUEUES else 1))

        def mkbody(eng_name, gids):
            def body(eng):
                hw = {}
                for gid in gids:
                    key, fn, deps = self.ops[gid]
                    need = {}
                    for d in deps:
                        dkey = self.ops[d][0]
                        deng = dkey if dkey in COMPUTE_ENGINES else ISSUER[dkey]
                        if deng == eng_name and dkey not in DMA_QUEUES and not self.hazards[d]:
                            continue
                        k2, val = target(d)
                        if need.get(k2, 0) < val:
                            need[k2] = val
                    for k2, val in sorted(need.items()):
                        if hw.get(k2, 0) >= val:
                            continue
                        eng.wait_ge(sems[k2], val)
                        hw[k2] = val
                    ins = fn(eng)
                    assert ins is not None, f"op {gid} ({key}) returned no instruction"
                    ins.then_inc(sems[key], 16 if key in DMA_QUEUES else 1)
                if eng_name == "s":
                    need = {}
                    for d in outputs_done:
                        k2, val = target(d)
                        if need.get(k2, 0) < val:
                            need[k2] = val
                    for k2, val in sorted(need.items()):
                        eng.wait_ge(sems[k2], val)
            return body

        block.tensor(mkbody("t", phys["t"]))
        block.scalar(mkbody("a", phys["a"]))
        block.vector(mkbody("v", phys["v"]))
        block.gpsimd(mkbody("g", phys["g"]))
        block.sync(mkbody("s", phys["s"]))


def alloc_sems(nc, stack):
    return {k: stack.enter_context(nc.semaphore("sem_" + k))
            for k in COMPUTE_ENGINES + DMA_QUEUES}


class NCProxy:

    def __init__(self, nc, prefix, x_in_handle, x_out_handle):
        self._nc = nc
        self._pf = prefix
        self._xin = x_in_handle
        self._xout = x_out_handle

    def dram_tensor(self, name, *a, **k):
        if name == "x_in":
            return self._xin
        if name == "x_out":
            return self._xout
        return self._nc.dram_tensor(self._pf + name, *a, **k)

    def sbuf_tensor(self, name, *a, **k):
        return self._nc.sbuf_tensor(self._pf + name, *a, **k)

    def psum_tensor(self, name, *a, **k):
        return self._nc.psum_tensor(self._pf + name, *a, **k)

    def __getattr__(self, n):
        return getattr(self._nc, n)



def emit_ln(sq, nc, stack, x_hbm, xn_sb, ones128, ones1, cst, col_off=1):
    xr = x_hbm.rearrange("(c p) t -> p c t", p=128)
    xt = stack.enter_context(nc.sbuf_tensor("ln_x", [128, 2, NCH, TT], F32))
    sqt = stack.enter_context(nc.sbuf_tensor("ln_q", [128, 2, NCH, TT], F32))
    t1 = stack.enter_context(nc.sbuf_tensor("ln_1", [128, 2, TT], F32))
    t2 = stack.enter_context(nc.sbuf_tensor("ln_2", [128, 2, TT], F32))
    ab = stack.enter_context(nc.sbuf_tensor("ln_ab", [1, 2, 2 * TT], F32))
    ps = stack.enter_context(nc.psum_tensor("ln_ps", [1, 2, 2 * TT], F32))
    bc = stack.enter_context(nc.psum_tensor("ln_bc", [128, 2, 2 * TT], F32))

    done = []
    for i in range(NTT):
        pr = i % 2
        dep = [] if i < 2 else [done[i - 2]]
        ld = sq.dma("ds", xt[:, pr], xr[:, :, i * TT:(i + 1) * TT], after=dep)

        def tree(v, i=i, pr=pr):
            for cc in range(NCH):
                v.tensor_tensor(out=sqt[:, pr, cc], in0=xt[:, pr, cc], in1=xt[:, pr, cc], op=ALU.mult)
            v.tensor_tensor(out=t1[:, pr], in0=xt[:, pr, 0], in1=xt[:, pr, 1], op=ALU.add)
            for cc in range(2, NCH):
                v.tensor_tensor(out=t1[:, pr], in0=t1[:, pr], in1=xt[:, pr, cc], op=ALU.add)
            v.tensor_tensor(out=t2[:, pr], in0=sqt[:, pr, 0], in1=sqt[:, pr, 1], op=ALU.add)
            ins = None
            for cc in range(2, NCH):
                ins = v.tensor_tensor(out=t2[:, pr], in0=t2[:, pr], in1=sqt[:, pr, cc], op=ALU.add)
            return ins
        otree = sq.emit("v", tree, after=[ld] + dep)

        def psums(t, pr=pr):
            t.matmul(ps[0:1, pr, 0:TT], ones128[:], t1[:, pr], start=True, stop=True)
            return t.matmul(ps[0:1, pr, TT:2 * TT], ones128[:], t2[:, pr], start=True, stop=True)
        opsum = sq.emit("t", psums, after=[otree] + dep)

        def stats(a, pr=pr):
            a.activation(out=ab[0:1, pr, 0:TT], in_=ps[0:1, pr, 0:TT], func=AF.Copy, scale=1.0 / D)
            return a.activation(out=ab[0:1, pr, TT:2 * TT], in_=ps[0:1, pr, TT:2 * TT], func=AF.Copy, scale=1.0 / D)
        omean = sq.emit("a", stats, after=[opsum] + dep)

        def var(v, pr=pr):
            v.tensor_tensor(out=t2[0:1, pr, 0:TT], in0=ab[0:1, pr, 0:TT], in1=ab[0:1, pr, 0:TT], op=ALU.mult)
            return v.tensor_tensor(out=t2[0:1, pr, 0:TT], in0=ab[0:1, pr, TT:2 * TT], in1=t2[0:1, pr, 0:TT], op=ALU.subtract)
        ovar = sq.emit("v", var, after=[omean])

        def rstd(a, pr=pr):
            a.activation(out=sqt[0:1, pr, 0, 0:TT], in_=t2[0:1, pr, 0:TT], func=AF.Ln, bias=cst[0:1, 0:1])
            return a.activation(out=ab[0:1, pr, TT:2 * TT], in_=sqt[0:1, pr, 0, 0:TT], func=AF.Exp, scale=-0.5)
        orstd = sq.emit("a", rstd, after=[ovar])

        def nab(v, pr=pr):
            v.tensor_tensor(out=ab[0:1, pr, 0:TT], in0=ab[0:1, pr, 0:TT], in1=ab[0:1, pr, TT:2 * TT], op=ALU.mult)
            return v.tensor_scalar(out=ab[0:1, pr, 0:TT], in0=ab[0:1, pr, 0:TT], scalar1=-1.0, scalar2=None, op0=ALU.mult)
        onab = sq.emit("v", nab, after=[orstd])

        def bcast(t, pr=pr):
            t.matmul(bc[:, pr, 0:TT], ones1[:], ab[0:1, pr, 0:TT], start=True, stop=True)
            return t.matmul(bc[:, pr, TT:2 * TT], ones1[:], ab[0:1, pr, TT:2 * TT], start=True, stop=True)
        obc = sq.emit("t", bcast, after=[onab] + dep)

        def apply(v, i=i, pr=pr):
            ins = None
            for cc in range(NCH):
                v.tensor_tensor(out=sqt[:, pr, cc], in0=xt[:, pr, cc], in1=bc[:, pr, TT:2 * TT], op=ALU.mult)
                ins = v.tensor_tensor(out=xn_sb[:, cc, col_off + i * TT:col_off + (i + 1) * TT],
                                      in0=sqt[:, pr, cc], in1=bc[:, pr, 0:TT], op=ALU.add)
            return ins
        oap = sq.emit("v", apply, after=[obc])
        done.append(oap)
    return done



WSTRICT = True
L2SUB = 9
L0SUB = 9


def build_l3(debug=False, ctx=None):
    from contextlib import ExitStack
    nc = bass.Bass("TRN2", target_bir_lowering=False, debug=False) if ctx is None else ctx["nc"]
    x_in = nc.dram_tensor("x_in", [D, S], F32, kind="ExternalInput").ap()
    wkT = nc.dram_tensor("wkT", [D, D], F32, kind="ExternalInput").ap()
    wvT = nc.dram_tensor("wvT", [D, D], F32, kind="ExternalInput").ap()
    wrT = nc.dram_tensor("wrT", [D, D], F32, kind="ExternalInput").ap()
    woT = nc.dram_tensor("woT", [D, D], F32, kind="ExternalInput").ap()
    w1T = nc.dram_tensor("w1T", [D, 4 * D], F32, kind="ExternalInput").ap()
    w2T = nc.dram_tensor("w2T", [4 * D, D], F32, kind="ExternalInput").ap()
    b1f = nc.dram_tensor("b1f", [128, 32], F32, kind="ExternalInput").ap()
    b2f = nc.dram_tensor("b2f", [128, 8], F32, kind="ExternalInput").ap()
    tdf = nc.dram_tensor("tdf", [128, 8], F32, kind="ExternalInput").ap()
    tff = nc.dram_tensor("tff", [128, 8], F32, kind="ExternalInput").ap()
    tmkf = nc.dram_tensor("tmkf", [128, 8], F32, kind="ExternalInput").ap()
    tmvf = nc.dram_tensor("tmvf", [128, 8], F32, kind="ExternalInput").ap()
    tmrf = nc.dram_tensor("tmrf", [128, 8], F32, kind="ExternalInput").ap()
    ones128_h = nc.dram_tensor("ones128_h", [128, 1], F32, kind="ExternalInput").ap()
    ones1_h = nc.dram_tensor("ones1_h", [1, 128], F32, kind="ExternalInput").ap()
    x_out = nc.dram_tensor("x_out", [D, S], F32, kind="ExternalOutput").ap()

    kind = "ExternalOutput" if debug else "Internal"
    w1t_h = nc.dram_tensor("w1t_h", [D, 4 * D], BF16, kind=kind).ap()
    w2t_h = nc.dram_tensor("w2t_h", [4 * D, D], BF16, kind=kind).ap()
    sry_h = nc.dram_tensor("sry_h", [D, S], BF16, kind=kind).ap()
    s_hbm = nc.dram_tensor("s_hbm", [D, S], BF16, kind=kind).ap()
    xn_h = nc.dram_tensor("xn_h", [D, S], BF16, kind=kind).ap() if debug else None
    thr_h = nc.dram_tensor("thr_h", [2, 128, 1], F32, kind=kind).ap() if debug else None
    g_h = nc.dram_tensor("g_h", [D, S], BF16, kind=kind).ap() if debug else None
    h_h = nc.dram_tensor("h_h", [4 * D, S], BF16, kind=kind).ap() if debug else None
    tred_h = nc.dram_tensor("tred_h", [2, 128, 1], F32, kind=kind).ap() if debug else None

    from contextlib import ExitStack
    stack = ExitStack()
    with stack:
        if ctx is None:
            sems = alloc_sems(nc, stack)
            block = stack.enter_context(nc.Block())
            sq = Seq(nc)
        else:
            sems, block, sq = ctx["sems"], ctx["block"], ctx["sq"]
        sb = lambda *a: stack.enter_context(nc.sbuf_tensor(*a))

        ones128 = sb("c_ones128", [128, 1], F32)
        ones1 = sb("c_ones1", [1, 128], F32)
        mixp = sb("mixp", [128, 12, 8], F32)
        cst = sb("cst", [128, 2], F32)
        xn = sb("xn", [128, NCH, S + 1], BF16)

        sq.dma("ds", ones128[:], ones128_h[:, :])
        sq.dma("ds", ones1[:], ones1_h[:, :])
        o_mx = []
        for col, src in [(0, tmkf), (2, tmvf), (4, tmrf), (6, tdf), (7, tff)]:
            o_mx.append(sq.dma("ds", mixp[:, col, :], src[:, :]))

        def premix(v):
            v.memset(cst[:, 0:1], 1e-5)
            for cc in range(NCH):
                v.memset(xn[:, cc, 0:1], 0.0)
            ins = None
            for src, dst in [(0, 1), (2, 3), (4, 5)]:
                ins = v.tensor_scalar(out=mixp[:, dst, :], in0=mixp[:, src, :], scalar1=-1.0, scalar2=1.0,
                                      op0=ALU.mult, op1=ALU.add)
            return ins
        o_premix = sq.emit("v", premix, after=o_mx)

        o_pp0 = sq.emit("a", lambda a: a.activation(out=mixp[:, 6, :], in_=mixp[:, 6, :], func=AF.Exp),
                        after=o_mx, hazard=True)
        o_pp1 = sq.emit("a", lambda a: a.activation(out=mixp[:, 6, :], in_=mixp[:, 6, :], func=AF.Exp, scale=-1.0),
                        after=[o_pp0], hazard=True)
        o_preparam = sq.emit("a", lambda a: a.activation(out=mixp[:, 7, :], in_=mixp[:, 7, :], func=AF.Exp),
                             after=[o_pp1], hazard=True)
        sq.set_floor()

        with ExitStack() as ph:
            ln_done = emit_ln(sq, nc, ph, x_in, xn, ones128, ones1, cst, col_off=1)
            sq.set_floor()

        if debug:
            xn_r = xn_h.rearrange("(c p) t -> p c t", p=128)
            for i in range(NTT):
                sq.dma("dg", xn_r[:, :, i * TT:(i + 1) * TT], xn[:, :, 1 + i * TT:1 + (i + 1) * TT],
                       after=[ln_done[i]])
            sq.set_floor()

        with ExitStack() as ph:
            psb = lambda *a: ph.enter_context(nc.sbuf_tensor(*a))
            ppt = lambda *a: ph.enter_context(nc.psum_tensor(*a))
            tw = psb("tw", [128, 2, 4096], F32)
            twq = psb("twq", [128, 4096], BF16)
            tred = psb("tred", [128, 33], F32)
            tth = psb("tth", [128, 1], F32)
            tsc = psb("tsc", [1, 1], F32)
            tps = ppt("tps", [128, 2], F32)

            for wi, (wh, wth, nrow, ncol) in enumerate([(w1T, w1t_h, D, 4 * D), (w2T, w2t_h, 4 * D, D)]):
                nrt = nrow // 128
                reds = []
                for r in range(nrt):
                    dep = [reds[r - 2]] if r >= 2 else []
                    ld = sq.dma("ds", tw[:, r % 2, 0:ncol], wh[r * 128:(r + 1) * 128, :], after=dep)

                    def red(v, r=r, ncol=ncol):
                        return v.tensor_reduce(out=tred[:, 1 + r:2 + r], in_=tw[:, r % 2, 0:ncol], axis=AX.X,
                                               op=ALU.add, apply_absolute_value=True)
                    ored = sq.emit("v", red, after=[ld], hazard=True)
                    reds.append(ored)

                def combine(v, nrt=nrt):
                    return v.tensor_reduce(out=tred[:, 0:1], in_=tred[:, 1:1 + nrt], axis=AX.X, op=ALU.add)
                reds.append(sq.emit("v", combine, after=reds[-3:], hazard=True))

                if debug:
                    sq.dma("dg", tred_h[wi], tred[:, 0:1], after=[reds[-1]])

                def colsum(t, wi=wi):
                    return t.matmul(tps[0:1, wi:wi + 1], ones128[:], tred[:, 0:1], start=True, stop=True)
                o_cs = sq.emit("t", colsum, after=[reds[-1]])

                def mkthr(a, wi=wi, nrow=nrow, ncol=ncol):
                    return a.activation(out=tsc[0:1, 0:1], in_=tps[0:1, wi:wi + 1], func=AF.Copy,
                                        scale=0.5 / (nrow * ncol))
                o_thr = sq.emit("a", mkthr, after=[o_cs])

                def bthr(t):
                    return t.matmul(tps[:, 0:1], ones1[:], tsc[0:1, 0:1], start=True, stop=True)
                o_bthr = sq.emit("t", bthr, after=[o_thr])

                def cpthr(a):
                    return a.copy(out=tth[:, 0:1], in_=tps[:, 0:1])
                o_cpt = sq.emit("a", cpthr, after=[o_bthr])
                if debug:
                    sq.dma("dg", thr_h[wi], tth[:, 0:1], after=[o_cpt])

                prev_q = None
                for r in range(nrt):
                    dep = [o_cpt] + ([prev_q] if prev_q is not None else [])
                    ld = sq.dma("ds", tw[:, r % 2, 0:ncol], wh[r * 128:(r + 1) * 128, :], after=dep)

                    def absop(a, r=r, ncol=ncol):
                        return a.activation(out=tw[:, (r + 1) % 2, 0:ncol], in_=tw[:, r % 2, 0:ncol], func=AF.Abs)
                    o_abs = sq.emit("a", absop, after=[ld])

                    def cmp(v, r=r, ncol=ncol):
                        return v.tensor_scalar(out=tw[:, (r + 1) % 2, 0:ncol], in0=tw[:, (r + 1) % 2, 0:ncol],
                                               scalar1=tth[:, 0:1], scalar2=None, op0=ALU.is_gt)
                    o_cmp = sq.emit("v", cmp, after=[o_abs])

                    def sgn(a, r=r, ncol=ncol):
                        return a.activation(out=twq[:, 0:ncol], in_=tw[:, r % 2, 0:ncol], func=AF.Sign)
                    o_sgn = sq.emit("a", sgn, after=[ld])

                    def mul(v, r=r, ncol=ncol):
                        return v.tensor_tensor(out=twq[:, 0:ncol], in0=twq[:, 0:ncol],
                                               in1=tw[:, (r + 1) % 2, 0:ncol], op=ALU.mult)
                    o_mul = sq.emit("v", mul, after=[o_cmp, o_sgn])
                    prev_q = sq.dma("dg", wth[r * 128:(r + 1) * 128, :], twq[:, 0:ncol], after=[o_mul])
                sq.set_floor()
            sq.set_floor()

        with ExitStack() as ph:
            psb = lambda *a: ph.enter_context(nc.sbuf_tensor(*a))
            ppt = lambda *a: ph.enter_context(nc.psum_tensor(*a))
            wk_sb = psb("wk_sb", [128, NCH, D], BF16)
            wv_sb = psb("wv_sb", [128, NCH, D], BF16)
            wr_sb = psb("wr_sb", [128, NCH, D], BF16)
            o_wk = sq.dma("dg", wk_sb[:], wkT.rearrange("(c p) f -> p c f", p=128))
            o_wv = sq.dma("dg", wv_sb[:], wvT.rearrange("(c p) f -> p c f", p=128))
            o_wr = sq.dma("dg", wr_sb[:], wrT.rearrange("(c p) f -> p c f", p=128))
            xk_sb = psb("xk_sb", [128, NCH, TT], BF16)
            xv_sb = psb("xv_sb", [128, NCH, TT], BF16)
            xr_sb = psb("xr_sb", [128, NCH, TT], BF16)
            mixtmp = psb("mixtmp", [128, TT], F32)
            kb = psb("kb", [128, TT], F32)
            vb = psb("vb", [128, TT], F32)
            rb = psb("rb", [128, TT], F32)
            e1 = psb("e1", [128, TT], F32)
            e2 = psb("e2", [128, TT], F32)
            e3 = psb("e3", [128, TT], F32)
            e4 = psb("e4", [128, TT], F32)
            Pt = psb("Pt", [128, NCH, TT + 1], F32)
            Qt = psb("Qt", [128, NCH, TT + 1], F32)
            sry_t = psb("sry_t", [128, NCH, TT], BF16)
            kps = ppt("kps", [128, 2, TT], F32)
            vps = ppt("vps", [128, 2, TT], F32)
            rps = ppt("rps", [128, 2, TT], F32)

            def zpq(v):
                ins = None
                for cc in range(NCH):
                    v.memset(Pt[:, cc, 0:1], 0.0)
                    ins = v.memset(Qt[:, cc, 0:1], 0.0)
                return ins
            sq.emit("v", zpq)

            prev_scan = [None] * NCH
            prev_evac = [None, None]
            prev_proj_last = None
            prev_store = None
            sry_r = sry_h.rearrange("(c p) t -> p c t", p=128)
            for i in range(NTT):
                def mix(v, i=i):
                    ins = None
                    for dc in range(NCH):
                        cur = xn[:, dc, 1 + i * TT:1 + (i + 1) * TT]
                        prv = xn[:, dc, i * TT:i * TT + TT]
                        for (pcol, dst) in [(0, xk_sb), (2, xv_sb), (4, xr_sb)]:
                            v.tensor_scalar(out=mixtmp[:], in0=prv, scalar1=mixp[:, pcol + 1, dc:dc + 1],
                                            scalar2=None, op0=ALU.mult)
                            ins = v.scalar_tensor_tensor(out=dst[:, dc], in0=cur, scalar=mixp[:, pcol, dc:dc + 1],
                                                         in1=mixtmp[:], op0=ALU.mult, op1=ALU.add)
                    return ins
                mdep = [ln_done[i]]
                if prev_proj_last is not None:
                    mdep.append(prev_proj_last)
                o_mix = sq.emit("v", mix, after=mdep)

                for cc in range(NCH):
                    pb = (i * NCH + cc) % 2

                    def proj(t, cc=cc, pb=pb):
                        for dc in range(NCH):
                            t.matmul(kps[:, pb], wk_sb[:, dc, cc * 128:(cc + 1) * 128], xk_sb[:, dc],
                                     start=(dc == 0), stop=(dc == NCH - 1))
                        for dc in range(NCH):
                            t.matmul(vps[:, pb], wv_sb[:, dc, cc * 128:(cc + 1) * 128], xv_sb[:, dc],
                                     start=(dc == 0), stop=(dc == NCH - 1))
                        ins = None
                        for dc in range(NCH):
                            ins = t.matmul(rps[:, pb], wr_sb[:, dc, cc * 128:(cc + 1) * 128], xr_sb[:, dc],
                                           start=(dc == 0), stop=(dc == NCH - 1))
                        return ins
                    pdep = [o_mix, o_wk, o_wv, o_wr]
                    if prev_evac[pb] is not None:
                        pdep.append(prev_evac[pb])
                    o_proj = sq.emit("t", proj, after=pdep)
                    prev_proj_last = o_proj

                    def evac(a, pb=pb):
                        a.copy(out=kb[:], in_=kps[:, pb])
                        a.copy(out=vb[:], in_=vps[:, pb])
                        return a.copy(out=rb[:], in_=rps[:, pb])
                    edep = [o_proj]
                    if prev_scan[(cc + NCH - 1) % NCH] is not None:
                        edep.append(prev_scan[(cc + NCH - 1) % NCH])
                    o_evac = sq.emit("a", evac, after=edep)
                    prev_evac[pb] = o_evac

                    def exps(a):
                        a.activation(out=e1[:], in_=kb[:], func=AF.Exp)
                        return a.activation(out=rb[:], in_=rb[:], func=AF.Sigmoid)
                    o_exps = sq.emit("a", exps, after=[o_evac])

                    def scan_a(v, cc=cc, i=i):
                        v.tensor_tensor(out=e2[:], in0=e1[:], in1=vb[:], op=ALU.mult)
                        if i > 0:
                            v.tensor_copy(out=Pt[:, cc, 0:1], in_=Pt[:, cc, TT:TT + 1])
                            v.tensor_copy(out=Qt[:, cc, 0:1], in_=Qt[:, cc, TT:TT + 1])
                        dbc = mixp[:, 6, cc:cc + 1].broadcast_to((128, TT))
                        v.tensor_tensor_scan(Pt[:, cc, 1:TT + 1], dbc, e2[:], Pt[:, cc, 0:1], ALU.mult, ALU.add)
                        return v.tensor_tensor_scan(Qt[:, cc, 1:TT + 1], dbc, e1[:], Qt[:, cc, 0:1], ALU.mult, ALU.add)
                    sdep = [o_exps]
                    if prev_store is not None:
                        sdep.append(prev_store)
                    o_scana = sq.emit("v", scan_a, after=sdep, hazard=True)

                    def scan_b(v, cc=cc):
                        v.tensor_scalar(out=e3[:], in0=e1[:], scalar1=mixp[:, 7, cc:cc + 1], scalar2=None, op0=ALU.mult)
                        v.tensor_tensor(out=e4[:], in0=e3[:], in1=vb[:], op=ALU.mult)
                        v.tensor_tensor(out=e2[:], in0=Pt[:, cc, 0:TT], in1=e4[:], op=ALU.add)
                        v.tensor_tensor(out=e3[:], in0=Qt[:, cc, 0:TT], in1=e3[:], op=ALU.add)
                        v.reciprocal(out=e3[:], in_=e3[:])
                        v.tensor_tensor(out=e2[:], in0=e2[:], in1=e3[:], op=ALU.mult)
                        return v.tensor_tensor(out=sry_t[:, cc], in0=e2[:], in1=rb[:], op=ALU.mult)
                    o_scan = sq.emit("v", scan_b, after=[o_scana])
                    prev_scan[cc] = o_scan

                prev_store = sq.dma("dg", sry_r[:, :, i * TT:(i + 1) * TT], sry_t[:],
                                    after=[prev_scan[NCH - 1]])
            sq.set_floor()

        with ExitStack() as ph:
            psb = lambda *a: ph.enter_context(nc.sbuf_tensor(*a))
            ppt = lambda *a: ph.enter_context(nc.psum_tensor(*a))
            wo_sb = psb("wo_sb", [128, NCH, D], BF16)
            o_wo = sq.dma("dg", wo_sb[:], woT.rearrange("(c p) f -> p c f", p=128))
            sy_in = psb("sy_in", [128, 2, NCH, TT], BF16)
            so_t = psb("so_t", [128, NCH, TT], BF16)
            sps = ppt("sps", [128, 2, TT], F32)
            sry_r = sry_h.rearrange("(c p) t -> p c t", p=128)
            s_r = s_hbm.rearrange("(c p) t -> p c t", p=128)
            prev_ev = [None, None]
            prev_st = None
            ld_user = [None, None]
            for i in range(NTT):
                dep = [] if ld_user[i % 2] is None else [ld_user[i % 2]]
                ld = sq.dma("ds", sy_in[:, i % 2], sry_r[:, :, i * TT:(i + 1) * TT], after=dep)
                last_ev = None
                for cc in range(NCH):
                    pb = (i * NCH + cc) % 2

                    def proj(t, cc=cc, i=i, pb=pb):
                        ins = None
                        for dc in range(NCH):
                            ins = t.matmul(sps[:, pb], wo_sb[:, dc, cc * 128:(cc + 1) * 128],
                                           sy_in[:, i % 2, dc], start=(dc == 0), stop=(dc == NCH - 1))
                        return ins
                    pdep = [ld, o_wo]
                    if prev_ev[pb] is not None:
                        pdep.append(prev_ev[pb])
                    o_p = sq.emit("t", proj, after=pdep)

                    def evac(a, cc=cc, pb=pb):
                        return a.copy(out=so_t[:, cc], in_=sps[:, pb])
                    edep = [o_p] + ([prev_st] if prev_st is not None else [])
                    o_e = sq.emit("a", evac, after=edep)
                    prev_ev[pb] = o_e
                    last_ev = o_e
                ld_user[i % 2] = last_ev
                prev_st = sq.dma("dg", s_r[:, :, i * TT:(i + 1) * TT], so_t[:], after=[last_ev])
            sq.set_floor()

        with ExitStack() as ph:
            psb = lambda *a: ph.enter_context(nc.sbuf_tensor(*a))
            ppt = lambda *a: ph.enter_context(nc.psum_tensor(*a))
            NFH = 4 * D // 128
            b1_sb = psb("b1_sb", [128, 32], F32)
            b2_sb = psb("b2_sb", [128, 8], F32)
            o_b1 = sq.dma("ds", b1_sb[:], b1f[:, :])
            o_b2 = sq.dma("ds", b2_sb[:], b2f[:, :])
            w1_t = psb("w1_t", [128, 4, NCH, 128], BF16)
            w2_t = psb("w2_t", [128, 4, NFH, 128], BF16)
            hbuf = psb("hbuf", [128, NFH, TT], BF16)
            gbuf = psb("gbuf", [128, TT], BF16)
            xres = psb("xres", [128, 2, NCH, TT], F32)
            sres = psb("sres", [128, 2, NCH, TT], BF16)
            onew = psb("onew", [128, NCH, TT], F32)
            fps = ppt("fps", [128, 2, TT], F32)
            gps = ppt("gps", [128, 2, TT], F32)
            w1r = w1t_h.rearrange("(c p) f -> p c f", p=128)
            w2r = w2t_h.rearrange("(c p) f -> p c f", p=128)
            xi_r = x_in.rearrange("(c p) t -> p c t", p=128)
            xo_r = x_out.rearrange("(c p) t -> p c t", p=128)
            s_r = s_hbm.rearrange("(c p) t -> p c t", p=128)

            prev_hev = [None, None]
            prev_gev = [None, None]
            prev_resid = None
            prev_xst = None
            h1hist = []
            h2hist = []
            xld_user = [None, None]
            out_stores = []
            for i in range(NTT):
                dep = [] if xld_user[i % 2] is None else [xld_user[i % 2]]
                o_lx = sq.dma("ds", xres[:, i % 2], xi_r[:, :, i * TT:(i + 1) * TT], after=dep)
                o_ls = sq.dma("ds", sres[:, i % 2], s_r[:, :, i * TT:(i + 1) * TT], after=dep)
                h_evs = []
                for fc in range(NFH):
                    wb = (i * NFH + fc) % 4
                    wdep = [h1hist[-2]] if len(h1hist) >= 2 else []
                    o_lw = sq.dma("ds", w1_t[:, wb], w1r[:, :, fc * 128:(fc + 1) * 128], after=wdep)

                    def hmm(t, fc=fc, wb=wb, i=i):
                        ins = None
                        for dc in range(NCH):
                            ins = t.matmul(fps[:, fc % 2], w1_t[:, wb, dc],
                                           xn[:, dc, 1 + i * TT:1 + (i + 1) * TT],
                                           start=(dc == 0), stop=(dc == NCH - 1))
                        return ins
                    hdep = [o_lw, ln_done[i]]
                    if prev_hev[fc % 2] is not None:
                        hdep.append(prev_hev[fc % 2])
                    o_hm = sq.emit("t", hmm, after=hdep)
                    h1hist.append(o_hm)

                    def hev(a, fc=fc):
                        return a.activation(out=hbuf[:, fc], in_=fps[:, fc % 2], func=AF.Gelu,
                                            bias=b1_sb[:, fc:fc + 1])
                    o_he = sq.emit("a", hev, after=[o_hm, o_b1])
                    prev_hev[fc % 2] = o_he
                    h_evs.append(o_he)
                    if debug:
                        hr = h_h.rearrange("(c p) t -> p c t", p=128)
                        o_hd = sq.dma("dg", hr[:, fc, i * TT:(i + 1) * TT], hbuf[:, fc], after=[o_he])
                        prev_hev[fc % 2] = o_hd
                        h_evs[-1] = o_hd

                res_ops = []
                for dc in range(NCH):
                    wb = (i * NCH + dc) % 4
                    wdep = [h2hist[-2]] if len(h2hist) >= 2 else []
                    o_lw2 = sq.dma("ds", w2_t[:, wb], w2r[:, :, dc * 128:(dc + 1) * 128], after=wdep)

                    def gmm(t, dc=dc, wb=wb):
                        ins = None
                        for hc in range(NFH):
                            ins = t.matmul(gps[:, dc % 2], w2_t[:, wb, hc], hbuf[:, hc],
                                           start=(hc == 0), stop=(hc == NFH - 1))
                        return ins
                    gdep = [o_lw2] + h_evs
                    if prev_gev[dc % 2] is not None:
                        gdep.append(prev_gev[dc % 2])
                    o_gm = sq.emit("t", gmm, after=gdep)
                    h2hist.append(o_gm)

                    def gev(a, dc=dc):
                        return a.activation(out=gbuf[:], in_=gps[:, dc % 2], func=AF.Silu,
                                            bias=b2_sb[:, dc:dc + 1])
                    gevdep = [o_gm, o_b2]
                    if prev_resid is not None:
                        gevdep.append(prev_resid)
                    o_ge = sq.emit("a", gev, after=gevdep)
                    prev_gev[dc % 2] = o_ge
                    o_gd = None
                    if debug:
                        gr = g_h.rearrange("(c p) t -> p c t", p=128)
                        o_gd = sq.dma("dg", gr[:, dc, i * TT:(i + 1) * TT], gbuf[:], after=[o_ge])

                    def resid(v, dc=dc, i=i):
                        v.tensor_tensor(out=onew[:, dc], in0=sres[:, i % 2, dc], in1=gbuf[:], op=ALU.mult)
                        return v.tensor_tensor(out=onew[:, dc], in0=onew[:, dc], in1=xres[:, i % 2, dc], op=ALU.add)
                    rdep = [o_ge, o_lx, o_ls]
                    if prev_xst is not None:
                        rdep.append(prev_xst)
                    o_r = sq.emit("v", resid, after=rdep)
                    prev_resid = o_r if not debug else sq.emit("v", lambda v: v.nop(), after=[o_r, o_gd])
                    res_ops.append(o_r)
                xld_user[i % 2] = res_ops[-1]
                prev_xst = sq.dma("dg", xo_r[:, :, i * TT:(i + 1) * TT], onew[:], after=res_ops)
                out_stores.append(prev_xst)

            if ctx is None:
                sq.finalize(block, sems, outputs_done=out_stores)
            else:
                ctx["out_stores"] = out_stores
                sq.set_floor()
    return nc



def fm(a):
    a = np.asarray(a).reshape(-1)
    return np.ascontiguousarray(a.reshape(-1, 128).T.astype(np.float32))


def l3_inmap(xcore_fm, I, pre=""):
    g = lambda n: I[pre + n] if pre else I[n]
    return {
        "x_in": xcore_fm,
        "wkT": np.ascontiguousarray(g("L3_attn_W_key").T),
        "wvT": np.ascontiguousarray(g("L3_attn_W_value").T),
        "wrT": np.ascontiguousarray(g("L3_attn_W_receptance").T),
        "woT": np.ascontiguousarray(g("L3_attn_W_output").T),
        "w1T": np.ascontiguousarray(g("L3_ffn_W1").T),
        "w2T": np.ascontiguousarray(g("L3_ffn_W2").T),
        "b1f": fm(g("L3_ffn_b1")),
        "b2f": fm(g("L3_ffn_b2")),
        "tdf": fm(g("L3_attn_time_decay")),
        "tff": fm(g("L3_attn_time_first")),
        "tmkf": fm(g("L3_attn_time_mix_k")),
        "tmvf": fm(g("L3_attn_time_mix_v")),
        "tmrf": fm(g("L3_attn_time_mix_r")),
        "ones128_h": np.ones((128, 1), np.float32),
        "ones1_h": np.ones((1, 128), np.float32),
    }





def build_l2(debug=False, upto=99, ctx=None):
    from contextlib import ExitStack
    nc = bass.Bass("TRN2", target_bir_lowering=False, debug=False) if ctx is None else ctx["nc"]
    x_in = nc.dram_tensor("x_in", [D, S], F32, kind="ExternalInput").ap()
    wqkvT = nc.dram_tensor("wqkvT", [D, 3 * D], F32, kind="ExternalInput").ap()
    woutT = nc.dram_tensor("woutT", [D, D], F32, kind="ExternalInput").ap()
    boutf = nc.dram_tensor("boutf", [128, 8], F32, kind="ExternalInput").ap()
    projh = nc.dram_tensor("projh", [8, 128, 128], F32, kind="ExternalInput").ap()
    w1T = nc.dram_tensor("w1T", [D, 4 * D], F32, kind="ExternalInput").ap()
    w2T = nc.dram_tensor("w2T", [4 * D, D], F32, kind="ExternalInput").ap()
    w1N = nc.dram_tensor("w1N", [4 * D, D], F32, kind="ExternalInput").ap()
    w2N = nc.dram_tensor("w2N", [D, 4 * D], F32, kind="ExternalInput").ap()
    b1f = nc.dram_tensor("b1f", [128, 32], F32, kind="ExternalInput").ap()
    b2f = nc.dram_tensor("b2f", [128, 8], F32, kind="ExternalInput").ap()
    iden_h = nc.dram_tensor("iden_h", [128, 128], BF16, kind="ExternalInput").ap()
    ones128_h = nc.dram_tensor("ones128_h", [128, 1], F32, kind="ExternalInput").ap()
    ones1_h = nc.dram_tensor("ones1_h", [1, 128], F32, kind="ExternalInput").ap()
    x_out = nc.dram_tensor("x_out", [D, S], F32, kind="ExternalOutput").ap()

    kind = "ExternalOutput" if debug else "Internal"
    qf_h = nc.dram_tensor("qf_h", [D, S], BF16, kind=kind).ap()
    kf_h = nc.dram_tensor("kf_h", [D, S], BF16, kind=kind).ap()
    vt_h = nc.dram_tensor("vt_h", [S, 8 * 129], BF16, kind=kind).ap()
    of_h = nc.dram_tensor("of_h", [D, S], BF16, kind=kind).ap()
    s_hbm = nc.dram_tensor("s_hbm", [D, S], BF16, kind=kind).ap()
    w1s_h = nc.dram_tensor("w1s_h", [D, 4 * D], BF16, kind="Internal").ap()
    w2s_h = nc.dram_tensor("w2s_h", [4 * D, D], BF16, kind="Internal").ap()
    xn_h = nc.dram_tensor("xn_h", [D, S], BF16, kind=kind).ap() if debug else None
    dbg_h = nc.dram_tensor("dbg_h", [4, 128, 520], F32, kind=kind).ap() if debug else None

    stack = ExitStack()
    with stack:
        if ctx is None:
            sems = alloc_sems(nc, stack)
            block = stack.enter_context(nc.Block())
            sq = Seq(nc)
        else:
            sems, block, sq = ctx["sems"], ctx["block"], ctx["sq"]
        sb = lambda *a: stack.enter_context(nc.sbuf_tensor(*a))

        ones128 = sb("c_ones128", [128, 1], F32)
        ones1 = sb("c_ones1", [1, 128], F32)
        iden = sb("c_iden", [128, 128], BF16)
        cst = sb("cst", [128, 2], F32)
        xn = sb("xn", [128, NCH, S], BF16)

        sq.dma("ds", ones128[:], ones128_h[:, :])
        sq.dma("ds", ones1[:], ones1_h[:, :])
        sq.dma("ds", iden[:], iden_h[:, :])

        def pre(v):
            return v.memset(cst[:, 0:1], 1e-5)
        sq.emit("v", pre)
        sq.set_floor()

        with ExitStack() as ph:
            ln_done = emit_ln(sq, nc, ph, x_in, xn, ones128, ones1, cst, col_off=0)
            sq.set_floor()
            if debug:
                xn_r = xn_h.rearrange("(c p) t -> p c t", p=128)
                for i in range(NTT):
                    sq.dma("dg", xn_r[:, :, i * TT:(i + 1) * TT], xn[:, :, i * TT:(i + 1) * TT])
                sq.set_floor()

        qf_r = qf_h.rearrange("(c p) t -> p c t", p=128)
        kf_r = kf_h.rearrange("(c p) t -> p c t", p=128)
        wq_r = wqkvT.rearrange("(c p) f -> p c f", p=128)

        with ExitStack() as ph:
            psb = lambda *a: ph.enter_context(nc.sbuf_tensor(*a))
            ppt = lambda *a: ph.enter_context(nc.psum_tensor(*a))
            wt = psb("wt", [128, 2, NCH, 128], BF16)
            ob = psb("ob", [128, 2, TT], BF16)
            vtm = psb("vtm", [128, 8, 129], BF16)
            qps = ppt("qps", [128, 2, TT], F32)
            vps = ppt("vps", [128, 2, 512], F32)
            lastw = [None, None]
            prev_ev = [None, None]
            prev_st = [None, None]
            for fc in range(16):
                wb = fc % 2
                wdep = [sq.last["t"]] if sq.last.get("t") is not None else []
                o_lw = sq.dma("dg", wt[:, wb], wq_r[:, :, fc * 128:(fc + 1) * 128], after=wdep)
                for i in range(NTT):
                    pb = (fc * NTT + i) % 2

                    def mm(t, fc=fc, wb=wb, i=i, pb=pb):
                        ins = None
                        for dc in range(NCH):
                            ins = t.matmul(qps[:, pb], wt[:, wb, dc], xn[:, dc, i * TT:(i + 1) * TT],
                                           start=(dc == 0), stop=(dc == NCH - 1))
                        return ins
                    mdep = [o_lw, ln_done[i]]
                    if prev_ev[pb] is not None:
                        mdep.append(prev_ev[pb])
                    o_mm = sq.emit("t", mm, after=mdep)
                    lastw[wb] = o_mm

                    def ev(a, pb=pb):
                        return a.copy(out=ob[:, pb], in_=qps[:, pb])
                    edep = [o_mm]
                    if prev_st[pb] is not None:
                        edep.append(prev_st[pb])
                    o_ev = sq.emit("a", ev, after=edep)
                    prev_ev[pb] = o_ev
                    dst = qf_r[:, fc, i * TT:(i + 1) * TT] if fc < 8 else kf_r[:, fc - 8, i * TT:(i + 1) * TT]
                    prev_st[pb] = sq.dma("dg", dst, ob[:, pb], after=[o_ev])
            def vones(v):
                ins = None
                for h in range(8):
                    ins = v.memset(vtm[:, h, 128:129], 1.0)
                return ins
            o_vone = sq.emit("v", vones)
            vwdep = [sq.last["t"]] if sq.last.get("t") is not None else []
            vw = psb("vw", [128, NCH, 1024], BF16)
            o_vw = sq.dma("dg", vw[:], wq_r[:, :, 2048:3072], after=vwdep)
            prev_vst = None
            prev_vev = None
            for i2 in range(S // 128):

                def vmm(t, i2=i2):
                    ins = None
                    for half in range(2):
                        for dc in range(NCH):
                            ins = t.matmul(vps[:, half], xn[:, dc, i2 * 128:(i2 + 1) * 128],
                                           vw[:, dc, half * 512:(half + 1) * 512],
                                           start=(dc == 0), stop=(dc == NCH - 1))
                    return ins
                vdep = [o_vw, ln_done[min(i2 // 4, NTT - 1)]]
                if prev_vev is not None:
                    vdep.append(prev_vev)
                o_vm = sq.emit("t", vmm, after=vdep)

                def vev(a, i2=i2):
                    ins = None
                    for h in range(8):
                        half, off = divmod(h * 128, 512)
                        ins = a.copy(out=vtm[:, h, 0:128], in_=vps[:, half, off:off + 128]) if False else                             a.copy(out=vtm[:, h, 0:128], in_=vps[:, half, off:off + 128])
                    return ins
                vedep = [o_vm, o_vone]
                if prev_vst is not None:
                    vedep.append(prev_vst)
                o_ve = sq.emit("a", vev, after=vedep)
                prev_vev = o_ve
                prev_vst = sq.dma("dg", vt_h[i2 * 128:(i2 + 1) * 128, :], vtm[:], after=[o_ve])
            sq.set_floor()

        with ExitStack() as ph:
            psb = lambda *a: ph.enter_context(nc.sbuf_tensor(*a))
            ppt = lambda *a: ph.enter_context(nc.psum_tensor(*a))
            prj = psb("prj", [128, 8, 128], BF16)
            o_prj = sq.dma("dg", prj[:], projh.transpose((1, 0, 2)))
            qs = psb("qs", [128, 2, 128], BF16)
            ks = psb("ks", [128, 2, 128], BF16)
            vts = psb("vts", [128, 2, 8, 129], BF16)
            kp_tm = psb("kp_tm", [128, 128], BF16)
            qp_fm = psb("qp_fm", [128, 128], BF16)
            kv_sb = psb("kv_sb", [128, 4, 129], BF16)
            et = psb("et", [128, 130], F32)
            bcs = psb("bcs", [128, 128], F32)
            obuf = psb("obuf", [128, NCH, 128], BF16)
            pps = ppt("pps", [128, 2, 128], F32)
            kvps = ppt("kvps", [128, 4, 512], F32)
            nups = ppt("nups", [128, 4, 32], F32)
            zps = ppt("zps", [1, 4, 32], F32)
            bcps = ppt("bcps", [128, 128], F32)
            of_r = of_h.rearrange("(c p) t -> p c t", p=128)

            prev = {}
            prev_ost = None
            for i2 in range(S // 128):
                for h in range(8):
                    hb = i2 % 2
                    dep_h = [sq.last["t"]] if sq.last.get("t") is not None else []
                    o_lq = sq.dma("ds", qs[:, hb], qf_r[:, h, i2 * 128:(i2 + 1) * 128], after=dep_h)
                    o_lk = sq.dma("ds", ks[:, hb], kf_r[:, h, i2 * 128:(i2 + 1) * 128], after=dep_h)
                    if h == 0:
                        o_lv = sq.dma("ds", vts[:, hb], vt_h[i2 * 128:(i2 + 1) * 128, :], after=dep_h)
                        prev[("lv", hb)] = o_lv
                    o_lv = prev[("lv", hb)]

                    def pmm(t, h=h, hb=hb):
                        t.matmul(pps[:, 0], ks[:, hb], prj[:, h], start=True, stop=True)
                        return t.matmul(pps[:, 1], prj[:, h], qs[:, hb], start=True, stop=True)
                    pdep = [o_lq, o_lk, o_prj]
                    if ("pev",) in prev:
                        pdep.append(prev[("pev",)])
                    o_pm = sq.emit("t", pmm, after=pdep)

                    o_e1 = sq.emit("v", lambda v: v.tensor_scalar(out=et[:, 0:128], in0=pps[:, 0], scalar1=0.0, scalar2=None, op0=ALU.min), after=[o_pm])
                    o_e2 = sq.emit("a", lambda a: a.activation(out=kp_tm[:], in_=et[:, 0:128], func=AF.Exp), after=[o_e1])
                    o_e3 = sq.emit("v", lambda v: v.scalar_tensor_tensor(out=kp_tm[:], in0=pps[:, 0], scalar=1.0, in1=kp_tm[:], op0=ALU.add, op1=ALU.max), after=[o_e2])
                    o_f1 = sq.emit("v", lambda v: v.tensor_scalar(out=et[:, 0:128], in0=pps[:, 1], scalar1=0.0, scalar2=None, op0=ALU.min), after=[o_pm, o_e3])
                    o_f2 = sq.emit("a", lambda a: a.activation(out=qp_fm[:], in_=et[:, 0:128], func=AF.Exp), after=[o_f1])
                    o_f3 = sq.emit("v", lambda v: v.scalar_tensor_tensor(out=qp_fm[:], in0=pps[:, 1], scalar=1.0, in1=qp_fm[:], op0=ALU.add, op1=ALU.max), after=[o_f2])
                    prev[("pev",)] = o_f3

                    def kvmm(t, hb=hb, h=h):
                        ins = None
                        for j in range(4):
                            ins = t.matmul(kvps[:, j, 0:129], kp_tm[32 * j:32 * (j + 1), :],
                                           vts[32 * j:32 * (j + 1), hb, h], start=True, stop=True,
                                           tile_position=(32 * j, 0))
                        return ins
                    kvdep = [o_e3, o_lv]
                    if ("kvev",) in prev:
                        kvdep.append(prev[("kvev",)])
                    o_kv = sq.emit("t", kvmm, after=kvdep)

                    def kvev(a):
                        ins = None
                        for j in range(4):
                            ins = a.copy(out=kv_sb[:, j], in_=kvps[:, j, 0:129])
                        return ins
                    kvedep = [o_kv]
                    if ("num",) in prev:
                        kvedep.append(prev[("num",)])
                    o_kve = sq.emit("a", kvev, after=kvedep)
                    prev[("kvev",)] = o_kve

                    def numm(t, h=h):
                        ins = None
                        for j in range(4):
                            t.matmul(nups[:, j], kv_sb[:, j, 0:128], qp_fm[:, 32 * j:32 * (j + 1)],
                                     start=True, stop=True)
                            ins = t.matmul(zps[0:1, j], kv_sb[:, j, 128:129], qp_fm[:, 32 * j:32 * (j + 1)],
                                           start=True, stop=True)
                        return ins
                    ndep = [o_f3, o_kve]
                    if ("nev",) in prev:
                        ndep.append(prev[("nev",)])
                    o_nu = sq.emit("t", numm, after=ndep)
                    prev[("num",)] = o_nu

                    def nrm_a(v):
                        v.tensor_scalar(out=et[0:1, 0:128], in0=zps[0:1, :].rearrange("p a b -> p (a b)"),
                                        scalar1=1e-8, scalar2=None, op0=ALU.add)
                        return v.reciprocal(out=et[0:1, 128:129] if False else et[0:1, 0:128], in_=et[0:1, 0:128])
                    o_na = sq.emit("v", nrm_a, after=[o_nu], hazard=True)

                    def bcmm(t):
                        return t.matmul(bcps[:], ones1[:], et[0:1, 0:128], start=True, stop=True)
                    bcdep = [o_na]
                    if ("omul",) in prev:
                        bcdep.append(prev[("omul",)])
                    o_bc = sq.emit("t", bcmm, after=bcdep)

                    def bcev(a):
                        return a.copy(out=bcs[:], in_=bcps[:])
                    o_bce = sq.emit("a", bcev, after=[o_bc])

                    def omul(v, h=h):
                        return v.tensor_tensor(out=obuf[:, h], in0=nups[:, :].rearrange("p a b -> p (a b)"),
                                               in1=bcs[:], op=ALU.mult)
                    odep = [o_bce, o_nu]
                    if prev_ost is not None:
                        odep.append(prev_ost)
                    o_om = sq.emit("v", omul, after=odep)
                    prev[("omul",)] = o_om
                    prev[("nev",)] = o_om
                    prev[("onorm", h)] = o_om

                prev_ost = sq.dma("dg", of_r[:, :, i2 * 128:(i2 + 1) * 128], obuf[:],
                                  after=[prev[("onorm", h2)] for h2 in range(8)])
            sq.set_floor()

        if upto <= 2:
            sq.finalize(block, sems)
            return nc

        with ExitStack() as ph:
            psb = lambda *a: ph.enter_context(nc.sbuf_tensor(*a))
            ppt = lambda *a: ph.enter_context(nc.psum_tensor(*a))
            wo_sb = psb("wo_sb", [128, NCH, D], BF16)
            bo_sb = psb("bo_sb", [128, 8], F32)
            sq.dma("dg", wo_sb[:], woutT.rearrange("(c p) f -> p c f", p=128))
            sq.dma("ds", bo_sb[:], boutf[:, :])
            sq.set_floor()
            oin = psb("oin", [128, 2, NCH, TT], BF16)
            so_t = psb("so_t", [128, NCH, TT], BF16)
            sps = ppt("sps", [128, 2, TT], F32)
            s_r = s_hbm.rearrange("(c p) t -> p c t", p=128)
            prev_ev = [None, None]
            prev_st = None
            ld_user = [None, None]
            for i in range(NTT):
                dep = [] if ld_user[i % 2] is None else [ld_user[i % 2]]
                ld = sq.dma("ds", oin[:, i % 2], of_r[:, :, i * TT:(i + 1) * TT], after=dep)
                last_ev = None
                for cc in range(NCH):
                    pb = (i * NCH + cc) % 2

                    def proj(t, cc=cc, i=i, pb=pb):
                        ins = None
                        for dc in range(NCH):
                            ins = t.matmul(sps[:, pb], wo_sb[:, dc, cc * 128:(cc + 1) * 128],
                                           oin[:, i % 2, dc], start=(dc == 0), stop=(dc == NCH - 1))
                        return ins
                    pdep = [ld]
                    if prev_ev[pb] is not None:
                        pdep.append(prev_ev[pb])
                    o_p = sq.emit("t", proj, after=pdep)

                    def evac(a, cc=cc, pb=pb):
                        return a.activation(out=so_t[:, cc], in_=sps[:, pb], func=AF.Identity,
                                            bias=bo_sb[:, cc:cc + 1])
                    edep = [o_p] + ([prev_st] if prev_st is not None else [])
                    o_e = sq.emit("a", evac, after=edep)
                    prev_ev[pb] = o_e
                    last_ev = o_e
                ld_user[i % 2] = last_ev
                prev_st = sq.dma("dg", s_r[:, :, i * TT:(i + 1) * TT], so_t[:], after=[last_ev])
            sq.set_floor()

        if upto <= 3:
            sq.finalize(block, sems)
            return nc

        a1_sb = sb("a1_sb", [128, 32], F32)
        a2_sb = sb("a2_sb", [128, 8], F32)
        with ExitStack() as ph:
            psb = lambda *a: ph.enter_context(nc.sbuf_tensor(*a))
            wn = psb("wn", [128, 2, 4096], F32)
            wq = psb("wq", [128, 4096], BF16)
            for wi, (whn, whT, wst, asb, nrt, ncolN, ncolT) in enumerate([
                    (w1N, w1T, w1s_h, a1_sb, 32, 1024, 4096), (w2N, w2T, w2s_h, a2_sb, 8, 4096, 1024)]):
                reds = []
                for r in range(nrt):
                    dep = [reds[r - 2]] if r >= 2 else []
                    ld = sq.dma("ds", wn[:, r % 2, 0:ncolN], whn[r * 128:(r + 1) * 128, :], after=dep)

                    def red(v, r=r, wi=wi, ncolN=ncolN, asb=asb):
                        return v.tensor_reduce(out=asb[:, r:r + 1], in_=wn[:, r % 2, 0:ncolN], axis=AX.X,
                                               op=ALU.add, apply_absolute_value=True)
                    reds.append(sq.emit("v", red, after=[ld], hazard=True))

                def ascale(v, asb=asb, nrt=nrt, ncolN=ncolN):
                    return v.tensor_scalar(out=asb[:, 0:nrt], in0=asb[:, 0:nrt], scalar1=1.0 / ncolN,
                                           scalar2=None, op0=ALU.mult)
                o_as = sq.emit("v", ascale, after=reds[-2:], hazard=True)
                sq.set_floor()
                nrtT = (ncolN * nrt * 128 // ncolT) // 128
                prev_q = None
                for r in range(nrtT):
                    dep = [prev_q] if prev_q is not None else []
                    ld = sq.dma("ds", wn[:, r % 2, 0:ncolT], whT[r * 128:(r + 1) * 128, :], after=dep)

                    def sgn(a, r=r, ncolT=ncolT):
                        return a.activation(out=wq[:, 0:ncolT], in_=wn[:, r % 2, 0:ncolT], func=AF.Sign)
                    o_s = sq.emit("a", sgn, after=[ld])
                    prev_q = sq.dma("dg", wst[r * 128:(r + 1) * 128, :], wq[:, 0:ncolT], after=[o_s])
                sq.set_floor()

        with ExitStack() as ph:
            psb = lambda *a: ph.enter_context(nc.sbuf_tensor(*a))
            ppt = lambda *a: ph.enter_context(nc.psum_tensor(*a))
            NFH = 4 * D // 128
            b1_sb = psb("b1_sb", [128, 32], F32)
            b2_sb = psb("b2_sb", [128, 8], F32)
            sq.dma("ds", b1_sb[:], b1f[:, :])
            sq.dma("ds", b2_sb[:], b2f[:, :])
            sq.set_floor()
            w1_t = psb("w1_t", [128, 4, NCH, 128], BF16)
            w2_t = psb("w2_t", [128, 4, NFH, 128], BF16)
            hbuf = psb("hbuf", [128, NFH, TT], BF16)
            gbuf = psb("gbuf", [128, TT], BF16)
            xres = psb("xres", [128, 2, NCH, TT], F32)
            sres = psb("sres", [128, 2, NCH, TT], BF16)
            onew = psb("onew", [128, NCH, TT], F32)
            fps = ppt("fps", [128, 2, TT], F32)
            gps = ppt("gps", [128, 2, TT], F32)
            w1r = w1s_h.rearrange("(c p) f -> p c f", p=128)
            w2r = w2s_h.rearrange("(c p) f -> p c f", p=128)
            xi_r = x_in.rearrange("(c p) t -> p c t", p=128)
            xo_r = x_out.rearrange("(c p) t -> p c t", p=128)
            s_r = s_hbm.rearrange("(c p) t -> p c t", p=128)

            prev_hev = [None, None]
            prev_gev = [None, None]
            prev_resid = None
            prev_xst = None
            h1hist = []
            h2hist = []
            xld_user = [None, None]
            out_stores = []
            for i in range(NTT):
                dep = [] if xld_user[i % 2] is None else [xld_user[i % 2]]
                o_lx = sq.dma("ds", xres[:, i % 2], xi_r[:, :, i * TT:(i + 1) * TT], after=dep)
                o_ls = sq.dma("ds", sres[:, i % 2], s_r[:, :, i * TT:(i + 1) * TT], after=dep)
                h_evs = []
                for fc in range(NFH):
                    wb = (i * NFH + fc) % 4
                    wdep = [h1hist[-2]] if len(h1hist) >= 2 else []
                    o_lw = sq.dma("ds", w1_t[:, wb], w1r[:, :, fc * 128:(fc + 1) * 128], after=wdep)

                    def hmm(t, fc=fc, wb=wb, i=i):
                        ins = None
                        for dc in range(NCH):
                            ins = t.matmul(fps[:, fc % 2], w1_t[:, wb, dc], xn[:, dc, i * TT:(i + 1) * TT],
                                           start=(dc == 0), stop=(dc == NCH - 1))
                        return ins
                    hdep = [o_lw, ln_done[i]]
                    if prev_hev[fc % 2] is not None:
                        hdep.append(prev_hev[fc % 2])
                    o_hm = sq.emit("t", hmm, after=hdep)
                    h1hist.append(o_hm)

                    def hev(a, fc=fc):
                        return a.activation(out=hbuf[:, fc], in_=fps[:, fc % 2], func=AF.Gelu,
                                            scale=a1_sb[:, fc:fc + 1], bias=b1_sb[:, fc:fc + 1])
                    o_he = sq.emit("a", hev, after=[o_hm])
                    prev_hev[fc % 2] = o_he
                    h_evs.append(o_he)

                res_ops = []
                for dc in range(NCH):
                    wb = (i * NCH + dc) % 4
                    wdep = [h2hist[-2]] if len(h2hist) >= 2 else []
                    o_lw2 = sq.dma("ds", w2_t[:, wb], w2r[:, :, dc * 128:(dc + 1) * 128], after=wdep)

                    def gmm(t, dc=dc, wb=wb):
                        ins = None
                        for hc in range(NFH):
                            ins = t.matmul(gps[:, dc % 2], w2_t[:, wb, hc], hbuf[:, hc],
                                           start=(hc == 0), stop=(hc == NFH - 1))
                        return ins
                    gdep = [o_lw2] + h_evs
                    if prev_gev[dc % 2] is not None:
                        gdep.append(prev_gev[dc % 2])
                    o_gm = sq.emit("t", gmm, after=gdep)
                    h2hist.append(o_gm)

                    def gev(a, dc=dc):
                        return a.activation(out=gbuf[:], in_=gps[:, dc % 2], func=AF.Silu,
                                            scale=a2_sb[:, dc:dc + 1], bias=b2_sb[:, dc:dc + 1])
                    gevdep = [o_gm]
                    if prev_resid is not None:
                        gevdep.append(prev_resid)
                    o_ge = sq.emit("a", gev, after=gevdep)
                    prev_gev[dc % 2] = o_ge

                    def resid(v, dc=dc, i=i):
                        v.tensor_tensor(out=onew[:, dc], in0=sres[:, i % 2, dc], in1=gbuf[:], op=ALU.mult)
                        return v.tensor_tensor(out=onew[:, dc], in0=onew[:, dc], in1=xres[:, i % 2, dc], op=ALU.add)
                    rdep = [o_ge, o_lx, o_ls]
                    if prev_xst is not None:
                        rdep.append(prev_xst)
                    o_r = sq.emit("v", resid, after=rdep)
                    prev_resid = o_r
                    res_ops.append(o_r)
                xld_user[i % 2] = res_ops[-1]
                prev_xst = sq.dma("dg", xo_r[:, :, i * TT:(i + 1) * TT], onew[:], after=res_ops)
                out_stores.append(prev_xst)

            if ctx is None:
                sq.finalize(block, sems, outputs_done=out_stores)
            else:
                ctx["out_stores"] = out_stores
                sq.set_floor()
    return nc


def l2_inmap(xcore_fm, I):
    return {
        "x_in": xcore_fm,
        "wqkvT": np.ascontiguousarray(I["L2_attn_w_qkv"].T),
        "woutT": np.ascontiguousarray(I["L2_attn_w_out"].T),
        "boutf": fm(I["L2_attn_b_out"]),
        "projh": np.ascontiguousarray(I["L2_attn_proj"]),
        "w1T": np.ascontiguousarray(I["L2_ffn_w1"].T),
        "w2T": np.ascontiguousarray(I["L2_ffn_w2"].T),
        "w1N": np.ascontiguousarray(I["L2_ffn_w1"]),
        "w2N": np.ascontiguousarray(I["L2_ffn_w2"]),
        "b1f": fm(I["L2_ffn_b1"]),
        "b2f": fm(I["L2_ffn_b2"]),
        "iden_h": np.eye(128, dtype=np.float32).astype(BF),
        "ones128_h": np.ones((128, 1), np.float32),
        "ones1_h": np.ones((1, 128), np.float32),
    }



def build_l1(debug=False, upto=99, ctx=None):
    from contextlib import ExitStack
    nc = bass.Bass("TRN2", target_bir_lowering=False, debug=False) if ctx is None else ctx["nc"]
    x_in = nc.dram_tensor("x_in", [D, S], F32, kind="ExternalInput").ap()
    wq = nc.dram_tensor("wq", [D, 256], F32, kind="ExternalInput").ap()
    wk = nc.dram_tensor("wk", [D, 256], F32, kind="ExternalInput").ap()
    wv = nc.dram_tensor("wv", [D, D], F32, kind="ExternalInput").ap()
    wo = nc.dram_tensor("wo", [D, D], F32, kind="ExternalInput").ap()
    wrT = nc.dram_tensor("wrT", [D, D], F32, kind="ExternalInput").ap()
    wkT2 = nc.dram_tensor("wkT2", [D, D], F32, kind="ExternalInput").ap()
    wvT2 = nc.dram_tensor("wvT2", [D, D], F32, kind="ExternalInput").ap()
    murf = nc.dram_tensor("murf", [128, 8], F32, kind="ExternalInput").ap()
    mukf = nc.dram_tensor("mukf", [128, 8], F32, kind="ExternalInput").ap()
    e1m = nc.dram_tensor("e1m", [16, 128, 128], BF16, kind="ExternalInput").ap()
    e2m = nc.dram_tensor("e2m", [16, 128, 128], BF16, kind="ExternalInput").ap()
    trimask = nc.dram_tensor("trimask", [128, 128], BF16, kind="ExternalInput").ap()
    iden_h = nc.dram_tensor("iden_h", [128, 128], BF16, kind="ExternalInput").ap()
    ones128_h = nc.dram_tensor("ones128_h", [128, 1], F32, kind="ExternalInput").ap()
    ones1_h = nc.dram_tensor("ones1_h", [1, 128], F32, kind="ExternalInput").ap()
    x_out = nc.dram_tensor("x_out", [D, S], F32, kind="ExternalOutput").ap()

    kind = "ExternalOutput" if debug else "Internal"
    xn_h = nc.dram_tensor("xn_h", [D, S], BF16, kind=kind).ap()
    qf_h = nc.dram_tensor("qf_h", [256, S], BF16, kind=kind).ap()
    kf_h = nc.dram_tensor("kf_h", [256, S], BF16, kind=kind).ap()
    kt_h = nc.dram_tensor("kt_h", [S, 256], BF16, kind=kind).ap()
    vt_h = nc.dram_tensor("vt_h", [S, 16 * 65], BF16, kind=kind).ap()
    of_h = nc.dram_tensor("of_h", [D, S], BF16, kind=kind).ap()
    s_hbm = nc.dram_tensor("s_hbm", [D, S], BF16, kind=kind).ap()
    dbg_h = nc.dram_tensor("dbg_h", [8, 128, 256], F32, kind=kind).ap() if debug else None

    stack = ExitStack()
    with stack:
        if ctx is None:
            sems = alloc_sems(nc, stack)
            block = stack.enter_context(nc.Block())
            sq = Seq(nc)
        else:
            sems, block, sq = ctx["sems"], ctx["block"], ctx["sq"]
        sb = lambda *a: stack.enter_context(nc.sbuf_tensor(*a))

        ones128 = sb("c_ones128", [128, 1], F32)
        ones1 = sb("c_ones1", [1, 128], F32)
        iden = sb("c_iden", [128, 128], BF16)
        trim = sb("c_trim", [128, 128], BF16)
        cst = sb("cst", [128, 2], F32)

        sq.dma("ds", ones128[:], ones128_h[:, :])
        sq.dma("ds", ones1[:], ones1_h[:, :])
        sq.dma("ds", iden[:], iden_h[:, :])
        sq.dma("ds", trim[:], trimask[:, :])
        sq.emit("v", lambda v: v.memset(cst[:, 0:1], 1e-5))
        sq.set_floor()

        phAB = ExitStack()
        xn = phAB.enter_context(nc.sbuf_tensor("xn", [128, NCH, S], BF16))
        with ExitStack() as ph:
            ln_done = emit_ln(sq, nc, ph, x_in, xn, ones128, ones1, cst, col_off=0)
            xn_r = xn_h.rearrange("(c p) t -> p c t", p=128)
            xn_st = []
            for i in range(NTT):
                xn_st.append(sq.dma("dg", xn_r[:, :, i * TT:(i + 1) * TT],
                                    xn[:, :, i * TT:(i + 1) * TT], after=[ln_done[i]]))
            sq.set_floor()

        if upto <= 0:
            phAB.close()
            sq.finalize(block, sems)
            return nc

        qf_r = qf_h.rearrange("(c p) t -> p c t", p=128)
        kf_r = kf_h.rearrange("(c p) t -> p c t", p=128)
        with ExitStack() as ph:
            psb = lambda *a: ph.enter_context(nc.sbuf_tensor(*a))
            ppt = lambda *a: ph.enter_context(nc.psum_tensor(*a))
            wqs = psb("wqs", [128, NCH, 256], BF16)
            wks = psb("wks", [128, NCH, 256], BF16)
            wvs = psb("wvs", [128, NCH, D], BF16)
            sq.dma("dg", wqs[:], wq.rearrange("(c p) f -> p c f", p=128))
            sq.dma("dg", wks[:], wk.rearrange("(c p) f -> p c f", p=128))
            sq.dma("dg", wvs[:], wv.rearrange("(c p) f -> p c f", p=128))
            sq.set_floor()
            ob = psb("ob", [128, 2, TT], BF16)
            ktt = psb("ktt", [128, 2, 256], BF16)
            vtm = psb("vtm", [128, 2, 16, 65], BF16)
            qps = ppt("qps", [128, 2, TT], F32)
            vps = ppt("vps", [128, 2, 512], F32)
            kps2 = ppt("kps2", [128, 256], F32)
            prev_ev = [None, None]
            prev_st = [None, None]
            for fc in range(4):
                for i in range(NTT):
                    pb = (fc * NTT + i) % 2
                    wsel = wqs if fc < 2 else wks
                    cchunk = fc % 2

                    def mm(t, wsel=wsel, cchunk=cchunk, i=i, pb=pb):
                        ins = None
                        for dc in range(NCH):
                            ins = t.matmul(qps[:, pb], wsel[:, dc, cchunk * 128:(cchunk + 1) * 128],
                                           xn[:, dc, i * TT:(i + 1) * TT],
                                           start=(dc == 0), stop=(dc == NCH - 1))
                        return ins
                    mdep = [ln_done[i]]
                    if prev_ev[pb] is not None:
                        mdep.append(prev_ev[pb])
                    o_mm = sq.emit("t", mm, after=mdep)

                    def ev(a, pb=pb, fc=fc):
                        return a.activation(out=ob[:, pb], in_=qps[:, pb], func=AF.Copy,
                                            scale=(0.25 if fc < 2 else 1.0))
                    edep = [o_mm]
                    if prev_st[pb] is not None:
                        edep.append(prev_st[pb])
                    o_ev = sq.emit("a", ev, after=edep)
                    prev_ev[pb] = o_ev
                    dst = qf_r[:, fc, i * TT:(i + 1) * TT] if fc < 2 else kf_r[:, fc - 2, i * TT:(i + 1) * TT]
                    prev_st[pb] = sq.dma("dg", dst, ob[:, pb], after=[o_ev])
            def vones(v):
                ins = None
                for hb in range(2):
                    for h in range(16):
                        ins = v.memset(vtm[:, hb, h, 64:65], 1.0)
                return ins
            o_vone = sq.emit("v", vones)
            prevk = [None, None]
            prevv = [None, None]
            prev_kev1 = None
            prev_vev1 = None
            for i2 in range(S // 128):
                hb = i2 % 2

                def ktm_mm(t, i2=i2):
                    ins = None
                    for dc in range(NCH):
                        ins = t.matmul(kps2[:, 0:256], xn[:, dc, i2 * 128:(i2 + 1) * 128],
                                       wks[:, dc, :], start=(dc == 0), stop=(dc == NCH - 1))
                    return ins
                kdep = [ln_done[min(i2 // 4, NTT - 1)]]
                if prevk[hb] is not None:
                    kdep.append(prevk[hb])
                if prev_kev1 is not None:
                    kdep.append(prev_kev1)
                o_km = sq.emit("t", ktm_mm, after=kdep)

                def ktm_ev(a, hb=hb):
                    return a.copy(out=ktt[:, hb], in_=kps2[:, 0:256])
                o_ke = sq.emit("a", ktm_ev, after=[o_km] + ([prevk[hb]] if prevk[hb] else []))
                prev_kev1 = o_ke
                o_ks = sq.dma("dg", kt_h[i2 * 128:(i2 + 1) * 128, :], ktt[:, hb], after=[o_ke])
                prevk[hb] = o_ks

                def vtm_mm(t, i2=i2):
                    ins = None
                    for half in range(2):
                        for dc in range(NCH):
                            ins = t.matmul(vps[:, half], xn[:, dc, i2 * 128:(i2 + 1) * 128],
                                           wvs[:, dc, half * 512:(half + 1) * 512],
                                           start=(dc == 0), stop=(dc == NCH - 1))
                    return ins
                vdep = [ln_done[min(i2 // 4, NTT - 1)]]
                if prevv[hb] is not None:
                    vdep.append(prevv[hb])
                if prev_vev1 is not None:
                    vdep.append(prev_vev1)
                o_vm = sq.emit("t", vtm_mm, after=vdep)

                def vtm_ev(a, hb=hb):
                    a.copy(out=vtm[:, hb, 0:8, 0:64], in_=vps[:, 0])
                    return a.copy(out=vtm[:, hb, 8:16, 0:64], in_=vps[:, 1])
                o_ve = sq.emit("a", vtm_ev, after=[o_vm, o_vone] + ([prevv[hb]] if prevv[hb] else []))
                prev_vev1 = o_ve
                o_vs = sq.dma("dg", vt_h[i2 * 128:(i2 + 1) * 128, :], vtm[:, hb], after=[o_ve])
                prevv[hb] = o_vs
            sq.set_floor()
        phAB.close()

        if upto <= 1:
            sq.finalize(block, sems)
            return nc

        with ExitStack() as ph:
            psb = lambda *a: ph.enter_context(nc.sbuf_tensor(*a))
            ppt = lambda *a: ph.enter_context(nc.psum_tensor(*a))
            qslot = psb("qslot", [128, 4, S], BF16)
            kslot = psb("kslot", [128, 4, S], BF16)
            qfm = psb("qfm", [128, 2, S], BF16)
            ktm = psb("ktm", [128, 32, 256], BF16)
            vtm2 = psb("vtm2", [128, 32, 16, 65], BF16)
            esel = psb("esel", [128, 4, 8, 128], BF16)
            Sst = psb("Sst", [128, 16, 2, 65], F32)
            Sbf = psb("Sbf", [128, 16, 2, 65], BF16)
            A2t = psb("A2t", [128, 128], BF16)
            phiq = psb("phiq", [128, 2, 128], BF16)
            phik = psb("phik", [128, 256], BF16)
            a1s = psb("a1s", [128, 128], F32)
            ebuf = psb("ebuf", [128, 4], F32)
            obuf = psb("obuf", [128, 16, 64], BF16)
            otr = psb("otr", [128, NCH, 128], BF16)
            dbgb = psb("dbgb", [128, 256], F32) if debug else None
            dbgb0 = psb("dbgb0", [128, 130], F32) if debug else None
            Aps = ppt("Aps", [128, 4, 512], F32)
            ops_ = ppt("ops_", [128, 65], F32)
            ups = ppt("ups", [128, 2, 256], F32)
            selps = ppt("selps", [128, 2, 128], F32)
            trps = ppt("trps", [128, 128], BF16)

            def zero_slots(v):
                v.memset(qslot[:], 0.0)
                v.memset(kslot[:], 0.0)
                v.memset(Sst[:], 0.0)
                return v.memset(Sbf[:], 0.0)
            o_zs = sq.emit("v", zero_slots)
            for sc in range(4):
                for b2 in range(4):
                    sq.dma("ds", qslot[32 * b2:32 * b2 + 16, sc, :],
                           qf_h[sc * 64 + 16 * b2:sc * 64 + 16 * b2 + 16, :], after=[o_zs])
                    sq.dma("ds", kslot[32 * b2:32 * b2 + 16, sc, :],
                           kf_h[sc * 64 + 16 * b2:sc * 64 + 16 * b2 + 16, :], after=[o_zs])
            sq.dma("ds", qfm[:], qf_h.rearrange("(c p) t -> p c t", p=128))
            sq.dma("ds", ktm[:], kt_h.rearrange("(r p) f -> p r f", p=128))
            sq.dma("ds", vtm2[:], vt_h.rearrange("(r p) (h f) -> p r h f", p=128, h=16))
            sq.dma("dg", esel[:, 0:2], e1m.transpose((1, 0, 2)).rearrange("p (x hh) f -> p x hh f", x=2))
            sq.dma("dg", esel[:, 2:4], e2m.transpose((1, 0, 2)).rearrange("p (x hh) f -> p x hh f", x=2))
            sq.set_floor()

            of_r = of_h.rearrange("(c p) t -> p c t", p=128)
            prev = {}
            prev_ost = None
            for i2 in range(S // 128):
                tsl = slice(i2 * 128, (i2 + 1) * 128)
                for h in range(16):
                    scn, j = divmod(h, 4)
                    if j == 0:
                        def amm(t, scn=scn, tsl=tsl):
                            ins = None
                            for jj in range(4):
                                ins = t.matmul(Aps[:, jj, 0:128], kslot[32 * jj:32 * (jj + 1), scn, tsl],
                                               qslot[32 * jj:32 * (jj + 1), scn, tsl], start=True, stop=True,
                                               tile_position=(32 * jj, 0))
                            return ins
                        adep = []
                        if ("sqdone",) in prev:
                            adep.append(prev[("sqdone",)])
                        prev[("amm", scn)] = sq.emit("t", amm, after=adep)
                    o_amm = prev[("amm", scn)]

                    o_sq = sq.emit("a", lambda a, j=j: a.activation(out=A2t[:], in_=Aps[:, j, 0:128], func=AF.Square),
                                   after=[o_amm] + ([prev[("a2used",)]] if ("a2used",) in prev else []))
                    o_msk = sq.emit("v", lambda v: v.tensor_tensor(out=A2t[:], in0=A2t[:], in1=trim[:], op=ALU.mult),
                                    after=[o_sq])
                    prev[("sqdone",)] = o_sq

                    hh = h % 8
                    xsel = 0 if h < 8 else 1
                    phis = []
                    for half in range(2):
                        def smm(t, hh=hh, half=half, xsel=xsel, tsl=tsl):
                            t.matmul(selps[:, 0], esel[:, half, hh, :], qfm[:, xsel, tsl], start=True, stop=True)
                            return t.matmul(selps[:, 1], esel[:, 2 + half, hh, :], qfm[:, xsel, tsl], start=True, stop=True)
                        sdep = []
                        if ("phimul",) in prev:
                            sdep.append(prev[("phimul",)])
                        if ("oused",) in prev:
                            sdep.append(prev[("oused",)])
                        o_sm = sq.emit("t", smm, after=sdep)
                        o_a1 = sq.emit("a", lambda a: a.copy(out=a1s[:], in_=selps[:, 0]),
                                       after=[o_sm] + ([prev[("phimul",)]] if ("phimul",) in prev else []))
                        phdep = [o_a1]
                        if ("oused",) in prev:
                            phdep.append(prev[("oused",)])
                        o_ph = sq.emit("v", lambda v, half=half: v.tensor_tensor(out=phiq[:, half], in0=selps[:, 1], in1=a1s[:], op=ALU.mult),
                                       after=phdep)
                        prev[("phimul",)] = o_ph
                        phis.append(o_ph)

                    def phik_op(v, h=h, i2=i2):
                        b1 = ktm[:, i2, 16 * h:16 * h + 16].unsqueeze(1).broadcast_to((128, 16, 16))
                        b2 = ktm[:, i2, 16 * h:16 * h + 16].unsqueeze(2).broadcast_to((128, 16, 16))
                        return v.tensor_tensor(out=phik[:].rearrange("p (a b) -> p a b", a=16), in0=b2, in1=b1, op=ALU.mult)
                    o_pk = sq.emit("v", phik_op, after=[prev[("ups",)]] if ("ups",) in prev else [])

                    def omm(t, h=h, i2=i2):
                        t.matmul(ops_[:, 0:65], phiq[:, 0], Sbf[:, h, 0, :], start=True, stop=False)
                        t.matmul(ops_[:, 0:65], phiq[:, 1], Sbf[:, h, 1, :], start=False, stop=False)
                        return t.matmul(ops_[:, 0:65], A2t[:], vtm2[:, i2, h, :], start=False, stop=True)
                    if debug and i2 == 0 and h == 0:
                        d_p = sq.emit("a", lambda a: a.copy(out=dbgb0[:, 0:130], in_=Sbf[:, 0].rearrange("p a b -> p (a b)")), after=phis)
                        sq.dma("dg", dbg_h[7, :, 0:130], dbgb0[:, 0:130], after=[d_p])
                    odep = phis + [o_msk]
                    if ("onrm",) in prev:
                        odep.append(prev[("onrm",)])
                    if ("supd_h", h) in prev:
                        odep.append(prev[("supd_h", h)])
                    if debug and i2 == 1 and h == 0:
                        d_p2 = sq.emit("a", lambda a: a.copy(out=dbgb0[:, 0:130], in_=Sbf[:, 0].rearrange("p a b -> p (a b)")), after=odep)
                        d_p2d = sq.dma("dg", dbg_h[4, :, 0:130], dbgb0[:, 0:130], after=[d_p2])
                        odep = odep + [d_p2]
                    o_om = sq.emit("t", omm, after=odep)

                    def umm(t, h=h, i2=i2):
                        t.matmul(ups[:, 0, 0:65], phik[:, 0:128], vtm2[:, i2, h, :], start=True, stop=True)
                        return t.matmul(ups[:, 1, 0:65], phik[:, 128:256], vtm2[:, i2, h, :], start=True, stop=True)
                    udep = [o_pk, o_om]
                    if ("supd",) in prev:
                        udep.append(prev[("supd",)])
                    o_um = sq.emit("t", umm, after=udep)
                    prev[("ups",)] = o_um

                    def supd(v, h=h):
                        v.tensor_tensor(out=Sst[:, h, 0, :], in0=Sst[:, h, 0, :], in1=ups[:, 0, 0:65], op=ALU.add)
                        v.tensor_tensor(out=Sst[:, h, 1, :], in0=Sst[:, h, 1, :], in1=ups[:, 1, 0:65], op=ALU.add)
                        v.tensor_copy(out=Sbf[:, h, 0, :], in_=Sst[:, h, 0, :])
                        return v.tensor_copy(out=Sbf[:, h, 1, :], in_=Sst[:, h, 1, :])
                    o_su = sq.emit("v", supd, after=[o_um])
                    prev[("supd",)] = o_su
                    prev[("supd_h", h)] = o_su

                    o_na0 = sq.emit("v", lambda v: v.tensor_scalar(out=ebuf[:, 0:1], in0=ops_[:, 64:65], scalar1=1e-5,
                                                                   scalar2=None, op0=ALU.add), after=[o_om], hazard=True)
                    o_na = sq.emit("v", lambda v: v.reciprocal(out=ebuf[:, 1:2], in_=ebuf[:, 0:1]),
                                   after=[o_na0], hazard=True)

                    def nrm_b(v, h=h):
                        return v.tensor_scalar(out=obuf[:, h], in0=ops_[:, 0:64], scalar1=ebuf[:, 1:2],
                                               scalar2=None, op0=ALU.mult)
                    ndep = [o_na]
                    if ("trlast",) in prev:
                        ndep.append(prev[("trlast",)])
                    o_nb = sq.emit("v", nrm_b, after=ndep, hazard=True)
                    prev[("onrm",)] = o_nb
                    prev[("a2used",)] = o_om
                    prev[("oused",)] = o_om
                    prev[("obh", h)] = o_nb
                    if debug and i2 == 1 and h == 0:
                        d_e2 = sq.emit("a", lambda a: a.copy(out=dbgb[:, 65:130], in_=ops_[:, 0:65]), after=[o_om, o_nb])
                        d_e2d = sq.dma("dg", dbg_h[5, :, 0:65], dbgb[:, 65:130], after=[d_e2])
                        prev[("onrm",)] = sq.emit("v", lambda v: v.nop(), after=[d_e2d, o_nb])
                        prev[("a2used",)] = prev[("onrm",)]
                        prev[("oused",)] = prev[("onrm",)]
                    if debug and i2 == 0 and h == 0:
                        sq.dma("dg", dbg_h[0, :, 0:128], A2t[:], after=[o_om])
                        sq.dma("dg", dbg_h[1, :, 0:256], phiq[:].rearrange("p a b -> p (a b)"), after=[o_om])
                        sq.dma("dg", dbg_h[2, :, 0:256], phik[:], after=[o_um])
                        d_e = sq.emit("a", lambda a: a.copy(out=dbgb[:, 0:65], in_=ops_[:, 0:65]), after=[o_om, o_nb])
                        d_ed = sq.dma("dg", dbg_h[3, :, 0:65], dbgb[:, 0:65], after=[d_e])
                        d_q = sq.emit("a", lambda a: a.copy(out=dbgb[:, 65:130], in_=vtm2[:, 0, 0, :]), after=[o_om])
                        d_qd = sq.dma("dg", dbg_h[6, :, 0:65], dbgb[:, 65:130], after=[d_q])
                        d_q2 = sq.emit("a", lambda a: a.copy(out=dbgb[:, 130:256], in_=vtm2[:].rearrange("p a b c -> p (a b c)")[:, 0:126]), after=[o_om])
                        sq.dma("dg", dbg_h[5, :, 0:126], dbgb[:, 130:256], after=[d_q2])
                        prev[("onrm",)] = sq.emit("v", lambda v: v.nop(), after=[d_ed, d_qd, o_nb])
                        prev[("a2used",)] = prev[("onrm",)]
                        prev[("oused",)] = prev[("onrm",)]

                tr_evs = []
                for dc in range(NCH):
                    def trmm(t, dc=dc):
                        return t.transpose(trps[:], obuf[:].rearrange("p a b -> p (a b)")[:, dc * 128:(dc + 1) * 128], iden[:])
                    tdep = [prev[("obh", h2)] for h2 in (2 * dc, 2 * dc + 1)]
                    if tr_evs:
                        tdep.append(tr_evs[-1])
                    o_tr = sq.emit("t", trmm, after=tdep)

                    def trev(a, dc=dc):
                        return a.copy(out=otr[:, dc], in_=trps[:])
                    edep = [o_tr]
                    if prev_ost is not None:
                        edep.append(prev_ost)
                    o_te = sq.emit("a", trev, after=edep)
                    tr_evs.append(o_te)
                prev[("trlast",)] = tr_evs[-1]
                prev_ost = sq.dma("dg", of_r[:, :, tsl], otr[:], after=[tr_evs[-1]])
            sq.set_floor()

        if upto <= 2:
            sq.finalize(block, sems)
            return nc

        with ExitStack() as ph:
            psb = lambda *a: ph.enter_context(nc.sbuf_tensor(*a))
            ppt = lambda *a: ph.enter_context(nc.psum_tensor(*a))
            wo_sb = psb("wo_sb", [128, NCH, D], BF16)
            sq.dma("dg", wo_sb[:], wo.rearrange("(c p) f -> p c f", p=128))
            sq.set_floor()
            oin = psb("oin", [128, 2, NCH, TT], BF16)
            so_t = psb("so_t", [128, NCH, TT], BF16)
            sps = ppt("sps", [128, 2, TT], F32)
            s_r = s_hbm.rearrange("(c p) t -> p c t", p=128)
            of_r = of_h.rearrange("(c p) t -> p c t", p=128)
            prev_ev = [None, None]
            prev_st = None
            ld_user = [None, None]
            for i in range(NTT):
                dep = [] if ld_user[i % 2] is None else [ld_user[i % 2]]
                ld = sq.dma("ds", oin[:, i % 2], of_r[:, :, i * TT:(i + 1) * TT], after=dep)
                last_ev = None
                for cc in range(NCH):
                    pb = (i * NCH + cc) % 2

                    def proj(t, cc=cc, i=i, pb=pb):
                        ins = None
                        for dc in range(NCH):
                            ins = t.matmul(sps[:, pb], wo_sb[:, dc, cc * 128:(cc + 1) * 128],
                                           oin[:, i % 2, dc], start=(dc == 0), stop=(dc == NCH - 1))
                        return ins
                    pdep = [ld]
                    if prev_ev[pb] is not None:
                        pdep.append(prev_ev[pb])
                    o_p = sq.emit("t", proj, after=pdep)

                    def evac(a, cc=cc, pb=pb):
                        return a.copy(out=so_t[:, cc], in_=sps[:, pb])
                    edep = [o_p] + ([prev_st] if prev_st is not None else [])
                    o_e = sq.emit("a", evac, after=edep)
                    prev_ev[pb] = o_e
                    last_ev = o_e
                ld_user[i % 2] = last_ev
                prev_st = sq.dma("dg", s_r[:, :, i * TT:(i + 1) * TT], so_t[:], after=[last_ev])
            sq.set_floor()

        if upto <= 3:
            sq.finalize(block, sems)
            return nc

        with ExitStack() as ph:
            psb = lambda *a: ph.enter_context(nc.sbuf_tensor(*a))
            ppt = lambda *a: ph.enter_context(nc.psum_tensor(*a))
            murs = psb("murs", [128, 4], F32)
            sq.dma("ds", murs[:, 0:1], murf[:, 0:1]) if False else None
            mur = psb("mur", [128, 2, 8], F32)
            mus = psb("mus", [128, 4, 8], F32)
            sq.dma("ds", mus[:, 0, :], murf[:, :])
            sq.dma("ds", mus[:, 1, :], mukf[:, :])

            def muprep(v):
                v.tensor_scalar(out=mus[:, 2, :], in0=mus[:, 0, :], scalar1=-1.0, scalar2=1.0, op0=ALU.mult, op1=ALU.add)
                return v.tensor_scalar(out=mus[:, 3, :], in0=mus[:, 1, :], scalar1=-1.0, scalar2=1.0, op0=ALU.mult, op1=ALU.add)
            sq.emit("v", muprep, after=[])
            wr_sb = psb("wr_sb", [128, NCH, D], BF16)
            wk_sb = psb("wk_sb", [128, NCH, D], BF16)
            wv_sb = psb("wv_sb", [128, NCH, D], BF16)
            sq.dma("dg", wr_sb[:], wrT.rearrange("(c p) f -> p c f", p=128))
            sq.dma("dg", wk_sb[:], wkT2.rearrange("(c p) f -> p c f", p=128))
            sq.dma("dg", wv_sb[:], wvT2.rearrange("(c p) f -> p c f", p=128))
            sq.set_floor()

            xng = psb("xng", [128, 2, NCH, TT + 1], BF16)
            xrb = psb("xrb", [128, NCH, TT], BF16)
            xkb = psb("xkb", [128, NCH, TT], BF16)
            mixt = psb("mixt", [128, TT], F32)
            rbuf = psb("rbuf", [128, NCH, TT], BF16)
            kab = psb("kab", [128, NCH, TT], BF16)
            gb = psb("gb", [128, TT], F32)
            xres = psb("xres", [128, 2, NCH, TT], F32)
            sres = psb("sres", [128, 2, NCH, TT], BF16)
            onew = psb("onew", [128, NCH, TT], F32)
            rps = ppt("rps", [128, 2, TT], F32)
            kps = ppt("kps", [128, 2, TT], F32)
            vps2 = ppt("vps2", [128, 2, TT], F32)
            xn_r = xn_h.rearrange("(c p) t -> p c t", p=128)
            xi_r = x_in.rearrange("(c p) t -> p c t", p=128)
            xo_r = x_out.rearrange("(c p) t -> p c t", p=128)
            s_r = s_hbm.rearrange("(c p) t -> p c t", p=128)

            prev_mix = None
            prev_ke2 = None
            prev_rev = [None, None]
            prev_kev = [None, None]
            prev_vev = [None, None]
            prev_gmul = None
            prev_xst = None
            xld_user = [None, None]
            out_stores = []
            for i in range(NTT):
                dep = [] if xld_user[i % 2] is None else [xld_user[i % 2]]
                if i == 0:
                    o_lxn = sq.dma("ds", xng[:, 0, :, 1:TT + 1], xn_r[:, :, 0:TT], after=dep)
                    o_g0 = sq.emit("v", lambda v: v.memset(xng[:, 0, :, 0:1], 0.0), after=dep)
                    o_lxn_all = [o_lxn, o_g0]
                else:
                    o_lxn = sq.dma("ds", xng[:, i % 2, :, 0:TT + 1], xn_r[:, :, i * TT - 1:(i + 1) * TT], after=dep)
                    o_lxn_all = [o_lxn]
                o_lx = sq.dma("ds", xres[:, i % 2], xi_r[:, :, i * TT:(i + 1) * TT], after=dep)
                o_ls = sq.dma("ds", sres[:, i % 2], s_r[:, :, i * TT:(i + 1) * TT], after=dep)

                def mix(v, i=i):
                    ins = None
                    for dc in range(NCH):
                        cur = xng[:, i % 2, dc, 1:TT + 1]
                        prv = xng[:, i % 2, dc, 0:TT]
                        v.tensor_scalar(out=mixt[:], in0=prv, scalar1=mus[:, 2, dc:dc + 1], scalar2=None, op0=ALU.mult)
                        v.scalar_tensor_tensor(out=xrb[:, dc], in0=cur, scalar=mus[:, 0, dc:dc + 1], in1=mixt[:], op0=ALU.mult, op1=ALU.add)
                        v.tensor_scalar(out=mixt[:], in0=prv, scalar1=mus[:, 3, dc:dc + 1], scalar2=None, op0=ALU.mult)
                        ins = v.scalar_tensor_tensor(out=xkb[:, dc], in0=cur, scalar=mus[:, 1, dc:dc + 1], in1=mixt[:], op0=ALU.mult, op1=ALU.add)
                    return ins
                mdep = o_lxn_all + ([prev_mix] if prev_mix else [])
                o_mix = sq.emit("v", mix, after=mdep)

                res_ops = []
                for cc in range(NCH):
                    pb = (i * NCH + cc) % 2

                    def rmm(t, cc=cc, pb=pb):
                        ins = None
                        for dc in range(NCH):
                            ins = t.matmul(rps[:, pb], wr_sb[:, dc, cc * 128:(cc + 1) * 128], xrb[:, dc],
                                           start=(dc == 0), stop=(dc == NCH - 1))
                        return ins
                    rdep = [o_mix]
                    if prev_rev[pb] is not None:
                        rdep.append(prev_rev[pb])
                    o_rm = sq.emit("t", rmm, after=rdep)

                    def rev(a, cc=cc, pb=pb):
                        return a.activation(out=rbuf[:, cc], in_=rps[:, pb], func=AF.Sigmoid)
                    o_re = sq.emit("a", rev, after=[o_rm])
                    prev_rev[pb] = o_re

                    def kmm(t, cc=cc, pb=pb):
                        ins = None
                        for dc in range(NCH):
                            ins = t.matmul(kps[:, pb], wk_sb[:, dc, cc * 128:(cc + 1) * 128], xkb[:, dc],
                                           start=(dc == 0), stop=(dc == NCH - 1))
                        return ins
                    kdep = [o_mix]
                    if prev_kev[pb] is not None:
                        kdep.append(prev_kev[pb])
                    o_km = sq.emit("t", kmm, after=kdep)

                    def kev(v, cc=cc, pb=pb):
                        return v.tensor_scalar(out=mixt[:], in0=kps[:, pb], scalar1=0.0, scalar2=None, op0=ALU.max)
                    kedep = [o_km, o_mix]
                    if prev_ke2 is not None:
                        kedep.append(prev_ke2)
                    o_ke = sq.emit("v", kev, after=kedep)

                    def kev2(a, cc=cc):
                        return a.activation(out=kab[:, cc], in_=mixt[:], func=AF.Square)
                    o_ke2 = sq.emit("a", kev2, after=[o_ke])
                    prev_kev[pb] = o_ke2
                    prev_ke2 = o_ke2

                kv_evs = []
                for cc in range(NCH):
                    pb = (i * NCH + cc) % 2

                    def vmm(t, cc=cc, pb=pb):
                        ins = None
                        for dc in range(NCH):
                            ins = t.matmul(vps2[:, pb], wv_sb[:, dc, cc * 128:(cc + 1) * 128], kab[:, dc],
                                           start=(dc == 0), stop=(dc == NCH - 1))
                        return ins
                    vdep = [d for d in (prev_kev[0], prev_kev[1]) if d is not None]
                    if prev_vev[pb] is not None:
                        vdep.append(prev_vev[pb])
                    o_vm = sq.emit("t", vmm, after=vdep)

                    def gmul(v, cc=cc, pb=pb):
                        return v.tensor_tensor(out=gb[:], in0=rbuf[:, cc], in1=vps2[:, pb], op=ALU.mult)
                    gdep = [o_vm, prev_rev[0], prev_rev[1]]
                    if prev_gmul is not None:
                        gdep.append(prev_gmul)
                    o_gm = sq.emit("v", gmul, after=gdep)
                    prev_vev[pb] = o_gm

                    def gsil(a, cc=cc):
                        return a.activation(out=gb[:], in_=gb[:], func=AF.Silu)
                    o_gs = sq.emit("a", gsil, after=[o_gm])

                    def resid(v, cc=cc, i=i):
                        v.tensor_tensor(out=onew[:, cc], in0=sres[:, i % 2, cc], in1=gb[:], op=ALU.mult)
                        return v.tensor_tensor(out=onew[:, cc], in0=onew[:, cc], in1=xres[:, i % 2, cc], op=ALU.add)
                    rdep2 = [o_gs, o_lx, o_ls]
                    if prev_xst is not None:
                        rdep2.append(prev_xst)
                    o_r = sq.emit("v", resid, after=rdep2)
                    prev_gmul = o_r
                    res_ops.append(o_r)
                prev_mix = res_ops[-1]
                xld_user[i % 2] = res_ops[-1]
                prev_xst = sq.dma("dg", xo_r[:, :, i * TT:(i + 1) * TT], onew[:], after=res_ops)
                out_stores.append(prev_xst)

            if ctx is None:
                sq.finalize(block, sems, outputs_done=out_stores)
            else:
                ctx["out_stores"] = out_stores
                sq.set_floor()
    return nc


def l1_inmap(xcore_fm, I):
    f = 16
    E1 = np.zeros((16, 128, 128), np.float32)
    E2 = np.zeros((16, 128, 128), np.float32)
    for hh in range(8):
        for half in range(2):
            for jj in range(8):
                for i in range(16):
                    col = jj * 16 + i
                    E1[half * 8 + hh, 16 * hh + i, col] = 1.0
                    E2[half * 8 + hh, 16 * hh + half * 8 + jj, col] = 1.0
    trimask = np.triu(np.ones((128, 128), np.float32))
    return {
        "x_in": xcore_fm,
        "wq": np.ascontiguousarray(I["L1_attn_Wq"]),
        "wk": np.ascontiguousarray(I["L1_attn_Wk"]),
        "wv": np.ascontiguousarray(I["L1_attn_Wv"]),
        "wo": np.ascontiguousarray(I["L1_attn_Wo"]),
        "wrT": np.ascontiguousarray(I["L1_ffn_W_r"].T),
        "wkT2": np.ascontiguousarray(I["L1_ffn_W_k"].T),
        "wvT2": np.ascontiguousarray(I["L1_ffn_W_v"].T),
        "murf": fm(I["L1_ffn_mu_r"]),
        "mukf": fm(I["L1_ffn_mu_k"]),
        "e1m": E1.astype(BF),
        "e2m": E2.astype(BF),
        "trimask": trimask.astype(BF),
        "iden_h": np.eye(128, dtype=np.float32).astype(BF),
        "ones128_h": np.ones((128, 1), np.float32),
        "ones1_h": np.ones((1, 128), np.float32),
    }




def build_l0(debug=False, upto=99, ctx=None):
    from contextlib import ExitStack
    nc = bass.Bass("TRN2", target_bir_lowering=False, debug=False) if ctx is None else ctx["nc"]
    x_in = nc.dram_tensor("x_in", [D, S], F32, kind="ExternalInput").ap()
    wqkvT = nc.dram_tensor("wqkvT", [D, 3 * D], F32, kind="ExternalInput").ap()
    bqkvf = nc.dram_tensor("bqkvf", [128, 16], F32, kind="ExternalInput").ap()
    bvrow = nc.dram_tensor("bvrow", [1, D], F32, kind="ExternalInput").ap()
    wqaT = nc.dram_tensor("wqaT", [D, D], F32, kind="ExternalInput").ap()
    bqaf = nc.dram_tensor("bqaf", [128, 8], F32, kind="ExternalInput").ap()
    wkaT = nc.dram_tensor("wkaT", [D, D], F32, kind="ExternalInput").ap()
    bkaf = nc.dram_tensor("bkaf", [128, 8], F32, kind="ExternalInput").ap()
    wdT = nc.dram_tensor("wdT", [D, D], F32, kind="ExternalInput").ap()
    bdf = nc.dram_tensor("bdf", [128, 8], F32, kind="ExternalInput").ap()
    g1h = nc.dram_tensor("g1h", [2, 4, 32, 32], F32, kind="ExternalInput").ap()
    g2h = nc.dram_tensor("g2h", [2, 4, 16, 16], F32, kind="ExternalInput").ap()
    brf = nc.dram_tensor("brf", [128, 8], F32, kind="ExternalInput").ap()
    seld_h = nc.dram_tensor("seld_h", [32, 512], BF16, kind="ExternalInput").ap()
    selg_h = nc.dram_tensor("selg_h", [16, 128], BF16, kind="ExternalInput").ap()
    cosT_h = nc.dram_tensor("cosT_h", [128, S], BF16, kind="ExternalInput").ap()
    sinT_h = nc.dram_tensor("sinT_h", [128, S], BF16, kind="ExternalInput").ap()
    wmaskd_h = nc.dram_tensor("wmaskd_h", [128, 128], BF16, kind="ExternalInput").ap()
    wmasks_h = nc.dram_tensor("wmasks_h", [128, 128], BF16, kind="ExternalInput").ap()
    cmask_h = nc.dram_tensor("cmask_h", [128, 4, 512], BF16, kind="ExternalInput").ap()
    iden_h = nc.dram_tensor("iden_h", [128, 128], BF16, kind="ExternalInput").ap()
    onescol_h = nc.dram_tensor("onescol_h", [1, 128], BF16, kind="ExternalInput").ap()
    ones128_h = nc.dram_tensor("ones128_h", [128, 1], F32, kind="ExternalInput").ap()
    ones1_h = nc.dram_tensor("ones1_h", [1, 128], F32, kind="ExternalInput").ap()
    onesf_h = nc.dram_tensor("onesf_h", [128, 128], F32, kind="ExternalInput").ap()
    x_out = nc.dram_tensor("x_out", [D, S], F32, kind="ExternalOutput").ap()

    kind = "ExternalOutput" if debug else "Internal"
    xn_h = nc.dram_tensor("xn_h", [D, S], BF16, kind=kind).ap()
    qf_h = nc.dram_tensor("qf_h", [D, S], BF16, kind=kind).ap()
    kf_h = nc.dram_tensor("kf_h", [D, S], BF16, kind=kind).ap()
    vt_h = nc.dram_tensor("vt_h", [S, 16 * 65], BF16, kind=kind).ap()
    at_h = nc.dram_tensor("at_h", [S, 16 * 65], BF16, kind=kind).ap()
    af_h = nc.dram_tensor("af_h", [D, S], BF16, kind=kind).ap()
    qaf_h = nc.dram_tensor("qaf_h", [D, S], BF16, kind=kind).ap()
    kaf_h = nc.dram_tensor("kaf_h", [D, S], BF16, kind=kind).ap()
    s_hbm = nc.dram_tensor("s_hbm", [D, S], BF16, kind=kind).ap()

    stack = ExitStack()
    with stack:
        if ctx is None:
            sems = alloc_sems(nc, stack)
            block = stack.enter_context(nc.Block())
            sq = Seq(nc)
        else:
            sems, block, sq = ctx["sems"], ctx["block"], ctx["sq"]
        sb = lambda *a: stack.enter_context(nc.sbuf_tensor(*a))

        ones128 = sb("c_ones128", [128, 1], F32)
        ones1 = sb("c_ones1", [1, 128], F32)
        onescol = sb("c_onescol", [1, 128], BF16)
        iden = sb("c_iden", [128, 128], BF16)
        cst = sb("cst", [128, 2], F32)

        sq.dma("ds", ones128[:], ones128_h[:, :])
        sq.dma("ds", ones1[:], ones1_h[:, :])
        sq.dma("ds", onescol[:], onescol_h[:, :])
        sq.dma("ds", iden[:], iden_h[:, :])
        sq.emit("v", lambda v: v.memset(cst[:, 0:1], 1e-5))
        sq.set_floor()

        phAB = ExitStack()
        xn = phAB.enter_context(nc.sbuf_tensor("xn", [128, NCH, S], BF16))
        with ExitStack() as ph:
            ln_done = emit_ln(sq, nc, ph, x_in, xn, ones128, ones1, cst, col_off=0)
            xn_r = xn_h.rearrange("(c p) t -> p c t", p=128)
            for i in range(NTT):
                sq.dma("dg", xn_r[:, :, i * TT:(i + 1) * TT], xn[:, :, i * TT:(i + 1) * TT], after=[ln_done[i]])
            sq.set_floor()

        if upto <= 0:
            phAB.close()
            sq.finalize(block, sems)
            return nc

        qf_r = qf_h.rearrange("(c p) t -> p c t", p=128)
        kf_r = kf_h.rearrange("(c p) t -> p c t", p=128)
        wq_r = wqkvT.rearrange("(c p) f -> p c f", p=128)
        with ExitStack() as ph:
            psb = lambda *a: ph.enter_context(nc.sbuf_tensor(*a))
            ppt = lambda *a: ph.enter_context(nc.psum_tensor(*a))
            cosT = psb("cosT", [128, S], BF16)
            sinT = psb("sinT", [128, S], BF16)
            sq.dma("ds", cosT[:], cosT_h[:, :])
            sq.dma("ds", sinT[:], sinT_h[:, :])
            wt = psb("wt", [128, 2, NCH, 128], BF16)
            fullch = psb("fullch", [128, 2, S], BF16)
            ropesc = psb("ropesc", [128, 2, S], BF16)
            bvs = psb("bvs", [1, D], BF16)
            vtm = psb("vtm", [128, 2, 16, 65], BF16)
            qps = ppt("qps", [128, 2, TT], F32)
            vps = ppt("vps", [128, 2, 512], F32)
            sq.dma("dg", bvs[:], bvrow[:, :])
            bqs = psb("bqs", [128, 16], F32)
            sq.dma("ds", bqs[:], bqkvf[:, :])
            sq.set_floor()

            prev_ev = [None, None]
            prev_chst = [None, None]
            for fc in ([] if L0SUB in (-1, -3) else range(16)):
                wb = fc % 2
                cb = fc % 2
                wdep = [sq.last["t"]] if sq.last.get("t") is not None else []
                o_lw = sq.dma("dg", wt[:, wb], wq_r[:, :, fc * 128:(fc + 1) * 128], after=wdep)
                evs = []
                for i in range(NTT):
                    pb = (fc * NTT + i) % 2

                    def mm(t, wb=wb, i=i, pb=pb):
                        ins = None
                        for dc in range(NCH):
                            ins = t.matmul(qps[:, pb], wt[:, wb, dc], xn[:, dc, i * TT:(i + 1) * TT],
                                           start=(dc == 0), stop=(dc == NCH - 1))
                        return ins
                    mdep = [o_lw, ln_done[i]]
                    if prev_ev[pb] is not None:
                        mdep.append(prev_ev[pb])
                    o_mm = sq.emit("t", mm, after=mdep)

                    def ev(a, fc=fc, i=i, pb=pb, cb=cb):
                        return a.activation(out=fullch[:, cb, i * TT:(i + 1) * TT], in_=qps[:, pb],
                                            func=AF.Identity, bias=bqs[:, fc:fc + 1])
                    edep = [o_mm]
                    if prev_chst[cb] is not None:
                        edep.append(prev_chst[cb])
                    o_ev = sq.emit("a", ev, after=edep)
                    prev_ev[pb] = o_ev
                    evs.append(o_ev)
                if L0SUB < 1:
                    dst = qf_r[:, fc, :] if fc < 8 else kf_r[:, fc - 8, :]
                    prev_chst[cb] = sq.dma("dg", dst, fullch[:, cb, :], after=evs)
                    continue
                o_g1 = sq.dma("ds", ropesc[0:8, cb, :], fullch[8:16, cb, :], after=evs)
                o_g2 = sq.dma("ds", ropesc[8:16, cb, :], fullch[0:8, cb, :], after=evs)
                o_g3 = sq.dma("ds", ropesc[64:72, cb, :], fullch[72:80, cb, :], after=evs)
                o_g4 = sq.dma("ds", ropesc[72:80, cb, :], fullch[64:72, cb, :], after=evs)

                def rope(v, cb=cb):
                    for b0 in (0, 64):
                        v.tensor_tensor(out=ropesc[b0:b0 + 16, cb, :], in0=ropesc[b0:b0 + 16, cb, :],
                                        in1=sinT[b0:b0 + 16, :], op=ALU.mult)
                        v.tensor_tensor(out=fullch[b0:b0 + 16, cb, :], in0=fullch[b0:b0 + 16, cb, :],
                                        in1=cosT[b0:b0 + 16, :], op=ALU.mult)
                    ins = None
                    for b0 in (0, 64):
                        ins = v.tensor_tensor(out=fullch[b0:b0 + 16, cb, :], in0=fullch[b0:b0 + 16, cb, :],
                                              in1=ropesc[b0:b0 + 16, cb, :], op=ALU.add)
                    return ins
                o_rope = sq.emit("v", rope, after=[o_g1, o_g2, o_g3, o_g4])
                dst = qf_r[:, fc, :] if fc < 8 else kf_r[:, fc - 8, :]
                prev_chst[cb] = sq.dma("dg", dst, fullch[:, cb, :], after=[o_rope])

            def vones(v):
                ins = None
                for hb in range(2):
                    for h in range(16):
                        ins = v.memset(vtm[:, hb, h, 64:65], 1.0)
                return ins
            o_vone = sq.emit("v", vones)
            vw = psb("vw", [128, NCH, D], BF16)
            o_vws = []
            for dc in range(NCH):
                o_vws.append(sq.dma("dg", vw[:, dc, :], wqkvT[dc * 128:(dc + 1) * 128, 2048:3072],
                                    after=[sq.last["t"]] if sq.last.get("t") is not None else []))
            o_vw = o_vws[-1]
            prevv = [None, None]
            prev_vev = None
            for i2 in ([] if L0SUB == -2 else range(1) if L0SUB == -3 else range(S // 128)):
                hb = i2 % 2

                def vmm(t, i2=i2):
                    ins = None
                    for half in range(2):
                        if L0SUB >= 2:
                            t.matmul(vps[:, half], onescol[:, 0:128], bvs[0:1, half * 512:(half + 1) * 512],
                                     start=True, stop=False)
                        for dc in range(NCH):
                            ins = t.matmul(vps[:, half], xn[:, dc, i2 * 128:(i2 + 1) * 128],
                                           vw[:, dc, half * 512:(half + 1) * 512],
                                           start=(dc == 0 and L0SUB < 2), stop=(dc == NCH - 1))
                    return ins
                vdep = o_vws + [ln_done[min(i2 // 4, NTT - 1)]]
                if prevv[hb] is not None:
                    vdep.append(prevv[hb])
                if prev_vev is not None:
                    vdep.append(prev_vev)
                o_vm = sq.emit("t", vmm, after=vdep)

                def vev(a, hb=hb):
                    a.copy(out=vtm[:, hb, 0:8, 0:64], in_=vps[:, 0])
                    return a.copy(out=vtm[:, hb, 8:16, 0:64], in_=vps[:, 1])
                o_ve = sq.emit("a", vev, after=[o_vm, o_vone] + ([prevv[hb]] if prevv[hb] else []))
                prev_vev = o_ve
                prevv[hb] = sq.dma("dg", vt_h[i2 * 128:(i2 + 1) * 128, :], vtm[:, hb], after=[o_ve])
            sq.set_floor()
        phAB.close()

        if upto <= 1:
            sq.finalize(block, sems)
            return nc

        with ExitStack() as ph:
            psb = lambda *a: ph.enter_context(nc.sbuf_tensor(*a))
            ppt = lambda *a: ph.enter_context(nc.psum_tensor(*a))
            wmd = psb("wmd", [128, 128], BF16)
            wms = psb("wms", [128, 128], BF16)
            sq.dma("ds", wmd[:], wmaskd_h[:, :])
            sq.dma("ds", wms[:], wmasks_h[:, :])
            qsl = psb("qsl", [128, NCH, 128], BF16)
            ksl = psb("ksl", [128, 2, NCH, 128], BF16)
            vsl = psb("vsl", [128, 2, 16, 65], BF16)
            pt_d = psb("pt_d", [128, 2, 128], BF16)
            pt_s = psb("pt_s", [128, 2, 128], BF16)
            ebuf = psb("ebuf", [128, 4], F32)
            att = psb("att", [128, 16, 65], BF16)
            aft = psb("aft", [128, NCH, 128], BF16)
            sps_ = ppt("sps_", [128, 4, 512], F32)
            aps_ = ppt("aps_", [128, 2, 512], F32)
            tps_ = ppt("tps_", [128, 128], BF16)
            sq.set_floor()

            prev = {}
            prev_atst = None
            prev_afst = None
            for i2 in range(S // 128):
                tsl = slice(i2 * 128, (i2 + 1) * 128)
                hb = i2 % 2
                qdep = [sq.last["t"]] if sq.last.get("t") is not None else []
                o_lq = sq.dma("ds", qsl[:], qf_r[:, :, tsl], after=qdep)
                o_lk = sq.dma("ds", ksl[:, hb], kf_r[:, :, tsl], after=qdep)
                o_lv = sq.dma("ds", vsl[:, hb], vt_h[tsl, :], after=qdep)
                for hp in range(8):
                    def smm(t, hp=hp, hb=hb, i2=i2):
                        ins = None
                        for hh in range(2):
                            b0 = 64 * hh
                            ins = t.matmul(sps_[:, hh, 0:128], ksl[b0:b0 + 64, hb, hp, :],
                                           qsl[b0:b0 + 64, hp, :], start=True, stop=True,
                                           tile_position=(b0, 0))
                        if i2 > 0:
                            for hh in range(2):
                                b0 = 64 * hh
                                ins = t.matmul(sps_[:, 2 + hh, 0:128], ksl[b0:b0 + 64, 1 - hb, hp, :],
                                               qsl[b0:b0 + 64, hp, :], start=True, stop=True,
                                               tile_position=(b0, 0))
                        return ins
                    sdep = [o_lq, o_lk]
                    if ("expdone",) in prev:
                        sdep.append(prev[("expdone",)])
                    o_sm = sq.emit("t", smm, after=sdep)

                    def exps(a, i2=i2):
                        a.activation(out=pt_d[:, 0], in_=sps_[:, 0, 0:128], func=AF.Exp, scale=0.125)
                        ins = a.activation(out=pt_d[:, 1], in_=sps_[:, 1, 0:128], func=AF.Exp, scale=0.125)
                        if i2 > 0:
                            a.activation(out=pt_s[:, 0], in_=sps_[:, 2, 0:128], func=AF.Exp, scale=0.125)
                            ins = a.activation(out=pt_s[:, 1], in_=sps_[:, 3, 0:128], func=AF.Exp, scale=0.125)
                        return ins
                    edep = [o_sm]
                    if ("mskdone",) in prev:
                        edep.append(prev[("mskdone",)])
                    o_ex = sq.emit("a", exps, after=edep)
                    prev[("expdone",)] = o_ex

                    def msk(v, i2=i2):
                        v.tensor_tensor(out=pt_d[:, 0], in0=pt_d[:, 0], in1=wmd[:], op=ALU.mult)
                        ins = v.tensor_tensor(out=pt_d[:, 1], in0=pt_d[:, 1], in1=wmd[:], op=ALU.mult)
                        if i2 > 0:
                            v.tensor_tensor(out=pt_s[:, 0], in0=pt_s[:, 0], in1=wms[:], op=ALU.mult)
                            ins = v.tensor_tensor(out=pt_s[:, 1], in0=pt_s[:, 1], in1=wms[:], op=ALU.mult)
                        return ins
                    o_mk = sq.emit("v", msk, after=[o_ex])
                    prev[("mskdone",)] = o_mk

                    for hh in range(2):
                        h = 2 * hp + hh
                        ab_ = h % 2

                        def acc(t, hh=hh, h=h, hb=hb, i2=i2, ab_=ab_):
                            if i2 > 0:
                                t.matmul(aps_[:, ab_, 0:65], pt_s[:, hh], vsl[:, 1 - hb, h, :],
                                         start=True, stop=False)
                                return t.matmul(aps_[:, ab_, 0:65], pt_d[:, hh], vsl[:, hb, h, :],
                                                start=False, stop=True)
                            return t.matmul(aps_[:, ab_, 0:65], pt_d[:, hh], vsl[:, hb, h, :],
                                            start=True, stop=True)
                        adep = [o_mk, o_lv]
                        if ("accev", ab_) in prev:
                            adep.append(prev[("accev", ab_)])
                        o_ac = sq.emit("t", acc, after=adep)

                        o_n0 = sq.emit("v", lambda v, ab_=ab_: v.tensor_copy(out=ebuf[:, 0:1], in_=aps_[:, ab_, 64:65]),
                                       after=[o_ac], hazard=True)
                        o_n1 = sq.emit("v", lambda v: v.reciprocal(out=ebuf[:, 1:2], in_=ebuf[:, 0:1]),
                                       after=[o_n0], hazard=True)
                        ndep = [o_n1]
                        if prev_atst is not None:
                            ndep.append(prev_atst)
                        o_n2 = sq.emit("v", lambda v, h=h, ab_=ab_: v.tensor_scalar(
                            out=att[:, h, 0:64], in0=aps_[:, ab_, 0:64], scalar1=ebuf[:, 1:2],
                            scalar2=None, op0=ALU.mult), after=ndep, hazard=True)
                        prev[("accev", ab_)] = o_n2
                        prev[("ath", h)] = o_n2

                def aones(v):
                    ins = None
                    for h in range(16):
                        ins = v.memset(att[:, h, 64:65], 1.0)
                    return ins
                if i2 == 0:
                    o_ao = sq.emit("v", aones)
                else:
                    o_ao = prev[("aones",)]
                prev[("aones",)] = o_ao

                o_atst = sq.dma("dg", at_h[tsl, :], att[:],
                                after=[prev[("ath", h2)] for h2 in range(16)] + [o_ao])
                prev_atst = o_atst

                tr_evs = []
                for dc in range(NCH):
                    def trmm(t, dc=dc):
                        t.transpose(tps_[0:64, :], att[:, 2 * dc, 0:64], iden[:])
                        return t.transpose(tps_[64:128, :], att[:, 2 * dc + 1, 0:64], iden[:])
                    tdep = [prev[("ath", 2 * dc)], prev[("ath", 2 * dc + 1)]]
                    if tr_evs:
                        tdep.append(tr_evs[-1])
                    o_tr = sq.emit("t", trmm, after=tdep)

                    def trev(a, dc=dc):
                        return a.copy(out=aft[:, dc], in_=tps_[:])
                    edep2 = [o_tr]
                    if prev_afst is not None:
                        edep2.append(prev_afst)
                    o_te = sq.emit("a", trev, after=edep2)
                    tr_evs.append(o_te)
                af_r = af_h.rearrange("(c p) t -> p c t", p=128)
                prev_afst = sq.dma("dg", af_r[:, :, tsl], aft[:], after=[tr_evs[-1]])
            sq.set_floor()

        if upto <= 2:
            sq.finalize(block, sems)
            return nc

        qaf_r = qaf_h.rearrange("(c p) t -> p c t", p=128)
        kaf_r = kaf_h.rearrange("(c p) t -> p c t", p=128)
        af_r = af_h.rearrange("(c p) t -> p c t", p=128)
        with ExitStack() as ph:
            psb = lambda *a: ph.enter_context(nc.sbuf_tensor(*a))
            ppt = lambda *a: ph.enter_context(nc.psum_tensor(*a))
            wqa_sb = psb("wqa_sb", [128, NCH, D], BF16)
            wka_sb = psb("wka_sb", [128, NCH, D], BF16)
            bqa_sb = psb("bqa_sb", [128, 16], F32)
            sq.dma("dg", wqa_sb[:], wqaT.rearrange("(c p) f -> p c f", p=128))
            sq.dma("dg", wka_sb[:], wkaT.rearrange("(c p) f -> p c f", p=128))
            sq.dma("ds", bqa_sb[:, 0:8], bqaf[:, :])
            sq.dma("ds", bqa_sb[:, 8:16], bkaf[:, :])
            sq.set_floor()
            ain = psb("ain", [128, 2, NCH, TT], BF16)
            ob2 = psb("ob2", [128, 2, TT], BF16)
            pps2 = ppt("pps2", [128, 2, TT], F32)
            prev_ev = [None, None]
            prev_st = [None, None]
            ld_user = [None, None]
            for i in range(NTT):
                dep = [] if ld_user[i % 2] is None else [ld_user[i % 2]]
                ld = sq.dma("ds", ain[:, i % 2], af_r[:, :, i * TT:(i + 1) * TT], after=dep)
                last = None
                for fc in range(16):
                    pb = (i * 16 + fc) % 2
                    wsel = wqa_sb if fc < 8 else wka_sb
                    cc = fc % 8

                    def proj(t, wsel=wsel, cc=cc, i=i, pb=pb):
                        ins = None
                        for dc in range(NCH):
                            ins = t.matmul(pps2[:, pb], wsel[:, dc, cc * 128:(cc + 1) * 128],
                                           ain[:, i % 2, dc], start=(dc == 0), stop=(dc == NCH - 1))
                        return ins
                    pdep = [ld]
                    if prev_ev[pb] is not None:
                        pdep.append(prev_ev[pb])
                    o_p = sq.emit("t", proj, after=pdep)

                    def evac(a, fc=fc, pb=pb):
                        return a.activation(out=ob2[:, pb], in_=pps2[:, pb], func=AF.Identity,
                                            bias=bqa_sb[:, fc:fc + 1])
                    edep = [o_p]
                    if prev_st[pb] is not None:
                        edep.append(prev_st[pb])
                    o_e = sq.emit("a", evac, after=edep)
                    prev_ev[pb] = o_e
                    dst = qaf_r[:, cc, i * TT:(i + 1) * TT] if fc < 8 else kaf_r[:, cc, i * TT:(i + 1) * TT]
                    prev_st[pb] = sq.dma("dg", dst, ob2[:, pb], after=[o_e])
                    last = o_e
                ld_user[i % 2] = last
            sq.set_floor()

        if upto <= 3:
            sq.finalize(block, sems)
            return nc

        with ExitStack() as ph:
            psb = lambda *a: ph.enter_context(nc.sbuf_tensor(*a))
            ppt = lambda *a: ph.enter_context(nc.psum_tensor(*a))
            cm = psb("cm", [128, 4, 512], BF16)
            sq.dma("ds", cm[:], cmask_h[:, :, :])
            atr = psb("atr", [128, 32, 16, 65], BF16)
            sq.dma("ds", atr[:], at_h.rearrange("(r p) (h f) -> p r h f", p=128, h=16))
            wd_sb = psb("wd_sb", [128, NCH, D], BF16)
            bd_sb = psb("bd_sb", [128, 8], F32)
            onesT = psb("onesT", [128, 128], F32)
            sq.dma("dg", wd_sb[:], wdT.rearrange("(c p) f -> p c f", p=128))
            sq.dma("ds", bd_sb[:], bdf[:, :])
            sq.dma("ds", onesT[:], onesf_h[:, :])
            sq.set_floor()
            qa_s = psb("qa_s", [128, 512], BF16)
            ka_s = psb("ka_s", [128, S], BF16)
            p2t = psb("p2t", [128, 2, 2, 512], BF16)
            zbuf = psb("zbuf", [128, 1536], F32)
            bcs2 = psb("bcs2", [128, 512], F32)
            o2t = psb("o2t", [128, NCH, 512], BF16)
            so_t = psb("so_t", [128, NCH, TT], BF16)
            s2ps = ppt("s2ps", [128, 4, 512], F32)
            a2ps = ppt("a2ps", [128, 2, 512], F32)
            bc2ps = ppt("bc2ps", [128, 512], F32)
            wdps = s2ps
            s_r = s_hbm.rearrange("(c p) t -> p c t", p=128)

            prev = {}
            prev_sost = None
            for qt in range(NTT):
                qsl2 = slice(qt * TT, (qt + 1) * TT)
                nkt = (qt + 1) * 4
                for hp in range(8):
                    dep = [sq.last["t"]] if sq.last.get("t") is not None else []
                    o_lqa = sq.dma("ds", qa_s[:], qaf_r[:, hp, qsl2], after=dep)
                    o_lka = sq.dma("ds", ka_s[:, 0:nkt * 128], kaf_r[:, hp, 0:nkt * 128], after=dep)
                    for kt in range(nkt):
                        diag = (kt >= qt * 4)
                        par = kt % 2

                        def s2mm(t, hp=hp, kt=kt, par=par):
                            ins = None
                            for hh in range(2):
                                b0 = 64 * hh
                                ins = t.matmul(s2ps[:, 2 * par + hh, :], ka_s[b0:b0 + 64, kt * 128:(kt + 1) * 128],
                                               qa_s[b0:b0 + 64, :], start=True, stop=True,
                                               tile_position=(b0, 0))
                            return ins
                        sdep = [o_lqa, o_lka]
                        if ("e2", par) in prev:
                            sdep.append(prev[("e2", par)])
                        if ("wdev", 0) in prev and kt < 2:
                            sdep.append(prev[("wdev", 0)])
                            if ("wdev", 1) in prev:
                                sdep.append(prev[("wdev", 1)])
                        o_s2 = sq.emit("t", s2mm, after=sdep)

                        def e2(a, par=par):
                            a.activation(out=p2t[:, par, 0], in_=s2ps[:, 2 * par, :], func=AF.Exp, scale=0.125)
                            return a.activation(out=p2t[:, par, 1], in_=s2ps[:, 2 * par + 1, :], func=AF.Exp, scale=0.125)
                        edep = [o_s2]
                        if ("a2", par) in prev:
                            edep.append(prev[("a2", par)])
                        o_e2 = sq.emit("a", e2, after=edep)
                        prev[("e2", par)] = o_e2

                        o_mk2 = None
                        if diag:
                            def mk2(v, kt=kt, qt=qt, par=par):
                                mi = kt - qt * 4
                                v.tensor_tensor(out=p2t[:, par, 0], in0=p2t[:, par, 0], in1=cm[:, mi], op=ALU.mult)
                                return v.tensor_tensor(out=p2t[:, par, 1], in0=p2t[:, par, 1], in1=cm[:, mi], op=ALU.mult)
                            o_mk2 = sq.emit("v", mk2, after=[o_e2])
                            prev[("e2", par)] = o_mk2

                        def a2mm(t, hp=hp, kt=kt, par=par, first=(kt == 0), last=(kt == nkt - 1)):
                            t.matmul(a2ps[0:65, 0, :], atr[:, kt, 2 * hp, :], p2t[:, par, 0],
                                     start=first, stop=last)
                            return t.matmul(a2ps[0:65, 1, :], atr[:, kt, 2 * hp + 1, :], p2t[:, par, 1],
                                            start=first, stop=last)
                        a2dep = [o_e2] + ([o_mk2] if o_mk2 else [])
                        if ("a2ev",) in prev and kt == 0:
                            a2dep.append(prev[("a2ev",)])
                        o_a2 = sq.emit("t", a2mm, after=a2dep)
                        prev[("a2", par)] = o_a2
                        prev[("p2used",)] = o_a2

                    o_z0 = sq.emit("a", lambda a: a.copy(out=zbuf[64:65, 0:512], in_=a2ps[64:65, 0, :]),
                                   after=[prev[("p2used",)]], hazard=True)
                    o_z1 = sq.emit("a", lambda a: a.copy(out=zbuf[64:65, 512:1024], in_=a2ps[64:65, 1, :]),
                                   after=[o_z0], hazard=True)
                    o_z2 = sq.emit("v", lambda v: v.reciprocal(out=zbuf[64:65, 1024:1536], in_=zbuf[64:65, 0:512]),
                                   after=[o_z1], hazard=True)
                    o_z3 = sq.emit("v", lambda v: v.reciprocal(out=zbuf[64:65, 0:512], in_=zbuf[64:65, 512:1024]),
                                   after=[o_z2], hazard=True)

                    def bc1(t):
                        t.matmul(bc2ps[0:64, :], onesT[64:65, 0:64], zbuf[64:65, 1024:1536],
                                 start=True, stop=True, tile_position=(64, 0))
                        return t.matmul(bc2ps[64:128, :], onesT[64:65, 0:64], zbuf[64:65, 0:512],
                                        start=True, stop=True, tile_position=(64, 64))
                    bcdep = [o_z2, o_z3]
                    if ("bcev",) in prev:
                        bcdep.append(prev[("bcev",)])
                    o_bc = sq.emit("t", bc1, after=bcdep)

                    def bcev(a):
                        return a.copy(out=bcs2[:], in_=bc2ps[:])
                    o_bce = sq.emit("a", bcev, after=[o_bc])
                    prev[("bcev",)] = o_bce

                    def oev(a, hp=hp):
                        a.copy(out=o2t[0:64, hp, :], in_=a2ps[0:64, 0, :])
                        return a.copy(out=o2t[64:128, hp, :], in_=a2ps[0:64, 1, :])
                    odep = [prev[("p2used",)]]
                    if ("wdused", hp) in prev:
                        odep.append(prev[("wdused", hp)])
                    o_oe = sq.emit("a", oev, after=odep)
                    prev[("a2ev",)] = o_oe

                    def onrm(v, hp=hp):
                        return v.tensor_tensor(out=o2t[:, hp, :], in0=o2t[:, hp, :], in1=bcs2[:], op=ALU.mult)
                    o_on = sq.emit("v", onrm, after=[o_oe, o_bce])
                    prev[("onrm", hp)] = o_on

                for cc in range(NCH):
                    pb = (qt * NCH + cc) % 2

                    def wdm(t, cc=cc, pb=pb):
                        ins = None
                        for dc in range(NCH):
                            ins = t.matmul(wdps[:, pb], wd_sb[:, dc, cc * 128:(cc + 1) * 128],
                                           o2t[:, dc, :], start=(dc == 0), stop=(dc == NCH - 1))
                        return ins
                    wdep2 = [prev[("onrm", hp2)] for hp2 in range(8)]
                    if ("wdev", pb) in prev:
                        wdep2.append(prev[("wdev", pb)])
                    o_wd = sq.emit("t", wdm, after=wdep2)
                    for hp2 in range(8):
                        prev[("wdused", hp2)] = o_wd

                    def wde(a, cc=cc, pb=pb):
                        return a.activation(out=so_t[:, cc], in_=wdps[:, pb], func=AF.Identity,
                                            bias=bd_sb[:, cc:cc + 1])
                    wedep = [o_wd]
                    if prev_sost is not None:
                        wedep.append(prev_sost)
                    o_we = sq.emit("a", wde, after=wedep)
                    prev[("wdev", pb)] = o_we
                prev_sost = sq.dma("dg", s_r[:, :, qsl2], so_t[:],
                                   after=[prev[("wdev", 0)], prev[("wdev", 1)]])
            sq.set_floor()

        if upto <= 4:
            sq.finalize(block, sems)
            return nc

        with ExitStack() as ph:
            psb = lambda *a: ph.enter_context(nc.sbuf_tensor(*a))
            ppt = lambda *a: ph.enter_context(nc.psum_tensor(*a))
            g1s = psb("g1s", [32, 2, 4, 32], F32)
            g2s = psb("g2s", [16, 2, 4, 16], F32)
            seld = psb("seld", [32, 512], BF16)
            selg = psb("selg", [16, 128], BF16)
            br_sb = psb("br_sb", [128, 8], F32)
            sq.dma("ds", g1s[:], g1h.transpose((2, 0, 1, 3)))
            sq.dma("ds", g2s[:], g2h.transpose((2, 0, 1, 3)))
            sq.dma("ds", seld[:], seld_h[:, :])
            sq.dma("ds", selg[:], selg_h[:, :])
            sq.dma("ds", br_sb[:], brf[:, :])
            gexp = psb("gexp", [32, 512], BF16)
            gexp2 = psb("gexp2", [16, 512], BF16)
            apart = psb("apart", [128, 512], F32)
            wacc = psb("wacc", [128, 3, 4, 512], F32)
            wq_sb = psb("wq_sb", [128, 3, 4, 512], BF16)
            selps2 = ppt("selps2", [128, 2, 512], F32)
            sq.set_floor()

            prev_sel = None
            for ri in range(2):
                for ch in range(4):
                    accs = []
                    for c in range(4):
                        def gex(v, ri=ri, c=c):
                            src = g1s[:, ri, c, :].unsqueeze(2).broadcast_to((32, 32, 16))
                            v.tensor_copy(out=gexp[:].rearrange("p (b e) -> p b e", b=32), in_=src)
                            src2 = g2s[:, ri, c, :].unsqueeze(1).broadcast_to((16, 32, 16))
                            return v.tensor_copy(out=gexp2[:].rearrange("p (b e) -> p b e", b=32), in_=src2)
                        gdep = [prev_sel] if prev_sel is not None else []
                        o_ge = sq.emit("v", gex, after=gdep)

                        def selmm(t, ch=ch):
                            t.matmul(selps2[:, 0, :], seld[:, ch * 128:(ch + 1) * 128], gexp[:], start=True, stop=True)
                            return t.matmul(selps2[:, 1, :], selg[:], gexp2[:], start=True, stop=True)
                        o_sm2 = sq.emit("t", selmm, after=[o_ge] + ([prev_sel] if prev_sel is not None else []))

                        def apev(a):
                            return a.copy(out=apart[:], in_=selps2[:, 0, :])
                        o_ap = sq.emit("a", apev, after=[o_sm2])

                        def wmul(v, ri=ri, ch=ch, c=c):
                            if c == 0:
                                return v.tensor_tensor(out=wacc[:, ri, ch, :], in0=apart[:], in1=selps2[:, 1, :], op=ALU.mult)
                            v.tensor_tensor(out=apart[:], in0=apart[:], in1=selps2[:, 1, :], op=ALU.mult)
                            return v.tensor_tensor(out=wacc[:, ri, ch, :], in0=wacc[:, ri, ch, :], in1=apart[:], op=ALU.add)
                        o_wm = sq.emit("v", wmul, after=[o_ap])
                        prev_sel = o_wm

                    def wcast(a, ri=ri, ch=ch):
                        return a.copy(out=wq_sb[:, ri, ch, :], in_=wacc[:, ri, ch, :])
                    prev_sel = sq.emit("a", wcast, after=[prev_sel])
            def wneg(a):
                return a.activation(out=wq_sb[:, 2, :, :].rearrange("p a b -> p (a b)"),
                                    in_=wq_sb[:, 1, :, :].rearrange("p a b -> p (a b)"), func=AF.Copy, scale=-1.0)
            o_wn = sq.emit("a", wneg, after=[prev_sel])
            sq.set_floor()

            xng2 = psb("xng2", [128, 2, NCH, TT], BF16)
            gb2 = psb("gb2", [128, TT], BF16)
            xres = psb("xres", [128, 2, NCH, TT], F32)
            sres = psb("sres", [128, 2, NCH, TT], BF16)
            onew = psb("onew", [128, NCH, TT], F32)
            fps2 = ppt("fps2", [128, 2, TT], F32)
            xn_r2 = xn_h.rearrange("(c p) t -> p c t", p=128)
            xi_r = x_in.rearrange("(c p) t -> p c t", p=128)
            xo_r = x_out.rearrange("(c p) t -> p c t", p=128)

            prev_gev = [None, None]
            prev_resid = None
            prev_xst = None
            xld_user = [None, None]
            out_stores = []
            for i in range(NTT):
                dep = [] if xld_user[i % 2] is None else [xld_user[i % 2]]
                o_lxn = sq.dma("ds", xng2[:, i % 2], xn_r2[:, :, i * TT:(i + 1) * TT], after=dep)
                o_lx = sq.dma("ds", xres[:, i % 2], xi_r[:, :, i * TT:(i + 1) * TT], after=dep)
                o_ls = sq.dma("ds", sres[:, i % 2], s_r[:, :, i * TT:(i + 1) * TT], after=dep)
                res_ops = []
                for oc in range(NCH):
                    pb = (i * NCH + oc) % 2

                    def fmm(t, oc=oc, pb=pb, i=i):
                        real = oc < 4
                        ch = oc % 4
                        ins = None
                        for dc in range(4):
                            t.matmul(fps2[:, pb], wq_sb[:, 0 if real else 1, dc, ch * 128:(ch + 1) * 128],
                                     xng2[:, i % 2, dc, :], start=(dc == 0), stop=False)
                        for dc in range(4):
                            ins = t.matmul(fps2[:, pb], wq_sb[:, 2 if real else 0, dc, ch * 128:(ch + 1) * 128],
                                           xng2[:, i % 2, 4 + dc, :], start=False, stop=(dc == 3))
                        return ins
                    fdep = [o_lxn, o_wn]
                    if prev_gev[pb] is not None:
                        fdep.append(prev_gev[pb])
                    o_fm = sq.emit("t", fmm, after=fdep)

                    def gev(a, oc=oc, pb=pb):
                        return a.activation(out=gb2[:], in_=fps2[:, pb], func=AF.Silu,
                                            bias=br_sb[:, oc:oc + 1])
                    gdep2 = [o_fm]
                    if prev_resid is not None:
                        gdep2.append(prev_resid)
                    o_gv = sq.emit("a", gev, after=gdep2)
                    prev_gev[pb] = o_gv

                    def resid(v, oc=oc, i=i):
                        v.tensor_tensor(out=onew[:, oc], in0=sres[:, i % 2, oc], in1=gb2[:], op=ALU.mult)
                        return v.tensor_tensor(out=onew[:, oc], in0=onew[:, oc], in1=xres[:, i % 2, oc], op=ALU.add)
                    rdep = [o_gv, o_lx, o_ls]
                    if prev_xst is not None:
                        rdep.append(prev_xst)
                    o_r = sq.emit("v", resid, after=rdep)
                    prev_resid = o_r
                    res_ops.append(o_r)
                xld_user[i % 2] = res_ops[-1]
                prev_xst = sq.dma("dg", xo_r[:, :, i * TT:(i + 1) * TT], onew[:], after=res_ops)
                out_stores.append(prev_xst)

            if ctx is None:
                sq.finalize(block, sems, outputs_done=out_stores)
            else:
                ctx["out_stores"] = out_stores
                sq.set_floor()
    return nc


def l0_inmap(xcore_fm, I):
    RD = 16
    inv = (1.0 / 10000.0 ** (np.arange(0, RD, 2, dtype=np.float64) / RD)).astype(np.float64)
    t = np.arange(S, dtype=np.float64)
    fr = np.outer(inv, t)
    cos8 = np.cos(fr); sin8 = np.sin(fr)
    cosT = np.zeros((128, S), np.float32)
    sinT = np.zeros((128, S), np.float32)
    for b0 in (0, 64):
        cosT[b0 + 0:b0 + 8] = cos8
        cosT[b0 + 8:b0 + 16] = cos8
        sinT[b0 + 0:b0 + 8] = -sin8
        sinT[b0 + 8:b0 + 16] = sin8
    kk, qq = np.meshgrid(np.arange(128), np.arange(128), indexing="ij")
    wmd = ((qq - kk >= 0) & (qq - kk <= 15)).astype(np.float32)
    wms = ((qq + 128 - kk >= 0) & (qq + 128 - kk <= 15)).astype(np.float32)
    cmask = np.zeros((128, 4, 512), np.float32)
    for mi in range(4):
        kx, qx = np.meshgrid(np.arange(128) + mi * 128, np.arange(512), indexing="ij")
        cmask[:, mi, :] = (kx <= qx).astype(np.float32)
    seld = np.zeros((32, 512), np.float32)
    for dd in range(32):
        for g in range(16):
            seld[dd, (dd * 16 + g) % 512] = 0.0
    seld = np.zeros((32, 512), np.float32)
    for r in range(512):
        dd = r // 16
        seld[dd, r] = 1.0
    selg = np.zeros((16, 128), np.float32)
    for r in range(128):
        selg[r % 16, r] = 1.0
    G1R = I["L0_ffn_G1_R"][0]
    G2R = I["L0_ffn_G2_R"]
    G1I = I["L0_ffn_G1_I"][0]
    G2I = I["L0_ffn_G2_I"]
    g1 = np.stack([np.ascontiguousarray(G1R.transpose(1, 2, 0)),
                   np.ascontiguousarray(G1I.transpose(1, 2, 0))])
    g2 = np.stack([np.ascontiguousarray(G2R[:, :, 0, :].transpose(0, 2, 1)),
                   np.ascontiguousarray(G2I[:, :, 0, :].transpose(0, 2, 1))])
    bri = np.concatenate([I["L0_ffn_bias_real"], I["L0_ffn_bias_imag"]])
    return {
        "x_in": xcore_fm,
        "wqkvT": np.ascontiguousarray(I["L0_attn_Wqkv"].T),
        "bqkvf": fm(I["L0_attn_bqkv"][0:2048]),
        "bvrow": np.ascontiguousarray(I["L0_attn_bqkv"][2048:3072].reshape(1, D)),
        "wqaT": np.ascontiguousarray(I["L0_attn_Wqa"].T),
        "bqaf": fm(I["L0_attn_bqa"]),
        "wkaT": np.ascontiguousarray(I["L0_attn_Wka"].T),
        "bkaf": fm(I["L0_attn_bka"]),
        "wdT": np.ascontiguousarray(I["L0_attn_Wd"].T),
        "bdf": fm(I["L0_attn_bd"]),
        "g1h": g1.astype(np.float32),
        "g2h": g2.astype(np.float32),
        "brf": fm(bri),
        "seld_h": seld.astype(BF),
        "selg_h": selg.astype(BF),
        "cosT_h": cosT.astype(BF),
        "sinT_h": sinT.astype(BF),
        "wmaskd_h": wmd.astype(BF),
        "wmasks_h": wms.astype(BF),
        "cmask_h": cmask.astype(BF),
        "iden_h": np.eye(128, dtype=np.float32).astype(BF),
        "onescol_h": np.ones((1, 128), np.float32).astype(BF),
        "ones128_h": np.ones((128, 1), np.float32),
        "ones1_h": np.ones((1, 128), np.float32),
        "onesf_h": np.ones((128, 128), np.float32),
    }



def build_all():
    from contextlib import ExitStack
    nc0 = bass.Bass("TRN2", target_bir_lowering=False, debug=False)
    x0h = nc0.dram_tensor("x_in", [D, S], F32, kind="ExternalInput")
    x1h = nc0.dram_tensor("ix1", [D, S], F32, kind="Internal")
    x2h = nc0.dram_tensor("ix2", [D, S], F32, kind="Internal")
    x3h = nc0.dram_tensor("ix3", [D, S], F32, kind="Internal")
    xoh = nc0.dram_tensor("x_out", [D, S], F32, kind="ExternalOutput")
    stack = ExitStack()
    with stack:
        sems = alloc_sems(nc0, stack)
        block = stack.enter_context(nc0.Block())
        sq = Seq(nc0)
        ctx = {"sems": sems, "block": block, "sq": sq}
        ios = [(x0h, x1h, "A_", build_l0), (x1h, x2h, "B_", build_l1),
               (x2h, x3h, "C_", build_l2), (x3h, xoh, "D_", build_l3)]
        for xin, xout, pf, bfn in ios:
            ctx["nc"] = NCProxy(nc0, pf, xin, xout)
            bfn(ctx=ctx)
            sq.set_floor()
        sq.finalize(block, sems, outputs_done=ctx["out_stores"])
    return nc0


def all_inmap(xcore_fm, I):
    m = {"x_in": xcore_fm}
    for pf, f in [("A_", l0_inmap), ("B_", l1_inmap), ("C_", l2_inmap), ("D_", l3_inmap)]:
        sub = f(xcore_fm, I)
        for k, v in sub.items():
            if k != "x_in":
                m[pf + k] = v
    return m



_BUILDS = {}


def _get_build(name, fn):
    if name not in _BUILDS:
        _BUILDS[name] = fn()
    return _BUILDS[name]


FUSED = True


def kernel(**inputs):
    I = inputs
    x = np.asarray(I["L0_attn_hidden_states"], dtype=np.float32)
    ncores = 8
    x_fm = [np.ascontiguousarray(x[c].T) for c in range(ncores)]

    if FUSED:
        nc = _get_build("all", build_all)
        in_maps = [all_inmap(x_fm[c], I) for c in range(ncores)]
        res = run_bass_kernel_spmd(nc, in_maps, core_ids=list(range(ncores)))
        x_fm = [np.asarray(res.results[c]["x_out"]) for c in range(ncores)]
    else:
        for name, build, inmap in [
            ("l0", build_l0, l0_inmap),
            ("l1", build_l1, l1_inmap),
            ("l2", build_l2, l2_inmap),
            ("l3", build_l3, l3_inmap),
        ]:
            nc = _get_build(name, build)
            in_maps = [inmap(x_fm[c], I) for c in range(ncores)]
            res = run_bass_kernel_spmd(nc, in_maps, core_ids=list(range(ncores)))
            x_fm = [np.asarray(res.results[c]["x_out"]) for c in range(ncores)]

    out = np.stack([xc.T for xc in x_fm]).astype(np.float32)
    return out


# version 08
# speedup vs baseline: 1.2899x; 1.1310x over previous
import numpy as np
import ml_dtypes
import concourse.bass as bass
import concourse.mybir as mybir
from concourse.bass_utils import run_bass_kernel_spmd

F32 = mybir.dt.float32
BF16 = mybir.dt.bfloat16
AF = mybir.ActivationFunctionType
ALU = mybir.AluOpType
AX = mybir.AxisListType

S = 4096
D = 1024
NCH = 8
TT = 512
NTT = S // TT
BF = np.dtype(ml_dtypes.bfloat16)

COMPUTE_ENGINES = ("t", "a", "v", "g")
DMA_QUEUES = ("ds", "dg", "da", "dv")
ISSUER = {"ds": "s", "dg": "g", "da": "a", "dv": "v"}


class Seq:

    STRICT = False
    STRICT_DS = False
    STRICT_KEYS = ()

    def __init__(self, nc):
        self.nc = nc
        self.ops = []
        self.local_idx = []
        self.hazards = []
        self.counts = {k: 0 for k in COMPUTE_ENGINES + DMA_QUEUES}
        self.last = {}
        self.floor = []

    def emit(self, key, fn, after=(), hazard=False):
        deps = list(after) + self.floor
        if Seq.STRICT and self.ops:
            deps = deps + [len(self.ops) - 1]
            hazard = True
        if Seq.STRICT_DS and key in DMA_QUEUES and self.ops:
            deps = deps + [len(self.ops) - 1]
        if key in Seq.STRICT_KEYS and self.ops:
            deps = deps + [len(self.ops) - 1]
        for d in deps:
            assert 0 <= d < len(self.ops), f"dep {d} not yet emitted"
        self.ops.append((key, fn, tuple(deps)))
        self.local_idx.append(self.counts[key])
        self.hazards.append(bool(hazard))
        self.counts[key] += 1
        gid = len(self.ops) - 1
        self.last[key] = gid
        return gid

    def dma(self, queue, out, in_, after=()):
        assert queue in DMA_QUEUES

        def fn(eng, out=out, in_=in_):
            return eng.dma_start(out=out, in_=in_)
        return self.emit(queue, fn, after)

    def set_floor(self):
        self.floor = list(self.last.values())

    def finalize(self, block, sems, outputs_done=()):
        phys = {"t": [], "a": [], "v": [], "g": [], "s": []}
        for gid, (key, fn, deps) in enumerate(self.ops):
            eng = key if key in COMPUTE_ENGINES else ISSUER[key]
            phys[eng].append(gid)

        def target(gid):
            key = self.ops[gid][0]
            li = self.local_idx[gid]
            return (key, (li + 1) * (16 if key in DMA_QUEUES else 1))

        def mkbody(eng_name, gids):
            def body(eng):
                hw = {}
                for gid in gids:
                    key, fn, deps = self.ops[gid]
                    need = {}
                    for d in deps:
                        dkey = self.ops[d][0]
                        deng = dkey if dkey in COMPUTE_ENGINES else ISSUER[dkey]
                        if deng == eng_name and dkey not in DMA_QUEUES and not self.hazards[d]:
                            continue
                        k2, val = target(d)
                        if need.get(k2, 0) < val:
                            need[k2] = val
                    for k2, val in sorted(need.items()):
                        if hw.get(k2, 0) >= val:
                            continue
                        eng.wait_ge(sems[k2], val)
                        hw[k2] = val
                    ins = fn(eng)
                    assert ins is not None, f"op {gid} ({key}) returned no instruction"
                    ins.then_inc(sems[key], 16 if key in DMA_QUEUES else 1)
                if eng_name == "s":
                    need = {}
                    for d in outputs_done:
                        k2, val = target(d)
                        if need.get(k2, 0) < val:
                            need[k2] = val
                    for k2, val in sorted(need.items()):
                        eng.wait_ge(sems[k2], val)
            return body

        block.tensor(mkbody("t", phys["t"]))
        block.scalar(mkbody("a", phys["a"]))
        block.vector(mkbody("v", phys["v"]))
        block.gpsimd(mkbody("g", phys["g"]))
        block.sync(mkbody("s", phys["s"]))


def alloc_sems(nc, stack):
    return {k: stack.enter_context(nc.semaphore("sem_" + k))
            for k in COMPUTE_ENGINES + DMA_QUEUES}


class NCProxy:

    def __init__(self, nc, prefix, x_in_handle, x_out_handle):
        self._nc = nc
        self._pf = prefix
        self._xin = x_in_handle
        self._xout = x_out_handle

    def dram_tensor(self, name, *a, **k):
        if name == "x_in":
            return self._xin
        if name == "x_out":
            return self._xout
        return self._nc.dram_tensor(self._pf + name, *a, **k)

    def sbuf_tensor(self, name, *a, **k):
        return self._nc.sbuf_tensor(self._pf + name, *a, **k)

    def psum_tensor(self, name, *a, **k):
        return self._nc.psum_tensor(self._pf + name, *a, **k)

    def __getattr__(self, n):
        return getattr(self._nc, n)



def emit_ln(sq, nc, stack, x_hbm, xn_sb, ones128, ones1, cst, col_off=1):
    xr = x_hbm.rearrange("(c p) t -> p c t", p=128)
    xt = stack.enter_context(nc.sbuf_tensor("ln_x", [128, 2, NCH, TT], F32))
    sqt = stack.enter_context(nc.sbuf_tensor("ln_q", [128, 2, NCH, TT], F32))
    t1 = stack.enter_context(nc.sbuf_tensor("ln_1", [128, 2, TT], F32))
    t2 = stack.enter_context(nc.sbuf_tensor("ln_2", [128, 2, TT], F32))
    ab = stack.enter_context(nc.sbuf_tensor("ln_ab", [1, 2, 2 * TT], F32))
    ps = stack.enter_context(nc.psum_tensor("ln_ps", [1, 2, 2 * TT], F32))
    bc = stack.enter_context(nc.psum_tensor("ln_bc", [128, 2, 2 * TT], F32))

    done = []
    for i in range(NTT):
        pr = i % 2
        dep = [] if i < 2 else [done[i - 2]]
        ld = sq.dma("ds", xt[:, pr], xr[:, :, i * TT:(i + 1) * TT], after=dep)

        def tree(v, i=i, pr=pr):
            for cc in range(NCH):
                v.tensor_tensor(out=sqt[:, pr, cc], in0=xt[:, pr, cc], in1=xt[:, pr, cc], op=ALU.mult)
            v.tensor_tensor(out=t1[:, pr], in0=xt[:, pr, 0], in1=xt[:, pr, 1], op=ALU.add)
            for cc in range(2, NCH):
                v.tensor_tensor(out=t1[:, pr], in0=t1[:, pr], in1=xt[:, pr, cc], op=ALU.add)
            v.tensor_tensor(out=t2[:, pr], in0=sqt[:, pr, 0], in1=sqt[:, pr, 1], op=ALU.add)
            ins = None
            for cc in range(2, NCH):
                ins = v.tensor_tensor(out=t2[:, pr], in0=t2[:, pr], in1=sqt[:, pr, cc], op=ALU.add)
            return ins
        otree = sq.emit("v", tree, after=[ld] + dep)

        def psums(t, pr=pr):
            t.matmul(ps[0:1, pr, 0:TT], ones128[:], t1[:, pr], start=True, stop=True)
            return t.matmul(ps[0:1, pr, TT:2 * TT], ones128[:], t2[:, pr], start=True, stop=True)
        opsum = sq.emit("t", psums, after=[otree] + dep)

        def stats(a, pr=pr):
            a.activation(out=ab[0:1, pr, 0:TT], in_=ps[0:1, pr, 0:TT], func=AF.Copy, scale=1.0 / D)
            return a.activation(out=ab[0:1, pr, TT:2 * TT], in_=ps[0:1, pr, TT:2 * TT], func=AF.Copy, scale=1.0 / D)
        omean = sq.emit("a", stats, after=[opsum] + dep)

        def var(v, pr=pr):
            v.tensor_tensor(out=t2[0:1, pr, 0:TT], in0=ab[0:1, pr, 0:TT], in1=ab[0:1, pr, 0:TT], op=ALU.mult)
            return v.tensor_tensor(out=t2[0:1, pr, 0:TT], in0=ab[0:1, pr, TT:2 * TT], in1=t2[0:1, pr, 0:TT], op=ALU.subtract)
        ovar = sq.emit("v", var, after=[omean])

        def rstd(a, pr=pr):
            a.activation(out=sqt[0:1, pr, 0, 0:TT], in_=t2[0:1, pr, 0:TT], func=AF.Ln, bias=cst[0:1, 0:1])
            return a.activation(out=ab[0:1, pr, TT:2 * TT], in_=sqt[0:1, pr, 0, 0:TT], func=AF.Exp, scale=-0.5)
        orstd = sq.emit("a", rstd, after=[ovar])

        def nab(v, pr=pr):
            v.tensor_tensor(out=ab[0:1, pr, 0:TT], in0=ab[0:1, pr, 0:TT], in1=ab[0:1, pr, TT:2 * TT], op=ALU.mult)
            return v.tensor_scalar(out=ab[0:1, pr, 0:TT], in0=ab[0:1, pr, 0:TT], scalar1=-1.0, scalar2=None, op0=ALU.mult)
        onab = sq.emit("v", nab, after=[orstd])

        def bcast(t, pr=pr):
            t.matmul(bc[:, pr, 0:TT], ones1[:], ab[0:1, pr, 0:TT], start=True, stop=True)
            return t.matmul(bc[:, pr, TT:2 * TT], ones1[:], ab[0:1, pr, TT:2 * TT], start=True, stop=True)
        obc = sq.emit("t", bcast, after=[onab] + dep)

        def apply(v, i=i, pr=pr):
            ins = None
            for cc in range(NCH):
                v.tensor_tensor(out=sqt[:, pr, cc], in0=xt[:, pr, cc], in1=bc[:, pr, TT:2 * TT], op=ALU.mult)
                ins = v.tensor_tensor(out=xn_sb[:, cc, col_off + i * TT:col_off + (i + 1) * TT],
                                      in0=sqt[:, pr, cc], in1=bc[:, pr, 0:TT], op=ALU.add)
            return ins
        oap = sq.emit("v", apply, after=[obc])
        done.append(oap)
    return done



WSTRICT = True
L2SUB = 9
L0SUB = 9


def build_l3(debug=False, ctx=None):
    from contextlib import ExitStack
    nc = bass.Bass("TRN2", target_bir_lowering=False, debug=False) if ctx is None else ctx["nc"]
    x_in = nc.dram_tensor("x_in", [D, S], F32, kind="ExternalInput").ap()
    wkT = nc.dram_tensor("wkT", [D, D], F32, kind="ExternalInput").ap()
    wvT = nc.dram_tensor("wvT", [D, D], F32, kind="ExternalInput").ap()
    wrT = nc.dram_tensor("wrT", [D, D], F32, kind="ExternalInput").ap()
    woT = nc.dram_tensor("woT", [D, D], F32, kind="ExternalInput").ap()
    w1T = nc.dram_tensor("w1T", [D, 4 * D], F32, kind="ExternalInput").ap()
    w2T = nc.dram_tensor("w2T", [4 * D, D], F32, kind="ExternalInput").ap()
    b1f = nc.dram_tensor("b1f", [128, 32], F32, kind="ExternalInput").ap()
    b2f = nc.dram_tensor("b2f", [128, 8], F32, kind="ExternalInput").ap()
    tdf = nc.dram_tensor("tdf", [128, 8], F32, kind="ExternalInput").ap()
    tff = nc.dram_tensor("tff", [128, 8], F32, kind="ExternalInput").ap()
    tmkf = nc.dram_tensor("tmkf", [128, 8], F32, kind="ExternalInput").ap()
    tmvf = nc.dram_tensor("tmvf", [128, 8], F32, kind="ExternalInput").ap()
    tmrf = nc.dram_tensor("tmrf", [128, 8], F32, kind="ExternalInput").ap()
    ones128_h = nc.dram_tensor("ones128_h", [128, 1], F32, kind="ExternalInput").ap()
    ones1_h = nc.dram_tensor("ones1_h", [1, 128], F32, kind="ExternalInput").ap()
    x_out = nc.dram_tensor("x_out", [D, S], F32, kind="ExternalOutput").ap()

    kind = "ExternalOutput" if debug else "Internal"
    w1t_h = nc.dram_tensor("w1t_h", [D, 4 * D], BF16, kind=kind).ap()
    w2t_h = nc.dram_tensor("w2t_h", [4 * D, D], BF16, kind=kind).ap()
    sry_h = nc.dram_tensor("sry_h", [D, S], BF16, kind=kind).ap()
    s_hbm = nc.dram_tensor("s_hbm", [D, S], BF16, kind=kind).ap()
    xn_h = nc.dram_tensor("xn_h", [D, S], BF16, kind=kind).ap() if debug else None
    thr_h = nc.dram_tensor("thr_h", [2, 128, 1], F32, kind=kind).ap() if debug else None
    g_h = nc.dram_tensor("g_h", [D, S], BF16, kind=kind).ap() if debug else None
    h_h = nc.dram_tensor("h_h", [4 * D, S], BF16, kind=kind).ap() if debug else None
    tred_h = nc.dram_tensor("tred_h", [2, 128, 1], F32, kind=kind).ap() if debug else None

    from contextlib import ExitStack
    stack = ExitStack()
    with stack:
        if ctx is None:
            sems = alloc_sems(nc, stack)
            block = stack.enter_context(nc.Block())
            sq = Seq(nc)
        else:
            sems, block, sq = ctx["sems"], ctx["block"], ctx["sq"]
        sb = lambda *a: stack.enter_context(nc.sbuf_tensor(*a))

        ones128 = sb("c_ones128", [128, 1], F32)
        ones1 = sb("c_ones1", [1, 128], F32)
        mixp = sb("mixp", [128, 12, 8], F32)
        cst = sb("cst", [128, 2], F32)
        xn = sb("xn", [128, NCH, S + 1], BF16)

        sq.dma("ds", ones128[:], ones128_h[:, :])
        sq.dma("ds", ones1[:], ones1_h[:, :])
        o_mx = []
        for col, src in [(0, tmkf), (2, tmvf), (4, tmrf), (6, tdf), (7, tff)]:
            o_mx.append(sq.dma("ds", mixp[:, col, :], src[:, :]))

        def premix(v):
            v.memset(cst[:, 0:1], 1e-5)
            for cc in range(NCH):
                v.memset(xn[:, cc, 0:1], 0.0)
            ins = None
            for src, dst in [(0, 1), (2, 3), (4, 5)]:
                ins = v.tensor_scalar(out=mixp[:, dst, :], in0=mixp[:, src, :], scalar1=-1.0, scalar2=1.0,
                                      op0=ALU.mult, op1=ALU.add)
            return ins
        o_premix = sq.emit("v", premix, after=o_mx)

        o_pp0 = sq.emit("a", lambda a: a.activation(out=mixp[:, 6, :], in_=mixp[:, 6, :], func=AF.Exp),
                        after=o_mx, hazard=True)
        o_pp1 = sq.emit("a", lambda a: a.activation(out=mixp[:, 6, :], in_=mixp[:, 6, :], func=AF.Exp, scale=-1.0),
                        after=[o_pp0], hazard=True)
        o_preparam = sq.emit("a", lambda a: a.activation(out=mixp[:, 7, :], in_=mixp[:, 7, :], func=AF.Exp),
                             after=[o_pp1], hazard=True)
        sq.set_floor()

        with ExitStack() as ph:
            ln_done = emit_ln(sq, nc, ph, x_in, xn, ones128, ones1, cst, col_off=1)
            sq.set_floor()

        if debug:
            xn_r = xn_h.rearrange("(c p) t -> p c t", p=128)
            for i in range(NTT):
                sq.dma("dg", xn_r[:, :, i * TT:(i + 1) * TT], xn[:, :, 1 + i * TT:1 + (i + 1) * TT],
                       after=[ln_done[i]])
            sq.set_floor()

        with ExitStack() as ph:
            psb = lambda *a: ph.enter_context(nc.sbuf_tensor(*a))
            ppt = lambda *a: ph.enter_context(nc.psum_tensor(*a))
            tw = psb("tw", [128, 2, 4096], F32)
            twq = psb("twq", [128, 4096], BF16)
            tred = psb("tred", [128, 33], F32)
            tth = psb("tth", [128, 1], F32)
            tsc = psb("tsc", [1, 1], F32)
            tps = ppt("tps", [128, 2], F32)

            for wi, (wh, wth, nrow, ncol) in enumerate([(w1T, w1t_h, D, 4 * D), (w2T, w2t_h, 4 * D, D)]):
                nrt = nrow // 128
                reds = []
                for r in range(nrt):
                    dep = [reds[r - 2]] if r >= 2 else []
                    ld = sq.dma("ds", tw[:, r % 2, 0:ncol], wh[r * 128:(r + 1) * 128, :], after=dep)

                    def red(v, r=r, ncol=ncol):
                        return v.tensor_reduce(out=tred[:, 1 + r:2 + r], in_=tw[:, r % 2, 0:ncol], axis=AX.X,
                                               op=ALU.add, apply_absolute_value=True)
                    ored = sq.emit("v", red, after=[ld], hazard=True)
                    reds.append(ored)

                def combine(v, nrt=nrt):
                    return v.tensor_reduce(out=tred[:, 0:1], in_=tred[:, 1:1 + nrt], axis=AX.X, op=ALU.add)
                reds.append(sq.emit("v", combine, after=reds[-3:], hazard=True))

                if debug:
                    sq.dma("dg", tred_h[wi], tred[:, 0:1], after=[reds[-1]])

                def colsum(t, wi=wi):
                    return t.matmul(tps[0:1, wi:wi + 1], ones128[:], tred[:, 0:1], start=True, stop=True)
                o_cs = sq.emit("t", colsum, after=[reds[-1]])

                def mkthr(a, wi=wi, nrow=nrow, ncol=ncol):
                    return a.activation(out=tsc[0:1, 0:1], in_=tps[0:1, wi:wi + 1], func=AF.Copy,
                                        scale=0.5 / (nrow * ncol))
                o_thr = sq.emit("a", mkthr, after=[o_cs])

                def bthr(t):
                    return t.matmul(tps[:, 0:1], ones1[:], tsc[0:1, 0:1], start=True, stop=True)
                o_bthr = sq.emit("t", bthr, after=[o_thr])

                def cpthr(a):
                    return a.copy(out=tth[:, 0:1], in_=tps[:, 0:1])
                o_cpt = sq.emit("a", cpthr, after=[o_bthr])
                if debug:
                    sq.dma("dg", thr_h[wi], tth[:, 0:1], after=[o_cpt])

                prev_q = None
                for r in range(nrt):
                    dep = [o_cpt] + ([prev_q] if prev_q is not None else [])
                    ld = sq.dma("ds", tw[:, r % 2, 0:ncol], wh[r * 128:(r + 1) * 128, :], after=dep)

                    def absop(a, r=r, ncol=ncol):
                        return a.activation(out=tw[:, (r + 1) % 2, 0:ncol], in_=tw[:, r % 2, 0:ncol], func=AF.Abs)
                    o_abs = sq.emit("a", absop, after=[ld])

                    def cmp(v, r=r, ncol=ncol):
                        return v.tensor_scalar(out=tw[:, (r + 1) % 2, 0:ncol], in0=tw[:, (r + 1) % 2, 0:ncol],
                                               scalar1=tth[:, 0:1], scalar2=None, op0=ALU.is_gt)
                    o_cmp = sq.emit("v", cmp, after=[o_abs])

                    def sgn(a, r=r, ncol=ncol):
                        return a.activation(out=twq[:, 0:ncol], in_=tw[:, r % 2, 0:ncol], func=AF.Sign)
                    o_sgn = sq.emit("a", sgn, after=[ld])

                    def mul(v, r=r, ncol=ncol):
                        return v.tensor_tensor(out=twq[:, 0:ncol], in0=twq[:, 0:ncol],
                                               in1=tw[:, (r + 1) % 2, 0:ncol], op=ALU.mult)
                    o_mul = sq.emit("v", mul, after=[o_cmp, o_sgn])
                    prev_q = sq.dma("dg", wth[r * 128:(r + 1) * 128, :], twq[:, 0:ncol], after=[o_mul])
                sq.set_floor()
            sq.set_floor()

        with ExitStack() as ph:
            psb = lambda *a: ph.enter_context(nc.sbuf_tensor(*a))
            ppt = lambda *a: ph.enter_context(nc.psum_tensor(*a))
            wk_sb = psb("wk_sb", [128, NCH, D], BF16)
            wv_sb = psb("wv_sb", [128, NCH, D], BF16)
            wr_sb = psb("wr_sb", [128, NCH, D], BF16)
            o_wk = sq.dma("dg", wk_sb[:], wkT.rearrange("(c p) f -> p c f", p=128))
            o_wv = sq.dma("dg", wv_sb[:], wvT.rearrange("(c p) f -> p c f", p=128))
            o_wr = sq.dma("dg", wr_sb[:], wrT.rearrange("(c p) f -> p c f", p=128))
            xk_sb = psb("xk_sb", [128, NCH, TT], BF16)
            xv_sb = psb("xv_sb", [128, NCH, TT], BF16)
            xr_sb = psb("xr_sb", [128, NCH, TT], BF16)
            mixtmp = psb("mixtmp", [128, TT], F32)
            kb = psb("kb", [128, TT], F32)
            vb = psb("vb", [128, TT], F32)
            rb = psb("rb", [128, TT], F32)
            e1 = psb("e1", [128, TT], F32)
            e2 = psb("e2", [128, TT], F32)
            e3 = psb("e3", [128, TT], F32)
            e4 = psb("e4", [128, TT], F32)
            Pt = psb("Pt", [128, NCH, TT + 1], F32)
            Qt = psb("Qt", [128, NCH, TT + 1], F32)
            sry_t = psb("sry_t", [128, NCH, TT], BF16)
            kps = ppt("kps", [128, 2, TT], F32)
            vps = ppt("vps", [128, 2, TT], F32)
            rps = ppt("rps", [128, 2, TT], F32)

            def zpq(v):
                ins = None
                for cc in range(NCH):
                    v.memset(Pt[:, cc, 0:1], 0.0)
                    ins = v.memset(Qt[:, cc, 0:1], 0.0)
                return ins
            sq.emit("v", zpq)

            prev_scan = [None] * NCH
            prev_evac = [None, None]
            prev_proj_last = None
            prev_store = None
            sry_r = sry_h.rearrange("(c p) t -> p c t", p=128)
            for i in range(NTT):
                def mix(v, i=i):
                    ins = None
                    for dc in range(NCH):
                        cur = xn[:, dc, 1 + i * TT:1 + (i + 1) * TT]
                        prv = xn[:, dc, i * TT:i * TT + TT]
                        for (pcol, dst) in [(0, xk_sb), (2, xv_sb), (4, xr_sb)]:
                            v.tensor_scalar(out=mixtmp[:], in0=prv, scalar1=mixp[:, pcol + 1, dc:dc + 1],
                                            scalar2=None, op0=ALU.mult)
                            ins = v.scalar_tensor_tensor(out=dst[:, dc], in0=cur, scalar=mixp[:, pcol, dc:dc + 1],
                                                         in1=mixtmp[:], op0=ALU.mult, op1=ALU.add)
                    return ins
                mdep = [ln_done[i]]
                if prev_proj_last is not None:
                    mdep.append(prev_proj_last)
                o_mix = sq.emit("v", mix, after=mdep)

                for cc in range(NCH):
                    pb = (i * NCH + cc) % 2

                    def proj(t, cc=cc, pb=pb):
                        for dc in range(NCH):
                            t.matmul(kps[:, pb], wk_sb[:, dc, cc * 128:(cc + 1) * 128], xk_sb[:, dc],
                                     start=(dc == 0), stop=(dc == NCH - 1))
                        for dc in range(NCH):
                            t.matmul(vps[:, pb], wv_sb[:, dc, cc * 128:(cc + 1) * 128], xv_sb[:, dc],
                                     start=(dc == 0), stop=(dc == NCH - 1))
                        ins = None
                        for dc in range(NCH):
                            ins = t.matmul(rps[:, pb], wr_sb[:, dc, cc * 128:(cc + 1) * 128], xr_sb[:, dc],
                                           start=(dc == 0), stop=(dc == NCH - 1))
                        return ins
                    pdep = [o_mix, o_wk, o_wv, o_wr]
                    if prev_evac[pb] is not None:
                        pdep.append(prev_evac[pb])
                    o_proj = sq.emit("t", proj, after=pdep)
                    prev_proj_last = o_proj

                    def evac(a, pb=pb):
                        a.copy(out=kb[:], in_=kps[:, pb])
                        a.copy(out=vb[:], in_=vps[:, pb])
                        return a.copy(out=rb[:], in_=rps[:, pb])
                    edep = [o_proj]
                    if prev_scan[(cc + NCH - 1) % NCH] is not None:
                        edep.append(prev_scan[(cc + NCH - 1) % NCH])
                    o_evac = sq.emit("a", evac, after=edep)
                    prev_evac[pb] = o_evac

                    def exps(a):
                        a.activation(out=e1[:], in_=kb[:], func=AF.Exp)
                        return a.activation(out=rb[:], in_=rb[:], func=AF.Sigmoid)
                    o_exps = sq.emit("a", exps, after=[o_evac])

                    def scan_a(v, cc=cc, i=i):
                        v.tensor_tensor(out=e2[:], in0=e1[:], in1=vb[:], op=ALU.mult)
                        if i > 0:
                            v.tensor_copy(out=Pt[:, cc, 0:1], in_=Pt[:, cc, TT:TT + 1])
                            v.tensor_copy(out=Qt[:, cc, 0:1], in_=Qt[:, cc, TT:TT + 1])
                        dbc = mixp[:, 6, cc:cc + 1].broadcast_to((128, TT))
                        v.tensor_tensor_scan(Pt[:, cc, 1:TT + 1], dbc, e2[:], Pt[:, cc, 0:1], ALU.mult, ALU.add)
                        return v.tensor_tensor_scan(Qt[:, cc, 1:TT + 1], dbc, e1[:], Qt[:, cc, 0:1], ALU.mult, ALU.add)
                    sdep = [o_exps]
                    if prev_store is not None:
                        sdep.append(prev_store)
                    o_scana = sq.emit("v", scan_a, after=sdep, hazard=True)

                    def scan_b(v, cc=cc):
                        v.tensor_scalar(out=e3[:], in0=e1[:], scalar1=mixp[:, 7, cc:cc + 1], scalar2=None, op0=ALU.mult)
                        v.tensor_tensor(out=e4[:], in0=e3[:], in1=vb[:], op=ALU.mult)
                        v.tensor_tensor(out=e2[:], in0=Pt[:, cc, 0:TT], in1=e4[:], op=ALU.add)
                        v.tensor_tensor(out=e3[:], in0=Qt[:, cc, 0:TT], in1=e3[:], op=ALU.add)
                        v.reciprocal(out=e3[:], in_=e3[:])
                        v.tensor_tensor(out=e2[:], in0=e2[:], in1=e3[:], op=ALU.mult)
                        return v.tensor_tensor(out=sry_t[:, cc], in0=e2[:], in1=rb[:], op=ALU.mult)
                    o_scan = sq.emit("v", scan_b, after=[o_scana])
                    prev_scan[cc] = o_scan

                prev_store = sq.dma("dg", sry_r[:, :, i * TT:(i + 1) * TT], sry_t[:],
                                    after=[prev_scan[NCH - 1]])
            sq.set_floor()

        with ExitStack() as ph:
            psb = lambda *a: ph.enter_context(nc.sbuf_tensor(*a))
            ppt = lambda *a: ph.enter_context(nc.psum_tensor(*a))
            wo_sb = psb("wo_sb", [128, NCH, D], BF16)
            o_wo = sq.dma("dg", wo_sb[:], woT.rearrange("(c p) f -> p c f", p=128))
            sy_in = psb("sy_in", [128, 2, NCH, TT], BF16)
            so_t = psb("so_t", [128, NCH, TT], BF16)
            sps = ppt("sps", [128, 2, TT], F32)
            sry_r = sry_h.rearrange("(c p) t -> p c t", p=128)
            s_r = s_hbm.rearrange("(c p) t -> p c t", p=128)
            prev_ev = [None, None]
            prev_st = None
            ld_user = [None, None]
            for i in range(NTT):
                dep = [] if ld_user[i % 2] is None else [ld_user[i % 2]]
                ld = sq.dma("ds", sy_in[:, i % 2], sry_r[:, :, i * TT:(i + 1) * TT], after=dep)
                last_ev = None
                for cc in range(NCH):
                    pb = (i * NCH + cc) % 2

                    def proj(t, cc=cc, i=i, pb=pb):
                        ins = None
                        for dc in range(NCH):
                            ins = t.matmul(sps[:, pb], wo_sb[:, dc, cc * 128:(cc + 1) * 128],
                                           sy_in[:, i % 2, dc], start=(dc == 0), stop=(dc == NCH - 1))
                        return ins
                    pdep = [ld, o_wo]
                    if prev_ev[pb] is not None:
                        pdep.append(prev_ev[pb])
                    o_p = sq.emit("t", proj, after=pdep)

                    def evac(a, cc=cc, pb=pb):
                        return a.copy(out=so_t[:, cc], in_=sps[:, pb])
                    edep = [o_p] + ([prev_st] if prev_st is not None else [])
                    o_e = sq.emit("a", evac, after=edep)
                    prev_ev[pb] = o_e
                    last_ev = o_e
                ld_user[i % 2] = last_ev
                prev_st = sq.dma("dg", s_r[:, :, i * TT:(i + 1) * TT], so_t[:], after=[last_ev])
            sq.set_floor()

        with ExitStack() as ph:
            psb = lambda *a: ph.enter_context(nc.sbuf_tensor(*a))
            ppt = lambda *a: ph.enter_context(nc.psum_tensor(*a))
            NFH = 4 * D // 128
            b1_sb = psb("b1_sb", [128, 32], F32)
            b2_sb = psb("b2_sb", [128, 8], F32)
            o_b1 = sq.dma("ds", b1_sb[:], b1f[:, :])
            o_b2 = sq.dma("ds", b2_sb[:], b2f[:, :])
            w1_t = psb("w1_t", [128, 4, NCH, 128], BF16)
            w2_t = psb("w2_t", [128, 4, NFH, 128], BF16)
            hbuf = psb("hbuf", [128, NFH, TT], BF16)
            gbuf = psb("gbuf", [128, TT], BF16)
            xres = psb("xres", [128, 2, NCH, TT], F32)
            sres = psb("sres", [128, 2, NCH, TT], BF16)
            onew = psb("onew", [128, NCH, TT], F32)
            fps = ppt("fps", [128, 2, TT], F32)
            gps = ppt("gps", [128, 2, TT], F32)
            w1r = w1t_h.rearrange("(c p) f -> p c f", p=128)
            w2r = w2t_h.rearrange("(c p) f -> p c f", p=128)
            xi_r = x_in.rearrange("(c p) t -> p c t", p=128)
            xo_r = x_out.rearrange("(c p) t -> p c t", p=128)
            s_r = s_hbm.rearrange("(c p) t -> p c t", p=128)

            prev_hev = [None, None]
            prev_gev = [None, None]
            prev_resid = None
            prev_xst = None
            h1hist = []
            h2hist = []
            xld_user = [None, None]
            out_stores = []
            for i in range(NTT):
                dep = [] if xld_user[i % 2] is None else [xld_user[i % 2]]
                o_lx = sq.dma("ds", xres[:, i % 2], xi_r[:, :, i * TT:(i + 1) * TT], after=dep)
                o_ls = sq.dma("ds", sres[:, i % 2], s_r[:, :, i * TT:(i + 1) * TT], after=dep)
                h_evs = []
                for fc in range(NFH):
                    wb = (i * NFH + fc) % 4
                    wdep = [h1hist[-2]] if len(h1hist) >= 2 else []
                    o_lw = sq.dma("ds", w1_t[:, wb], w1r[:, :, fc * 128:(fc + 1) * 128], after=wdep)

                    def hmm(t, fc=fc, wb=wb, i=i):
                        ins = None
                        for dc in range(NCH):
                            ins = t.matmul(fps[:, fc % 2], w1_t[:, wb, dc],
                                           xn[:, dc, 1 + i * TT:1 + (i + 1) * TT],
                                           start=(dc == 0), stop=(dc == NCH - 1))
                        return ins
                    hdep = [o_lw, ln_done[i]]
                    if prev_hev[fc % 2] is not None:
                        hdep.append(prev_hev[fc % 2])
                    o_hm = sq.emit("t", hmm, after=hdep)
                    h1hist.append(o_hm)

                    def hev(a, fc=fc):
                        return a.activation(out=hbuf[:, fc], in_=fps[:, fc % 2], func=AF.Gelu,
                                            bias=b1_sb[:, fc:fc + 1])
                    o_he = sq.emit("a", hev, after=[o_hm, o_b1])
                    prev_hev[fc % 2] = o_he
                    h_evs.append(o_he)
                    if debug:
                        hr = h_h.rearrange("(c p) t -> p c t", p=128)
                        o_hd = sq.dma("dg", hr[:, fc, i * TT:(i + 1) * TT], hbuf[:, fc], after=[o_he])
                        prev_hev[fc % 2] = o_hd
                        h_evs[-1] = o_hd

                res_ops = []
                for dc in range(NCH):
                    wb = (i * NCH + dc) % 4
                    wdep = [h2hist[-2]] if len(h2hist) >= 2 else []
                    o_lw2 = sq.dma("ds", w2_t[:, wb], w2r[:, :, dc * 128:(dc + 1) * 128], after=wdep)

                    def gmm(t, dc=dc, wb=wb):
                        ins = None
                        for hc in range(NFH):
                            ins = t.matmul(gps[:, dc % 2], w2_t[:, wb, hc], hbuf[:, hc],
                                           start=(hc == 0), stop=(hc == NFH - 1))
                        return ins
                    gdep = [o_lw2] + h_evs
                    if prev_gev[dc % 2] is not None:
                        gdep.append(prev_gev[dc % 2])
                    o_gm = sq.emit("t", gmm, after=gdep)
                    h2hist.append(o_gm)

                    def gev(a, dc=dc):
                        return a.activation(out=gbuf[:], in_=gps[:, dc % 2], func=AF.Silu,
                                            bias=b2_sb[:, dc:dc + 1])
                    gevdep = [o_gm, o_b2]
                    if prev_resid is not None:
                        gevdep.append(prev_resid)
                    o_ge = sq.emit("a", gev, after=gevdep)
                    prev_gev[dc % 2] = o_ge
                    o_gd = None
                    if debug:
                        gr = g_h.rearrange("(c p) t -> p c t", p=128)
                        o_gd = sq.dma("dg", gr[:, dc, i * TT:(i + 1) * TT], gbuf[:], after=[o_ge])

                    def resid(v, dc=dc, i=i):
                        v.tensor_tensor(out=onew[:, dc], in0=sres[:, i % 2, dc], in1=gbuf[:], op=ALU.mult)
                        return v.tensor_tensor(out=onew[:, dc], in0=onew[:, dc], in1=xres[:, i % 2, dc], op=ALU.add)
                    rdep = [o_ge, o_lx, o_ls]
                    if prev_xst is not None:
                        rdep.append(prev_xst)
                    o_r = sq.emit("v", resid, after=rdep)
                    prev_resid = o_r if not debug else sq.emit("v", lambda v: v.nop(), after=[o_r, o_gd])
                    res_ops.append(o_r)
                xld_user[i % 2] = res_ops[-1]
                prev_xst = sq.dma("dg", xo_r[:, :, i * TT:(i + 1) * TT], onew[:], after=res_ops)
                out_stores.append(prev_xst)

            if ctx is None:
                sq.finalize(block, sems, outputs_done=out_stores)
            else:
                ctx["out_stores"] = out_stores
                sq.set_floor()
    return nc



def fm(a):
    a = np.asarray(a).reshape(-1)
    return np.ascontiguousarray(a.reshape(-1, 128).T.astype(np.float32))


def l3_inmap(xcore_fm, I, pre=""):
    g = lambda n: I[pre + n] if pre else I[n]
    return {
        "x_in": xcore_fm,
        "wkT": np.ascontiguousarray(g("L3_attn_W_key").T),
        "wvT": np.ascontiguousarray(g("L3_attn_W_value").T),
        "wrT": np.ascontiguousarray(g("L3_attn_W_receptance").T),
        "woT": np.ascontiguousarray(g("L3_attn_W_output").T),
        "w1T": np.ascontiguousarray(g("L3_ffn_W1").T),
        "w2T": np.ascontiguousarray(g("L3_ffn_W2").T),
        "b1f": fm(g("L3_ffn_b1")),
        "b2f": fm(g("L3_ffn_b2")),
        "tdf": fm(g("L3_attn_time_decay")),
        "tff": fm(g("L3_attn_time_first")),
        "tmkf": fm(g("L3_attn_time_mix_k")),
        "tmvf": fm(g("L3_attn_time_mix_v")),
        "tmrf": fm(g("L3_attn_time_mix_r")),
        "ones128_h": np.ones((128, 1), np.float32),
        "ones1_h": np.ones((1, 128), np.float32),
    }





def build_l2(debug=False, upto=99, ctx=None):
    from contextlib import ExitStack
    nc = bass.Bass("TRN2", target_bir_lowering=False, debug=False) if ctx is None else ctx["nc"]
    x_in = nc.dram_tensor("x_in", [D, S], F32, kind="ExternalInput").ap()
    wqkvT = nc.dram_tensor("wqkvT", [D, 3 * D], F32, kind="ExternalInput").ap()
    woutT = nc.dram_tensor("woutT", [D, D], F32, kind="ExternalInput").ap()
    boutf = nc.dram_tensor("boutf", [128, 8], F32, kind="ExternalInput").ap()
    projh = nc.dram_tensor("projh", [8, 128, 128], F32, kind="ExternalInput").ap()
    w1T = nc.dram_tensor("w1T", [D, 4 * D], F32, kind="ExternalInput").ap()
    w2T = nc.dram_tensor("w2T", [4 * D, D], F32, kind="ExternalInput").ap()
    w1N = nc.dram_tensor("w1N", [4 * D, D], F32, kind="ExternalInput").ap()
    w2N = nc.dram_tensor("w2N", [D, 4 * D], F32, kind="ExternalInput").ap()
    b1f = nc.dram_tensor("b1f", [128, 32], F32, kind="ExternalInput").ap()
    b2f = nc.dram_tensor("b2f", [128, 8], F32, kind="ExternalInput").ap()
    iden_h = nc.dram_tensor("iden_h", [128, 128], BF16, kind="ExternalInput").ap()
    ones128_h = nc.dram_tensor("ones128_h", [128, 1], F32, kind="ExternalInput").ap()
    ones1_h = nc.dram_tensor("ones1_h", [1, 128], F32, kind="ExternalInput").ap()
    x_out = nc.dram_tensor("x_out", [D, S], F32, kind="ExternalOutput").ap()

    kind = "ExternalOutput" if debug else "Internal"
    qf_h = nc.dram_tensor("qf_h", [D, S], BF16, kind=kind).ap()
    kf_h = nc.dram_tensor("kf_h", [D, S], BF16, kind=kind).ap()
    vt_h = nc.dram_tensor("vt_h", [S, 8 * 129], BF16, kind=kind).ap()
    of_h = nc.dram_tensor("of_h", [D, S], BF16, kind=kind).ap()
    s_hbm = nc.dram_tensor("s_hbm", [D, S], BF16, kind=kind).ap()
    w1s_h = nc.dram_tensor("w1s_h", [D, 4 * D], BF16, kind="Internal").ap()
    w2s_h = nc.dram_tensor("w2s_h", [4 * D, D], BF16, kind="Internal").ap()
    xn_h = nc.dram_tensor("xn_h", [D, S], BF16, kind=kind).ap() if debug else None
    dbg_h = nc.dram_tensor("dbg_h", [4, 128, 520], F32, kind=kind).ap() if debug else None

    stack = ExitStack()
    with stack:
        if ctx is None:
            sems = alloc_sems(nc, stack)
            block = stack.enter_context(nc.Block())
            sq = Seq(nc)
        else:
            sems, block, sq = ctx["sems"], ctx["block"], ctx["sq"]
        sb = lambda *a: stack.enter_context(nc.sbuf_tensor(*a))

        ones128 = sb("c_ones128", [128, 1], F32)
        ones1 = sb("c_ones1", [1, 128], F32)
        iden = sb("c_iden", [128, 128], BF16)
        cst = sb("cst", [128, 2], F32)
        xn = sb("xn", [128, NCH, S], BF16)

        sq.dma("ds", ones128[:], ones128_h[:, :])
        sq.dma("ds", ones1[:], ones1_h[:, :])
        sq.dma("ds", iden[:], iden_h[:, :])

        def pre(v):
            return v.memset(cst[:, 0:1], 1e-5)
        sq.emit("v", pre)
        sq.set_floor()

        with ExitStack() as ph:
            ln_done = emit_ln(sq, nc, ph, x_in, xn, ones128, ones1, cst, col_off=0)
            sq.set_floor()
            if debug:
                xn_r = xn_h.rearrange("(c p) t -> p c t", p=128)
                for i in range(NTT):
                    sq.dma("dg", xn_r[:, :, i * TT:(i + 1) * TT], xn[:, :, i * TT:(i + 1) * TT])
                sq.set_floor()

        qf_r = qf_h.rearrange("(c p) t -> p c t", p=128)
        kf_r = kf_h.rearrange("(c p) t -> p c t", p=128)
        wq_r = wqkvT.rearrange("(c p) f -> p c f", p=128)

        with ExitStack() as ph:
            psb = lambda *a: ph.enter_context(nc.sbuf_tensor(*a))
            ppt = lambda *a: ph.enter_context(nc.psum_tensor(*a))
            wt = psb("wt", [128, 2, NCH, 128], BF16)
            ob = psb("ob", [128, 2, TT], BF16)
            vtm = psb("vtm", [128, 8, 129], BF16)
            qps = ppt("qps", [128, 2, TT], F32)
            vps = ppt("vps", [128, 2, 512], F32)
            lastw = [None, None]
            prev_ev = [None, None]
            prev_st = [None, None]
            for fc in range(16):
                wb = fc % 2
                wdep = [sq.last["t"]] if sq.last.get("t") is not None else []
                o_lw = sq.dma("dg", wt[:, wb], wq_r[:, :, fc * 128:(fc + 1) * 128], after=wdep)
                for i in range(NTT):
                    pb = (fc * NTT + i) % 2

                    def mm(t, fc=fc, wb=wb, i=i, pb=pb):
                        ins = None
                        for dc in range(NCH):
                            ins = t.matmul(qps[:, pb], wt[:, wb, dc], xn[:, dc, i * TT:(i + 1) * TT],
                                           start=(dc == 0), stop=(dc == NCH - 1))
                        return ins
                    mdep = [o_lw, ln_done[i]]
                    if prev_ev[pb] is not None:
                        mdep.append(prev_ev[pb])
                    o_mm = sq.emit("t", mm, after=mdep)
                    lastw[wb] = o_mm

                    def ev(a, pb=pb):
                        return a.copy(out=ob[:, pb], in_=qps[:, pb])
                    edep = [o_mm]
                    if prev_st[pb] is not None:
                        edep.append(prev_st[pb])
                    o_ev = sq.emit("a", ev, after=edep)
                    prev_ev[pb] = o_ev
                    dst = qf_r[:, fc, i * TT:(i + 1) * TT] if fc < 8 else kf_r[:, fc - 8, i * TT:(i + 1) * TT]
                    prev_st[pb] = sq.dma("dg", dst, ob[:, pb], after=[o_ev])
            def vones(v):
                ins = None
                for h in range(8):
                    ins = v.memset(vtm[:, h, 128:129], 1.0)
                return ins
            o_vone = sq.emit("v", vones)
            vwdep = [sq.last["t"]] if sq.last.get("t") is not None else []
            vw = psb("vw", [128, NCH, 1024], BF16)
            o_vw = sq.dma("dg", vw[:], wq_r[:, :, 2048:3072], after=vwdep)
            prev_vst = None
            prev_vev = None
            for i2 in range(S // 128):

                def vmm(t, i2=i2):
                    ins = None
                    for half in range(2):
                        for dc in range(NCH):
                            ins = t.matmul(vps[:, half], xn[:, dc, i2 * 128:(i2 + 1) * 128],
                                           vw[:, dc, half * 512:(half + 1) * 512],
                                           start=(dc == 0), stop=(dc == NCH - 1))
                    return ins
                vdep = [o_vw, ln_done[min(i2 // 4, NTT - 1)]]
                if prev_vev is not None:
                    vdep.append(prev_vev)
                o_vm = sq.emit("t", vmm, after=vdep)

                def vev(a, i2=i2):
                    ins = None
                    for h in range(8):
                        half, off = divmod(h * 128, 512)
                        ins = a.copy(out=vtm[:, h, 0:128], in_=vps[:, half, off:off + 128]) if False else                             a.copy(out=vtm[:, h, 0:128], in_=vps[:, half, off:off + 128])
                    return ins
                vedep = [o_vm, o_vone]
                if prev_vst is not None:
                    vedep.append(prev_vst)
                o_ve = sq.emit("a", vev, after=vedep)
                prev_vev = o_ve
                prev_vst = sq.dma("dg", vt_h[i2 * 128:(i2 + 1) * 128, :], vtm[:], after=[o_ve])
            sq.set_floor()

        with ExitStack() as ph:
            psb = lambda *a: ph.enter_context(nc.sbuf_tensor(*a))
            ppt = lambda *a: ph.enter_context(nc.psum_tensor(*a))
            prj = psb("prj", [128, 8, 128], BF16)
            o_prj = sq.dma("dg", prj[:], projh.transpose((1, 0, 2)))
            qs = psb("qs", [128, 2, 128], BF16)
            ks = psb("ks", [128, 2, 128], BF16)
            vts = psb("vts", [128, 2, 8, 129], BF16)
            kp_tm = psb("kp_tm", [128, 128], BF16)
            qp_fm = psb("qp_fm", [128, 128], BF16)
            kv_sb = psb("kv_sb", [128, 4, 129], BF16)
            et = psb("et", [128, 130], F32)
            bcs = psb("bcs", [128, 128], F32)
            obuf = psb("obuf", [128, NCH, 128], BF16)
            pps = ppt("pps", [128, 2, 128], F32)
            kvps = ppt("kvps", [128, 4, 512], F32)
            nups = ppt("nups", [128, 4, 32], F32)
            zps = ppt("zps", [1, 4, 32], F32)
            bcps = ppt("bcps", [128, 128], F32)
            of_r = of_h.rearrange("(c p) t -> p c t", p=128)

            prev = {}
            prev_ost = None
            for i2 in range(S // 128):
                for h in range(8):
                    hb = i2 % 2
                    dep_h = [sq.last["t"]] if sq.last.get("t") is not None else []
                    o_lq = sq.dma("ds", qs[:, hb], qf_r[:, h, i2 * 128:(i2 + 1) * 128], after=dep_h)
                    o_lk = sq.dma("ds", ks[:, hb], kf_r[:, h, i2 * 128:(i2 + 1) * 128], after=dep_h)
                    if h == 0:
                        o_lv = sq.dma("ds", vts[:, hb], vt_h[i2 * 128:(i2 + 1) * 128, :], after=dep_h)
                        prev[("lv", hb)] = o_lv
                    o_lv = prev[("lv", hb)]

                    def pmm(t, h=h, hb=hb):
                        t.matmul(pps[:, 0], ks[:, hb], prj[:, h], start=True, stop=True)
                        return t.matmul(pps[:, 1], prj[:, h], qs[:, hb], start=True, stop=True)
                    pdep = [o_lq, o_lk, o_prj]
                    if ("pev",) in prev:
                        pdep.append(prev[("pev",)])
                    o_pm = sq.emit("t", pmm, after=pdep)

                    o_e1 = sq.emit("v", lambda v: v.tensor_scalar(out=et[:, 0:128], in0=pps[:, 0], scalar1=0.0, scalar2=None, op0=ALU.min), after=[o_pm])
                    o_e2 = sq.emit("a", lambda a: a.activation(out=kp_tm[:], in_=et[:, 0:128], func=AF.Exp), after=[o_e1])
                    o_e3 = sq.emit("v", lambda v: v.scalar_tensor_tensor(out=kp_tm[:], in0=pps[:, 0], scalar=1.0, in1=kp_tm[:], op0=ALU.add, op1=ALU.max), after=[o_e2])
                    o_f1 = sq.emit("v", lambda v: v.tensor_scalar(out=et[:, 0:128], in0=pps[:, 1], scalar1=0.0, scalar2=None, op0=ALU.min), after=[o_pm, o_e3])
                    o_f2 = sq.emit("a", lambda a: a.activation(out=qp_fm[:], in_=et[:, 0:128], func=AF.Exp), after=[o_f1])
                    o_f3 = sq.emit("v", lambda v: v.scalar_tensor_tensor(out=qp_fm[:], in0=pps[:, 1], scalar=1.0, in1=qp_fm[:], op0=ALU.add, op1=ALU.max), after=[o_f2])
                    prev[("pev",)] = o_f3

                    def kvmm(t, hb=hb, h=h):
                        ins = None
                        for j in range(4):
                            ins = t.matmul(kvps[:, j, 0:129], kp_tm[32 * j:32 * (j + 1), :],
                                           vts[32 * j:32 * (j + 1), hb, h], start=True, stop=True,
                                           tile_position=(32 * j, 0))
                        return ins
                    kvdep = [o_e3, o_lv]
                    if ("kvev",) in prev:
                        kvdep.append(prev[("kvev",)])
                    o_kv = sq.emit("t", kvmm, after=kvdep)

                    def kvev(a):
                        ins = None
                        for j in range(4):
                            ins = a.copy(out=kv_sb[:, j], in_=kvps[:, j, 0:129])
                        return ins
                    kvedep = [o_kv]
                    if ("num",) in prev:
                        kvedep.append(prev[("num",)])
                    o_kve = sq.emit("a", kvev, after=kvedep)
                    prev[("kvev",)] = o_kve

                    def numm(t, h=h):
                        ins = None
                        for j in range(4):
                            t.matmul(nups[:, j], kv_sb[:, j, 0:128], qp_fm[:, 32 * j:32 * (j + 1)],
                                     start=True, stop=True)
                            ins = t.matmul(zps[0:1, j], kv_sb[:, j, 128:129], qp_fm[:, 32 * j:32 * (j + 1)],
                                           start=True, stop=True)
                        return ins
                    ndep = [o_f3, o_kve]
                    if ("nev",) in prev:
                        ndep.append(prev[("nev",)])
                    o_nu = sq.emit("t", numm, after=ndep)
                    prev[("num",)] = o_nu

                    def nrm_a(v):
                        v.tensor_scalar(out=et[0:1, 0:128], in0=zps[0:1, :].rearrange("p a b -> p (a b)"),
                                        scalar1=1e-8, scalar2=None, op0=ALU.add)
                        return v.reciprocal(out=et[0:1, 128:129] if False else et[0:1, 0:128], in_=et[0:1, 0:128])
                    o_na = sq.emit("v", nrm_a, after=[o_nu], hazard=True)

                    def bcmm(t):
                        return t.matmul(bcps[:], ones1[:], et[0:1, 0:128], start=True, stop=True)
                    bcdep = [o_na]
                    if ("omul",) in prev:
                        bcdep.append(prev[("omul",)])
                    o_bc = sq.emit("t", bcmm, after=bcdep)

                    def bcev(a):
                        return a.copy(out=bcs[:], in_=bcps[:])
                    o_bce = sq.emit("a", bcev, after=[o_bc])

                    def omul(v, h=h):
                        return v.tensor_tensor(out=obuf[:, h], in0=nups[:, :].rearrange("p a b -> p (a b)"),
                                               in1=bcs[:], op=ALU.mult)
                    odep = [o_bce, o_nu]
                    if prev_ost is not None:
                        odep.append(prev_ost)
                    o_om = sq.emit("v", omul, after=odep)
                    prev[("omul",)] = o_om
                    prev[("nev",)] = o_om
                    prev[("onorm", h)] = o_om

                prev_ost = sq.dma("dg", of_r[:, :, i2 * 128:(i2 + 1) * 128], obuf[:],
                                  after=[prev[("onorm", h2)] for h2 in range(8)])
            sq.set_floor()

        if upto <= 2:
            sq.finalize(block, sems)
            return nc

        with ExitStack() as ph:
            psb = lambda *a: ph.enter_context(nc.sbuf_tensor(*a))
            ppt = lambda *a: ph.enter_context(nc.psum_tensor(*a))
            wo_sb = psb("wo_sb", [128, NCH, D], BF16)
            bo_sb = psb("bo_sb", [128, 8], F32)
            sq.dma("dg", wo_sb[:], woutT.rearrange("(c p) f -> p c f", p=128))
            sq.dma("ds", bo_sb[:], boutf[:, :])
            sq.set_floor()
            oin = psb("oin", [128, 2, NCH, TT], BF16)
            so_t = psb("so_t", [128, NCH, TT], BF16)
            sps = ppt("sps", [128, 2, TT], F32)
            s_r = s_hbm.rearrange("(c p) t -> p c t", p=128)
            prev_ev = [None, None]
            prev_st = None
            ld_user = [None, None]
            for i in range(NTT):
                dep = [] if ld_user[i % 2] is None else [ld_user[i % 2]]
                ld = sq.dma("ds", oin[:, i % 2], of_r[:, :, i * TT:(i + 1) * TT], after=dep)
                last_ev = None
                for cc in range(NCH):
                    pb = (i * NCH + cc) % 2

                    def proj(t, cc=cc, i=i, pb=pb):
                        ins = None
                        for dc in range(NCH):
                            ins = t.matmul(sps[:, pb], wo_sb[:, dc, cc * 128:(cc + 1) * 128],
                                           oin[:, i % 2, dc], start=(dc == 0), stop=(dc == NCH - 1))
                        return ins
                    pdep = [ld]
                    if prev_ev[pb] is not None:
                        pdep.append(prev_ev[pb])
                    o_p = sq.emit("t", proj, after=pdep)

                    def evac(a, cc=cc, pb=pb):
                        return a.activation(out=so_t[:, cc], in_=sps[:, pb], func=AF.Identity,
                                            bias=bo_sb[:, cc:cc + 1])
                    edep = [o_p] + ([prev_st] if prev_st is not None else [])
                    o_e = sq.emit("a", evac, after=edep)
                    prev_ev[pb] = o_e
                    last_ev = o_e
                ld_user[i % 2] = last_ev
                prev_st = sq.dma("dg", s_r[:, :, i * TT:(i + 1) * TT], so_t[:], after=[last_ev])
            sq.set_floor()

        if upto <= 3:
            sq.finalize(block, sems)
            return nc

        a1_sb = sb("a1_sb", [128, 32], F32)
        a2_sb = sb("a2_sb", [128, 8], F32)
        with ExitStack() as ph:
            psb = lambda *a: ph.enter_context(nc.sbuf_tensor(*a))
            wn = psb("wn", [128, 2, 4096], F32)
            wq = psb("wq", [128, 4096], BF16)
            for wi, (whn, whT, wst, asb, nrt, ncolN, ncolT) in enumerate([
                    (w1N, w1T, w1s_h, a1_sb, 32, 1024, 4096), (w2N, w2T, w2s_h, a2_sb, 8, 4096, 1024)]):
                reds = []
                for r in range(nrt):
                    dep = [reds[r - 2]] if r >= 2 else []
                    ld = sq.dma("ds", wn[:, r % 2, 0:ncolN], whn[r * 128:(r + 1) * 128, :], after=dep)

                    def red(v, r=r, wi=wi, ncolN=ncolN, asb=asb):
                        return v.tensor_reduce(out=asb[:, r:r + 1], in_=wn[:, r % 2, 0:ncolN], axis=AX.X,
                                               op=ALU.add, apply_absolute_value=True)
                    reds.append(sq.emit("v", red, after=[ld], hazard=True))

                def ascale(v, asb=asb, nrt=nrt, ncolN=ncolN):
                    return v.tensor_scalar(out=asb[:, 0:nrt], in0=asb[:, 0:nrt], scalar1=1.0 / ncolN,
                                           scalar2=None, op0=ALU.mult)
                o_as = sq.emit("v", ascale, after=reds[-2:], hazard=True)
                sq.set_floor()
                nrtT = (ncolN * nrt * 128 // ncolT) // 128
                prev_q = None
                for r in range(nrtT):
                    dep = [prev_q] if prev_q is not None else []
                    ld = sq.dma("ds", wn[:, r % 2, 0:ncolT], whT[r * 128:(r + 1) * 128, :], after=dep)

                    def sgn(a, r=r, ncolT=ncolT):
                        return a.activation(out=wq[:, 0:ncolT], in_=wn[:, r % 2, 0:ncolT], func=AF.Sign)
                    o_s = sq.emit("a", sgn, after=[ld])
                    prev_q = sq.dma("dg", wst[r * 128:(r + 1) * 128, :], wq[:, 0:ncolT], after=[o_s])
                sq.set_floor()

        with ExitStack() as ph:
            psb = lambda *a: ph.enter_context(nc.sbuf_tensor(*a))
            ppt = lambda *a: ph.enter_context(nc.psum_tensor(*a))
            NFH = 4 * D // 128
            b1_sb = psb("b1_sb", [128, 32], F32)
            b2_sb = psb("b2_sb", [128, 8], F32)
            sq.dma("ds", b1_sb[:], b1f[:, :])
            sq.dma("ds", b2_sb[:], b2f[:, :])
            sq.set_floor()
            w1_t = psb("w1_t", [128, 4, NCH, 128], BF16)
            w2_t = psb("w2_t", [128, 4, NFH, 128], BF16)
            hbuf = psb("hbuf", [128, NFH, TT], BF16)
            gbuf = psb("gbuf", [128, TT], BF16)
            xres = psb("xres", [128, 2, NCH, TT], F32)
            sres = psb("sres", [128, 2, NCH, TT], BF16)
            onew = psb("onew", [128, NCH, TT], F32)
            fps = ppt("fps", [128, 2, TT], F32)
            gps = ppt("gps", [128, 2, TT], F32)
            w1r = w1s_h.rearrange("(c p) f -> p c f", p=128)
            w2r = w2s_h.rearrange("(c p) f -> p c f", p=128)
            xi_r = x_in.rearrange("(c p) t -> p c t", p=128)
            xo_r = x_out.rearrange("(c p) t -> p c t", p=128)
            s_r = s_hbm.rearrange("(c p) t -> p c t", p=128)

            prev_hev = [None, None]
            prev_gev = [None, None]
            prev_resid = None
            prev_xst = None
            h1hist = []
            h2hist = []
            xld_user = [None, None]
            out_stores = []
            for i in range(NTT):
                dep = [] if xld_user[i % 2] is None else [xld_user[i % 2]]
                o_lx = sq.dma("ds", xres[:, i % 2], xi_r[:, :, i * TT:(i + 1) * TT], after=dep)
                o_ls = sq.dma("ds", sres[:, i % 2], s_r[:, :, i * TT:(i + 1) * TT], after=dep)
                h_evs = []
                for fc in range(NFH):
                    wb = (i * NFH + fc) % 4
                    wdep = [h1hist[-2]] if len(h1hist) >= 2 else []
                    o_lw = sq.dma("ds", w1_t[:, wb], w1r[:, :, fc * 128:(fc + 1) * 128], after=wdep)

                    def hmm(t, fc=fc, wb=wb, i=i):
                        ins = None
                        for dc in range(NCH):
                            ins = t.matmul(fps[:, fc % 2], w1_t[:, wb, dc], xn[:, dc, i * TT:(i + 1) * TT],
                                           start=(dc == 0), stop=(dc == NCH - 1))
                        return ins
                    hdep = [o_lw, ln_done[i]]
                    if prev_hev[fc % 2] is not None:
                        hdep.append(prev_hev[fc % 2])
                    o_hm = sq.emit("t", hmm, after=hdep)
                    h1hist.append(o_hm)

                    def hev(a, fc=fc):
                        return a.activation(out=hbuf[:, fc], in_=fps[:, fc % 2], func=AF.Gelu,
                                            scale=a1_sb[:, fc:fc + 1], bias=b1_sb[:, fc:fc + 1])
                    o_he = sq.emit("a", hev, after=[o_hm])
                    prev_hev[fc % 2] = o_he
                    h_evs.append(o_he)

                res_ops = []
                for dc in range(NCH):
                    wb = (i * NCH + dc) % 4
                    wdep = [h2hist[-2]] if len(h2hist) >= 2 else []
                    o_lw2 = sq.dma("ds", w2_t[:, wb], w2r[:, :, dc * 128:(dc + 1) * 128], after=wdep)

                    def gmm(t, dc=dc, wb=wb):
                        ins = None
                        for hc in range(NFH):
                            ins = t.matmul(gps[:, dc % 2], w2_t[:, wb, hc], hbuf[:, hc],
                                           start=(hc == 0), stop=(hc == NFH - 1))
                        return ins
                    gdep = [o_lw2] + h_evs
                    if prev_gev[dc % 2] is not None:
                        gdep.append(prev_gev[dc % 2])
                    o_gm = sq.emit("t", gmm, after=gdep)
                    h2hist.append(o_gm)

                    def gev(a, dc=dc):
                        return a.activation(out=gbuf[:], in_=gps[:, dc % 2], func=AF.Silu,
                                            scale=a2_sb[:, dc:dc + 1], bias=b2_sb[:, dc:dc + 1])
                    gevdep = [o_gm]
                    if prev_resid is not None:
                        gevdep.append(prev_resid)
                    o_ge = sq.emit("a", gev, after=gevdep)
                    prev_gev[dc % 2] = o_ge

                    def resid(v, dc=dc, i=i):
                        v.tensor_tensor(out=onew[:, dc], in0=sres[:, i % 2, dc], in1=gbuf[:], op=ALU.mult)
                        return v.tensor_tensor(out=onew[:, dc], in0=onew[:, dc], in1=xres[:, i % 2, dc], op=ALU.add)
                    rdep = [o_ge, o_lx, o_ls]
                    if prev_xst is not None:
                        rdep.append(prev_xst)
                    o_r = sq.emit("v", resid, after=rdep)
                    prev_resid = o_r
                    res_ops.append(o_r)
                xld_user[i % 2] = res_ops[-1]
                prev_xst = sq.dma("dg", xo_r[:, :, i * TT:(i + 1) * TT], onew[:], after=res_ops)
                out_stores.append(prev_xst)

            if ctx is None:
                sq.finalize(block, sems, outputs_done=out_stores)
            else:
                ctx["out_stores"] = out_stores
                sq.set_floor()
    return nc


def l2_inmap(xcore_fm, I):
    return {
        "x_in": xcore_fm,
        "wqkvT": np.ascontiguousarray(I["L2_attn_w_qkv"].T),
        "woutT": np.ascontiguousarray(I["L2_attn_w_out"].T),
        "boutf": fm(I["L2_attn_b_out"]),
        "projh": np.ascontiguousarray(I["L2_attn_proj"]),
        "w1T": np.ascontiguousarray(I["L2_ffn_w1"].T),
        "w2T": np.ascontiguousarray(I["L2_ffn_w2"].T),
        "w1N": np.ascontiguousarray(I["L2_ffn_w1"]),
        "w2N": np.ascontiguousarray(I["L2_ffn_w2"]),
        "b1f": fm(I["L2_ffn_b1"]),
        "b2f": fm(I["L2_ffn_b2"]),
        "iden_h": np.eye(128, dtype=np.float32).astype(BF),
        "ones128_h": np.ones((128, 1), np.float32),
        "ones1_h": np.ones((1, 128), np.float32),
    }



def build_l1(debug=False, upto=99, ctx=None):
    from contextlib import ExitStack
    nc = bass.Bass("TRN2", target_bir_lowering=False, debug=False) if ctx is None else ctx["nc"]
    x_in = nc.dram_tensor("x_in", [D, S], F32, kind="ExternalInput").ap()
    wq = nc.dram_tensor("wq", [D, 256], F32, kind="ExternalInput").ap()
    wk = nc.dram_tensor("wk", [D, 256], F32, kind="ExternalInput").ap()
    wv = nc.dram_tensor("wv", [D, D], F32, kind="ExternalInput").ap()
    wo = nc.dram_tensor("wo", [D, D], F32, kind="ExternalInput").ap()
    wrT = nc.dram_tensor("wrT", [D, D], F32, kind="ExternalInput").ap()
    wkT2 = nc.dram_tensor("wkT2", [D, D], F32, kind="ExternalInput").ap()
    wvT2 = nc.dram_tensor("wvT2", [D, D], F32, kind="ExternalInput").ap()
    murf = nc.dram_tensor("murf", [128, 8], F32, kind="ExternalInput").ap()
    mukf = nc.dram_tensor("mukf", [128, 8], F32, kind="ExternalInput").ap()
    e1m = nc.dram_tensor("e1m", [16, 128, 128], BF16, kind="ExternalInput").ap()
    e2m = nc.dram_tensor("e2m", [16, 128, 128], BF16, kind="ExternalInput").ap()
    trimask = nc.dram_tensor("trimask", [128, 128], BF16, kind="ExternalInput").ap()
    iden_h = nc.dram_tensor("iden_h", [128, 128], BF16, kind="ExternalInput").ap()
    ones128_h = nc.dram_tensor("ones128_h", [128, 1], F32, kind="ExternalInput").ap()
    ones1_h = nc.dram_tensor("ones1_h", [1, 128], F32, kind="ExternalInput").ap()
    x_out = nc.dram_tensor("x_out", [D, S], F32, kind="ExternalOutput").ap()

    kind = "ExternalOutput" if debug else "Internal"
    xn_h = nc.dram_tensor("xn_h", [D, S], BF16, kind=kind).ap()
    qf_h = nc.dram_tensor("qf_h", [256, S], BF16, kind=kind).ap()
    kf_h = nc.dram_tensor("kf_h", [256, S], BF16, kind=kind).ap()
    kt_h = nc.dram_tensor("kt_h", [S, 256], BF16, kind=kind).ap()
    vt_h = nc.dram_tensor("vt_h", [S, 16 * 65], BF16, kind=kind).ap()
    of_h = nc.dram_tensor("of_h", [D, S], BF16, kind=kind).ap()
    s_hbm = nc.dram_tensor("s_hbm", [D, S], BF16, kind=kind).ap()
    dbg_h = nc.dram_tensor("dbg_h", [8, 128, 256], F32, kind=kind).ap() if debug else None

    stack = ExitStack()
    with stack:
        if ctx is None:
            sems = alloc_sems(nc, stack)
            block = stack.enter_context(nc.Block())
            sq = Seq(nc)
        else:
            sems, block, sq = ctx["sems"], ctx["block"], ctx["sq"]
        sb = lambda *a: stack.enter_context(nc.sbuf_tensor(*a))

        ones128 = sb("c_ones128", [128, 1], F32)
        ones1 = sb("c_ones1", [1, 128], F32)
        iden = sb("c_iden", [128, 128], BF16)
        trim = sb("c_trim", [128, 128], BF16)
        cst = sb("cst", [128, 2], F32)

        sq.dma("ds", ones128[:], ones128_h[:, :])
        sq.dma("ds", ones1[:], ones1_h[:, :])
        sq.dma("ds", iden[:], iden_h[:, :])
        sq.dma("ds", trim[:], trimask[:, :])
        sq.emit("v", lambda v: v.memset(cst[:, 0:1], 1e-5))
        sq.set_floor()

        phAB = ExitStack()
        xn = phAB.enter_context(nc.sbuf_tensor("xn", [128, NCH, S], BF16))
        with ExitStack() as ph:
            ln_done = emit_ln(sq, nc, ph, x_in, xn, ones128, ones1, cst, col_off=0)
            xn_r = xn_h.rearrange("(c p) t -> p c t", p=128)
            xn_st = []
            for i in range(NTT):
                xn_st.append(sq.dma("dg", xn_r[:, :, i * TT:(i + 1) * TT],
                                    xn[:, :, i * TT:(i + 1) * TT], after=[ln_done[i]]))
            sq.set_floor()

        if upto <= 0:
            phAB.close()
            sq.finalize(block, sems)
            return nc

        qf_r = qf_h.rearrange("(c p) t -> p c t", p=128)
        kf_r = kf_h.rearrange("(c p) t -> p c t", p=128)
        with ExitStack() as ph:
            psb = lambda *a: ph.enter_context(nc.sbuf_tensor(*a))
            ppt = lambda *a: ph.enter_context(nc.psum_tensor(*a))
            wqs = psb("wqs", [128, NCH, 256], BF16)
            wks = psb("wks", [128, NCH, 256], BF16)
            wvs = psb("wvs", [128, NCH, D], BF16)
            sq.dma("dg", wqs[:], wq.rearrange("(c p) f -> p c f", p=128))
            sq.dma("dg", wks[:], wk.rearrange("(c p) f -> p c f", p=128))
            sq.dma("dg", wvs[:], wv.rearrange("(c p) f -> p c f", p=128))
            sq.set_floor()
            ob = psb("ob", [128, 2, TT], BF16)
            ktt = psb("ktt", [128, 2, 256], BF16)
            vtm = psb("vtm", [128, 2, 16, 65], BF16)
            qps = ppt("qps", [128, 2, TT], F32)
            vps = ppt("vps", [128, 2, 512], F32)
            kps2 = ppt("kps2", [128, 256], F32)
            prev_ev = [None, None]
            prev_st = [None, None]
            for fc in range(4):
                for i in range(NTT):
                    pb = (fc * NTT + i) % 2
                    wsel = wqs if fc < 2 else wks
                    cchunk = fc % 2

                    def mm(t, wsel=wsel, cchunk=cchunk, i=i, pb=pb):
                        ins = None
                        for dc in range(NCH):
                            ins = t.matmul(qps[:, pb], wsel[:, dc, cchunk * 128:(cchunk + 1) * 128],
                                           xn[:, dc, i * TT:(i + 1) * TT],
                                           start=(dc == 0), stop=(dc == NCH - 1))
                        return ins
                    mdep = [ln_done[i]]
                    if prev_ev[pb] is not None:
                        mdep.append(prev_ev[pb])
                    o_mm = sq.emit("t", mm, after=mdep)

                    def ev(a, pb=pb, fc=fc):
                        return a.activation(out=ob[:, pb], in_=qps[:, pb], func=AF.Copy,
                                            scale=(0.25 if fc < 2 else 1.0))
                    edep = [o_mm]
                    if prev_st[pb] is not None:
                        edep.append(prev_st[pb])
                    o_ev = sq.emit("a", ev, after=edep)
                    prev_ev[pb] = o_ev
                    dst = qf_r[:, fc, i * TT:(i + 1) * TT] if fc < 2 else kf_r[:, fc - 2, i * TT:(i + 1) * TT]
                    prev_st[pb] = sq.dma("dg", dst, ob[:, pb], after=[o_ev])
            def vones(v):
                ins = None
                for hb in range(2):
                    for h in range(16):
                        ins = v.memset(vtm[:, hb, h, 64:65], 1.0)
                return ins
            o_vone = sq.emit("v", vones)
            prevk = [None, None]
            prevv = [None, None]
            prev_kev1 = None
            prev_vev1 = None
            for i2 in range(S // 128):
                hb = i2 % 2

                def ktm_mm(t, i2=i2):
                    ins = None
                    for dc in range(NCH):
                        ins = t.matmul(kps2[:, 0:256], xn[:, dc, i2 * 128:(i2 + 1) * 128],
                                       wks[:, dc, :], start=(dc == 0), stop=(dc == NCH - 1))
                    return ins
                kdep = [ln_done[min(i2 // 4, NTT - 1)]]
                if prevk[hb] is not None:
                    kdep.append(prevk[hb])
                if prev_kev1 is not None:
                    kdep.append(prev_kev1)
                o_km = sq.emit("t", ktm_mm, after=kdep)

                def ktm_ev(a, hb=hb):
                    return a.copy(out=ktt[:, hb], in_=kps2[:, 0:256])
                o_ke = sq.emit("a", ktm_ev, after=[o_km] + ([prevk[hb]] if prevk[hb] else []))
                prev_kev1 = o_ke
                o_ks = sq.dma("dg", kt_h[i2 * 128:(i2 + 1) * 128, :], ktt[:, hb], after=[o_ke])
                prevk[hb] = o_ks

                def vtm_mm(t, i2=i2):
                    ins = None
                    for half in range(2):
                        for dc in range(NCH):
                            ins = t.matmul(vps[:, half], xn[:, dc, i2 * 128:(i2 + 1) * 128],
                                           wvs[:, dc, half * 512:(half + 1) * 512],
                                           start=(dc == 0), stop=(dc == NCH - 1))
                    return ins
                vdep = [ln_done[min(i2 // 4, NTT - 1)]]
                if prevv[hb] is not None:
                    vdep.append(prevv[hb])
                if prev_vev1 is not None:
                    vdep.append(prev_vev1)
                o_vm = sq.emit("t", vtm_mm, after=vdep)

                def vtm_ev(a, hb=hb):
                    a.copy(out=vtm[:, hb, 0:8, 0:64], in_=vps[:, 0])
                    return a.copy(out=vtm[:, hb, 8:16, 0:64], in_=vps[:, 1])
                o_ve = sq.emit("a", vtm_ev, after=[o_vm, o_vone] + ([prevv[hb]] if prevv[hb] else []))
                prev_vev1 = o_ve
                o_vs = sq.dma("dg", vt_h[i2 * 128:(i2 + 1) * 128, :], vtm[:, hb], after=[o_ve])
                prevv[hb] = o_vs
            sq.set_floor()
        phAB.close()

        if upto <= 1:
            sq.finalize(block, sems)
            return nc

        with ExitStack() as ph:
            psb = lambda *a: ph.enter_context(nc.sbuf_tensor(*a))
            ppt = lambda *a: ph.enter_context(nc.psum_tensor(*a))
            qslot = psb("qslot", [128, 4, S], BF16)
            kslot = psb("kslot", [128, 4, S], BF16)
            qfm = psb("qfm", [128, 2, S], BF16)
            ktm = psb("ktm", [128, 32, 256], BF16)
            vtm2 = psb("vtm2", [128, 32, 16, 65], BF16)
            esel = psb("esel", [128, 4, 8, 128], BF16)
            Sst = psb("Sst", [128, 16, 2, 65], F32)
            Sbf = psb("Sbf", [128, 16, 2, 65], BF16)
            A2t = psb("A2t", [128, 128], BF16)
            phiq = psb("phiq", [128, 2, 128], BF16)
            phik = psb("phik", [128, 256], BF16)
            a1s = psb("a1s", [128, 128], F32)
            ebuf = psb("ebuf", [128, 4], F32)
            obuf = psb("obuf", [128, 16, 64], BF16)
            otr = psb("otr", [128, NCH, 128], BF16)
            dbgb = psb("dbgb", [128, 256], F32) if debug else None
            dbgb0 = psb("dbgb0", [128, 130], F32) if debug else None
            Aps = ppt("Aps", [128, 4, 512], F32)
            ops_ = ppt("ops_", [128, 65], F32)
            ups = ppt("ups", [128, 2, 256], F32)
            selps = ppt("selps", [128, 2, 128], F32)
            trps = ppt("trps", [128, 128], BF16)

            def zero_slots(v):
                v.memset(qslot[:], 0.0)
                v.memset(kslot[:], 0.0)
                v.memset(Sst[:], 0.0)
                return v.memset(Sbf[:], 0.0)
            o_zs = sq.emit("v", zero_slots)
            for sc in range(4):
                for b2 in range(4):
                    sq.dma("ds", qslot[32 * b2:32 * b2 + 16, sc, :],
                           qf_h[sc * 64 + 16 * b2:sc * 64 + 16 * b2 + 16, :], after=[o_zs])
                    sq.dma("ds", kslot[32 * b2:32 * b2 + 16, sc, :],
                           kf_h[sc * 64 + 16 * b2:sc * 64 + 16 * b2 + 16, :], after=[o_zs])
            sq.dma("ds", qfm[:], qf_h.rearrange("(c p) t -> p c t", p=128))
            sq.dma("ds", ktm[:], kt_h.rearrange("(r p) f -> p r f", p=128))
            sq.dma("ds", vtm2[:], vt_h.rearrange("(r p) (h f) -> p r h f", p=128, h=16))
            sq.dma("dg", esel[:, 0:2], e1m.transpose((1, 0, 2)).rearrange("p (x hh) f -> p x hh f", x=2))
            sq.dma("dg", esel[:, 2:4], e2m.transpose((1, 0, 2)).rearrange("p (x hh) f -> p x hh f", x=2))
            sq.set_floor()

            of_r = of_h.rearrange("(c p) t -> p c t", p=128)
            prev = {}
            prev_ost = None
            for i2 in range(S // 128):
                tsl = slice(i2 * 128, (i2 + 1) * 128)
                for h in range(16):
                    scn, j = divmod(h, 4)
                    if j == 0:
                        def amm(t, scn=scn, tsl=tsl):
                            ins = None
                            for jj in range(4):
                                ins = t.matmul(Aps[:, jj, 0:128], kslot[32 * jj:32 * (jj + 1), scn, tsl],
                                               qslot[32 * jj:32 * (jj + 1), scn, tsl], start=True, stop=True,
                                               tile_position=(32 * jj, 0))
                            return ins
                        adep = []
                        if ("sqdone",) in prev:
                            adep.append(prev[("sqdone",)])
                        prev[("amm", scn)] = sq.emit("t", amm, after=adep)
                    o_amm = prev[("amm", scn)]

                    o_sq = sq.emit("a", lambda a, j=j: a.activation(out=A2t[:], in_=Aps[:, j, 0:128], func=AF.Square),
                                   after=[o_amm] + ([prev[("a2used",)]] if ("a2used",) in prev else []))
                    o_msk = sq.emit("v", lambda v: v.tensor_tensor(out=A2t[:], in0=A2t[:], in1=trim[:], op=ALU.mult),
                                    after=[o_sq])
                    prev[("sqdone",)] = o_sq

                    hh = h % 8
                    xsel = 0 if h < 8 else 1
                    phis = []
                    for half in range(2):
                        def smm(t, hh=hh, half=half, xsel=xsel, tsl=tsl):
                            t.matmul(selps[:, 0], esel[:, half, hh, :], qfm[:, xsel, tsl], start=True, stop=True)
                            return t.matmul(selps[:, 1], esel[:, 2 + half, hh, :], qfm[:, xsel, tsl], start=True, stop=True)
                        sdep = []
                        if ("phimul",) in prev:
                            sdep.append(prev[("phimul",)])
                        if ("oused",) in prev:
                            sdep.append(prev[("oused",)])
                        o_sm = sq.emit("t", smm, after=sdep)
                        o_a1 = sq.emit("a", lambda a: a.copy(out=a1s[:], in_=selps[:, 0]),
                                       after=[o_sm] + ([prev[("phimul",)]] if ("phimul",) in prev else []))
                        phdep = [o_a1]
                        if ("oused",) in prev:
                            phdep.append(prev[("oused",)])
                        o_ph = sq.emit("v", lambda v, half=half: v.tensor_tensor(out=phiq[:, half], in0=selps[:, 1], in1=a1s[:], op=ALU.mult),
                                       after=phdep)
                        prev[("phimul",)] = o_ph
                        phis.append(o_ph)

                    def phik_op(v, h=h, i2=i2):
                        b1 = ktm[:, i2, 16 * h:16 * h + 16].unsqueeze(1).broadcast_to((128, 16, 16))
                        b2 = ktm[:, i2, 16 * h:16 * h + 16].unsqueeze(2).broadcast_to((128, 16, 16))
                        return v.tensor_tensor(out=phik[:].rearrange("p (a b) -> p a b", a=16), in0=b2, in1=b1, op=ALU.mult)
                    o_pk = sq.emit("v", phik_op, after=[prev[("ups",)]] if ("ups",) in prev else [])

                    def omm(t, h=h, i2=i2):
                        t.matmul(ops_[:, 0:65], phiq[:, 0], Sbf[:, h, 0, :], start=True, stop=False)
                        t.matmul(ops_[:, 0:65], phiq[:, 1], Sbf[:, h, 1, :], start=False, stop=False)
                        return t.matmul(ops_[:, 0:65], A2t[:], vtm2[:, i2, h, :], start=False, stop=True)
                    if debug and i2 == 0 and h == 0:
                        d_p = sq.emit("a", lambda a: a.copy(out=dbgb0[:, 0:130], in_=Sbf[:, 0].rearrange("p a b -> p (a b)")), after=phis)
                        sq.dma("dg", dbg_h[7, :, 0:130], dbgb0[:, 0:130], after=[d_p])
                    odep = phis + [o_msk]
                    if ("onrm",) in prev:
                        odep.append(prev[("onrm",)])
                    if ("supd_h", h) in prev:
                        odep.append(prev[("supd_h", h)])
                    if debug and i2 == 1 and h == 0:
                        d_p2 = sq.emit("a", lambda a: a.copy(out=dbgb0[:, 0:130], in_=Sbf[:, 0].rearrange("p a b -> p (a b)")), after=odep)
                        d_p2d = sq.dma("dg", dbg_h[4, :, 0:130], dbgb0[:, 0:130], after=[d_p2])
                        odep = odep + [d_p2]
                    o_om = sq.emit("t", omm, after=odep)

                    def umm(t, h=h, i2=i2):
                        t.matmul(ups[:, 0, 0:65], phik[:, 0:128], vtm2[:, i2, h, :], start=True, stop=True)
                        return t.matmul(ups[:, 1, 0:65], phik[:, 128:256], vtm2[:, i2, h, :], start=True, stop=True)
                    udep = [o_pk, o_om]
                    if ("supd",) in prev:
                        udep.append(prev[("supd",)])
                    o_um = sq.emit("t", umm, after=udep)
                    prev[("ups",)] = o_um

                    def supd(v, h=h):
                        v.tensor_tensor(out=Sst[:, h, 0, :], in0=Sst[:, h, 0, :], in1=ups[:, 0, 0:65], op=ALU.add)
                        v.tensor_tensor(out=Sst[:, h, 1, :], in0=Sst[:, h, 1, :], in1=ups[:, 1, 0:65], op=ALU.add)
                        v.tensor_copy(out=Sbf[:, h, 0, :], in_=Sst[:, h, 0, :])
                        return v.tensor_copy(out=Sbf[:, h, 1, :], in_=Sst[:, h, 1, :])
                    o_su = sq.emit("v", supd, after=[o_um])
                    prev[("supd",)] = o_su
                    prev[("supd_h", h)] = o_su

                    o_na0 = sq.emit("v", lambda v: v.tensor_scalar(out=ebuf[:, 0:1], in0=ops_[:, 64:65], scalar1=1e-5,
                                                                   scalar2=None, op0=ALU.add), after=[o_om], hazard=True)
                    o_na = sq.emit("v", lambda v: v.reciprocal(out=ebuf[:, 1:2], in_=ebuf[:, 0:1]),
                                   after=[o_na0], hazard=True)

                    def nrm_b(v, h=h):
                        return v.tensor_scalar(out=obuf[:, h], in0=ops_[:, 0:64], scalar1=ebuf[:, 1:2],
                                               scalar2=None, op0=ALU.mult)
                    ndep = [o_na]
                    if ("trlast",) in prev:
                        ndep.append(prev[("trlast",)])
                    o_nb = sq.emit("v", nrm_b, after=ndep, hazard=True)
                    prev[("onrm",)] = o_nb
                    prev[("a2used",)] = o_om
                    prev[("oused",)] = o_om
                    prev[("obh", h)] = o_nb
                    if debug and i2 == 1 and h == 0:
                        d_e2 = sq.emit("a", lambda a: a.copy(out=dbgb[:, 65:130], in_=ops_[:, 0:65]), after=[o_om, o_nb])
                        d_e2d = sq.dma("dg", dbg_h[5, :, 0:65], dbgb[:, 65:130], after=[d_e2])
                        prev[("onrm",)] = sq.emit("v", lambda v: v.nop(), after=[d_e2d, o_nb])
                        prev[("a2used",)] = prev[("onrm",)]
                        prev[("oused",)] = prev[("onrm",)]
                    if debug and i2 == 0 and h == 0:
                        sq.dma("dg", dbg_h[0, :, 0:128], A2t[:], after=[o_om])
                        sq.dma("dg", dbg_h[1, :, 0:256], phiq[:].rearrange("p a b -> p (a b)"), after=[o_om])
                        sq.dma("dg", dbg_h[2, :, 0:256], phik[:], after=[o_um])
                        d_e = sq.emit("a", lambda a: a.copy(out=dbgb[:, 0:65], in_=ops_[:, 0:65]), after=[o_om, o_nb])
                        d_ed = sq.dma("dg", dbg_h[3, :, 0:65], dbgb[:, 0:65], after=[d_e])
                        d_q = sq.emit("a", lambda a: a.copy(out=dbgb[:, 65:130], in_=vtm2[:, 0, 0, :]), after=[o_om])
                        d_qd = sq.dma("dg", dbg_h[6, :, 0:65], dbgb[:, 65:130], after=[d_q])
                        d_q2 = sq.emit("a", lambda a: a.copy(out=dbgb[:, 130:256], in_=vtm2[:].rearrange("p a b c -> p (a b c)")[:, 0:126]), after=[o_om])
                        sq.dma("dg", dbg_h[5, :, 0:126], dbgb[:, 130:256], after=[d_q2])
                        prev[("onrm",)] = sq.emit("v", lambda v: v.nop(), after=[d_ed, d_qd, o_nb])
                        prev[("a2used",)] = prev[("onrm",)]
                        prev[("oused",)] = prev[("onrm",)]

                tr_evs = []
                for dc in range(NCH):
                    def trmm(t, dc=dc):
                        return t.transpose(trps[:], obuf[:].rearrange("p a b -> p (a b)")[:, dc * 128:(dc + 1) * 128], iden[:])
                    tdep = [prev[("obh", h2)] for h2 in (2 * dc, 2 * dc + 1)]
                    if tr_evs:
                        tdep.append(tr_evs[-1])
                    o_tr = sq.emit("t", trmm, after=tdep)

                    def trev(a, dc=dc):
                        return a.copy(out=otr[:, dc], in_=trps[:])
                    edep = [o_tr]
                    if prev_ost is not None:
                        edep.append(prev_ost)
                    o_te = sq.emit("a", trev, after=edep)
                    tr_evs.append(o_te)
                prev[("trlast",)] = tr_evs[-1]
                prev_ost = sq.dma("dg", of_r[:, :, tsl], otr[:], after=[tr_evs[-1]])
            sq.set_floor()

        if upto <= 2:
            sq.finalize(block, sems)
            return nc

        with ExitStack() as ph:
            psb = lambda *a: ph.enter_context(nc.sbuf_tensor(*a))
            ppt = lambda *a: ph.enter_context(nc.psum_tensor(*a))
            wo_sb = psb("wo_sb", [128, NCH, D], BF16)
            sq.dma("dg", wo_sb[:], wo.rearrange("(c p) f -> p c f", p=128))
            sq.set_floor()
            oin = psb("oin", [128, 2, NCH, TT], BF16)
            so_t = psb("so_t", [128, NCH, TT], BF16)
            sps = ppt("sps", [128, 2, TT], F32)
            s_r = s_hbm.rearrange("(c p) t -> p c t", p=128)
            of_r = of_h.rearrange("(c p) t -> p c t", p=128)
            prev_ev = [None, None]
            prev_st = None
            ld_user = [None, None]
            for i in range(NTT):
                dep = [] if ld_user[i % 2] is None else [ld_user[i % 2]]
                ld = sq.dma("ds", oin[:, i % 2], of_r[:, :, i * TT:(i + 1) * TT], after=dep)
                last_ev = None
                for cc in range(NCH):
                    pb = (i * NCH + cc) % 2

                    def proj(t, cc=cc, i=i, pb=pb):
                        ins = None
                        for dc in range(NCH):
                            ins = t.matmul(sps[:, pb], wo_sb[:, dc, cc * 128:(cc + 1) * 128],
                                           oin[:, i % 2, dc], start=(dc == 0), stop=(dc == NCH - 1))
                        return ins
                    pdep = [ld]
                    if prev_ev[pb] is not None:
                        pdep.append(prev_ev[pb])
                    o_p = sq.emit("t", proj, after=pdep)

                    def evac(a, cc=cc, pb=pb):
                        return a.copy(out=so_t[:, cc], in_=sps[:, pb])
                    edep = [o_p] + ([prev_st] if prev_st is not None else [])
                    o_e = sq.emit("a", evac, after=edep)
                    prev_ev[pb] = o_e
                    last_ev = o_e
                ld_user[i % 2] = last_ev
                prev_st = sq.dma("dg", s_r[:, :, i * TT:(i + 1) * TT], so_t[:], after=[last_ev])
            sq.set_floor()

        if upto <= 3:
            sq.finalize(block, sems)
            return nc

        with ExitStack() as ph:
            psb = lambda *a: ph.enter_context(nc.sbuf_tensor(*a))
            ppt = lambda *a: ph.enter_context(nc.psum_tensor(*a))
            murs = psb("murs", [128, 4], F32)
            sq.dma("ds", murs[:, 0:1], murf[:, 0:1]) if False else None
            mur = psb("mur", [128, 2, 8], F32)
            mus = psb("mus", [128, 4, 8], F32)
            sq.dma("ds", mus[:, 0, :], murf[:, :])
            sq.dma("ds", mus[:, 1, :], mukf[:, :])

            def muprep(v):
                v.tensor_scalar(out=mus[:, 2, :], in0=mus[:, 0, :], scalar1=-1.0, scalar2=1.0, op0=ALU.mult, op1=ALU.add)
                return v.tensor_scalar(out=mus[:, 3, :], in0=mus[:, 1, :], scalar1=-1.0, scalar2=1.0, op0=ALU.mult, op1=ALU.add)
            sq.emit("v", muprep, after=[])
            wr_sb = psb("wr_sb", [128, NCH, D], BF16)
            wk_sb = psb("wk_sb", [128, NCH, D], BF16)
            wv_sb = psb("wv_sb", [128, NCH, D], BF16)
            sq.dma("dg", wr_sb[:], wrT.rearrange("(c p) f -> p c f", p=128))
            sq.dma("dg", wk_sb[:], wkT2.rearrange("(c p) f -> p c f", p=128))
            sq.dma("dg", wv_sb[:], wvT2.rearrange("(c p) f -> p c f", p=128))
            sq.set_floor()

            xng = psb("xng", [128, 2, NCH, TT + 1], BF16)
            xrb = psb("xrb", [128, NCH, TT], BF16)
            xkb = psb("xkb", [128, NCH, TT], BF16)
            mixt = psb("mixt", [128, TT], F32)
            rbuf = psb("rbuf", [128, NCH, TT], BF16)
            kab = psb("kab", [128, NCH, TT], BF16)
            gb = psb("gb", [128, TT], F32)
            xres = psb("xres", [128, 2, NCH, TT], F32)
            sres = psb("sres", [128, 2, NCH, TT], BF16)
            onew = psb("onew", [128, NCH, TT], F32)
            rps = ppt("rps", [128, 2, TT], F32)
            kps = ppt("kps", [128, 2, TT], F32)
            vps2 = ppt("vps2", [128, 2, TT], F32)
            xn_r = xn_h.rearrange("(c p) t -> p c t", p=128)
            xi_r = x_in.rearrange("(c p) t -> p c t", p=128)
            xo_r = x_out.rearrange("(c p) t -> p c t", p=128)
            s_r = s_hbm.rearrange("(c p) t -> p c t", p=128)

            prev_mix = None
            prev_ke2 = None
            prev_rev = [None, None]
            prev_kev = [None, None]
            prev_vev = [None, None]
            prev_gmul = None
            prev_xst = None
            xld_user = [None, None]
            out_stores = []
            for i in range(NTT):
                dep = [] if xld_user[i % 2] is None else [xld_user[i % 2]]
                if i == 0:
                    o_lxn = sq.dma("ds", xng[:, 0, :, 1:TT + 1], xn_r[:, :, 0:TT], after=dep)
                    o_g0 = sq.emit("v", lambda v: v.memset(xng[:, 0, :, 0:1], 0.0), after=dep)
                    o_lxn_all = [o_lxn, o_g0]
                else:
                    o_lxn = sq.dma("ds", xng[:, i % 2, :, 0:TT + 1], xn_r[:, :, i * TT - 1:(i + 1) * TT], after=dep)
                    o_lxn_all = [o_lxn]
                o_lx = sq.dma("ds", xres[:, i % 2], xi_r[:, :, i * TT:(i + 1) * TT], after=dep)
                o_ls = sq.dma("ds", sres[:, i % 2], s_r[:, :, i * TT:(i + 1) * TT], after=dep)

                def mix(v, i=i):
                    ins = None
                    for dc in range(NCH):
                        cur = xng[:, i % 2, dc, 1:TT + 1]
                        prv = xng[:, i % 2, dc, 0:TT]
                        v.tensor_scalar(out=mixt[:], in0=prv, scalar1=mus[:, 2, dc:dc + 1], scalar2=None, op0=ALU.mult)
                        v.scalar_tensor_tensor(out=xrb[:, dc], in0=cur, scalar=mus[:, 0, dc:dc + 1], in1=mixt[:], op0=ALU.mult, op1=ALU.add)
                        v.tensor_scalar(out=mixt[:], in0=prv, scalar1=mus[:, 3, dc:dc + 1], scalar2=None, op0=ALU.mult)
                        ins = v.scalar_tensor_tensor(out=xkb[:, dc], in0=cur, scalar=mus[:, 1, dc:dc + 1], in1=mixt[:], op0=ALU.mult, op1=ALU.add)
                    return ins
                mdep = o_lxn_all + ([prev_mix] if prev_mix else [])
                o_mix = sq.emit("v", mix, after=mdep)

                res_ops = []
                for cc in range(NCH):
                    pb = (i * NCH + cc) % 2

                    def rmm(t, cc=cc, pb=pb):
                        ins = None
                        for dc in range(NCH):
                            ins = t.matmul(rps[:, pb], wr_sb[:, dc, cc * 128:(cc + 1) * 128], xrb[:, dc],
                                           start=(dc == 0), stop=(dc == NCH - 1))
                        return ins
                    rdep = [o_mix]
                    if prev_rev[pb] is not None:
                        rdep.append(prev_rev[pb])
                    o_rm = sq.emit("t", rmm, after=rdep)

                    def rev(a, cc=cc, pb=pb):
                        return a.activation(out=rbuf[:, cc], in_=rps[:, pb], func=AF.Sigmoid)
                    o_re = sq.emit("a", rev, after=[o_rm])
                    prev_rev[pb] = o_re

                    def kmm(t, cc=cc, pb=pb):
                        ins = None
                        for dc in range(NCH):
                            ins = t.matmul(kps[:, pb], wk_sb[:, dc, cc * 128:(cc + 1) * 128], xkb[:, dc],
                                           start=(dc == 0), stop=(dc == NCH - 1))
                        return ins
                    kdep = [o_mix]
                    if prev_kev[pb] is not None:
                        kdep.append(prev_kev[pb])
                    o_km = sq.emit("t", kmm, after=kdep)

                    def kev(v, cc=cc, pb=pb):
                        return v.tensor_scalar(out=mixt[:], in0=kps[:, pb], scalar1=0.0, scalar2=None, op0=ALU.max)
                    kedep = [o_km, o_mix]
                    if prev_ke2 is not None:
                        kedep.append(prev_ke2)
                    o_ke = sq.emit("v", kev, after=kedep)

                    def kev2(a, cc=cc):
                        return a.activation(out=kab[:, cc], in_=mixt[:], func=AF.Square)
                    o_ke2 = sq.emit("a", kev2, after=[o_ke])
                    prev_kev[pb] = o_ke2
                    prev_ke2 = o_ke2

                kv_evs = []
                for cc in range(NCH):
                    pb = (i * NCH + cc) % 2

                    def vmm(t, cc=cc, pb=pb):
                        ins = None
                        for dc in range(NCH):
                            ins = t.matmul(vps2[:, pb], wv_sb[:, dc, cc * 128:(cc + 1) * 128], kab[:, dc],
                                           start=(dc == 0), stop=(dc == NCH - 1))
                        return ins
                    vdep = [d for d in (prev_kev[0], prev_kev[1]) if d is not None]
                    if prev_vev[pb] is not None:
                        vdep.append(prev_vev[pb])
                    o_vm = sq.emit("t", vmm, after=vdep)

                    def gmul(v, cc=cc, pb=pb):
                        return v.tensor_tensor(out=gb[:], in0=rbuf[:, cc], in1=vps2[:, pb], op=ALU.mult)
                    gdep = [o_vm, prev_rev[0], prev_rev[1]]
                    if prev_gmul is not None:
                        gdep.append(prev_gmul)
                    o_gm = sq.emit("v", gmul, after=gdep)
                    prev_vev[pb] = o_gm

                    def gsil(a, cc=cc):
                        return a.activation(out=gb[:], in_=gb[:], func=AF.Silu)
                    o_gs = sq.emit("a", gsil, after=[o_gm])

                    def resid(v, cc=cc, i=i):
                        v.tensor_tensor(out=onew[:, cc], in0=sres[:, i % 2, cc], in1=gb[:], op=ALU.mult)
                        return v.tensor_tensor(out=onew[:, cc], in0=onew[:, cc], in1=xres[:, i % 2, cc], op=ALU.add)
                    rdep2 = [o_gs, o_lx, o_ls]
                    if prev_xst is not None:
                        rdep2.append(prev_xst)
                    o_r = sq.emit("v", resid, after=rdep2)
                    prev_gmul = o_r
                    res_ops.append(o_r)
                prev_mix = res_ops[-1]
                xld_user[i % 2] = res_ops[-1]
                prev_xst = sq.dma("dg", xo_r[:, :, i * TT:(i + 1) * TT], onew[:], after=res_ops)
                out_stores.append(prev_xst)

            if ctx is None:
                sq.finalize(block, sems, outputs_done=out_stores)
            else:
                ctx["out_stores"] = out_stores
                sq.set_floor()
    return nc


def l1_inmap(xcore_fm, I):
    f = 16
    E1 = np.zeros((16, 128, 128), np.float32)
    E2 = np.zeros((16, 128, 128), np.float32)
    for hh in range(8):
        for half in range(2):
            for jj in range(8):
                for i in range(16):
                    col = jj * 16 + i
                    E1[half * 8 + hh, 16 * hh + i, col] = 1.0
                    E2[half * 8 + hh, 16 * hh + half * 8 + jj, col] = 1.0
    trimask = np.triu(np.ones((128, 128), np.float32))
    return {
        "x_in": xcore_fm,
        "wq": np.ascontiguousarray(I["L1_attn_Wq"]),
        "wk": np.ascontiguousarray(I["L1_attn_Wk"]),
        "wv": np.ascontiguousarray(I["L1_attn_Wv"]),
        "wo": np.ascontiguousarray(I["L1_attn_Wo"]),
        "wrT": np.ascontiguousarray(I["L1_ffn_W_r"].T),
        "wkT2": np.ascontiguousarray(I["L1_ffn_W_k"].T),
        "wvT2": np.ascontiguousarray(I["L1_ffn_W_v"].T),
        "murf": fm(I["L1_ffn_mu_r"]),
        "mukf": fm(I["L1_ffn_mu_k"]),
        "e1m": E1.astype(BF),
        "e2m": E2.astype(BF),
        "trimask": trimask.astype(BF),
        "iden_h": np.eye(128, dtype=np.float32).astype(BF),
        "ones128_h": np.ones((128, 1), np.float32),
        "ones1_h": np.ones((1, 128), np.float32),
    }




def build_l0(debug=False, upto=99, ctx=None):
    from contextlib import ExitStack
    nc = bass.Bass("TRN2", target_bir_lowering=False, debug=False) if ctx is None else ctx["nc"]
    x_in = nc.dram_tensor("x_in", [D, S], F32, kind="ExternalInput").ap()
    wqkvT = nc.dram_tensor("wqkvT", [D, 3 * D], F32, kind="ExternalInput").ap()
    bqkvf = nc.dram_tensor("bqkvf", [128, 16], F32, kind="ExternalInput").ap()
    bvrow = nc.dram_tensor("bvrow", [1, D], F32, kind="ExternalInput").ap()
    wqaT = nc.dram_tensor("wqaT", [D, D], F32, kind="ExternalInput").ap()
    bqaf = nc.dram_tensor("bqaf", [128, 8], F32, kind="ExternalInput").ap()
    wkaT = nc.dram_tensor("wkaT", [D, D], F32, kind="ExternalInput").ap()
    bkaf = nc.dram_tensor("bkaf", [128, 8], F32, kind="ExternalInput").ap()
    wdT = nc.dram_tensor("wdT", [D, D], F32, kind="ExternalInput").ap()
    bdf = nc.dram_tensor("bdf", [128, 8], F32, kind="ExternalInput").ap()
    g1h = nc.dram_tensor("g1h", [2, 4, 32, 32], F32, kind="ExternalInput").ap()
    g2h = nc.dram_tensor("g2h", [2, 4, 16, 16], F32, kind="ExternalInput").ap()
    brf = nc.dram_tensor("brf", [128, 8], F32, kind="ExternalInput").ap()
    seld_h = nc.dram_tensor("seld_h", [32, 512], BF16, kind="ExternalInput").ap()
    selg_h = nc.dram_tensor("selg_h", [16, 128], BF16, kind="ExternalInput").ap()
    cosT_h = nc.dram_tensor("cosT_h", [128, S], BF16, kind="ExternalInput").ap()
    sinT_h = nc.dram_tensor("sinT_h", [128, S], BF16, kind="ExternalInput").ap()
    wmaskd_h = nc.dram_tensor("wmaskd_h", [128, 128], BF16, kind="ExternalInput").ap()
    wmasks_h = nc.dram_tensor("wmasks_h", [128, 128], BF16, kind="ExternalInput").ap()
    cmask_h = nc.dram_tensor("cmask_h", [128, 4, 512], BF16, kind="ExternalInput").ap()
    iden_h = nc.dram_tensor("iden_h", [128, 128], BF16, kind="ExternalInput").ap()
    onescol_h = nc.dram_tensor("onescol_h", [1, 128], BF16, kind="ExternalInput").ap()
    ones128_h = nc.dram_tensor("ones128_h", [128, 1], F32, kind="ExternalInput").ap()
    ones1_h = nc.dram_tensor("ones1_h", [1, 128], F32, kind="ExternalInput").ap()
    onesf_h = nc.dram_tensor("onesf_h", [128, 128], F32, kind="ExternalInput").ap()
    x_out = nc.dram_tensor("x_out", [D, S], F32, kind="ExternalOutput").ap()

    kind = "ExternalOutput" if debug else "Internal"
    xn_h = nc.dram_tensor("xn_h", [D, S], BF16, kind=kind).ap()
    qf_h = nc.dram_tensor("qf_h", [D, S], BF16, kind=kind).ap()
    kf_h = nc.dram_tensor("kf_h", [D, S], BF16, kind=kind).ap()
    vt_h = nc.dram_tensor("vt_h", [S, 16 * 65], BF16, kind=kind).ap()
    at_h = nc.dram_tensor("at_h", [S, 16 * 65], BF16, kind=kind).ap()
    af_h = nc.dram_tensor("af_h", [D, S], BF16, kind=kind).ap()
    qaf_h = nc.dram_tensor("qaf_h", [D, S], BF16, kind=kind).ap()
    kaf_h = nc.dram_tensor("kaf_h", [D, S], BF16, kind=kind).ap()
    s_hbm = nc.dram_tensor("s_hbm", [D, S], BF16, kind=kind).ap()

    stack = ExitStack()
    with stack:
        if ctx is None:
            sems = alloc_sems(nc, stack)
            block = stack.enter_context(nc.Block())
            sq = Seq(nc)
        else:
            sems, block, sq = ctx["sems"], ctx["block"], ctx["sq"]
        sb = lambda *a: stack.enter_context(nc.sbuf_tensor(*a))

        ones128 = sb("c_ones128", [128, 1], F32)
        ones1 = sb("c_ones1", [1, 128], F32)
        onescol = sb("c_onescol", [1, 128], BF16)
        iden = sb("c_iden", [128, 128], BF16)
        cst = sb("cst", [128, 2], F32)

        sq.dma("ds", ones128[:], ones128_h[:, :])
        sq.dma("ds", ones1[:], ones1_h[:, :])
        sq.dma("ds", onescol[:], onescol_h[:, :])
        sq.dma("ds", iden[:], iden_h[:, :])
        sq.emit("v", lambda v: v.memset(cst[:, 0:1], 1e-5))
        sq.set_floor()

        phAB = ExitStack()
        xn = phAB.enter_context(nc.sbuf_tensor("xn", [128, NCH, S], BF16))
        with ExitStack() as ph:
            ln_done = emit_ln(sq, nc, ph, x_in, xn, ones128, ones1, cst, col_off=0)
            xn_r = xn_h.rearrange("(c p) t -> p c t", p=128)
            for i in range(NTT):
                sq.dma("dg", xn_r[:, :, i * TT:(i + 1) * TT], xn[:, :, i * TT:(i + 1) * TT], after=[ln_done[i]])
            sq.set_floor()

        if upto <= 0:
            phAB.close()
            sq.finalize(block, sems)
            return nc

        qf_r = qf_h.rearrange("(c p) t -> p c t", p=128)
        kf_r = kf_h.rearrange("(c p) t -> p c t", p=128)
        wq_r = wqkvT.rearrange("(c p) f -> p c f", p=128)
        with ExitStack() as ph:
            psb = lambda *a: ph.enter_context(nc.sbuf_tensor(*a))
            ppt = lambda *a: ph.enter_context(nc.psum_tensor(*a))
            cosT = psb("cosT", [128, S], BF16)
            sinT = psb("sinT", [128, S], BF16)
            sq.dma("ds", cosT[:], cosT_h[:, :])
            sq.dma("ds", sinT[:], sinT_h[:, :])
            wt = psb("wt", [128, 2, NCH, 128], BF16)
            fullch = psb("fullch", [128, 2, S], BF16)
            ropesc = psb("ropesc", [128, 2, S], BF16)
            bvs = psb("bvs", [1, D], BF16)
            vtm = psb("vtm", [128, 2, 16, 65], BF16)
            qps = ppt("qps", [128, 2, TT], F32)
            vps = ppt("vps", [128, 2, 512], F32)
            sq.dma("dg", bvs[:], bvrow[:, :])
            bqs = psb("bqs", [128, 16], F32)
            sq.dma("ds", bqs[:], bqkvf[:, :])
            sq.set_floor()

            prev_ev = [None, None]
            prev_chst = [None, None]
            for fc in ([] if L0SUB in (-1, -3) else range(16)):
                wb = fc % 2
                cb = fc % 2
                wdep = [sq.last["t"]] if sq.last.get("t") is not None else []
                o_lw = sq.dma("dg", wt[:, wb], wq_r[:, :, fc * 128:(fc + 1) * 128], after=wdep)
                evs = []
                for i in range(NTT):
                    pb = (fc * NTT + i) % 2

                    def mm(t, wb=wb, i=i, pb=pb):
                        ins = None
                        for dc in range(NCH):
                            ins = t.matmul(qps[:, pb], wt[:, wb, dc], xn[:, dc, i * TT:(i + 1) * TT],
                                           start=(dc == 0), stop=(dc == NCH - 1))
                        return ins
                    mdep = [o_lw, ln_done[i]]
                    if prev_ev[pb] is not None:
                        mdep.append(prev_ev[pb])
                    o_mm = sq.emit("t", mm, after=mdep)

                    def ev(a, fc=fc, i=i, pb=pb, cb=cb):
                        return a.activation(out=fullch[:, cb, i * TT:(i + 1) * TT], in_=qps[:, pb],
                                            func=AF.Identity, bias=bqs[:, fc:fc + 1])
                    edep = [o_mm]
                    if prev_chst[cb] is not None:
                        edep.append(prev_chst[cb])
                    o_ev = sq.emit("a", ev, after=edep)
                    prev_ev[pb] = o_ev
                    evs.append(o_ev)
                if L0SUB < 1:
                    dst = qf_r[:, fc, :] if fc < 8 else kf_r[:, fc - 8, :]
                    prev_chst[cb] = sq.dma("dg", dst, fullch[:, cb, :], after=evs)
                    continue
                o_g1 = sq.dma("ds", ropesc[0:8, cb, :], fullch[8:16, cb, :], after=evs)
                o_g2 = sq.dma("ds", ropesc[8:16, cb, :], fullch[0:8, cb, :], after=evs)
                o_g3 = sq.dma("ds", ropesc[64:72, cb, :], fullch[72:80, cb, :], after=evs)
                o_g4 = sq.dma("ds", ropesc[72:80, cb, :], fullch[64:72, cb, :], after=evs)

                def rope(v, cb=cb):
                    for b0 in (0, 64):
                        v.tensor_tensor(out=ropesc[b0:b0 + 16, cb, :], in0=ropesc[b0:b0 + 16, cb, :],
                                        in1=sinT[b0:b0 + 16, :], op=ALU.mult)
                        v.tensor_tensor(out=fullch[b0:b0 + 16, cb, :], in0=fullch[b0:b0 + 16, cb, :],
                                        in1=cosT[b0:b0 + 16, :], op=ALU.mult)
                    ins = None
                    for b0 in (0, 64):
                        ins = v.tensor_tensor(out=fullch[b0:b0 + 16, cb, :], in0=fullch[b0:b0 + 16, cb, :],
                                              in1=ropesc[b0:b0 + 16, cb, :], op=ALU.add)
                    return ins
                o_rope = sq.emit("v", rope, after=[o_g1, o_g2, o_g3, o_g4])
                dst = qf_r[:, fc, :] if fc < 8 else kf_r[:, fc - 8, :]
                prev_chst[cb] = sq.dma("dg", dst, fullch[:, cb, :], after=[o_rope])

            def vones(v):
                ins = None
                for hb in range(2):
                    for h in range(16):
                        ins = v.memset(vtm[:, hb, h, 64:65], 1.0)
                return ins
            o_vone = sq.emit("v", vones)
            vw = psb("vw", [128, NCH, D], BF16)
            o_vws = []
            for dc in range(NCH):
                o_vws.append(sq.dma("dg", vw[:, dc, :], wqkvT[dc * 128:(dc + 1) * 128, 2048:3072],
                                    after=[sq.last["t"]] if sq.last.get("t") is not None else []))
            o_vw = o_vws[-1]
            prevv = [None, None]
            prev_vev = None
            for i2 in ([] if L0SUB == -2 else range(1) if L0SUB == -3 else range(S // 128)):
                hb = i2 % 2

                def vmm(t, i2=i2):
                    ins = None
                    for half in range(2):
                        if L0SUB >= 2:
                            t.matmul(vps[:, half], onescol[:, 0:128], bvs[0:1, half * 512:(half + 1) * 512],
                                     start=True, stop=False)
                        for dc in range(NCH):
                            ins = t.matmul(vps[:, half], xn[:, dc, i2 * 128:(i2 + 1) * 128],
                                           vw[:, dc, half * 512:(half + 1) * 512],
                                           start=(dc == 0 and L0SUB < 2), stop=(dc == NCH - 1))
                    return ins
                vdep = o_vws + [ln_done[min(i2 // 4, NTT - 1)]]
                if prevv[hb] is not None:
                    vdep.append(prevv[hb])
                if prev_vev is not None:
                    vdep.append(prev_vev)
                o_vm = sq.emit("t", vmm, after=vdep)

                def vev(a, hb=hb):
                    a.copy(out=vtm[:, hb, 0:8, 0:64], in_=vps[:, 0])
                    return a.copy(out=vtm[:, hb, 8:16, 0:64], in_=vps[:, 1])
                o_ve = sq.emit("a", vev, after=[o_vm, o_vone] + ([prevv[hb]] if prevv[hb] else []))
                prev_vev = o_ve
                prevv[hb] = sq.dma("dg", vt_h[i2 * 128:(i2 + 1) * 128, :], vtm[:, hb], after=[o_ve])
            sq.set_floor()
        phAB.close()

        if upto <= 1:
            sq.finalize(block, sems)
            return nc

        with ExitStack() as ph:
            psb = lambda *a: ph.enter_context(nc.sbuf_tensor(*a))
            ppt = lambda *a: ph.enter_context(nc.psum_tensor(*a))
            wmd = psb("wmd", [128, 128], BF16)
            wms = psb("wms", [128, 128], BF16)
            sq.dma("ds", wmd[:], wmaskd_h[:, :])
            sq.dma("ds", wms[:], wmasks_h[:, :])
            qsl = psb("qsl", [128, NCH, 128], BF16)
            ksl = psb("ksl", [128, 2, NCH, 128], BF16)
            vsl = psb("vsl", [128, 2, 16, 65], BF16)
            pt_d = psb("pt_d", [128, 2, 128], BF16)
            pt_s = psb("pt_s", [128, 2, 128], BF16)
            ebuf = psb("ebuf", [128, 4], F32)
            att = psb("att", [128, 16, 65], BF16)
            aft = psb("aft", [128, NCH, 128], BF16)
            sps_ = ppt("sps_", [128, 4, 512], F32)
            aps_ = ppt("aps_", [128, 2, 512], F32)
            tps_ = ppt("tps_", [128, 128], BF16)
            sq.set_floor()

            prev = {}
            prev_atst = None
            prev_afst = None
            for i2 in range(S // 128):
                tsl = slice(i2 * 128, (i2 + 1) * 128)
                hb = i2 % 2
                qdep = [sq.last["t"]] if sq.last.get("t") is not None else []
                o_lq = sq.dma("ds", qsl[:], qf_r[:, :, tsl], after=qdep)
                o_lk = sq.dma("ds", ksl[:, hb], kf_r[:, :, tsl], after=qdep)
                o_lv = sq.dma("ds", vsl[:, hb], vt_h[tsl, :], after=qdep)
                for hp in range(8):
                    def smm(t, hp=hp, hb=hb, i2=i2):
                        ins = None
                        for hh in range(2):
                            b0 = 64 * hh
                            ins = t.matmul(sps_[:, hh, 0:128], ksl[b0:b0 + 64, hb, hp, :],
                                           qsl[b0:b0 + 64, hp, :], start=True, stop=True,
                                           tile_position=(b0, 0))
                        if i2 > 0:
                            for hh in range(2):
                                b0 = 64 * hh
                                ins = t.matmul(sps_[:, 2 + hh, 0:128], ksl[b0:b0 + 64, 1 - hb, hp, :],
                                               qsl[b0:b0 + 64, hp, :], start=True, stop=True,
                                               tile_position=(b0, 0))
                        return ins
                    sdep = [o_lq, o_lk]
                    if ("expdone",) in prev:
                        sdep.append(prev[("expdone",)])
                    o_sm = sq.emit("t", smm, after=sdep)

                    def exps(a, i2=i2):
                        a.activation(out=pt_d[:, 0], in_=sps_[:, 0, 0:128], func=AF.Exp, scale=0.125)
                        ins = a.activation(out=pt_d[:, 1], in_=sps_[:, 1, 0:128], func=AF.Exp, scale=0.125)
                        if i2 > 0:
                            a.activation(out=pt_s[:, 0], in_=sps_[:, 2, 0:128], func=AF.Exp, scale=0.125)
                            ins = a.activation(out=pt_s[:, 1], in_=sps_[:, 3, 0:128], func=AF.Exp, scale=0.125)
                        return ins
                    edep = [o_sm]
                    if ("mskdone",) in prev:
                        edep.append(prev[("mskdone",)])
                    o_ex = sq.emit("a", exps, after=edep)
                    prev[("expdone",)] = o_ex

                    def msk(v, i2=i2):
                        v.tensor_tensor(out=pt_d[:, 0], in0=pt_d[:, 0], in1=wmd[:], op=ALU.mult)
                        ins = v.tensor_tensor(out=pt_d[:, 1], in0=pt_d[:, 1], in1=wmd[:], op=ALU.mult)
                        if i2 > 0:
                            v.tensor_tensor(out=pt_s[:, 0], in0=pt_s[:, 0], in1=wms[:], op=ALU.mult)
                            ins = v.tensor_tensor(out=pt_s[:, 1], in0=pt_s[:, 1], in1=wms[:], op=ALU.mult)
                        return ins
                    o_mk = sq.emit("v", msk, after=[o_ex])
                    prev[("mskdone",)] = o_mk

                    for hh in range(2):
                        h = 2 * hp + hh
                        ab_ = h % 2

                        def acc(t, hh=hh, h=h, hb=hb, i2=i2, ab_=ab_):
                            if i2 > 0:
                                t.matmul(aps_[:, ab_, 0:65], pt_s[:, hh], vsl[:, 1 - hb, h, :],
                                         start=True, stop=False)
                                return t.matmul(aps_[:, ab_, 0:65], pt_d[:, hh], vsl[:, hb, h, :],
                                                start=False, stop=True)
                            return t.matmul(aps_[:, ab_, 0:65], pt_d[:, hh], vsl[:, hb, h, :],
                                            start=True, stop=True)
                        adep = [o_mk, o_lv]
                        if ("accev", ab_) in prev:
                            adep.append(prev[("accev", ab_)])
                        o_ac = sq.emit("t", acc, after=adep)

                        o_n0 = sq.emit("v", lambda v, ab_=ab_: v.tensor_copy(out=ebuf[:, 0:1], in_=aps_[:, ab_, 64:65]),
                                       after=[o_ac], hazard=True)
                        o_n1 = sq.emit("v", lambda v: v.reciprocal(out=ebuf[:, 1:2], in_=ebuf[:, 0:1]),
                                       after=[o_n0], hazard=True)
                        ndep = [o_n1]
                        if prev_atst is not None:
                            ndep.append(prev_atst)
                        o_n2 = sq.emit("v", lambda v, h=h, ab_=ab_: v.tensor_scalar(
                            out=att[:, h, 0:64], in0=aps_[:, ab_, 0:64], scalar1=ebuf[:, 1:2],
                            scalar2=None, op0=ALU.mult), after=ndep, hazard=True)
                        prev[("accev", ab_)] = o_n2
                        prev[("ath", h)] = o_n2

                def aones(v):
                    ins = None
                    for h in range(16):
                        ins = v.memset(att[:, h, 64:65], 1.0)
                    return ins
                if i2 == 0:
                    o_ao = sq.emit("v", aones)
                else:
                    o_ao = prev[("aones",)]
                prev[("aones",)] = o_ao

                o_atst = sq.dma("dg", at_h[tsl, :], att[:],
                                after=[prev[("ath", h2)] for h2 in range(16)] + [o_ao])
                prev_atst = o_atst

                tr_evs = []
                for dc in range(NCH):
                    def trmm(t, dc=dc):
                        t.transpose(tps_[0:64, :], att[:, 2 * dc, 0:64], iden[:])
                        return t.transpose(tps_[64:128, :], att[:, 2 * dc + 1, 0:64], iden[:])
                    tdep = [prev[("ath", 2 * dc)], prev[("ath", 2 * dc + 1)]]
                    if tr_evs:
                        tdep.append(tr_evs[-1])
                    o_tr = sq.emit("t", trmm, after=tdep)

                    def trev(a, dc=dc):
                        return a.copy(out=aft[:, dc], in_=tps_[:])
                    edep2 = [o_tr]
                    if prev_afst is not None:
                        edep2.append(prev_afst)
                    o_te = sq.emit("a", trev, after=edep2)
                    tr_evs.append(o_te)
                af_r = af_h.rearrange("(c p) t -> p c t", p=128)
                prev_afst = sq.dma("dg", af_r[:, :, tsl], aft[:], after=[tr_evs[-1]])
            sq.set_floor()

        if upto <= 2:
            sq.finalize(block, sems)
            return nc

        qaf_r = qaf_h.rearrange("(c p) t -> p c t", p=128)
        kaf_r = kaf_h.rearrange("(c p) t -> p c t", p=128)
        af_r = af_h.rearrange("(c p) t -> p c t", p=128)
        with ExitStack() as ph:
            psb = lambda *a: ph.enter_context(nc.sbuf_tensor(*a))
            ppt = lambda *a: ph.enter_context(nc.psum_tensor(*a))
            wqa_sb = psb("wqa_sb", [128, NCH, D], BF16)
            wka_sb = psb("wka_sb", [128, NCH, D], BF16)
            bqa_sb = psb("bqa_sb", [128, 16], F32)
            sq.dma("dg", wqa_sb[:], wqaT.rearrange("(c p) f -> p c f", p=128))
            sq.dma("dg", wka_sb[:], wkaT.rearrange("(c p) f -> p c f", p=128))
            sq.dma("ds", bqa_sb[:, 0:8], bqaf[:, :])
            sq.dma("ds", bqa_sb[:, 8:16], bkaf[:, :])
            sq.set_floor()
            ain = psb("ain", [128, 2, NCH, TT], BF16)
            ob2 = psb("ob2", [128, 2, TT], BF16)
            pps2 = ppt("pps2", [128, 2, TT], F32)
            prev_ev = [None, None]
            prev_st = [None, None]
            ld_user = [None, None]
            for i in range(NTT):
                dep = [] if ld_user[i % 2] is None else [ld_user[i % 2]]
                ld = sq.dma("ds", ain[:, i % 2], af_r[:, :, i * TT:(i + 1) * TT], after=dep)
                last = None
                for fc in range(16):
                    pb = (i * 16 + fc) % 2
                    wsel = wqa_sb if fc < 8 else wka_sb
                    cc = fc % 8

                    def proj(t, wsel=wsel, cc=cc, i=i, pb=pb):
                        ins = None
                        for dc in range(NCH):
                            ins = t.matmul(pps2[:, pb], wsel[:, dc, cc * 128:(cc + 1) * 128],
                                           ain[:, i % 2, dc], start=(dc == 0), stop=(dc == NCH - 1))
                        return ins
                    pdep = [ld]
                    if prev_ev[pb] is not None:
                        pdep.append(prev_ev[pb])
                    o_p = sq.emit("t", proj, after=pdep)

                    def evac(a, fc=fc, pb=pb):
                        return a.activation(out=ob2[:, pb], in_=pps2[:, pb], func=AF.Identity,
                                            bias=bqa_sb[:, fc:fc + 1])
                    edep = [o_p]
                    if prev_st[pb] is not None:
                        edep.append(prev_st[pb])
                    o_e = sq.emit("a", evac, after=edep)
                    prev_ev[pb] = o_e
                    dst = qaf_r[:, cc, i * TT:(i + 1) * TT] if fc < 8 else kaf_r[:, cc, i * TT:(i + 1) * TT]
                    prev_st[pb] = sq.dma("dg", dst, ob2[:, pb], after=[o_e])
                    last = o_e
                ld_user[i % 2] = last
            sq.set_floor()

        if upto <= 3:
            sq.finalize(block, sems)
            return nc

        with ExitStack() as ph:
            psb = lambda *a: ph.enter_context(nc.sbuf_tensor(*a))
            ppt = lambda *a: ph.enter_context(nc.psum_tensor(*a))
            cm = psb("cm", [128, 4, 512], BF16)
            sq.dma("ds", cm[:], cmask_h[:, :, :])
            atr = psb("atr", [128, 32, 16, 65], BF16)
            sq.dma("ds", atr[:], at_h.rearrange("(r p) (h f) -> p r h f", p=128, h=16))
            wd_sb = psb("wd_sb", [128, NCH, D], BF16)
            bd_sb = psb("bd_sb", [128, 8], F32)
            onesT = psb("onesT", [128, 128], F32)
            sq.dma("dg", wd_sb[:], wdT.rearrange("(c p) f -> p c f", p=128))
            sq.dma("ds", bd_sb[:], bdf[:, :])
            sq.dma("ds", onesT[:], onesf_h[:, :])
            sq.set_floor()
            qa_s = psb("qa_s", [128, 2, 512], BF16)
            ka_s = psb("ka_s", [128, 2, S], BF16)
            p2t = psb("p2t", [128, 4, 2, 512], BF16)
            zbuf = psb("zbuf", [128, 1536], F32)
            bcs2 = psb("bcs2", [128, 512], F32)
            o2t = psb("o2t", [128, NCH, 512], BF16)
            so_t = psb("so_t", [128, NCH, TT], BF16)
            s2ps = ppt("s2ps", [128, 6, 512], F32)
            a2ps = ppt("a2ps", [128, 2, 512], F32)
            wdps = s2ps
            s_r = s_hbm.rearrange("(c p) t -> p c t", p=128)

            prev = {}
            prev_sost = None
            heads = [(qt, hp) for qt in range(NTT) for hp in range(8)]
            lds = {}
            lasts2 = {}

            def eload(g, dep):
                qt_, hp_ = heads[g]
                pb_ = g % 2
                nk_ = (qt_ + 1) * 4
                o_a = sq.dma("ds", qa_s[:, pb_], qaf_r[:, hp_, qt_ * TT:(qt_ + 1) * TT], after=dep)
                o_b = sq.dma("ds", ka_s[:, pb_, 0:nk_ * 128], kaf_r[:, hp_, 0:nk_ * 128], after=dep)
                lds[g] = (o_a, o_b)

            eload(0, [])
            g = -1
            for qt in range(NTT):
                qsl2 = slice(qt * TT, (qt + 1) * TT)
                nkt = (qt + 1) * 4
                for hp in range(8):
                    g += 1
                    if g + 1 < len(heads):
                        eload(g + 1, [lasts2[g - 1]] if g >= 1 else [])
                    o_lqa, o_lka = lds[g]
                    pb2 = g % 2
                    a2q = []

                    def a2flush():
                        kt0, fn, deps = a2q.pop(0)
                        o = sq.emit("t", fn, after=deps)
                        prev[("a2", kt0 % 4)] = o
                        prev[("p2used",)] = o
                    for kt in range(nkt):
                        diag = (kt >= qt * 4)
                        par = kt % 3
                        p4 = kt % 4

                        def s2mm(t, hp=hp, kt=kt, par=par, pb2=pb2):
                            ins = None
                            for hh in range(2):
                                b0 = 64 * hh
                                ins = t.matmul(s2ps[:, 2 * par + hh, :], ka_s[b0:b0 + 64, pb2, kt * 128:(kt + 1) * 128],
                                               qa_s[b0:b0 + 64, pb2, :], start=True, stop=True,
                                               tile_position=(b0, 0))
                            return ins
                        sdep = [o_lqa, o_lka]
                        if ("e2", par) in prev:
                            sdep.append(prev[("e2", par)])
                        if kt < 3:
                            for wk in (("wdev", 0), ("wdev", 1), ("bcv3",)):
                                if wk in prev:
                                    sdep.append(prev[wk])
                        o_s2 = sq.emit("t", s2mm, after=sdep)
                        lasts2[g] = o_s2

                        def e2(a, par=par, p4=p4):
                            a.activation(out=p2t[:, p4, 0], in_=s2ps[:, 2 * par, :], func=AF.Exp, scale=0.125)
                            return a.activation(out=p2t[:, p4, 1], in_=s2ps[:, 2 * par + 1, :], func=AF.Exp, scale=0.125)
                        edep = [o_s2]
                        if ("a2", p4) in prev:
                            edep.append(prev[("a2", p4)])
                        o_e2 = sq.emit("a", e2, after=edep)
                        prev[("e2", par)] = o_e2

                        o_mk2 = None
                        if diag:
                            def mk2(v, kt=kt, qt=qt, p4=p4):
                                mi = kt - qt * 4
                                v.tensor_tensor(out=p2t[:, p4, 0], in0=p2t[:, p4, 0], in1=cm[:, mi], op=ALU.mult)
                                return v.tensor_tensor(out=p2t[:, p4, 1], in0=p2t[:, p4, 1], in1=cm[:, mi], op=ALU.mult)
                            o_mk2 = sq.emit("v", mk2, after=[o_e2])
                            prev[("e2", par)] = o_mk2

                        def a2mm(t, hp=hp, kt=kt, p4=p4, first=(kt == 0), last=(kt == nkt - 1)):
                            t.matmul(a2ps[0:65, 0, :], atr[:, kt, 2 * hp, :], p2t[:, p4, 0],
                                     start=first, stop=last)
                            return t.matmul(a2ps[0:65, 1, :], atr[:, kt, 2 * hp + 1, :], p2t[:, p4, 1],
                                            start=first, stop=last)
                        a2dep = [o_e2] + ([o_mk2] if o_mk2 else [])
                        if ("a2ev",) in prev and kt == 0:
                            a2dep.append(prev[("a2ev",)])
                        a2q.append((kt, a2mm, a2dep))
                        if len(a2q) > 2:
                            a2flush()

                    while a2q:
                        a2flush()

                    def oev(a, hp=hp):
                        a.copy(out=o2t[0:64, hp, :], in_=a2ps[0:64, 0, :])
                        return a.copy(out=o2t[64:128, hp, :], in_=a2ps[0:64, 1, :])
                    odep = [prev[("p2used",)]]
                    if ("wdused", hp) in prev:
                        odep.append(prev[("wdused", hp)])
                    o_oe = sq.emit("a", oev, after=odep)
                    prev[("a2ev",)] = o_oe

                    o_z0 = sq.emit("a", lambda a: a.copy(out=zbuf[64:65, 0:512], in_=a2ps[64:65, 0, :]),
                                   after=[prev[("p2used",)]], hazard=True)
                    o_z1 = sq.emit("a", lambda a: a.copy(out=zbuf[64:65, 512:1024], in_=a2ps[64:65, 1, :]),
                                   after=[o_z0], hazard=True)
                    o_z2 = sq.emit("v", lambda v: v.reciprocal(out=zbuf[64:65, 1024:1536], in_=zbuf[64:65, 0:512]),
                                   after=[o_z1], hazard=True)
                    o_z3 = sq.emit("v", lambda v: v.reciprocal(out=zbuf[64:65, 0:512], in_=zbuf[64:65, 512:1024]),
                                   after=[o_z2], hazard=True)

                    def bc1(t):
                        t.matmul(s2ps[0:64, 4, :], onesT[64:65, 0:64], zbuf[64:65, 1024:1536],
                                 start=True, stop=True, tile_position=(64, 0))
                        return t.matmul(s2ps[64:128, 4, :], onesT[64:65, 0:64], zbuf[64:65, 0:512],
                                        start=True, stop=True, tile_position=(64, 64))
                    bcdep = [o_z2, o_z3]
                    if ("bcev",) in prev:
                        bcdep.append(prev[("bcev",)])
                    o_bc = sq.emit("t", bc1, after=bcdep)

                    def bcev(a):
                        return a.copy(out=bcs2[:], in_=s2ps[:, 4, :])
                    o_bce = sq.emit("a", bcev, after=[o_bc])
                    prev[("bcev",)] = o_bce
                    prev[("bcv3",)] = o_bce

                    def onrm(v, hp=hp):
                        return v.tensor_tensor(out=o2t[:, hp, :], in0=o2t[:, hp, :], in1=bcs2[:], op=ALU.mult)
                    o_on = sq.emit("v", onrm, after=[o_oe, o_bce])
                    prev[("onrm", hp)] = o_on

                for cc in range(NCH):
                    pb = (qt * NCH + cc) % 2

                    def wdm(t, cc=cc, pb=pb):
                        ins = None
                        for dc in range(NCH):
                            ins = t.matmul(wdps[:, pb], wd_sb[:, dc, cc * 128:(cc + 1) * 128],
                                           o2t[:, dc, :], start=(dc == 0), stop=(dc == NCH - 1))
                        return ins
                    wdep2 = [prev[("onrm", hp2)] for hp2 in range(8)]
                    if ("wdev", pb) in prev:
                        wdep2.append(prev[("wdev", pb)])
                    o_wd = sq.emit("t", wdm, after=wdep2)
                    for hp2 in range(8):
                        prev[("wdused", hp2)] = o_wd

                    def wde(a, cc=cc, pb=pb):
                        return a.activation(out=so_t[:, cc], in_=wdps[:, pb], func=AF.Identity,
                                            bias=bd_sb[:, cc:cc + 1])
                    wedep = [o_wd]
                    if prev_sost is not None:
                        wedep.append(prev_sost)
                    o_we = sq.emit("a", wde, after=wedep)
                    prev[("wdev", pb)] = o_we
                prev_sost = sq.dma("dg", s_r[:, :, qsl2], so_t[:],
                                   after=[prev[("wdev", 0)], prev[("wdev", 1)]])
            sq.set_floor()

        if upto <= 4:
            sq.finalize(block, sems)
            return nc

        with ExitStack() as ph:
            psb = lambda *a: ph.enter_context(nc.sbuf_tensor(*a))
            ppt = lambda *a: ph.enter_context(nc.psum_tensor(*a))
            g1s = psb("g1s", [32, 2, 4, 32], F32)
            g2s = psb("g2s", [16, 2, 4, 16], F32)
            seld = psb("seld", [32, 512], BF16)
            selg = psb("selg", [16, 128], BF16)
            br_sb = psb("br_sb", [128, 8], F32)
            sq.dma("ds", g1s[:], g1h.transpose((2, 0, 1, 3)))
            sq.dma("ds", g2s[:], g2h.transpose((2, 0, 1, 3)))
            sq.dma("ds", seld[:], seld_h[:, :])
            sq.dma("ds", selg[:], selg_h[:, :])
            sq.dma("ds", br_sb[:], brf[:, :])
            gexp = psb("gexp", [32, 512], BF16)
            gexp2 = psb("gexp2", [16, 512], BF16)
            apart = psb("apart", [128, 512], F32)
            wacc = psb("wacc", [128, 3, 4, 512], F32)
            wq_sb = psb("wq_sb", [128, 3, 4, 512], BF16)
            selps2 = ppt("selps2", [128, 2, 512], F32)
            sq.set_floor()

            prev_sel = None
            for ri in range(2):
                for ch in range(4):
                    accs = []
                    for c in range(4):
                        def gex(v, ri=ri, c=c):
                            src = g1s[:, ri, c, :].unsqueeze(2).broadcast_to((32, 32, 16))
                            v.tensor_copy(out=gexp[:].rearrange("p (b e) -> p b e", b=32), in_=src)
                            src2 = g2s[:, ri, c, :].unsqueeze(1).broadcast_to((16, 32, 16))
                            return v.tensor_copy(out=gexp2[:].rearrange("p (b e) -> p b e", b=32), in_=src2)
                        gdep = [prev_sel] if prev_sel is not None else []
                        o_ge = sq.emit("v", gex, after=gdep)

                        def selmm(t, ch=ch):
                            t.matmul(selps2[:, 0, :], seld[:, ch * 128:(ch + 1) * 128], gexp[:], start=True, stop=True)
                            return t.matmul(selps2[:, 1, :], selg[:], gexp2[:], start=True, stop=True)
                        o_sm2 = sq.emit("t", selmm, after=[o_ge] + ([prev_sel] if prev_sel is not None else []))

                        def apev(a):
                            return a.copy(out=apart[:], in_=selps2[:, 0, :])
                        o_ap = sq.emit("a", apev, after=[o_sm2])

                        def wmul(v, ri=ri, ch=ch, c=c):
                            if c == 0:
                                return v.tensor_tensor(out=wacc[:, ri, ch, :], in0=apart[:], in1=selps2[:, 1, :], op=ALU.mult)
                            v.tensor_tensor(out=apart[:], in0=apart[:], in1=selps2[:, 1, :], op=ALU.mult)
                            return v.tensor_tensor(out=wacc[:, ri, ch, :], in0=wacc[:, ri, ch, :], in1=apart[:], op=ALU.add)
                        o_wm = sq.emit("v", wmul, after=[o_ap])
                        prev_sel = o_wm

                    def wcast(a, ri=ri, ch=ch):
                        return a.copy(out=wq_sb[:, ri, ch, :], in_=wacc[:, ri, ch, :])
                    prev_sel = sq.emit("a", wcast, after=[prev_sel])
            def wneg(a):
                return a.activation(out=wq_sb[:, 2, :, :].rearrange("p a b -> p (a b)"),
                                    in_=wq_sb[:, 1, :, :].rearrange("p a b -> p (a b)"), func=AF.Copy, scale=-1.0)
            o_wn = sq.emit("a", wneg, after=[prev_sel])
            sq.set_floor()

            xng2 = psb("xng2", [128, 2, NCH, TT], BF16)
            gb2 = psb("gb2", [128, TT], BF16)
            xres = psb("xres", [128, 2, NCH, TT], F32)
            sres = psb("sres", [128, 2, NCH, TT], BF16)
            onew = psb("onew", [128, NCH, TT], F32)
            fps2 = ppt("fps2", [128, 2, TT], F32)
            xn_r2 = xn_h.rearrange("(c p) t -> p c t", p=128)
            xi_r = x_in.rearrange("(c p) t -> p c t", p=128)
            xo_r = x_out.rearrange("(c p) t -> p c t", p=128)

            prev_gev = [None, None]
            prev_resid = None
            prev_xst = None
            xld_user = [None, None]
            out_stores = []
            for i in range(NTT):
                dep = [] if xld_user[i % 2] is None else [xld_user[i % 2]]
                o_lxn = sq.dma("ds", xng2[:, i % 2], xn_r2[:, :, i * TT:(i + 1) * TT], after=dep)
                o_lx = sq.dma("ds", xres[:, i % 2], xi_r[:, :, i * TT:(i + 1) * TT], after=dep)
                o_ls = sq.dma("ds", sres[:, i % 2], s_r[:, :, i * TT:(i + 1) * TT], after=dep)
                res_ops = []
                for oc in range(NCH):
                    pb = (i * NCH + oc) % 2

                    def fmm(t, oc=oc, pb=pb, i=i):
                        real = oc < 4
                        ch = oc % 4
                        ins = None
                        for dc in range(4):
                            t.matmul(fps2[:, pb], wq_sb[:, 0 if real else 1, dc, ch * 128:(ch + 1) * 128],
                                     xng2[:, i % 2, dc, :], start=(dc == 0), stop=False)
                        for dc in range(4):
                            ins = t.matmul(fps2[:, pb], wq_sb[:, 2 if real else 0, dc, ch * 128:(ch + 1) * 128],
                                           xng2[:, i % 2, 4 + dc, :], start=False, stop=(dc == 3))
                        return ins
                    fdep = [o_lxn, o_wn]
                    if prev_gev[pb] is not None:
                        fdep.append(prev_gev[pb])
                    o_fm = sq.emit("t", fmm, after=fdep)

                    def gev(a, oc=oc, pb=pb):
                        return a.activation(out=gb2[:], in_=fps2[:, pb], func=AF.Silu,
                                            bias=br_sb[:, oc:oc + 1])
                    gdep2 = [o_fm]
                    if prev_resid is not None:
                        gdep2.append(prev_resid)
                    o_gv = sq.emit("a", gev, after=gdep2)
                    prev_gev[pb] = o_gv

                    def resid(v, oc=oc, i=i):
                        v.tensor_tensor(out=onew[:, oc], in0=sres[:, i % 2, oc], in1=gb2[:], op=ALU.mult)
                        return v.tensor_tensor(out=onew[:, oc], in0=onew[:, oc], in1=xres[:, i % 2, oc], op=ALU.add)
                    rdep = [o_gv, o_lx, o_ls]
                    if prev_xst is not None:
                        rdep.append(prev_xst)
                    o_r = sq.emit("v", resid, after=rdep)
                    prev_resid = o_r
                    res_ops.append(o_r)
                xld_user[i % 2] = res_ops[-1]
                prev_xst = sq.dma("dg", xo_r[:, :, i * TT:(i + 1) * TT], onew[:], after=res_ops)
                out_stores.append(prev_xst)

            if ctx is None:
                sq.finalize(block, sems, outputs_done=out_stores)
            else:
                ctx["out_stores"] = out_stores
                sq.set_floor()
    return nc


def l0_inmap(xcore_fm, I):
    RD = 16
    inv = (1.0 / 10000.0 ** (np.arange(0, RD, 2, dtype=np.float64) / RD)).astype(np.float64)
    t = np.arange(S, dtype=np.float64)
    fr = np.outer(inv, t)
    cos8 = np.cos(fr); sin8 = np.sin(fr)
    cosT = np.zeros((128, S), np.float32)
    sinT = np.zeros((128, S), np.float32)
    for b0 in (0, 64):
        cosT[b0 + 0:b0 + 8] = cos8
        cosT[b0 + 8:b0 + 16] = cos8
        sinT[b0 + 0:b0 + 8] = -sin8
        sinT[b0 + 8:b0 + 16] = sin8
    kk, qq = np.meshgrid(np.arange(128), np.arange(128), indexing="ij")
    wmd = ((qq - kk >= 0) & (qq - kk <= 15)).astype(np.float32)
    wms = ((qq + 128 - kk >= 0) & (qq + 128 - kk <= 15)).astype(np.float32)
    cmask = np.zeros((128, 4, 512), np.float32)
    for mi in range(4):
        kx, qx = np.meshgrid(np.arange(128) + mi * 128, np.arange(512), indexing="ij")
        cmask[:, mi, :] = (kx <= qx).astype(np.float32)
    seld = np.zeros((32, 512), np.float32)
    for dd in range(32):
        for g in range(16):
            seld[dd, (dd * 16 + g) % 512] = 0.0
    seld = np.zeros((32, 512), np.float32)
    for r in range(512):
        dd = r // 16
        seld[dd, r] = 1.0
    selg = np.zeros((16, 128), np.float32)
    for r in range(128):
        selg[r % 16, r] = 1.0
    G1R = I["L0_ffn_G1_R"][0]
    G2R = I["L0_ffn_G2_R"]
    G1I = I["L0_ffn_G1_I"][0]
    G2I = I["L0_ffn_G2_I"]
    g1 = np.stack([np.ascontiguousarray(G1R.transpose(1, 2, 0)),
                   np.ascontiguousarray(G1I.transpose(1, 2, 0))])
    g2 = np.stack([np.ascontiguousarray(G2R[:, :, 0, :].transpose(0, 2, 1)),
                   np.ascontiguousarray(G2I[:, :, 0, :].transpose(0, 2, 1))])
    bri = np.concatenate([I["L0_ffn_bias_real"], I["L0_ffn_bias_imag"]])
    return {
        "x_in": xcore_fm,
        "wqkvT": np.ascontiguousarray(I["L0_attn_Wqkv"].T),
        "bqkvf": fm(I["L0_attn_bqkv"][0:2048]),
        "bvrow": np.ascontiguousarray(I["L0_attn_bqkv"][2048:3072].reshape(1, D)),
        "wqaT": np.ascontiguousarray(I["L0_attn_Wqa"].T),
        "bqaf": fm(I["L0_attn_bqa"]),
        "wkaT": np.ascontiguousarray(I["L0_attn_Wka"].T),
        "bkaf": fm(I["L0_attn_bka"]),
        "wdT": np.ascontiguousarray(I["L0_attn_Wd"].T),
        "bdf": fm(I["L0_attn_bd"]),
        "g1h": g1.astype(np.float32),
        "g2h": g2.astype(np.float32),
        "brf": fm(bri),
        "seld_h": seld.astype(BF),
        "selg_h": selg.astype(BF),
        "cosT_h": cosT.astype(BF),
        "sinT_h": sinT.astype(BF),
        "wmaskd_h": wmd.astype(BF),
        "wmasks_h": wms.astype(BF),
        "cmask_h": cmask.astype(BF),
        "iden_h": np.eye(128, dtype=np.float32).astype(BF),
        "onescol_h": np.ones((1, 128), np.float32).astype(BF),
        "ones128_h": np.ones((128, 1), np.float32),
        "ones1_h": np.ones((1, 128), np.float32),
        "onesf_h": np.ones((128, 128), np.float32),
    }



def build_all():
    from contextlib import ExitStack
    nc0 = bass.Bass("TRN2", target_bir_lowering=False, debug=False)
    x0h = nc0.dram_tensor("x_in", [D, S], F32, kind="ExternalInput")
    x1h = nc0.dram_tensor("ix1", [D, S], F32, kind="Internal")
    x2h = nc0.dram_tensor("ix2", [D, S], F32, kind="Internal")
    x3h = nc0.dram_tensor("ix3", [D, S], F32, kind="Internal")
    xoh = nc0.dram_tensor("x_out", [D, S], F32, kind="ExternalOutput")
    stack = ExitStack()
    with stack:
        sems = alloc_sems(nc0, stack)
        block = stack.enter_context(nc0.Block())
        sq = Seq(nc0)
        ctx = {"sems": sems, "block": block, "sq": sq}
        ios = [(x0h, x1h, "A_", build_l0), (x1h, x2h, "B_", build_l1),
               (x2h, x3h, "C_", build_l2), (x3h, xoh, "D_", build_l3)]
        for xin, xout, pf, bfn in ios:
            ctx["nc"] = NCProxy(nc0, pf, xin, xout)
            bfn(ctx=ctx)
            sq.set_floor()
        sq.finalize(block, sems, outputs_done=ctx["out_stores"])
    return nc0


def all_inmap(xcore_fm, I):
    m = {"x_in": xcore_fm}
    for pf, f in [("A_", l0_inmap), ("B_", l1_inmap), ("C_", l2_inmap), ("D_", l3_inmap)]:
        sub = f(xcore_fm, I)
        for k, v in sub.items():
            if k != "x_in":
                m[pf + k] = v
    return m



_BUILDS = {}


def _get_build(name, fn):
    if name not in _BUILDS:
        _BUILDS[name] = fn()
    return _BUILDS[name]


FUSED = True


def kernel(**inputs):
    I = inputs
    x = np.asarray(I["L0_attn_hidden_states"], dtype=np.float32)
    ncores = 8
    x_fm = [np.ascontiguousarray(x[c].T) for c in range(ncores)]

    if FUSED:
        nc = _get_build("all", build_all)
        in_maps = [all_inmap(x_fm[c], I) for c in range(ncores)]
        res = run_bass_kernel_spmd(nc, in_maps, core_ids=list(range(ncores)))
        x_fm = [np.asarray(res.results[c]["x_out"]) for c in range(ncores)]
    else:
        for name, build, inmap in [
            ("l0", build_l0, l0_inmap),
            ("l1", build_l1, l1_inmap),
            ("l2", build_l2, l2_inmap),
            ("l3", build_l3, l3_inmap),
        ]:
            nc = _get_build(name, build)
            in_maps = [inmap(x_fm[c], I) for c in range(ncores)]
            res = run_bass_kernel_spmd(nc, in_maps, core_ids=list(range(ncores)))
            x_fm = [np.asarray(res.results[c]["x_out"]) for c in range(ncores)]

    out = np.stack([xc.T for xc in x_fm]).astype(np.float32)
    return out
